# version 02
# baseline (speedup 1.0000x reference)
import contextlib
import numpy as np
import concourse.bass as bass
import concourse.mybir as mybir
from concourse.bass_utils import run_bass_kernel_spmd

F32 = mybir.dt.float32
BF16 = mybir.dt.bfloat16
I32 = mybir.dt.int32
AF = mybir.ActivationFunctionType
ALU = mybir.AluOpType

ENGINES = ("sync", "vector", "scalar", "tensor", "gpsimd")

D = 1024
H = 16
DH = 64



class _Op:
    __slots__ = ("eng", "fn", "after", "dma_sem", "needs_inc", "inc_val", "pos", "tag", "dma_floor")

    def __init__(self, eng, fn, after, dma_sem, tag):
        self.eng = eng
        self.fn = fn
        self.after = after
        self.dma_sem = dma_sem
        self.needs_inc = False
        self.inc_val = None
        self.pos = None
        self.tag = tag
        self.dma_floor = None


class Tok:
    __slots__ = ("op",)

    def __init__(self, op):
        self.op = op


def _key(b):
    if isinstance(b, tuple):
        t, k = b
        return (id(t), k)
    return (id(b), "")


class MK:
    def __init__(self, nc):
        self.nc = nc
        self.eng_ops = {e: [] for e in ENGINES}
        self.dma_sem_names = []
        self.state = {}
        self.dma_counts = {}

    def _deps(self, reads, writes):
        deps = []
        for b in reads:
            st = self.state.get(_key(b))
            if st and st["w"] is not None:
                deps.append(st["w"])
        for b in writes:
            st = self.state.get(_key(b))
            if st:
                if st["w"] is not None:
                    deps.append(st["w"])
                deps.extend(st["r"])
        return deps

    def _commit(self, tok, reads, writes):
        for b in reads:
            st = self.state.setdefault(_key(b), {"w": None, "r": []})
            st["r"].append(tok)
        for b in writes:
            st = self.state.setdefault(_key(b), {"w": None, "r": []})
            st["w"] = tok
            st["r"] = []

    def op(self, eng, fn, reads=(), writes=(), also=(), tag="", track=True):
        after = self._deps(reads, writes) + [t for t in also if t is not None]
        o = _Op(eng, fn, after, None, tag)
        o.pos = len(self.eng_ops[eng])
        if any(t.op.dma_sem is not None for t in after):
            o.dma_floor = dict(self.dma_counts)
        self.eng_ops[eng].append(o)
        tok = Tok(o)
        if track:
            self._commit(tok, reads, writes)
        return tok

    def raw_op(self, eng, fn, also=(), tag=""):
        return self.op(eng, fn, also=[t for t in also if t is not None], tag=tag, track=False)

    def dma(self, eng, sem, out, in_, reads=(), writes=(), also=(), accum=False, tag="", **kw):
        if sem not in self.dma_sem_names:
            self.dma_sem_names.append(sem)
        if accum:
            kw = dict(kw)
            kw["accum_op"] = ALU.add
            assert eng == "gpsimd", "accum dma must use gpsimd (software DGE)"
        reads = list(reads)
        writes = list(writes)
        after = self._deps(reads, writes) + [t for t in also if t is not None]

        def fn(e, out=out, in_=in_, kw=kw):
            return e.dma_start(out=out, in_=in_, **kw)

        o = _Op(eng, fn, after, sem, tag)
        o.pos = len(self.eng_ops[eng])
        if any(t.op.dma_sem is not None for t in after):
            o.dma_floor = dict(self.dma_counts)
        self.dma_counts[sem] = self.dma_counts.get(sem, 0) + 16
        o.inc_val = self.dma_counts[sem]
        self.eng_ops[eng].append(o)
        tok = Tok(o)
        self._commit(tok, reads, writes)
        return tok

    def mm_group(self, mms, reads=(), writes=(), also=(), tag=""):
        toks = []
        for i, fn in enumerate(mms):
            if i == 0:
                t = self.op("tensor", fn, reads=reads, writes=writes, also=also, tag=tag + "_0")
            else:
                t = self.raw_op("tensor", fn, tag=f"{tag}_{i}")
            toks.append(t)
        for b in writes:
            st = self.state.setdefault(_key(b), {"w": None, "r": []})
            st["w"] = toks[-1]
            st["r"] = []
        if len(toks) > 1:
            for b in reads:
                st = self.state.setdefault(_key(b), {"w": None, "r": []})
                st["r"].append(toks[-1])
        return toks[-1]

    def writer_of(self, b):
        st = self.state.get(_key(b))
        return st["w"] if st else None

    def build(self, stack):
        nc = self.nc
        for eng in ENGINES:
            for o in self.eng_ops[eng]:
                for t in o.after:
                    if t.op.dma_sem is None and t.op.eng != o.eng:
                        t.op.needs_inc = True
        eng_counts = {e: 0 for e in ENGINES}
        for eng in ENGINES:
            for o in self.eng_ops[eng]:
                if o.dma_sem is None and o.needs_inc:
                    eng_counts[eng] += 1
                    o.inc_val = eng_counts[eng]
        eng_sems = {e: stack.enter_context(nc.semaphore(f"S_{e}")) for e in ENGINES if self.eng_ops[e]}
        dma_sems = {s: stack.enter_context(nc.semaphore(f"D_{s}")) for s in self.dma_sem_names}
        block = stack.enter_context(nc.Block())
        mkself = self

        def make_body(eng):
            def body(e):
                highwater = {}
                drained_upto = 0
                cur_pos = 0
                for o in mkself.eng_ops[eng]:
                    need_drain = False
                    for t in o.after:
                        top = t.op
                        if top.dma_sem is not None:
                            key = ("d", top.dma_sem)
                            val = top.inc_val
                            if o.dma_floor is not None:
                                val = max(val, o.dma_floor.get(top.dma_sem, val))
                            if highwater.get(key, 0) < val:
                                e.wait_ge(dma_sems[top.dma_sem], val)
                                highwater[key] = val
                        elif top.eng != eng:
                            key = ("e", top.eng)
                            val = top.inc_val
                            if highwater.get(key, 0) < val:
                                e.wait_ge(eng_sems[top.eng], val)
                                highwater[key] = val
                        else:
                            if top.pos >= drained_upto:
                                need_drain = True
                    if need_drain and eng != "tensor":
                        e.drain()
                        drained_upto = cur_pos
                    inst = o.fn(e)
                    if o.dma_sem is not None:
                        inst.then_inc(dma_sems[o.dma_sem], 16)
                    elif o.needs_inc:
                        inst.then_inc(eng_sems[eng], 1)
                    cur_pos += 1

            return body

        for eng in ENGINES:
            if not self.eng_ops[eng]:
                continue
            getattr(block, eng)(make_body(eng))


class Cyc:
    def __init__(self, items):
        self.items = list(items)

    def __getitem__(self, i):
        return self.items[i % len(self.items)]




def emit_load_weight_bf16(mk, P, src_dram, dst_sb, rows, cols, tag):
    nch = rows // 128
    for c in range(nch):
        stg = P["wstage"][c % 2]
        mk.dma("sync", f"wld{c%2}", stg[:, 0:cols], src_dram[c * 128:(c + 1) * 128, :],
               writes=[stg], tag=f"wld_{tag}_{c}")
        mk.op("vector",
              lambda e, dst_sb=dst_sb, c=c, stg=stg, cols=cols: e.tensor_copy(
                  dst_sb[:, c * cols:(c + 1) * cols], stg[:, 0:cols]),
              reads=[stg], writes=[(dst_sb, f"c{c}")], tag=f"wcv_{tag}_{c}")


def emit_bcast_row(mk, P, psum, src_handle, src_cols, dst_sb, tag, nparts=128):
    nch = (src_cols + 511) // 512
    t = None
    for i in range(nch):
        lo, hi = i * 512, min(src_cols, i * 512 + 512)
        mk.op("tensor",
              lambda e, lo=lo, hi=hi: e.matmul(
                  psum[0:nparts, 0:hi - lo], P["s_ones128"][0:1, 0:nparts], src_handle[0:1, lo:hi],
                  start=True, stop=True),
              reads=[P["s_ones128"], src_handle], writes=[psum], tag=f"{tag}_mm{i}")
        t = mk.op("vector",
                  lambda e, lo=lo, hi=hi: e.tensor_copy(dst_sb[0:nparts, lo:hi], psum[0:nparts, 0:hi - lo]),
                  reads=[psum], writes=[(dst_sb, f"bc{i}")], tag=f"{tag}_ev{i}")
    st = mk.state.setdefault(_key(dst_sb), {"w": None, "r": []})
    st["w"] = t
    st["r"] = []
    return t




def emit_phaseA(mk, P, NCH):
    T = NCH * 128
    PP0, PP1 = P["PP0"], P["PP1"]
    PA, PO, PC, PY, PS, PBT = P["PA"], P["PO"], P["PC"], P["PY"], P["PS"], P["PBT"]
    ident = P["s_ident"]
    statL = P["statL"]
    scrA, scrB, scrH = P["scrA"], P["scrB"], P["scrH"]

    emit_load_weight_bf16(mk, P, P["L0_attn_WqT"], P["WqT_sb"], D, D, "WqT")
    emit_load_weight_bf16(mk, P, P["L0_attn_WkT"], P["WkT_sb"], D, D, "WkT")
    emit_load_weight_bf16(mk, P, P["L0_attn_WvT"], P["WvT_sb"], D, D, "WvT")
    emit_load_weight_bf16(mk, P, P["L0_attn_WpT"], P["WpT_sb"], D, D, "WpT")
    emit_load_weight_bf16(mk, P, P["L0_ffn_wq"], P["wq_sb"], D, D, "wq")
    emit_load_weight_bf16(mk, P, P["L0_ffn_wk"], P["wk_sb"], D, D, "wk")
    emit_load_weight_bf16(mk, P, P["L0_ffn_wv"], P["wv_sb"], D, D, "wv")

    for nm, dst in [("identbf", "s_ident"), ("trimask", "s_tri"), ("shiftE", "s_shE"),
                    ("shiftO", "s_shO"), ("ones1x128", "s_ones128"),
                    ("maskA", "s_maskA"), ("maskB", "s_maskB"), ("identf32", "s_identf")]:
        mk.dma("sync", "cst", P[dst][:, :], P[nm], writes=[P[dst]], tag=f"c_{nm}")
    mk.dma("sync", "crow", P["scrH"][0:1, 0:D], P["L0_attn_bp"], writes=[P["scrH"]], tag="c_bp")
    emit_bcast_row(mk, P, PP0, P["scrH"], D, P["scrA"], "bcbp")
    mk.op("vector", lambda e: e.tensor_scalar(P["s_bpA"][:, :], P["scrA"][:, :], P["s_maskA"][:, 0:1], None, ALU.mult),
          reads=[P["scrA"], P["s_maskA"]], writes=[P["s_bpA"]], tag="bpA")
    mk.op("vector", lambda e: e.tensor_scalar(P["s_bpB"][:, :], P["scrA"][:, :], P["s_maskB"][:, 0:1], None, ALU.mult),
          reads=[P["scrA"], P["s_maskB"]], writes=[P["s_bpB"]], tag="bpB")
    mk.dma("sync", "crow", P["scrH"][0:1, 0:D], P["L0_ffn_norm1_w"], writes=[P["scrH"]], tag="c_n1")
    emit_bcast_row(mk, P, PP1, P["scrH"], D, P["s_n1b"], "bcn1")

    for hh in range(H):
        mk.op("vector", lambda e, hh=hh: e.memset(P["S32"][hh][:, :], 0.0), writes=[P["S32"][hh]], tag=f"S0_{hh}")
        mk.op("vector", lambda e, hh=hh: e.memset(P["S16"][hh][:, :], 0.0), writes=[P["S16"][hh]], tag=f"Sb0_{hh}")
    mk.op("vector", lambda e: e.memset(P["qkss"][:, :], 0.0), writes=[P["qkss"]], tag="qkss0")

    c_sx = Cyc([P["sx0"], P["sx1"]])
    c_xn = Cyc([P["xn0"]])
    c_xnsel = Cyc([P["xnsel0"]])
    c_attmp = Cyc([P["attmp0"]])
    c_vaug = Cyc([P["vaug0"]])
    c_vaugf = Cyc([P["vaugf0"]])
    c_Pm = Cyc([P["Pm0"]])
    c_yh = Cyc([P[f"yh{i}"] for i in range(4)])
    c_rb = Cyc([P["rb0"]])
    c_yfm = Cyc([P["yfm0"], P["yfm1"]])
    c_qp = Cyc([P["qp0"]])
    c_kp = Cyc([P["kp0"]])
    c_hb = Cyc([P["hb0"]])
    c_hfm = Cyc([P["hfm0"], P["hfm1"]])
    c_xnf = Cyc([P["xnf0"], P["xnf1"]])
    c_xnb = Cyc([P["xnb0"]])
    c_qpf = Cyc([P["qpfa0"]])
    c_kpf = Cyc([P["kpfa0"]])
    c_kfa = Cyc([P["kfa0"]])
    c_qfa = Cyc([P["qfa0"]])

    def proj_tokmajor(g, src_fm, W_sb, dst_bf, featmap, tag):
        for half in range(2):
            pp = PP0 if half == 0 else PP1
            mms = [
                (lambda e, c=c, half=half, pp=pp, src_fm=src_fm, W_sb=W_sb: e.matmul(
                    pp[:, 0:512], src_fm[:, c * 128:(c + 1) * 128],
                    W_sb[:, c * 1024 + half * 512: c * 1024 + half * 512 + 512],
                    start=(c == 0), stop=(c == 7)))
                for c in range(8)
            ]
            mk.mm_group(mms, reads=[src_fm, W_sb], writes=[pp], tag=f"{tag}mm{half}_g{g}")
            sl = slice(half * 512, half * 512 + 512)
            if not featmap:
                mk.op("vector", lambda e, pp=pp, sl=sl, dst_bf=dst_bf: e.tensor_copy(dst_bf[:, sl], pp[:, 0:512]),
                      reads=[pp], writes=[(dst_bf, f"h{half}")], tag=f"{tag}ev{half}_g{g}")
            else:
                mk.op("vector", lambda e, pp=pp, sl=sl: e.tensor_scalar(scrA[:, sl], pp[:, 0:512], 0.0, None, ALU.min),
                      reads=[pp], writes=[(scrA, half)], tag=f"{tag}f1{half}_g{g}")
                mk.op("scalar", lambda e, sl=sl: e.activation(scrB[:, 0:512], scrA[:, sl], AF.Exp),
                      reads=[(scrA, half)], writes=[scrB], tag=f"{tag}f2{half}_g{g}")
                mk.op("vector", lambda e, pp=pp, sl=sl: e.tensor_scalar(scrA[:, sl], pp[:, 0:512], 0.0, None, ALU.max),
                      reads=[pp], writes=[(scrA, half)], tag=f"{tag}f3{half}_g{g}")
                mk.op("vector", lambda e, sl=sl, dst_bf=dst_bf: e.tensor_tensor(dst_bf[:, sl], scrA[:, sl], scrB[:, 0:512], ALU.add),
                      reads=[(scrA, half), scrB], writes=[(dst_bf, f"h{half}")],
                      tag=f"{tag}f4{half}_g{g}")

    def transpose_128(g, src_bf, dst_fm, tag):
        last = None
        for c in range(8):
            last = mk.op("tensor",
                         lambda e, c=c, src_bf=src_bf: e.transpose(PBT[:, c * 128:(c + 1) * 128], src_bf[:, c * 128:(c + 1) * 128], ident[:, :]),
                         reads=[src_bf, ident], writes=[PBT] if c == 0 else [], tag=f"{tag}tr{c}_g{g}", track=(c == 0))
        st = mk.state[_key(PBT)]
        st["w"] = last
        st["r"] = []
        mk.op("vector", lambda e, dst_fm=dst_fm: e.tensor_copy(dst_fm[:, :], PBT[:, 0:1024]),
              reads=[PBT], writes=[dst_fm], tag=f"{tag}trev_g{g}")

    def head_transposes(g, src_tm, src_keys, dst_all, tag):
        for half in range(2):
            last = None
            first = True
            for hl in range(8):
                hh = half * 8 + hl
                rd = [(src_tm, src_keys[half]), ident] if first else []
                last = mk.op("tensor",
                             lambda e, hh=hh, hl=hl, src_tm=src_tm: e.transpose(
                                 PBT[0:64, hl * 128:(hl + 1) * 128], src_tm[:, hh * 64:(hh + 1) * 64], ident[:, :]),
                             reads=rd + ([PBT] if first else []), writes=[PBT] if first else [],
                             tag=f"{tag}ht{hh}_g{g}", track=first)
                first = False
            st = mk.state[_key(PBT)]
            st["w"] = last
            st["r"] = []
            mk.op("vector",
                  lambda e, half=half, dst_all=dst_all: e.tensor_copy(
                      dst_all[:, half * 1024:(half + 1) * 1024], PBT[0:64, 0:1024]),
                  reads=[PBT], writes=[(dst_all, half)], tag=f"{tag}hte{half}_g{g}")
        st = mk.state.setdefault(_key(dst_all), {"w": None, "r": []})
        st["w"] = mk.state[_key((dst_all, 1))]["w"]
        st["r"] = []

    for g in range(NCH):
        own_idx = g % (NCH // 2)
        first_half = g < NCH // 2
        mask_ap = P["s_maskA"] if first_half else P["s_maskB"]
        bp_masked = P["s_bpA"] if first_half else P["s_bpB"]

        sx = c_sx[g]
        mk.dma("sync", f"ldx{g%2}", sx[:, :], P["XIN"][g * 128:(g + 1) * 128, :], writes=[sx], tag=f"x_g{g}")

        mk.op("vector", lambda e, sx=sx: e.reduce_sum(statL[:, 0:1], sx[:, :], axis=mybir.AxisListType.X),
              reads=[sx], writes=[(statL, 0)], tag=f"s1_g{g}")
        mk.op("scalar", lambda e, sx=sx: e.square(scrH[:, 0:1024], sx[:, :]),
              reads=[sx], writes=[scrH], tag=f"sq_g{g}")
        mk.op("vector", lambda e: e.reduce_sum(statL[:, 1:2], scrH[:, 0:1024], axis=mybir.AxisListType.X),
              reads=[scrH], writes=[(statL, 1)], tag=f"s2_g{g}")
        mk.op("vector", lambda e: e.tensor_scalar(statL[:, 2:3], statL[:, 0:1], -1.0 / D, None, ALU.mult),
              reads=[(statL, 0)], writes=[(statL, 2)], tag=f"mn_g{g}")
        mk.op("vector", lambda e: e.tensor_tensor(statL[:, 3:4], statL[:, 2:3], statL[:, 2:3], ALU.mult),
              reads=[(statL, 2)], writes=[(statL, 3)], tag=f"m2_g{g}")
        mk.op("vector", lambda e: e.tensor_scalar(statL[:, 4:5], statL[:, 1:2], 1.0 / D, statL[:, 3:4], ALU.mult, ALU.subtract),
              reads=[(statL, 1), (statL, 3)], writes=[(statL, 4)], tag=f"va_g{g}")
        mk.op("vector", lambda e: e.tensor_scalar(statL[:, 4:5], statL[:, 4:5], 1e-5, None, ALU.add),
              reads=[(statL, 4)], writes=[(statL, 4)], tag=f"vae_g{g}")
        mk.op("scalar", lambda e: e.activation(statL[:, 5:6], statL[:, 4:5], AF.Ln),
              reads=[(statL, 4)], writes=[(statL, 5)], tag=f"ln_g{g}")
        mk.op("scalar", lambda e: e.activation(statL[:, 6:7], statL[:, 5:6], AF.Exp, scale=-0.5),
              reads=[(statL, 5)], writes=[(statL, 6)], tag=f"rs_g{g}")
        mk.op("vector", lambda e: e.tensor_tensor(statL[:, 7:8], statL[:, 2:3], statL[:, 6:7], ALU.mult),
              reads=[(statL, 2), (statL, 6)], writes=[(statL, 7)], tag=f"bb_g{g}")
        xn = c_xn[g]
        mk.op("vector", lambda e, sx=sx, xn=xn: e.tensor_scalar(xn[:, :], sx[:, :], statL[:, 6:7], statL[:, 7:8], ALU.mult, ALU.add),
              reads=[sx, (statL, 6), (statL, 7)], writes=[xn], tag=f"xn_g{g}")

        xnb = c_xnb[g]
        mk.op("vector", lambda e, xn=xn, xnb=xnb: e.tensor_copy(xnb[:, :], xn[:, :]),
              reads=[xn], writes=[xnb], tag=f"xnb_g{g}")
        xnf = c_xnf[g]
        transpose_128(g, xnb, xnf, "xn")

        xnsel = c_xnsel[g]
        mk.op("vector", lambda e, xn=xn, xnsel=xnsel, mask_ap=mask_ap: e.tensor_scalar(
            xnsel[:, :], xn[:, :], mask_ap[:, 0:1], None, ALU.mult),
              reads=[xn, mask_ap], writes=[xnsel], tag=f"xs_g{g}")
        mk.dma("gpsimd", f"xno{g%2}", P["XN0"][own_idx * 128:(own_idx + 1) * 128, :], xnsel[:, :],
               reads=[xnsel], writes=[(P["XN0_t"], own_idx)], accum=not first_half, tag=f"xsd_g{g}")

        qp = c_qp[g]
        proj_tokmajor(g, xnf, P["WqT_sb"], qp, True, "q_")
        kp = c_kp[g]
        proj_tokmajor(g, xnf, P["WkT_sb"], kp, True, "k_")
        qpfa = c_qpf[g]
        head_transposes(g, qp, ["h0", "h1"], qpfa, "qp")
        kpfa = c_kpf[g]
        head_transposes(g, kp, ["h0", "h1"], kpfa, "kp")

        vaug = c_vaug[g]
        mk.op("vector", lambda e, vaug=vaug: e.memset(vaug[:, :], 1.0),
              writes=[vaug, (vaug, "v0"), (vaug, "v1")], tag=f"vm_g{g}")
        for half in range(2):
            pp = PP0 if half == 0 else PP1
            mms = [
                (lambda e, c=c, pp=pp, half=half, xnf=xnf: e.matmul(
                    pp[:, 0:512], xnf[:, c * 128:(c + 1) * 128],
                    P["WvT_sb"][:, c * 1024 + half * 512: c * 1024 + half * 512 + 512],
                    start=(c == 0), stop=(c == 7)))
                for c in range(8)
            ]
            mk.mm_group(mms, reads=[xnf, P["WvT_sb"]], writes=[pp], tag=f"vmm{half}_g{g}")
            mk.op("vector",
                  lambda e, pp=pp, half=half, vaug=vaug: e.tensor_copy(
                      vaug[:, :].rearrange("p (h s) -> p h s", h=H)[:, 8 * half:8 * half + 8, 0:64],
                      pp[:, 0:512].rearrange("p (h s) -> p h s", h=8)),
                  reads=[pp, vaug], writes=[(vaug, f"v{half}")], tag=f"vev{half}_g{g}")

        yfm = c_yfm[g]
        y_pair = {}
        for hh in range(H):
            qpf_ap = qpfa[:, hh * 128:(hh + 1) * 128]
            kpf_ap = kpfa[:, hh * 128:(hh + 1) * 128]
            mk.op("tensor", lambda e, kpf_ap=kpf_ap, qpf_ap=qpf_ap: e.matmul(
                PA[0:128, 0:128], kpf_ap, qpf_ap, start=True, stop=True),
                  reads=[(kpfa, hh // 8), (qpfa, hh // 8)], writes=[PA], tag=f"at_h{hh}_g{g}")
            Pm = c_Pm[g * H + hh]
            mk.op("vector", lambda e, Pm=Pm: e.tensor_tensor(Pm[:, :], PA[0:128, 0:128], P["s_tri"][:, :], ALU.mult),
                  reads=[PA, P["s_tri"]], writes=[Pm], tag=f"pm_h{hh}_g{g}")
            mms = [
                (lambda e, hh=hh, vaug=vaug, Pm=Pm: e.matmul(
                    PO[0:65, 0:128], vaug[:, hh * 65:(hh + 1) * 65], Pm[:, :], start=True, stop=False)),
                (lambda e, hh=hh, qpf_ap=qpf_ap: e.matmul(
                    PO[0:65, 0:128], P["S16"][hh][:, :], qpf_ap, start=False, stop=True)),
            ]
            mk.mm_group(mms, reads=[(vaug, f"v{hh//8}"), Pm, P["S16"][hh], (qpfa, hh // 8)], writes=[PO], tag=f"o65_h{hh}_g{g}")
            mk.op("vector", lambda e: e.tensor_scalar(P["den"][0:1, 0:128], PO[64:65, 0:128], 1e-6, None, ALU.add),
                  reads=[PO], writes=[P["den"]], tag=f"dn_h{hh}_g{g}")
            mk.op("vector", lambda e: e.reciprocal(P["rec"][0:1, 0:128], P["den"][0:1, 0:128]),
                  reads=[P["den"]], writes=[P["rec"]], tag=f"rc_h{hh}_g{g}")
            mk.op("tensor", lambda e: e.matmul(PC[0:64, 0:128], P["s_ones64"][0:1, 0:64], P["rec"][0:1, 0:128], start=True, stop=True),
                  reads=[P["rec"], P["s_ones64"]], writes=[PC], tag=f"bc_h{hh}_g{g}")
            rb = c_rb[g * H + hh]
            mk.op("vector", lambda e, rb=rb: e.tensor_copy(rb[:, :], PC[0:64, 0:128]),
                  reads=[PC], writes=[rb], tag=f"rbe_h{hh}_g{g}")
            yh = c_yh[g * H + hh]
            mk.op("vector", lambda e, yh=yh, rb=rb: e.tensor_tensor(yh[:, :], PO[0:64, 0:128], rb[:, :], ALU.mult),
                  reads=[PO, rb], writes=[yh], tag=f"y_h{hh}_g{g}")
            mk.op("tensor", lambda e, hh=hh, kp=kp, vaug=vaug: e.matmul(
                PS[0:64, 0:65], kp[:, hh * 64:(hh + 1) * 64], vaug[:, hh * 65:(hh + 1) * 65], start=True, stop=True),
                  reads=[(kp, f"h{hh//8}"), (vaug, f"v{hh//8}")], writes=[PS], tag=f"su_h{hh}_g{g}")
            mk.op("vector", lambda e, hh=hh: e.tensor_tensor(P["S32"][hh][:, :], P["S32"][hh][:, :], PS[0:64, 0:65], ALU.add),
                  reads=[PS, P["S32"][hh]], writes=[P["S32"][hh]], tag=f"sa_h{hh}_g{g}")
            mk.op("scalar", lambda e, hh=hh: e.copy(P["S16"][hh][:, :], P["S32"][hh][:, :]),
                  reads=[P["S32"][hh]], writes=[P["S16"][hh]], tag=f"sc_h{hh}_g{g}")
            y_pair[hh] = yh
            if hh % 2 == 1:
                pi = hh // 2
                ye, yo = y_pair[hh - 1], y_pair[hh]
                mms = [
                    (lambda e, ye=ye: e.matmul(PY[0:128, 0:128], P["s_shE"][:, :], ye[:, :], start=True, stop=False)),
                    (lambda e, yo=yo: e.matmul(PY[0:128, 0:128], P["s_shO"][:, :], yo[:, :], start=False, stop=True)),
                ]
                mk.mm_group(mms, reads=[ye, yo, P["s_shE"], P["s_shO"]], writes=[PY], tag=f"sh_p{pi}_g{g}")
                mk.op("vector", lambda e, pi=pi, yfm=yfm: e.tensor_copy(yfm[:, pi * 128:(pi + 1) * 128], PY[0:128, 0:128]),
                      reads=[PY], writes=[(yfm, f"p{pi}")], tag=f"she_p{pi}_g{g}")

        attmp = c_attmp[g]
        for half in range(2):
            pp = PP0 if half == 0 else PP1
            mms = [
                (lambda e, c=c, pp=pp, half=half, yfm=yfm: e.matmul(
                    pp[:, 0:512], yfm[:, c * 128:(c + 1) * 128],
                    P["WpT_sb"][:, c * 1024 + half * 512: c * 1024 + half * 512 + 512],
                    start=(c == 0), stop=(c == 7)))
                for c in range(8)
            ]
            mk.mm_group(mms, reads=[(yfm, f"p{c2}") for c2 in range(8)] + [P["WpT_sb"]], writes=[pp], tag=f"opmm{half}_g{g}")
            sl = slice(half * 512, half * 512 + 512)
            mk.op("vector", lambda e, pp=pp, sl=sl, attmp=attmp, bp_masked=bp_masked, mask_ap=mask_ap: e.scalar_tensor_tensor(
                attmp[:, sl], pp[:, 0:512], mask_ap[:, 0:1], bp_masked[:, sl], ALU.mult, ALU.add),
                  reads=[pp, mask_ap, bp_masked], writes=[(attmp, half)], tag=f"ope{half}_g{g}")
        mk.dma("gpsimd", f"at0{g%2}", P["AT0"][own_idx * 128:(own_idx + 1) * 128, :], attmp[:, :],
               reads=[(attmp, 0), (attmp, 1)], writes=[(P["AT0_t"], own_idx)], accum=not first_half, tag=f"atd_g{g}")

        mk.op("scalar", lambda e, xn=xn: e.square(scrH[:, 0:1024], xn[:, :]),
              reads=[xn], writes=[scrH], tag=f"hsq_g{g}")
        mk.op("vector", lambda e: e.reduce_sum(statL[:, 0:1], scrH[:, 0:1024], axis=mybir.AxisListType.X),
              reads=[scrH], writes=[(statL, 0)], tag=f"hss_g{g}")
        mk.op("vector", lambda e: e.tensor_scalar(statL[:, 0:1], statL[:, 0:1], 1.0 / D, 1e-6, ALU.mult, ALU.add),
              reads=[(statL, 0)], writes=[(statL, 0)], tag=f"hle_g{g}")
        mk.op("scalar", lambda e: e.activation(statL[:, 1:2], statL[:, 0:1], AF.Ln),
              reads=[(statL, 0)], writes=[(statL, 1)], tag=f"hln_g{g}")
        mk.op("scalar", lambda e: e.activation(statL[:, 2:3], statL[:, 1:2], AF.Exp, scale=-0.5),
              reads=[(statL, 1)], writes=[(statL, 2)], tag=f"hrs_g{g}")
        mk.op("vector", lambda e, xn=xn: e.tensor_scalar(scrH[:, 0:1024], xn[:, :], statL[:, 2:3], None, ALU.mult),
              reads=[xn, (statL, 2)], writes=[scrH], tag=f"h1_g{g}")
        hb = c_hb[g]
        mk.op("vector", lambda e, hb=hb: e.tensor_tensor(hb[:, :], scrH[:, 0:1024], P["s_n1b"][:, :], ALU.mult),
              reads=[scrH, P["s_n1b"]], writes=[hb], tag=f"hb_g{g}")
        hfm = c_hfm[g]
        transpose_128(g, hb, hfm, "h")

        kfa = c_kfa[g]
        proj_tokmajor(g, hfm, P["wk_sb"], P["qp0"], False, "fk_")
        head_transposes(g, P["qp0"], ["h0", "h1"], kfa, "fk")
        mk.dma("sync", f"kfd{g%2}", P["KF0"].rearrange("h p s -> p h s")[:, :, g * 128:(g + 1) * 128], kfa[:, :].rearrange("p (h s) -> p h s", h=H),
               reads=[(kfa, 0), (kfa, 1)], writes=[(P["KF0_t"], g)], tag=f"fkd_g{g}")
        mk.op("scalar", lambda e: e.square(P["scrA"][:, 0:1024], P["qp0"][:, :]),
              reads=[(P["qp0"], "h0"), (P["qp0"], "h1")], writes=[P["scrA"]], tag=f"ksq_g{g}")
        mk.op("vector", lambda e: e.tensor_reduce(statL[:, 8:9].rearrange("p a -> p a") if False else P["hss16"][:, 0:16], P["scrA"][:, 0:1024].rearrange("p (h d) -> p h d", d=64), axis=mybir.AxisListType.X, op=ALU.add),
              reads=[P["scrA"]], writes=[P["hss16"]], tag=f"kss_g{g}")
        mk.op("vector", lambda e: e.tensor_tensor(P["qkss"][:, 16:32], P["qkss"][:, 16:32], P["hss16"][:, 0:16], ALU.max),
              reads=[P["hss16"], P["qkss"]], writes=[P["qkss"]], tag=f"kssm_g{g}")
        qfa = c_qfa[g]
        proj_tokmajor(g, hfm, P["wq_sb"], P["qp0"], False, "fq_")
        head_transposes(g, P["qp0"], ["h0", "h1"], qfa, "fq")
        mk.op("vector", lambda e, qfa=qfa, mask_ap=mask_ap: e.tensor_scalar(
            qfa[:, :], qfa[:, :], mask_ap[0:64, 0:1], None, ALU.mult),
              reads=[(qfa, 0), (qfa, 1), mask_ap], writes=[(qfa, 0), (qfa, 1)], tag=f"fqm_g{g}")
        mk.dma("gpsimd", f"qfd{g%2}", P["QF0"].rearrange("h p s -> p h s")[:, :, own_idx * 128:(own_idx + 1) * 128], qfa[:, :].rearrange("p (h s) -> p h s", h=H),
               reads=[(qfa, 0), (qfa, 1)], writes=[(P["QF0_t"], own_idx)], accum=not first_half, tag=f"fqd_g{g}")
        mk.op("scalar", lambda e: e.square(P["scrA"][:, 0:1024], P["qp0"][:, :]),
              reads=[(P["qp0"], "h0"), (P["qp0"], "h1")], writes=[P["scrA"]], tag=f"qsq_g{g}")
        mk.op("vector", lambda e: e.tensor_reduce(P["hss16"][:, 0:16], P["scrA"][:, 0:1024].rearrange("p (h d) -> p h d", d=64), axis=mybir.AxisListType.X, op=ALU.add),
              reads=[P["scrA"]], writes=[P["hss16"]], tag=f"qss_g{g}")
        mk.op("vector", lambda e: e.tensor_tensor(P["qkss"][:, 0:16], P["qkss"][:, 0:16], P["hss16"][:, 0:16], ALU.max),
              reads=[P["hss16"], P["qkss"]], writes=[P["qkss"]], tag=f"qssm_g{g}")

        vaugf = c_vaugf[g]
        mk.op("vector", lambda e, vaugf=vaugf: e.memset(vaugf[:, :], 1.0),
              writes=[vaugf, (vaugf, "v0"), (vaugf, "v1")], tag=f"vfm_g{g}")
        for half in range(2):
            pp = PP0 if half == 0 else PP1
            mms = [
                (lambda e, c=c, pp=pp, half=half, hfm=hfm: e.matmul(
                    pp[:, 0:512], hfm[:, c * 128:(c + 1) * 128],
                    P["wv_sb"][:, c * 1024 + half * 512: c * 1024 + half * 512 + 512],
                    start=(c == 0), stop=(c == 7)))
                for c in range(8)
            ]
            mk.mm_group(mms, reads=[hfm, P["wv_sb"]], writes=[pp], tag=f"fvmm{half}_g{g}")
            mk.op("vector",
                  lambda e, pp=pp, half=half, vaugf=vaugf: e.tensor_copy(
                      vaugf[:, :].rearrange("p (h s) -> p h s", h=H)[:, 8 * half:8 * half + 8, 0:64],
                      pp[:, 0:512].rearrange("p (h s) -> p h s", h=8)),
                  reads=[pp, vaugf], writes=[(vaugf, f"v{half}")], tag=f"fve{half}_g{g}")
        mk.dma("sync", f"va0{g%2}", P["VA0"][g * 128:(g + 1) * 128, :], vaugf[:, :],
               reads=[(vaugf, "v0"), (vaugf, "v1")], writes=[(P["VA0_t"], g)], tag=f"vad_g{g}")

    mk.op("tensor", lambda e: e.transpose(P["PA"][0:32, 0:128], P["qkss"][:, 0:32], P["s_identf"][:, :]),
          reads=[P["qkss"], P["s_identf"], P["PA"]], writes=[P["PA"]], tag="qkstr")
    mk.op("vector", lambda e: e.tensor_copy(P["hss16"][0:32, 16:17].rearrange("p a -> p a") if False else P["qkT"][0:32, 0:128], P["PA"][0:32, 0:128]),
          reads=[P["PA"]], writes=[P["qkT"]], tag="qkse")
    mk.op("vector", lambda e: e.reduce_max(P["qkT"][0:32, 128:129], P["qkT"][0:32, 0:128], axis=mybir.AxisListType.X),
          reads=[P["qkT"]], writes=[(P["qkT"], "m")], tag="qksm")
    mk.dma("sync", "qkm", P["QKM0"][:, :], P["qkT"][0:32, 128:129], reads=[(P["qkT"], "m")], writes=[(P["QKM0_t"], 0)], tag="qkm")


def emit_phaseB(mk, P, NCH):
    T = NCH * 128
    OWN = T // 2
    NT = OWN // 128
    NQQ = max(1, OWN // 512)
    QW = OWN // NQQ
    PX = [P[f"PX{i}"] for i in range(4)]
    PSC0, PSC1 = P["PSC0"], P["PSC1"]
    PL, PT = P["PL"], P["PT"]
    identf = P["s_identf"]

    emit_load_weight_bf16(mk, P, P["L0_ffn_wo"], P["wo_sb"], D, D, "wo")
    for nm, dst in [("identf32", "s_identf"), ("ones1x64", "s_ones64"), ("ones1x128", "s_ones128"),
                    ("shiftE", "s_shE"), ("shiftO", "s_shO")]:
        mk.dma("sync", "cst", P[dst][:, :], P[nm], writes=[P[dst]], tag=f"c_{nm}")
    for c in range(8):
        mk.dma("sync", "cst", P["router_sb"][:, c * 8:(c + 1) * 8], P["L0_ffn_router_w"][c * 128:(c + 1) * 128, :],
               writes=[(P["router_sb"], c)], tag=f"c_rw{c}")
    mk.dma("sync", "crow", P["qkrow"][0:1, 0:32], P["QKM0"].rearrange("a b -> b a")[0:1, :],
           writes=[P["qkrow"]], tag="c_qkm")
    mk.dma("sync", "crow", P["scrq"][0:1, 0:D], P["L0_ffn_norm2_w"], writes=[P["scrq"]], tag="c_n2")
    emit_bcast_row(mk, P, PX[0], P["scrq"], D, P["s_n2b"], "bcn2")

    mk.op("vector", lambda e: e.reduce_max(P["stat1"][0:1, 0:1], P["qkrow"][0:1, 0:16], axis=mybir.AxisListType.X),
          reads=[P["qkrow"]], writes=[(P["stat1"], 0)], tag="qmx")
    mk.op("vector", lambda e: e.reduce_max(P["stat1"][0:1, 1:2], P["qkrow"][0:1, 16:32], axis=mybir.AxisListType.X),
          reads=[P["qkrow"]], writes=[(P["stat1"], 1)], tag="kmx")
    mk.op("vector", lambda e: e.tensor_tensor(P["stat1"][0:1, 2:3], P["stat1"][0:1, 0:1], P["stat1"][0:1, 1:2], ALU.mult),
          reads=[(P["stat1"], 0), (P["stat1"], 1)], writes=[(P["stat1"], 2)], tag="qk")
    mk.op("vector", lambda e: e.tensor_scalar(P["stat1"][0:1, 2:3], P["stat1"][0:1, 2:3], 1.0, 1e-12, ALU.mult, ALU.add),
          reads=[(P["stat1"], 2)], writes=[(P["stat1"], 2)], tag="qke")
    mk.op("scalar", lambda e: e.activation(P["stat1"][0:1, 3:4], P["stat1"][0:1, 2:3], AF.Ln),
          reads=[(P["stat1"], 2)], writes=[(P["stat1"], 3)], tag="qkl")
    mk.op("scalar", lambda e: e.activation(P["stat1"][0:1, 2:3], P["stat1"][0:1, 3:4], AF.Exp, scale=0.5),
          reads=[(P["stat1"], 3)], writes=[(P["stat1"], 2)], tag="qkx")
    mk.op("vector", lambda e: e.tensor_scalar(P["stat1"][0:1, 3:4], P["stat1"][0:1, 2:3], -0.125, -480.0, ALU.mult, ALU.max),
          reads=[(P["stat1"], 2)], writes=[(P["stat1"], 3)], tag="negc")
    mk.op("tensor", lambda e: e.matmul(PL[0:128, 0:1], P["s_ones128"][0:1, 0:128], P["stat1"][0:1, 3:4], start=True, stop=True),
          reads=[P["s_ones128"], (P["stat1"], 3)], writes=[PL], tag="negc_bc")
    mk.op("vector", lambda e: e.tensor_copy(P["negCb"][:, :], PL[0:128, 0:1]),
          reads=[PL], writes=[P["negCb"]], tag="negc_ev")

    c_kh = Cyc([P["kh0"], P["kh1"]])
    c_qh = Cyc([P["qh0"], P["qh1"]])
    c_va = Cyc([P[f"va{i}"] for i in range(4)])
    c_pt = Cyc([P["pt0"], P["pt1"]])
    yfm = P["yfm_all"]

    for hh in range(H):
        kh = c_kh[hh]
        mk.dma("sync", f"khld{hh%2}", kh[:, :], P["KF0"][hh, :, :], writes=[kh], tag=f"khld_{hh}")
        qh = c_qh[hh]
        mk.dma("scalar", f"qhld{hh%2}", qh[:, :], P["QF0"][hh, :, :], writes=[qh], tag=f"qhld_{hh}")
        ctx_open = [False] * 4
        for j in range(NCH):
            va = c_va[hh * NCH + j]
            mk.dma("gpsimd", f"vald{(hh*NCH+j)%4}", va[:, :], P["VA0"][j * 128:(j + 1) * 128, hh * 65:(hh + 1) * 65],
                   writes=[va], tag=f"vald_{hh}_{j}")
            for qq in range(NQQ):
                psc = PSC0 if qq % 2 == 0 else PSC1
                mk.op("tensor", lambda e, kh=kh, qh=qh, j=j, qq=qq, psc=psc: e.matmul(
                    psc[:, 0:QW], kh[:, j * 128:(j + 1) * 128], qh[:, qq * QW:(qq + 1) * QW],
                    start=True, stop=True),
                      reads=[kh, qh], writes=[psc], tag=f"sc_h{hh}_j{j}_q{qq}")
                pt = c_pt[qq % 2]
                mk.op("scalar", lambda e, pt=pt, psc=psc: e.activation(
                    pt[:, 0:QW], psc[:, 0:QW], AF.Exp, bias=P["negCb"][:, 0:1], scale=0.125),
                      reads=[psc, P["negCb"]], writes=[pt], tag=f"ex_h{hh}_j{j}_q{qq}")
                px = PX[qq]
                if not ctx_open[qq]:
                    mk.op("tensor", lambda e, va=va, pt=pt, px=px, j=j: e.matmul(
                        px[0:65, 0:QW], va[:, :], pt[:, 0:QW],
                        start=True, stop=(j == NCH - 1)),
                          reads=[va, pt], writes=[px], tag=f"cx_h{hh}_j{j}_q{qq}")
                    ctx_open[qq] = True
                else:
                    tok = mk.raw_op("tensor", lambda e, va=va, pt=pt, px=px, j=j: e.matmul(
                        px[0:65, 0:QW], va[:, :], pt[:, 0:QW],
                        start=False, stop=(j == NCH - 1)),
                        also=[mk.writer_of(pt), mk.writer_of(va)], tag=f"cx_h{hh}_j{j}_q{qq}")
                    st = mk.state[_key(px)]
                    st["w"] = tok
                    mk.state[_key(pt)]["r"].append(tok)
                    mk.state[_key(va)]["r"].append(tok)
        yh_buf = P["yhE"] if hh % 2 == 0 else P["yhO"]
        for qq in range(NQQ):
            px = PX[qq]
            mk.op("vector", lambda e, px=px: e.reciprocal(P["rec1"][0:1, 0:QW], px[64:65, 0:QW]),
                  reads=[px], writes=[P["rec1"]], tag=f"rc_h{hh}_q{qq}")
            psc = PSC0 if qq % 2 == 0 else PSC1
            mk.op("tensor", lambda e, psc=psc: e.matmul(psc[0:64, 0:QW], P["s_ones64"][0:1, 0:64], P["rec1"][0:1, 0:QW], start=True, stop=True),
                  reads=[P["rec1"], P["s_ones64"]], writes=[psc], tag=f"bc_h{hh}_q{qq}")
            mk.op("vector", lambda e, psc=psc: e.tensor_copy(P["rb1"][0:64, 0:QW], psc[0:64, 0:QW]),
                  reads=[psc], writes=[P["rb1"]], tag=f"rbe_h{hh}_q{qq}")
            mk.op("vector", lambda e, px=px, qq=qq, yh_buf=yh_buf: e.tensor_tensor(
                yh_buf[:, qq * QW:(qq + 1) * QW], px[0:64, 0:QW], P["rb1"][0:64, 0:QW], ALU.mult),
                  reads=[px, P["rb1"]], writes=[(yh_buf, qq)], tag=f"y_h{hh}_q{qq}")
        if hh % 2 == 1:
            pi = hh // 2
            for seg in range(NQQ):
                mms = [
                    (lambda e, seg=seg: e.matmul(PSC0[:, 0:QW], P["s_shE"][:, :], P["yhE"][:, seg * QW:(seg + 1) * QW], start=True, stop=False)),
                    (lambda e, seg=seg: e.matmul(PSC0[:, 0:QW], P["s_shO"][:, :], P["yhO"][:, seg * QW:(seg + 1) * QW], start=False, stop=True)),
                ]
                mk.mm_group(mms, reads=[(P["yhE"], seg), (P["yhO"], seg), P["s_shE"], P["s_shO"]],
                            writes=[PSC0], tag=f"sh_p{pi}_s{seg}")
                mk.op("vector", lambda e, pi=pi, seg=seg: e.tensor_copy(
                    yfm[:, pi * OWN + seg * QW: pi * OWN + (seg + 1) * QW], PSC0[:, 0:QW]),
                      reads=[PSC0], writes=[(yfm, f"{pi}_{seg}")], tag=f"she_p{pi}_s{seg}")

    c_xt = Cyc([P["xt0"], P["xt1"]])
    for tt in range(NT):
        for half in range(2):
            px = PX[half]
            mms = [
                (lambda e, pr=pr, half=half, tt=tt, px=px: e.matmul(
                    px[:, 0:512], yfm[:, pr * OWN + tt * 128: pr * OWN + tt * 128 + 128],
                    P["wo_sb"][:, pr * 1024 + half * 512: pr * 1024 + half * 512 + 512],
                    start=(pr == 0), stop=(pr == 7)))
                for pr in range(8)
            ]
            mk.mm_group(mms, reads=[(yfm, f"{pr2}_{(tt*128)//QW}") for pr2 in range(8)] + [P["wo_sb"]],
                        writes=[px], tag=f"wo{half}_t{tt}")
        xt = c_xt[tt]
        mk.dma("sync", f"xnld{tt%2}", xt[:, :], P["XN0"][tt * 128:(tt + 1) * 128, :], writes=[xt], tag=f"xnld_t{tt}")
        for half in range(2):
            px = PX[half]
            mk.op("vector", lambda e, xt=xt, half=half, px=px: e.tensor_tensor(
                P["acc_t"][:, half * 512:(half + 1) * 512], xt[:, half * 512:(half + 1) * 512], px[:, 0:512], ALU.add),
                  reads=[xt, px], writes=[(P["acc_t"], half)], tag=f"acc{half}_t{tt}")
        mk.dma("gpsimd", f"accd{tt%2}", P["ACC"][tt * 128:(tt + 1) * 128, :], P["acc_t"][:, :],
               reads=[(P["acc_t"], 0), (P["acc_t"], 1)], writes=[(P["ACC_t"], tt)], tag=f"accd_t{tt}")
        mk.op("scalar", lambda e: e.square(P["scrq"][:, 0:1024], P["acc_t"][:, :]),
              reads=[(P["acc_t"], 0), (P["acc_t"], 1)], writes=[P["scrq"]], tag=f"h2sq_t{tt}")
        mk.op("vector", lambda e: e.reduce_sum(P["stat1"][0:128, 4:5], P["scrq"][:, 0:1024], axis=mybir.AxisListType.X),
              reads=[P["scrq"]], writes=[(P["stat1"], 4)], tag=f"h2ss_t{tt}")
        mk.op("vector", lambda e: e.tensor_scalar(P["stat1"][0:128, 4:5], P["stat1"][0:128, 4:5], 1.0 / D, 1e-6, ALU.mult, ALU.add),
              reads=[(P["stat1"], 4)], writes=[(P["stat1"], 4)], tag=f"h2e_t{tt}")
        mk.op("scalar", lambda e: e.activation(P["stat1"][0:128, 5:6], P["stat1"][0:128, 4:5], AF.Ln),
              reads=[(P["stat1"], 4)], writes=[(P["stat1"], 5)], tag=f"h2l_t{tt}")
        mk.op("scalar", lambda e: e.activation(P["stat1"][0:128, 6:7], P["stat1"][0:128, 5:6], AF.Exp, scale=-0.5),
              reads=[(P["stat1"], 5)], writes=[(P["stat1"], 6)], tag=f"h2r_t{tt}")
        mk.op("vector", lambda e: e.tensor_scalar(P["scrq"][:, 0:1024], P["acc_t"][:, :], P["stat1"][0:128, 6:7], None, ALU.mult),
              reads=[(P["acc_t"], 0), (P["acc_t"], 1), (P["stat1"], 6)], writes=[P["scrq"]], tag=f"h2a_t{tt}")
        mk.op("vector", lambda e: e.tensor_tensor(P["scrq"][:, 0:1024], P["scrq"][:, 0:1024], P["s_n2b"][:, :], ALU.mult),
              reads=[P["scrq"], P["s_n2b"]], writes=[P["scrq"]], tag=f"h2w_t{tt}")
        mk.op("vector", lambda e: e.tensor_copy(P["h2b_t"][:, :], P["scrq"][:, 0:1024]),
              reads=[P["scrq"]], writes=[P["h2b_t"]], tag=f"h2b_t{tt}")
        mk.dma("scalar", f"h2td{tt%2}", P["H2T"][tt * 128:(tt + 1) * 128, :], P["h2b_t"][:, :],
               reads=[P["h2b_t"]], writes=[(P["H2T_t"], tt)], tag=f"h2td_t{tt}")
        for rnd in range(2):
            last = None
            first = True
            for cc in range(4):
                c = rnd * 4 + cc
                last = mk.op("tensor", lambda e, c=c, cc=cc: e.transpose(
                    PT[:, cc * 128:(cc + 1) * 128], P["scrq"][:, c * 128:(c + 1) * 128], identf[:, :]),
                    reads=([P["scrq"], identf, PT] if first else []), writes=[PT] if first else [],
                    tag=f"h2tr{c}_t{tt}", track=first)
                first = False
            st = mk.state[_key(PT)]
            st["w"] = last
            st["r"] = []
            mk.op("vector", lambda e, rnd=rnd: e.tensor_copy(P["h2fm"][:, rnd * 512:(rnd + 1) * 512], PT[:, 0:512]),
                  reads=[PT], writes=[(P["h2fm"], rnd)], tag=f"h2te{rnd}_t{tt}")
        mk.op("vector", lambda e: e.tensor_copy(P["h2fmb"][:, :], P["h2fm"][:, 0:1024]),
              reads=[(P["h2fm"], 0), (P["h2fm"], 1)], writes=[P["h2fmb"]], tag=f"h2fb_t{tt}")
        mk.dma("sync", f"h2fd{tt%2}", P["H2F"].rearrange("(c p) t -> p c t", p=128)[:, :, tt * 128:(tt + 1) * 128],
               P["h2fmb"][:, :].rearrange("p (c t) -> p c t", c=8),
               reads=[P["h2fmb"]], writes=[(P["H2F_t"], tt)], tag=f"h2fd_t{tt}")
        mms = [
            (lambda e, c=c: e.matmul(PL[0:128, 0:8], P["h2fm"][:, c * 128:(c + 1) * 128],
                                     P["router_sb"][:, c * 8:(c + 1) * 8], start=(c == 0), stop=(c == 7)))
            for c in range(8)
        ]
        mk.mm_group(mms, reads=[(P["h2fm"], 0), (P["h2fm"], 1)] + [(P["router_sb"], c2) for c2 in range(8)],
                    writes=[PL], tag=f"lg_t{tt}")
        mk.op("vector", lambda e: e.tensor_copy(P["lg_t"][:, :], PL[0:128, 0:8]),
              reads=[PL], writes=[P["lg_t"]], tag=f"lge_t{tt}")
        mk.dma("gpsimd", f"lgd{tt%2}", P["LOG0"][tt * 128:(tt + 1) * 128, :], P["lg_t"][:, :],
               reads=[P["lg_t"]], writes=[(P["LOG0_t"], tt)], tag=f"lgd_t{tt}")


def alloc_phaseB(nc, S, P, NCH):
    def sb(name, shape, dt=F32):
        t = S.enter_context(nc.sbuf_tensor(name, shape, dt))
        P[name] = t
        return t

    def ps(name, shape, dt=F32):
        t = S.enter_context(nc.psum_tensor(name, shape, dt))
        P[name] = t
        return t

    sb("wo_sb", [128, 8 * 1024], BF16)
    P["wstage"] = [sb("xt0", [128, 1024], F32), sb("xt1", [128, 1024], F32)]
    sb("s_identf", [128, 128], F32)
    sb("s_ones64", [1, 64], F32)
    sb("s_ones128", [1, 128], F32)
    sb("s_shE", [64, 128], BF16)
    sb("s_shO", [64, 128], BF16)
    sb("router_sb", [128, 64], F32)
    sb("s_n2b", [128, 1024], F32)
    sb("qkrow", [1, 128], F32)
    sb("stat1", [128, 8], F32)
    sb("negCb", [128, 1], F32)
    T = NCH * 128
    OWN = T // 2
    sb("kh0", [64, T], BF16)
    sb("kh1", [64, T], BF16)
    sb("qh0", [64, OWN], BF16)
    sb("qh1", [64, OWN], BF16)
    for i in range(4):
        sb(f"va{i}", [128, 65], BF16)
    sb("pt0", [128, 512], BF16)
    sb("pt1", [128, 512], BF16)
    sb("den1", [1, 512], F32)
    sb("rec1", [1, 512], F32)
    sb("rb1", [64, 512], F32)
    sb("yhE", [64, OWN], BF16)
    sb("yhO", [64, OWN], BF16)
    sb("yfm_all", [128, 8 * OWN], BF16)
    sb("acc_t", [128, 1024], F32)
    sb("scrq", [128, 1024], F32)
    sb("h2b_t", [128, 1024], BF16)
    sb("h2fm", [128, 1024], F32)
    sb("h2fmb", [128, 1024], BF16)
    sb("lg_t", [128, 8], F32)
    for i in range(4):
        ps(f"PX{i}", [128, 512], F32)
    ps("PSC0", [128, 512], F32)
    ps("PSC1", [128, 512], F32)
    ps("PL", [128, 512], F32)
    ps("PT", [128, 512], F32)
    for nm in ["ACC_t", "H2T_t", "H2F_t", "LOG0_t"]:
        P[nm] = object()
    return P


def build_phaseB(NCH):
    T = NCH * 128
    nc = bass.Bass("TRN2", target_bir_lowering=False, debug=False)
    P = {}
    P["KF0"] = nc.dram_tensor("KF0", [H, 64, T], BF16, kind="ExternalInput").ap()
    P["QF0"] = nc.dram_tensor("QF0", [H, 64, T // 2], BF16, kind="ExternalInput").ap()
    P["VA0"] = nc.dram_tensor("VA0", [T, H * 65], BF16, kind="ExternalInput").ap()
    P["XN0"] = nc.dram_tensor("XN0", [T // 2, D], F32, kind="ExternalInput").ap()
    P["QKM0"] = nc.dram_tensor("QKM0", [32, 1], F32, kind="ExternalInput").ap()
    P["L0_ffn_wo"] = nc.dram_tensor("L0_ffn_wo", [D, D], F32, kind="ExternalInput").ap()
    P["L0_ffn_router_w"] = nc.dram_tensor("L0_ffn_router_w", [D, 8], F32, kind="ExternalInput").ap()
    P["L0_ffn_norm2_w"] = nc.dram_tensor("L0_ffn_norm2_w", [1, D], F32, kind="ExternalInput").ap()
    for nm, shape, dt in [("identf32", [128, 128], F32), ("ones1x64", [1, 64], F32),
                          ("ones1x128", [1, 128], F32), ("shiftE", [64, 128], BF16), ("shiftO", [64, 128], BF16)]:
        P[nm] = nc.dram_tensor(nm, shape, dt, kind="ExternalInput").ap()
    P["ACC"] = nc.dram_tensor("ACC", [T // 2, D], F32, kind="ExternalOutput").ap()
    P["H2T"] = nc.dram_tensor("H2T", [T // 2, D], BF16, kind="ExternalOutput").ap()
    P["H2F"] = nc.dram_tensor("H2F", [D, T // 2], BF16, kind="ExternalOutput").ap()
    P["LOG0"] = nc.dram_tensor("LOG0", [T // 2, 8], F32, kind="ExternalOutput").ap()
    with contextlib.ExitStack() as S:
        alloc_phaseB(nc, S, P, NCH)
        mk = MK(nc)
        emit_phaseB(mk, P, NCH)
        emit_final_barrier(mk, [P["ACC_t"], P["H2T_t"], P["H2F_t"], P["LOG0_t"]])
        mk.build(S)
    return nc


def emit_moe_expert(mk, P, NCH, ex_tag, gsrc, usrc, dsrc, NFC, use_ce, ce_idx, xf_dram, acc_dram, acc_key):
    T = NCH * 128
    OWN = T // 2
    HHALF = OWN // 2
    NTT = HHALF // 128
    PG = [P[f"PG{i}"] for i in range(4)]
    PU = [P[f"PU{i}"] for i in range(4)]
    DFF = NFC * 128
    c_h2 = Cyc([P["h2f_a"], P["h2f_b"]])
    for c in range(8):
        stg = P["wstage"][c % 2]
        mk.dma("sync", f"wld{c%2}", stg[:, 0:DFF], gsrc[c * 128:(c + 1) * 128, :], writes=[stg], tag=f"gld{ex_tag}_{c}")
        mk.op("vector", lambda e, c=c, stg=stg: e.tensor_copy(P["gate_sb"][:, c * DFF:(c + 1) * DFF], stg[:, 0:DFF]),
              reads=[stg], writes=[(P["gate_sb"], c)], tag=f"gcv{ex_tag}_{c}")
    for c in range(8):
        stg = P["wstage"][c % 2]
        mk.dma("sync", f"wld{c%2}", stg[:, 0:DFF], usrc[c * 128:(c + 1) * 128, :], writes=[stg], tag=f"uld{ex_tag}_{c}")
        mk.op("scalar", lambda e, c=c, stg=stg: e.copy(P["up_sb"][:, c * DFF:(c + 1) * DFF], stg[:, 0:DFF]),
              reads=[stg], writes=[(P["up_sb"], c)], tag=f"ucv{ex_tag}_{c}")
    for c in range(NFC):
        stg = P["wstage"][c % 2]
        mk.dma("scalar", f"wl2{c%2}", stg[:, 0:1024], dsrc[c * 128:(c + 1) * 128, :], writes=[stg], tag=f"dld{ex_tag}_{c}")
        mk.op("vector", lambda e, c=c, stg=stg: e.tensor_copy(P["down_sb"][:, c * 1024:(c + 1) * 1024], stg[:, 0:1024]),
              reads=[stg], writes=[(P["down_sb"], c)], tag=f"dcv{ex_tag}_{c}")
    if use_ce:
        mk.dma("sync", "ced", P["ce_row"][0:1, 0:OWN], P["CTD"].rearrange("a b t -> b a t")[ce_idx:ce_idx + 1, :, :],
               reads=[(P["CTD_t"], 0)], writes=[P["ce_row"]], tag=f"celd{ex_tag}")
        for qq in range(max(1, OWN // 512)):
            w_ = min(512, OWN)
            mk.op("tensor", lambda e, qq=qq, w_=w_: e.matmul(PG[0][0:128, 0:w_], P["s_ones128"][0:1, 0:128], P["ce_row"][0:1, qq * w_:(qq + 1) * w_], start=True, stop=True),
                  reads=[P["ce_row"], P["s_ones128"]], writes=[PG[0]], tag=f"cebc{ex_tag}_{qq}")
            mk.op("vector", lambda e, qq=qq, w_=w_: e.tensor_copy(P["ce_b"][:, qq * w_:(qq + 1) * w_], PG[0][0:128, 0:w_]),
                  reads=[PG[0]], writes=[(P["ce_b"], qq)], tag=f"cebe{ex_tag}_{qq}")
    for half in range(2):
        h2f = c_h2[half]
        mk.dma("gpsimd", f"h2ld{half}", h2f[:, 0:8 * HHALF].rearrange("p (c t) -> p c t", c=8),
               xf_dram.rearrange("(c p) t -> p c t", p=128)[:, :, half * HHALF:(half + 1) * HHALF],
               writes=[h2f], tag=f"h2ld{ex_tag}_{half}")
        NWQ = max(1, HHALF // 512)
        WQ = HHALF // NWQ
        for fc in range(NFC):
            for wq_i in range(NWQ):
                pg = PG[wq_i % 4]
                pu = PU[wq_i % 4]
                mms = [
                    (lambda e, c=c, fc=fc, wq_i=wq_i, pg=pg, h2f=h2f, WQ=WQ, HHALF=HHALF, DFF=DFF: e.matmul(
                        pg[:, 0:WQ], P["gate_sb"][:, c * DFF + fc * 128: c * DFF + fc * 128 + 128],
                        h2f[:, c * HHALF + wq_i * WQ: c * HHALF + wq_i * WQ + WQ],
                        start=(c == 0), stop=(c == 7)))
                    for c in range(8)
                ]
                mk.mm_group(mms, reads=[h2f] + [(P["gate_sb"], c2) for c2 in range(8)], writes=[pg], tag=f"g{ex_tag}_{half}_{fc}_{wq_i}")
                mms = [
                    (lambda e, c=c, fc=fc, wq_i=wq_i, pu=pu, h2f=h2f, WQ=WQ, HHALF=HHALF, DFF=DFF: e.matmul(
                        pu[:, 0:WQ], P["up_sb"][:, c * DFF + fc * 128: c * DFF + fc * 128 + 128],
                        h2f[:, c * HHALF + wq_i * WQ: c * HHALF + wq_i * WQ + WQ],
                        start=(c == 0), stop=(c == 7)))
                    for c in range(8)
                ]
                mk.mm_group(mms, reads=[h2f] + [(P["up_sb"], c2) for c2 in range(8)], writes=[pu], tag=f"u{ex_tag}_{half}_{fc}_{wq_i}")
                mk.op("scalar", lambda e, pg=pg, WQ=WQ: e.activation(P["scrS"][:, 0:WQ], pg[:, 0:WQ], AF.Silu),
                      reads=[pg], writes=[P["scrS"]], tag=f"si{ex_tag}_{half}_{fc}_{wq_i}")
                if not use_ce:
                    mk.op("vector", lambda e, pu=pu, fc=fc, wq_i=wq_i, WQ=WQ, HHALF=HHALF: e.tensor_tensor(
                        P["act_sb"][:, fc * HHALF + wq_i * WQ: fc * HHALF + wq_i * WQ + WQ], P["scrS"][:, 0:WQ], pu[:, 0:WQ], ALU.mult),
                          reads=[P["scrS"], pu], writes=[(P["act_sb"], (fc, wq_i))], tag=f"a{ex_tag}_{half}_{fc}_{wq_i}")
                else:
                    mk.op("vector", lambda e, pu=pu, WQ=WQ: e.tensor_tensor(P["scrS2"][:, 0:WQ], P["scrS"][:, 0:WQ], pu[:, 0:WQ], ALU.mult),
                          reads=[P["scrS"], pu], writes=[P["scrS2"]], tag=f"am{ex_tag}_{half}_{fc}_{wq_i}")
                    mk.op("vector", lambda e, fc=fc, wq_i=wq_i, half=half, WQ=WQ, HHALF=HHALF: e.tensor_tensor(
                        P["act_sb"][:, fc * HHALF + wq_i * WQ: fc * HHALF + wq_i * WQ + WQ], P["scrS2"][:, 0:WQ],
                        P["ce_b"][:, half * HHALF + wq_i * WQ: half * HHALF + wq_i * WQ + WQ],
                        ALU.mult),
                          reads=[P["scrS2"], (P["ce_b"], 0), (P["ce_b"], 1), (P["ce_b"], 2), (P["ce_b"], 3)],
                          writes=[(P["act_sb"], (fc, wq_i))], tag=f"a{ex_tag}_{half}_{fc}_{wq_i}")
        for tt in range(NTT):
            gt = tt + half * NTT
            WQ2 = HHALF // max(1, HHALF // 512)
            for s2 in range(2):
                pg = PG[s2]
                mms = [
                    (lambda e, fc=fc, tt=tt, s2=s2, pg=pg, HHALF=HHALF: e.matmul(
                        pg[:, 0:512], P["act_sb"][:, fc * HHALF + tt * 128: fc * HHALF + tt * 128 + 128],
                        P["down_sb"][:, fc * 1024 + s2 * 512: fc * 1024 + s2 * 512 + 512],
                        start=(fc == 0), stop=(fc == NFC - 1)))
                    for fc in range(NFC)
                ]
                mk.mm_group(mms, reads=[(P["act_sb"], (fc2, (tt * 128) // WQ2)) for fc2 in range(NFC)] + [(P["down_sb"], c2) for c2 in range(NFC)],
                            writes=[pg], tag=f"d{ex_tag}_{half}_{tt}_{s2}")
                mk.op("vector", lambda e, s2=s2, pg=pg: e.tensor_copy(P["dn_t"][:, s2 * 512:(s2 + 1) * 512], pg[:, 0:512]),
                      reads=[pg], writes=[(P["dn_t"], s2)], tag=f"de{ex_tag}_{half}_{tt}_{s2}")
            mk.dma("gpsimd", f"dnd{gt%2}", acc_dram[gt * 128:(gt + 1) * 128, :], P["dn_t"][:, :],
                   reads=[(P["dn_t"], 0), (P["dn_t"], 1)], writes=[(P[acc_key], gt)], accum=True, tag=f"dnd{ex_tag}_{gt}")


def emit_phaseC(mk, P, NCH):
    T = NCH * 128
    OWN = T // 2
    NT = OWN // 128
    PG = [P[f"PG{i}"] for i in range(4)]
    PU = [P[f"PU{i}"] for i in range(4)]
    iota8 = P["s_iota8"]

    mk.dma("sync", "cst", P["s_iota8"][:, :], P["iota8"], writes=[P["s_iota8"]], tag="c_iota")
    mk.dma("sync", "cst", P["s_ones128"][:, :], P["ones1x128"], writes=[P["s_ones128"]], tag="c_o128")
    mk.dma("sync", "cst", P["s_identf"][:, :], P["identf32"], writes=[P["s_identf"]], tag="c_idf")

    st1 = P["stat1"]
    CT = P["ct_all"]
    for tt in range(NT):
        lg = P["lg_t"]
        mk.dma("sync", f"lgld{tt%2}", lg[:, :], P["LOG0"][tt * 128:(tt + 1) * 128, :], writes=[lg], tag=f"lgld_{tt}")
        mk.op("vector", lambda e, lg=lg: e.reduce_max(st1[:, 0:1], lg[:, :], axis=mybir.AxisListType.X),
              reads=[lg], writes=[(st1, 0)], tag=f"rm_{tt}")
        mk.op("vector", lambda e: e.tensor_scalar(st1[:, 1:2], st1[:, 0:1], -1.0, None, ALU.mult),
              reads=[(st1, 0)], writes=[(st1, 1)], tag=f"nm_{tt}")
        mk.op("scalar", lambda e, lg=lg: e.activation(P["prb"][:, :], lg[:, :], AF.Exp, bias=st1[:, 1:2]),
              reads=[lg, (st1, 1)], writes=[P["prb"]], tag=f"pe_{tt}")
        mk.op("vector", lambda e: e.reduce_sum(st1[:, 2:3], P["prb"][:, :], axis=mybir.AxisListType.X),
              reads=[P["prb"]], writes=[(st1, 2)], tag=f"ps_{tt}")
        mk.op("vector", lambda e: e.reciprocal(st1[:, 3:4], st1[:, 2:3]),
              reads=[(st1, 2)], writes=[(st1, 3)], tag=f"pr_{tt}")
        mk.op("vector", lambda e: e.tensor_scalar(P["prb"][:, :], P["prb"][:, :], st1[:, 3:4], None, ALU.mult),
              reads=[P["prb"], (st1, 3)], writes=[P["prb"]], tag=f"pn_{tt}")
        mk.op("vector", lambda e: e.reduce_max(st1[:, 4:5], P["prb"][:, :], axis=mybir.AxisListType.X),
              reads=[P["prb"]], writes=[(st1, 4)], tag=f"m1_{tt}")
        mk.op("vector", lambda e: e.tensor_scalar(P["eqb"][:, :], P["prb"][:, :], st1[:, 4:5], None, ALU.is_ge),
              reads=[P["prb"], (st1, 4)], writes=[P["eqb"]], tag=f"e1_{tt}")
        mk.op("vector", lambda e: e.tensor_scalar(P["eqb"][:, :], P["eqb"][:, :], -1e9, 1e9, ALU.mult, ALU.add),
              reads=[P["eqb"]], writes=[P["eqb"]], tag=f"e1b_{tt}")
        mk.op("vector", lambda e: e.tensor_tensor(P["eqb"][:, :], P["eqb"][:, :], iota8[:, :], ALU.add),
              reads=[P["eqb"], iota8], writes=[P["eqb"]], tag=f"e1c_{tt}")
        mk.op("vector", lambda e: e.tensor_reduce(st1[:, 5:6], P["eqb"][:, :], axis=mybir.AxisListType.X, op=ALU.min),
              reads=[P["eqb"]], writes=[(st1, 5)], tag=f"i1_{tt}")
        mk.op("vector", lambda e: e.tensor_scalar(P["oh1"][:, :], iota8[:, :], st1[:, 5:6], None, ALU.is_equal),
              reads=[iota8, (st1, 5)], writes=[P["oh1"]], tag=f"o1_{tt}")
        mk.op("vector", lambda e: e.scalar_tensor_tensor(P["msk"][:, :], P["oh1"][:, :], -1e9, P["prb"][:, :], ALU.mult, ALU.add),
              reads=[P["oh1"], P["prb"]], writes=[P["msk"]], tag=f"mk_{tt}")
        mk.op("vector", lambda e: e.reduce_max(st1[:, 6:7], P["msk"][:, :], axis=mybir.AxisListType.X),
              reads=[P["msk"]], writes=[(st1, 6)], tag=f"m2_{tt}")
        mk.op("vector", lambda e: e.tensor_scalar(P["eqb"][:, :], P["msk"][:, :], st1[:, 6:7], None, ALU.is_ge),
              reads=[P["msk"], (st1, 6)], writes=[P["eqb"]], tag=f"e2_{tt}")
        mk.op("vector", lambda e: e.tensor_scalar(P["eqb"][:, :], P["eqb"][:, :], -1e9, 1e9, ALU.mult, ALU.add),
              reads=[P["eqb"]], writes=[P["eqb"]], tag=f"e2b_{tt}")
        mk.op("vector", lambda e: e.tensor_tensor(P["eqb"][:, :], P["eqb"][:, :], iota8[:, :], ALU.add),
              reads=[P["eqb"], iota8], writes=[P["eqb"]], tag=f"e2c_{tt}")
        mk.op("vector", lambda e: e.tensor_reduce(st1[:, 7:8], P["eqb"][:, :], axis=mybir.AxisListType.X, op=ALU.min),
              reads=[P["eqb"]], writes=[(st1, 7)], tag=f"i2_{tt}")
        mk.op("vector", lambda e: e.tensor_scalar(P["oh2"][:, :], iota8[:, :], st1[:, 7:8], None, ALU.is_equal),
              reads=[iota8, (st1, 7)], writes=[P["oh2"]], tag=f"o2_{tt}")
        mk.op("scalar", lambda e: e.activation(st1[:, 0:1], st1[:, 4:5], AF.Exp),
              reads=[(st1, 4)], writes=[(st1, 0)], tag=f"x1_{tt}")
        mk.op("scalar", lambda e: e.activation(st1[:, 1:2], st1[:, 6:7], AF.Exp),
              reads=[(st1, 6)], writes=[(st1, 1)], tag=f"x2_{tt}")
        mk.op("vector", lambda e: e.tensor_tensor(st1[:, 2:3], st1[:, 0:1], st1[:, 1:2], ALU.add),
              reads=[(st1, 0), (st1, 1)], writes=[(st1, 2)], tag=f"sw_{tt}")
        mk.op("vector", lambda e: e.reciprocal(st1[:, 3:4], st1[:, 2:3]),
              reads=[(st1, 2)], writes=[(st1, 3)], tag=f"rw_{tt}")
        mk.op("vector", lambda e: e.tensor_tensor(st1[:, 0:1], st1[:, 0:1], st1[:, 3:4], ALU.mult),
              reads=[(st1, 0), (st1, 3)], writes=[(st1, 0)], tag=f"w1_{tt}")
        mk.op("vector", lambda e: e.tensor_tensor(st1[:, 1:2], st1[:, 1:2], st1[:, 3:4], ALU.mult),
              reads=[(st1, 1), (st1, 3)], writes=[(st1, 1)], tag=f"w2_{tt}")
        mk.op("vector", lambda e, tt=tt: e.tensor_scalar(CT[:, tt * 8:(tt + 1) * 8], P["oh1"][:, :], st1[:, 0:1], None, ALU.mult),
              reads=[P["oh1"], (st1, 0)], writes=[(CT, tt)], tag=f"c1_{tt}")
        mk.op("vector", lambda e, tt=tt: e.scalar_tensor_tensor(CT[:, tt * 8:(tt + 1) * 8], P["oh2"][:, :], st1[:, 1:2], CT[:, tt * 8:(tt + 1) * 8], ALU.mult, ALU.add),
              reads=[P["oh2"], (st1, 1), (CT, tt)], writes=[(CT, tt)], tag=f"c2_{tt}")
    mk.op("tensor", lambda e: e.transpose(P["PT2"][0:NT * 8, 0:128], CT[:, 0:NT * 8], P["s_identf"][:, :]),
          reads=[CT] + [(CT, tt2) for tt2 in range(NT)] + [P["s_identf"]], writes=[P["PT2"]], tag="cttr")
    mk.op("vector", lambda e: e.tensor_copy(P["ctT"][0:NT * 8, 0:128], P["PT2"][0:NT * 8, 0:128]),
          reads=[P["PT2"]], writes=[P["ctT"]], tag="ctev")
    mk.dma("sync", "ctd", P["CTD"].rearrange("a b t -> (a b) t"), P["ctT"][0:NT * 8, 0:128],
           reads=[P["ctT"]], writes=[(P["CTD_t"], 0)], tag="ctd")

    for ex in range(9):
        if ex == 0:
            emit_moe_expert(mk, P, NCH, "s", P["L0_ffn_sh_gate"], P["L0_ffn_sh_up"], P["L0_ffn_sh_down"],
                            16, False, 0, P["H2F"], P["ACC"], "ACC_t")
        else:
            emit_moe_expert(mk, P, NCH, f"e{ex-1}", P["L0_ffn_e_gate"][ex - 1], P["L0_ffn_e_up"][ex - 1],
                            P["L0_ffn_e_down"][ex - 1], 16, True, ex - 1, P["H2F"], P["ACC"], "ACC_t")

    for tt in range(NT):
        a1 = P["fx_a"]
        a2 = P["fx_b"]
        a3 = P["fx_c"]
        mk.dma("sync", f"f1{tt%2}", a1[:, 0:1024], P["ACC"][tt * 128:(tt + 1) * 128, :],
               reads=[(P["ACC_t"], tt)], writes=[a1, (a1, 0), (a1, 1)], tag=f"f1_{tt}")
        mk.dma("scalar", f"f2{tt%2}", a2[:, 0:1024], P["AT0"][tt * 128:(tt + 1) * 128, :],
               writes=[a2, (a2, 0), (a2, 1)], tag=f"f2_{tt}")
        mk.dma("gpsimd", f"f3{tt%2}", a3[:, 0:1024], P["XOWN"][tt * 128:(tt + 1) * 128, :],
               writes=[a3, (a3, 0), (a3, 1)], tag=f"f3_{tt}")
        mk.op("vector", lambda e, a1=a1, a2=a2: e.tensor_tensor(a1[:, 0:1024], a1[:, 0:1024], a2[:, 0:1024], ALU.add),
              reads=[a2] + [a1], writes=[a1], tag=f"fa_{tt}")
        mk.op("vector", lambda e, a1=a1, a3=a3: e.tensor_tensor(a1[:, 0:1024], a1[:, 0:1024], a3[:, 0:1024], ALU.add),
              reads=[a3, a1], writes=[a1], tag=f"fb_{tt}")
        mk.dma("sync", f"x1d{tt%2}", P["X1"][tt * 128:(tt + 1) * 128, :], a1[:, 0:1024],
               reads=[a1], writes=[(P["X1_t"], tt)], tag=f"x1d_{tt}")


def alloc_phaseC(nc, S, P, NCH):
    T = NCH * 128
    OWN = T // 2
    NT = OWN // 128
    HHALF = OWN // 2

    def sb(name, shape, dt=F32):
        t = S.enter_context(nc.sbuf_tensor(name, shape, dt))
        P[name] = t
        return t

    def ps(name, shape, dt=F32):
        t = S.enter_context(nc.psum_tensor(name, shape, dt))
        P[name] = t
        return t

    sb("s_iota8", [128, 8], F32)
    sb("s_ones128", [1, 128], F32)
    sb("s_identf", [128, 128], F32)
    sb("stat1", [128, 8], F32)
    sb("lg_t", [128, 8], F32)
    sb("prb", [128, 8], F32)
    sb("eqb", [128, 8], F32)
    sb("msk", [128, 8], F32)
    sb("oh1", [128, 8], F32)
    sb("oh2", [128, 8], F32)
    sb("ct_all", [128, NT * 8], F32)
    sb("ctT", [128, 128], F32)
    sb("ce_row", [1, OWN], F32)
    sb("ce_b", [128, OWN], BF16)
    sb("gate_sb", [128, 8 * 2048], BF16)
    sb("up_sb", [128, 8 * 2048], BF16)
    sb("down_sb", [128, 16 * 1024], BF16)
    P["wstage"] = [sb("wstage0", [128, 2048], F32), sb("wstage1", [128, 2048], F32)]
    sb("h2f_a", [128, 8 * HHALF], BF16)
    sb("h2f_b", [128, 8 * HHALF], BF16)
    sb("act_sb", [128, 16 * HHALF], BF16)
    sb("scrS", [128, 512], BF16)
    sb("scrS2", [128, 512], BF16)
    sb("dn_t", [128, 1024], F32)
    P["fx_a"] = P["dn_t"]
    P["fx_b"] = P["wstage"][0]
    P["fx_c"] = P["wstage"][1]
    for i in range(4):
        ps(f"PG{i}", [128, 512], F32)
        ps(f"PU{i}", [128, 512], F32)
    P["PT2"] = P["PU3"]
    for nm in ["CTD_t", "ACC_t", "X1_t"]:
        P[nm] = object()
    return P


def build_phaseC(NCH):
    T = NCH * 128
    OWN = T // 2
    nc = bass.Bass("TRN2", target_bir_lowering=False, debug=False)
    P = {}
    P["H2F"] = nc.dram_tensor("H2F", [D, OWN], BF16, kind="ExternalInput").ap()
    P["LOG0"] = nc.dram_tensor("LOG0", [OWN, 8], F32, kind="ExternalInput").ap()
    P["ACCIN"] = nc.dram_tensor("ACCIN", [OWN, D], F32, kind="ExternalInput").ap()
    P["AT0"] = nc.dram_tensor("AT0", [OWN, D], F32, kind="ExternalInput").ap()
    P["XOWN"] = nc.dram_tensor("XOWN", [OWN, D], F32, kind="ExternalInput").ap()
    P["L0_ffn_sh_gate"] = nc.dram_tensor("L0_ffn_sh_gate", [D, 2048], F32, kind="ExternalInput").ap()
    P["L0_ffn_sh_up"] = nc.dram_tensor("L0_ffn_sh_up", [D, 2048], F32, kind="ExternalInput").ap()
    P["L0_ffn_sh_down"] = nc.dram_tensor("L0_ffn_sh_down", [2048, D], F32, kind="ExternalInput").ap()
    P["L0_ffn_e_gate"] = nc.dram_tensor("L0_ffn_e_gate", [8, D, 2048], F32, kind="ExternalInput").ap()
    P["L0_ffn_e_up"] = nc.dram_tensor("L0_ffn_e_up", [8, D, 2048], F32, kind="ExternalInput").ap()
    P["L0_ffn_e_down"] = nc.dram_tensor("L0_ffn_e_down", [8, 2048, D], F32, kind="ExternalInput").ap()
    for nm, shape, dt in [("identf32", [128, 128], F32), ("ones1x128", [1, 128], F32), ("iota8", [128, 8], F32)]:
        P[nm] = nc.dram_tensor(nm, shape, dt, kind="ExternalInput").ap()
    P["ACC"] = nc.dram_tensor("ACCW", [OWN, D], F32)
    P["CTD"] = nc.dram_tensor("CTD", [OWN // 128, 8, 128], F32)
    P["X1"] = nc.dram_tensor("X1", [OWN, D], F32, kind="ExternalOutput").ap()
    with contextlib.ExitStack() as S:
        alloc_phaseC(nc, S, P, NCH)
        mk = MK(nc)
        NT = OWN // 128
        for tt in range(NT):
            mk.dma("sync", f"acs{tt%2}", P["ACC"][tt * 128:(tt + 1) * 128, :], P["ACCIN"][tt * 128:(tt + 1) * 128, :],
                   writes=[(P["ACC_t"], tt)], tag=f"acs_{tt}")
        emit_phaseC(mk, P, NCH)
        emit_final_barrier(mk, [P["X1_t"]])
        mk.build(S)
    return nc


def emit_phaseD(mk, P, NCH):
    T = NCH * 128
    OWN = T // 2
    PP0, PP1 = P["PP0"], P["PP1"]
    PBT, PT = P["PBT"], P["PT"]
    ident = P["s_ident"]
    identf = P["s_identf"]
    statL = P["statL"]
    scrA, scrH = P["scrA"], P["scrH"]

    emit_load_weight_bf16(mk, P, P["L1_WQ"], P["wq_sb"], D, D, "wq1")
    emit_load_weight_bf16(mk, P, P["L1_WK"], P["wk_sb"], D, D, "wk1")
    emit_load_weight_bf16(mk, P, P["L1_WV"], P["wv_sb"], D, D, "wv1")
    for nm, dst in [("identbf", "s_ident"), ("identf32", "s_identf"),
                    ("maskA", "s_maskA"), ("maskB", "s_maskB")]:
        mk.dma("sync", "cst", P[dst][:, :], P[nm], writes=[P[dst]], tag=f"c_{nm}")
    for c in range(8):
        mk.dma("sync", "cst", P["router_sb"][:, c * 8:(c + 1) * 8], P["L1_ffn_w_gate"][c * 128:(c + 1) * 128, :],
               writes=[(P["router_sb"], c)], tag=f"c_rw{c}")

    c_sx = Cyc([P["sx0"], P["sx1"]])
    c_xnb = Cyc([P["xnb0"]])
    c_xnf = Cyc([P["xnf0"], P["xnf1"]])
    c_qt = Cyc([P["qp0"]])
    c_kt = Cyc([P["kp0"]])
    c_vaug = Cyc([P["vaug0"]])
    c_qfa = Cyc([P["qfa0"]])
    c_kfa = Cyc([P["kfa0"]])

    def proj_tok(g, src_fm, W_sb, dst_f32_keyed, tag):
        for half in range(2):
            pp = PP0 if half == 0 else PP1
            mms = [
                (lambda e, c=c, half=half, pp=pp, src_fm=src_fm, W_sb=W_sb: e.matmul(
                    pp[:, 0:512], src_fm[:, c * 128:(c + 1) * 128],
                    W_sb[:, c * 1024 + half * 512: c * 1024 + half * 512 + 512],
                    start=(c == 0), stop=(c == 7)))
                for c in range(8)
            ]
            mk.mm_group(mms, reads=[src_fm, W_sb], writes=[pp], tag=f"{tag}mm{half}_g{g}")
            mk.op("vector", lambda e, pp=pp, half=half: e.tensor_copy(dst_f32_keyed[:, half * 512:(half + 1) * 512], pp[:, 0:512]),
                  reads=[pp], writes=[(dst_f32_keyed, half)], tag=f"{tag}ev{half}_g{g}")

    def rope_norm(g, buf, rtag, n_use):
        sl = P["ropeS"]
        cl = P["ropeC"]
        mk.dma("sync", f"rp{g%2}", sl[:, :], P["SINT"][g * 128:(g + 1) * 128, :], writes=[sl], tag=f"rps_{rtag}_g{g}")
        mk.dma("sync", f"rp{g%2}", cl[:, :], P["COST"][g * 128:(g + 1) * 128, :], writes=[cl], tag=f"rpc_{rtag}_g{g}")
        v3 = lambda t, o: t[:, :].rearrange("p (h two q) -> p (h two) q", two=2, q=32)[:, o::2, :]
        tv = P["ropeT"]
        mk.op("vector", lambda e, buf=buf: e.tensor_tensor(v3(tv, 0), v3(buf, 0), cl[:, :].rearrange("p (h q) -> p h q", q=32), ALU.mult),
              reads=[buf, (buf, 0), (buf, 1), cl], writes=[(tv, 0)], tag=f"r1a_{rtag}_g{g}")
        mk.op("vector", lambda e, buf=buf: e.tensor_tensor(P["ropeU"][:, 0:512].rearrange("p (h q) -> p h q", q=32), v3(buf, 1), sl[:, :].rearrange("p (h q) -> p h q", q=32), ALU.mult),
              reads=[buf, (buf, 0), (buf, 1), sl], writes=[P["ropeU"]], tag=f"r1b_{rtag}_g{g}")
        mk.op("vector", lambda e: e.tensor_tensor(v3(tv, 0), v3(tv, 0), P["ropeU"][:, 0:512].rearrange("p (h q) -> p h q", q=32), ALU.subtract),
              reads=[(tv, 0), P["ropeU"]], writes=[(tv, 0)], tag=f"r1c_{rtag}_g{g}")
        mk.op("vector", lambda e, buf=buf: e.tensor_tensor(v3(tv, 1), v3(buf, 0), sl[:, :].rearrange("p (h q) -> p h q", q=32), ALU.mult),
              reads=[buf, sl], writes=[(tv, 1)], tag=f"r2a_{rtag}_g{g}")
        mk.op("vector", lambda e, buf=buf: e.tensor_tensor(P["ropeU"][:, 0:512].rearrange("p (h q) -> p h q", q=32), v3(buf, 1), cl[:, :].rearrange("p (h q) -> p h q", q=32), ALU.mult),
              reads=[buf, (buf, 0), (buf, 1), cl], writes=[P["ropeU"]], tag=f"r2b_{rtag}_g{g}")
        mk.op("vector", lambda e: e.tensor_tensor(v3(tv, 1), v3(tv, 1), P["ropeU"][:, 0:512].rearrange("p (h q) -> p h q", q=32), ALU.add),
              reads=[(tv, 1), P["ropeU"]], writes=[(tv, 1)], tag=f"r2c_{rtag}_g{g}")
        mk.op("scalar", lambda e: e.square(scrH[:, 0:1024], tv[:, :]),
              reads=[(tv, 0), (tv, 1)], writes=[scrH], tag=f"nsq_{rtag}_g{g}")
        mk.op("vector", lambda e: e.tensor_reduce(P["hn"][:, 0:16], scrH[:, 0:1024].rearrange("p (h d) -> p h d", d=64), axis=mybir.AxisListType.X, op=ALU.add),
              reads=[scrH], writes=[P["hn"]], tag=f"nss_{rtag}_g{g}")
        mk.op("vector", lambda e: e.tensor_scalar(P["hn"][:, 0:16], P["hn"][:, 0:16], 1.0, 1e-6, ALU.mult, ALU.add),
              reads=[P["hn"]], writes=[P["hn"]], tag=f"ne_{rtag}_g{g}")
        mk.op("scalar", lambda e: e.activation(P["hn"][:, 16:32], P["hn"][:, 0:16], AF.Ln),
              reads=[P["hn"]], writes=[(P["hn"], "l")], tag=f"nl_{rtag}_g{g}")
        mk.op("scalar", lambda e: e.activation(P["hn"][:, 16:32], P["hn"][:, 16:32], AF.Exp, scale=-0.5),
              reads=[(P["hn"], "l")], writes=[(P["hn"], "l")], tag=f"nr_{rtag}_g{g}")
        for hh in range(H):
            mk.op("vector", lambda e, hh=hh, buf=buf: e.tensor_scalar(
                buf[:, hh * 64:(hh + 1) * 64], tv[:, hh * 64:(hh + 1) * 64], P["hn"][:, 16 + hh:17 + hh], None, ALU.mult),
                  reads=[(tv, 0), (tv, 1), (P["hn"], "l")], writes=[(buf, f"n{hh//8}")], tag=f"na{hh}_{rtag}_g{g}")

    def head_transposes2(g, src_tm, src_keys, dst_all, tag, to_bf):
        for half in range(2):
            last = None
            first = True
            for hl in range(8):
                hh = half * 8 + hl
                rd = [(src_tm, src_keys[half]), ident] if first else []
                last = mk.op("tensor",
                             lambda e, hh=hh, hl=hl, src_tm=src_tm: e.transpose(
                                 PBT[0:64, hl * 128:(hl + 1) * 128], src_tm[:, hh * 64:(hh + 1) * 64], ident[:, :]),
                             reads=rd + ([PBT] if first else []), writes=[PBT] if first else [],
                             tag=f"{tag}ht{hh}_g{g}", track=first)
                first = False
            st = mk.state[_key(PBT)]
            st["w"] = last
            st["r"] = []
            mk.op("vector",
                  lambda e, half=half, dst_all=dst_all: e.tensor_copy(
                      dst_all[:, half * 1024:(half + 1) * 1024], PBT[0:64, 0:1024]),
                  reads=[PBT], writes=[(dst_all, half)], tag=f"{tag}hte{half}_g{g}")

    for g in range(NCH):
        own_idx = g % (NCH // 2)
        first_half = g < NCH // 2
        mask_ap = P["s_maskA"] if first_half else P["s_maskB"]

        sx = c_sx[g]
        mk.dma("sync", f"ldx{g%2}", sx[:, :], P["X1F"][g * 128:(g + 1) * 128, :], writes=[sx], tag=f"x_g{g}")
        mk.op("vector", lambda e, sx=sx: e.reduce_sum(statL[:, 0:1], sx[:, :], axis=mybir.AxisListType.X),
              reads=[sx], writes=[(statL, 0)], tag=f"s1_g{g}")
        mk.op("scalar", lambda e, sx=sx: e.square(scrH[:, 0:1024], sx[:, :]),
              reads=[sx], writes=[scrH], tag=f"sq_g{g}")
        mk.op("vector", lambda e: e.reduce_sum(statL[:, 1:2], scrH[:, 0:1024], axis=mybir.AxisListType.X),
              reads=[scrH], writes=[(statL, 1)], tag=f"s2_g{g}")
        mk.op("vector", lambda e: e.tensor_scalar(statL[:, 2:3], statL[:, 0:1], -1.0 / D, None, ALU.mult),
              reads=[(statL, 0)], writes=[(statL, 2)], tag=f"mn_g{g}")
        mk.op("vector", lambda e: e.tensor_tensor(statL[:, 3:4], statL[:, 2:3], statL[:, 2:3], ALU.mult),
              reads=[(statL, 2)], writes=[(statL, 3)], tag=f"m2_g{g}")
        mk.op("vector", lambda e: e.tensor_scalar(statL[:, 4:5], statL[:, 1:2], 1.0 / D, statL[:, 3:4], ALU.mult, ALU.subtract),
              reads=[(statL, 1), (statL, 3)], writes=[(statL, 4)], tag=f"va_g{g}")
        mk.op("vector", lambda e: e.tensor_scalar(statL[:, 4:5], statL[:, 4:5], 1e-5, None, ALU.add),
              reads=[(statL, 4)], writes=[(statL, 4)], tag=f"vae_g{g}")
        mk.op("scalar", lambda e: e.activation(statL[:, 5:6], statL[:, 4:5], AF.Ln),
              reads=[(statL, 4)], writes=[(statL, 5)], tag=f"ln_g{g}")
        mk.op("scalar", lambda e: e.activation(statL[:, 6:7], statL[:, 5:6], AF.Exp, scale=-0.5),
              reads=[(statL, 5)], writes=[(statL, 6)], tag=f"rs_g{g}")
        mk.op("vector", lambda e: e.tensor_tensor(statL[:, 7:8], statL[:, 2:3], statL[:, 6:7], ALU.mult),
              reads=[(statL, 2), (statL, 6)], writes=[(statL, 7)], tag=f"bb_g{g}")
        xn = P["xn0"]
        mk.op("vector", lambda e, sx=sx, xn=xn: e.tensor_scalar(xn[:, :], sx[:, :], statL[:, 6:7], statL[:, 7:8], ALU.mult, ALU.add),
              reads=[sx, (statL, 6), (statL, 7)], writes=[xn], tag=f"xn_g{g}")

        mk.op("vector", lambda e, xn=xn, mask_ap=mask_ap: e.tensor_scalar(P["xnsel0"][:, :], xn[:, :], mask_ap[:, 0:1], None, ALU.mult),
              reads=[xn, mask_ap], writes=[P["xnsel0"]], tag=f"xs_g{g}")
        for rnd in range(2):
            last = None
            first = True
            for cc in range(4):
                c = rnd * 4 + cc
                last = mk.op("tensor", lambda e, c=c, cc=cc: e.transpose(
                    PT[:, cc * 128:(cc + 1) * 128], P["xnsel0"][:, c * 128:(c + 1) * 128], identf[:, :]),
                    reads=([P["xnsel0"], identf, PT] if first else []), writes=[PT] if first else [],
                    tag=f"str{c}_g{g}", track=first)
                first = False
            st = mk.state[_key(PT)]
            st["w"] = last
            st["r"] = []
            mk.op("vector", lambda e, rnd=rnd: e.tensor_copy(P["h2fm"][:, rnd * 512:(rnd + 1) * 512], PT[:, 0:512]),
                  reads=[PT], writes=[(P["h2fm"], rnd)], tag=f"ste{rnd}_g{g}")
        mk.op("vector", lambda e: e.tensor_copy(P["h2fmb"][:, :], P["h2fm"][:, 0:1024]),
              reads=[(P["h2fm"], 0), (P["h2fm"], 1)], writes=[P["h2fmb"]], tag=f"sfb_g{g}")
        mk.dma("gpsimd", f"xfd{g%2}", P["XN1F"].rearrange("(c p) t -> p c t", p=128)[:, :, own_idx * 128:(own_idx + 1) * 128],
               P["h2fmb"][:, :].rearrange("p (c t) -> p c t", c=8),
               reads=[P["h2fmb"]], writes=[(P["XN1F_t"], own_idx)], accum=not first_half, tag=f"xfd_g{g}")
        mms = [
            (lambda e, c=c: e.matmul(P["PL"][0:128, 0:8], P["h2fm"][:, c * 128:(c + 1) * 128],
                                     P["router_sb"][:, c * 8:(c + 1) * 8], start=(c == 0), stop=(c == 7)))
            for c in range(8)
        ]
        mk.mm_group(mms, reads=[(P["h2fm"], 0), (P["h2fm"], 1)] + [(P["router_sb"], c2) for c2 in range(8)],
                    writes=[P["PL"]], tag=f"lg_g{g}")
        mk.op("vector", lambda e: e.tensor_copy(P["lg_t"][:, :], P["PL"][0:128, 0:8]),
              reads=[P["PL"]], writes=[P["lg_t"]], tag=f"lge_g{g}")
        mk.dma("gpsimd", f"lgd{g%2}", P["LOG1"][own_idx * 128:(own_idx + 1) * 128, :], P["lg_t"][:, :],
               reads=[P["lg_t"]], writes=[(P["LOG1_t"], own_idx)], accum=not first_half, tag=f"lgd_g{g}")

        xnb = c_xnb[g]
        mk.op("vector", lambda e, xn=xn, xnb=xnb: e.tensor_copy(xnb[:, :], xn[:, :]),
              reads=[xn], writes=[xnb], tag=f"xnb_g{g}")
        xnf = c_xnf[g]
        last = None
        first = True
        for c in range(8):
            last = mk.op("tensor",
                         lambda e, c=c, xnb=xnb: e.transpose(PBT[:, c * 128:(c + 1) * 128], xnb[:, c * 128:(c + 1) * 128], ident[:, :]),
                         reads=([xnb, ident, PBT] if first else []), writes=[PBT] if first else [], tag=f"xtr{c}_g{g}", track=first)
            first = False
        st = mk.state[_key(PBT)]
        st["w"] = last
        st["r"] = []
        mk.op("vector", lambda e, xnf=xnf: e.tensor_copy(xnf[:, :], PBT[:, 0:1024]),
              reads=[PBT], writes=[xnf], tag=f"xtre_g{g}")

        qt = c_qt[g]
        proj_tok(g, xnf, P["wq_sb"], qt, "q_")
        rope_norm(g, qt, "q", True)
        mk.op("vector", lambda e, qt=qt, mask_ap=mask_ap: e.tensor_scalar(qt[:, :], qt[:, :], mask_ap[:, 0:1], None, ALU.mult),
              reads=[(qt, "n0"), (qt, "n1"), mask_ap], writes=[(qt, "n0"), (qt, "n1")], tag=f"qm_g{g}")
        kt = c_kt[g]
        proj_tok(g, xnf, P["wk_sb"], kt, "k_")
        rope_norm(g, kt, "k", True)
        mk.op("vector", lambda e, qt=qt: e.tensor_copy(P["qbf"][:, :], qt[:, :]),
              reads=[(qt, "n0"), (qt, "n1"), (qt, 0), (qt, 1)], writes=[(P["qbf"], "h0"), (P["qbf"], "h1")], tag=f"qb_g{g}")
        mk.op("vector", lambda e, kt=kt: e.tensor_copy(P["kbf"][:, :], kt[:, :]),
              reads=[(kt, "n0"), (kt, "n1"), (kt, 0), (kt, 1)], writes=[(P["kbf"], "h0"), (P["kbf"], "h1")], tag=f"kb_g{g}")
        qfa = c_qfa[g]
        head_transposes2(g, P["qbf"], ["h0", "h1"], qfa, "fq", True)
        mk.dma("gpsimd", f"qfd{g%2}", P["QF1"].rearrange("h p s -> p h s")[:, :, own_idx * 128:(own_idx + 1) * 128], qfa[:, :].rearrange("p (h s) -> p h s", h=H),
               reads=[(qfa, 0), (qfa, 1)], writes=[(P["QF1_t"], own_idx)], accum=not first_half, tag=f"fqd_g{g}")
        kfa = c_kfa[g]
        head_transposes2(g, P["kbf"], ["h0", "h1"], kfa, "fk", True)
        mk.dma("sync", f"kfd{g%2}", P["KF1"].rearrange("h p s -> p h s")[:, :, g * 128:(g + 1) * 128], kfa[:, :].rearrange("p (h s) -> p h s", h=H),
               reads=[(kfa, 0), (kfa, 1)], writes=[(P["KF1_t"], g)], tag=f"fkd_g{g}")
        vaug = c_vaug[g]
        mk.op("vector", lambda e, vaug=vaug: e.memset(vaug[:, :], 1.0),
              writes=[vaug, (vaug, "v0"), (vaug, "v1")], tag=f"vm_g{g}")
        for half in range(2):
            pp = PP0 if half == 0 else PP1
            mms = [
                (lambda e, c=c, pp=pp, half=half, xnf=xnf: e.matmul(
                    pp[:, 0:512], xnf[:, c * 128:(c + 1) * 128],
                    P["wv_sb"][:, c * 1024 + half * 512: c * 1024 + half * 512 + 512],
                    start=(c == 0), stop=(c == 7)))
                for c in range(8)
            ]
            mk.mm_group(mms, reads=[xnf, P["wv_sb"]], writes=[pp], tag=f"vmm{half}_g{g}")
            mk.op("vector",
                  lambda e, pp=pp, half=half, vaug=vaug: e.tensor_copy(
                      vaug[:, :].rearrange("p (h s) -> p h s", h=H)[:, 8 * half:8 * half + 8, 0:64],
                      pp[:, 0:512].rearrange("p (h s) -> p h s", h=8)),
                  reads=[pp, vaug], writes=[(vaug, f"v{half}")], tag=f"vev{half}_g{g}")
        mk.dma("sync", f"va0{g%2}", P["VA1"][g * 128:(g + 1) * 128, :], vaug[:, :],
               reads=[(vaug, "v0"), (vaug, "v1")], writes=[(P["VA1_t"], g)], tag=f"vad_g{g}")


def alloc_phaseD(nc, S, P, NCH):
    def sb(name, shape, dt=F32):
        t = S.enter_context(nc.sbuf_tensor(name, shape, dt))
        P[name] = t
        return t

    def ps(name, shape, dt=F32):
        t = S.enter_context(nc.psum_tensor(name, shape, dt))
        P[name] = t
        return t

    for nm in ["wq_sb", "wk_sb", "wv_sb"]:
        sb(nm, [128, 8 * 1024], BF16)
    sb("s_ident", [128, 128], BF16)
    sb("s_identf", [128, 128], F32)
    sb("s_maskA", [128, 1], F32)
    sb("s_maskB", [128, 1], F32)
    sb("router_sb", [128, 64], F32)
    for nm in ["sx0", "sx1", "xn0", "xnsel0", "scrA", "scrH", "h2fm"]:
        sb(nm, [128, 1024], F32)
    P["wstage"] = [P["sx0"], P["sx1"]]
    sb("qp0", [128, 1024], F32)
    sb("kp0", [128, 1024], F32)
    sb("ropeT", [128, 1024], F32)
    sb("ropeU", [128, 512], F32)
    sb("ropeS", [128, 512], F32)
    sb("ropeC", [128, 512], F32)
    sb("hn", [128, 32], F32)
    sb("statL", [128, 10], F32)
    sb("lg_t", [128, 8], F32)
    for nm in ["xnb0", "xnf0", "xnf1", "h2fmb", "qbf", "kbf"]:
        sb(nm, [128, 1024], BF16)
    sb("qfa0", [64, 2048], BF16)
    sb("kfa0", [64, 2048], BF16)
    sb("vaug0", [128, H * 65], BF16)
    ps("PP0", [128, 512], F32)
    ps("PP1", [128, 512], F32)
    ps("PT", [128, 512], F32)
    ps("PL", [128, 512], F32)
    ps("PBT", [128, 1024], BF16)
    for nm in ["XN1F_t", "LOG1_t", "QF1_t", "KF1_t", "VA1_t"]:
        P[nm] = object()
    return P


def build_phaseD(NCH):
    T = NCH * 128
    OWN = T // 2
    nc = bass.Bass("TRN2", target_bir_lowering=False, debug=False)
    P = {}
    P["X1F"] = nc.dram_tensor("X1F", [T, D], F32, kind="ExternalInput").ap()
    P["L1_WQ"] = nc.dram_tensor("L1_WQ", [D, D], F32, kind="ExternalInput").ap()
    P["L1_WK"] = nc.dram_tensor("L1_WK", [D, D], F32, kind="ExternalInput").ap()
    P["L1_WV"] = nc.dram_tensor("L1_WV", [D, D], F32, kind="ExternalInput").ap()
    P["L1_ffn_w_gate"] = nc.dram_tensor("L1_ffn_w_gate", [D, 8], F32, kind="ExternalInput").ap()
    P["SINT"] = nc.dram_tensor("SINT", [T, 512], F32, kind="ExternalInput").ap()
    P["COST"] = nc.dram_tensor("COST", [T, 512], F32, kind="ExternalInput").ap()
    for nm, shape, dt in [("identbf", [128, 128], BF16), ("identf32", [128, 128], F32),
                          ("maskA", [128, 1], F32), ("maskB", [128, 1], F32)]:
        P[nm] = nc.dram_tensor(nm, shape, dt, kind="ExternalInput").ap()
    P["KF1"] = nc.dram_tensor("KF1", [H, 64, T], BF16, kind="ExternalOutput").ap()
    P["QF1"] = nc.dram_tensor("QF1", [H, 64, OWN], BF16, kind="ExternalOutput").ap()
    P["VA1"] = nc.dram_tensor("VA1", [T, H * 65], BF16, kind="ExternalOutput").ap()
    P["XN1F"] = nc.dram_tensor("XN1F", [D, OWN], BF16, kind="ExternalOutput").ap()
    P["LOG1"] = nc.dram_tensor("LOG1", [OWN, 8], F32, kind="ExternalOutput").ap()
    with contextlib.ExitStack() as S:
        alloc_phaseD(nc, S, P, NCH)
        mk = MK(nc)
        emit_phaseD(mk, P, NCH)
        emit_final_barrier(mk, [P["XN1F_t"], P["LOG1_t"], P["QF1_t"], P["KF1_t"], P["VA1_t"]])
        mk.build(S)
    return nc


def emit_phaseE(mk, P, NCH):
    T = NCH * 128
    OWN = T // 2
    NT = OWN // 128
    NQQ = max(1, OWN // 512)
    QW = OWN // NQQ
    PX = [P[f"PX{i}"] for i in range(4)]
    PSC0, PSC1 = P["PSC0"], P["PSC1"]
    PL = P["PL"]

    emit_load_weight_bf16(mk, P, P["L1_attn_Wo"], P["wo_sb"], D, D, "wo1")
    for nm, dst in [("ones1x64", "s_ones64"), ("ones1x128", "s_ones128"),
                    ("shiftE", "s_shE"), ("shiftO", "s_shO"), ("neg18", "negCb")]:
        mk.dma("sync", "cst", P[dst][:, :], P[nm], writes=[P[dst]], tag=f"c_{nm}")
    mk.dma("sync", "crow", P["scrq"][0:1, 0:D], P["L1_attn_bo"], writes=[P["scrq"]], tag="c_bo")
    emit_bcast_row(mk, P, PX[0], P["scrq"], D, P["bo_b"], "bcbo")

    c_kh = Cyc([P["kh0"], P["kh1"]])
    c_qh = Cyc([P["qh0"], P["qh1"]])
    c_va = Cyc([P[f"va{i}"] for i in range(4)])
    c_pt = Cyc([P["pt0"], P["pt1"]])
    yfm = P["yfm_all"]

    for hh in range(H):
        kh = c_kh[hh]
        mk.dma("sync", f"khld{hh%2}", kh[:, :], P["KF1"][hh, :, :], writes=[kh], tag=f"khld_{hh}")
        qh = c_qh[hh]
        mk.dma("scalar", f"qhld{hh%2}", qh[:, :], P["QF1"][hh, :, :], writes=[qh], tag=f"qhld_{hh}")
        ctx_open = [False] * 4
        for j in range(NCH):
            va = c_va[hh * NCH + j]
            mk.dma("gpsimd", f"vald{(hh*NCH+j)%4}", va[:, :], P["VA1"][j * 128:(j + 1) * 128, hh * 65:(hh + 1) * 65],
                   writes=[va], tag=f"vald_{hh}_{j}")
            for qq in range(NQQ):
                psc = PSC0 if qq % 2 == 0 else PSC1
                mk.op("tensor", lambda e, kh=kh, qh=qh, j=j, qq=qq, psc=psc: e.matmul(
                    psc[:, 0:QW], kh[:, j * 128:(j + 1) * 128], qh[:, qq * QW:(qq + 1) * QW],
                    start=True, stop=True),
                      reads=[kh, qh], writes=[psc], tag=f"sc_h{hh}_j{j}_q{qq}")
                pt = c_pt[qq % 2]
                mk.op("scalar", lambda e, pt=pt, psc=psc: e.activation(
                    pt[:, 0:QW], psc[:, 0:QW], AF.Exp, bias=P["negCb"][:, 0:1], scale=0.125),
                      reads=[psc, P["negCb"]], writes=[pt], tag=f"ex_h{hh}_j{j}_q{qq}")
                px = PX[qq]
                if not ctx_open[qq]:
                    mk.op("tensor", lambda e, va=va, pt=pt, px=px, j=j: e.matmul(
                        px[0:65, 0:QW], va[:, :], pt[:, 0:QW],
                        start=True, stop=(j == NCH - 1)),
                          reads=[va, pt], writes=[px], tag=f"cx_h{hh}_j{j}_q{qq}")
                    ctx_open[qq] = True
                else:
                    tok = mk.raw_op("tensor", lambda e, va=va, pt=pt, px=px, j=j: e.matmul(
                        px[0:65, 0:QW], va[:, :], pt[:, 0:QW],
                        start=False, stop=(j == NCH - 1)),
                        also=[mk.writer_of(pt), mk.writer_of(va)], tag=f"cx_h{hh}_j{j}_q{qq}")
                    st = mk.state[_key(px)]
                    st["w"] = tok
                    mk.state[_key(pt)]["r"].append(tok)
                    mk.state[_key(va)]["r"].append(tok)
        yh_buf = P["yhE"] if hh % 2 == 0 else P["yhO"]
        for qq in range(NQQ):
            px = PX[qq]
            mk.op("vector", lambda e, px=px: e.reciprocal(P["rec1"][0:1, 0:QW], px[64:65, 0:QW]),
                  reads=[px], writes=[P["rec1"]], tag=f"rc_h{hh}_q{qq}")
            psc = PSC0 if qq % 2 == 0 else PSC1
            mk.op("tensor", lambda e, psc=psc: e.matmul(psc[0:64, 0:QW], P["s_ones64"][0:1, 0:64], P["rec1"][0:1, 0:QW], start=True, stop=True),
                  reads=[P["rec1"], P["s_ones64"]], writes=[psc], tag=f"bc_h{hh}_q{qq}")
            mk.op("vector", lambda e, psc=psc: e.tensor_copy(P["rb1"][0:64, 0:QW], psc[0:64, 0:QW]),
                  reads=[psc], writes=[P["rb1"]], tag=f"rbe_h{hh}_q{qq}")
            mk.op("vector", lambda e, px=px, qq=qq, yh_buf=yh_buf: e.tensor_tensor(
                yh_buf[:, qq * QW:(qq + 1) * QW], px[0:64, 0:QW], P["rb1"][0:64, 0:QW], ALU.mult),
                  reads=[px, P["rb1"]], writes=[(yh_buf, qq)], tag=f"y_h{hh}_q{qq}")
        if hh % 2 == 1:
            pi = hh // 2
            for seg in range(NQQ):
                mms = [
                    (lambda e, seg=seg: e.matmul(PSC0[:, 0:QW], P["s_shE"][:, :], P["yhE"][:, seg * QW:(seg + 1) * QW], start=True, stop=False)),
                    (lambda e, seg=seg: e.matmul(PSC0[:, 0:QW], P["s_shO"][:, :], P["yhO"][:, seg * QW:(seg + 1) * QW], start=False, stop=True)),
                ]
                mk.mm_group(mms, reads=[(P["yhE"], seg), (P["yhO"], seg), P["s_shE"], P["s_shO"]],
                            writes=[PSC0], tag=f"sh_p{pi}_s{seg}")
                mk.op("vector", lambda e, pi=pi, seg=seg: e.tensor_copy(
                    yfm[:, pi * OWN + seg * QW: pi * OWN + (seg + 1) * QW], PSC0[:, 0:QW]),
                      reads=[PSC0], writes=[(yfm, f"{pi}_{seg}")], tag=f"she_p{pi}_s{seg}")

    for tt in range(NT):
        for half in range(2):
            px = PX[half]
            mms = [
                (lambda e, pr=pr, half=half, tt=tt, px=px: e.matmul(
                    px[:, 0:512], yfm[:, pr * OWN + tt * 128: pr * OWN + tt * 128 + 128],
                    P["wo_sb"][:, pr * 1024 + half * 512: pr * 1024 + half * 512 + 512],
                    start=(pr == 0), stop=(pr == 7)))
                for pr in range(8)
            ]
            mk.mm_group(mms, reads=[(yfm, f"{pr2}_{(tt*128)//QW}") for pr2 in range(8)] + [P["wo_sb"]],
                        writes=[px], tag=f"wo{half}_t{tt}")
            mk.op("vector", lambda e, half=half, px=px: e.tensor_tensor(
                P["acc_t"][:, half * 512:(half + 1) * 512], px[:, 0:512], P["bo_b"][:, half * 512:(half + 1) * 512], ALU.add),
                  reads=[px, P["bo_b"]], writes=[(P["acc_t"], half)], tag=f"ae{half}_t{tt}")
        mk.dma("gpsimd", f"accd{tt%2}", P["ACC1"][tt * 128:(tt + 1) * 128, :], P["acc_t"][:, :],
               reads=[(P["acc_t"], 0), (P["acc_t"], 1)], writes=[(P["ACC1_t"], tt)], tag=f"accd_t{tt}")


def alloc_phaseE(nc, S, P, NCH):
    T = NCH * 128
    OWN = T // 2

    def sb(name, shape, dt=F32):
        t = S.enter_context(nc.sbuf_tensor(name, shape, dt))
        P[name] = t
        return t

    def ps(name, shape, dt=F32):
        t = S.enter_context(nc.psum_tensor(name, shape, dt))
        P[name] = t
        return t

    sb("wo_sb", [128, 8 * 1024], BF16)
    P["wstage"] = [sb("wst0", [128, 1024], F32), sb("wst1", [128, 1024], F32)]
    sb("s_ones64", [1, 64], F32)
    sb("s_ones128", [1, 128], F32)
    sb("s_shE", [64, 128], BF16)
    sb("s_shO", [64, 128], BF16)
    sb("negCb", [128, 1], F32)
    sb("scrq", [128, 1024], F32)
    sb("bo_b", [128, 1024], F32)
    sb("kh0", [64, T], BF16)
    sb("kh1", [64, T], BF16)
    sb("qh0", [64, OWN], BF16)
    sb("qh1", [64, OWN], BF16)
    for i in range(4):
        sb(f"va{i}", [128, 65], BF16)
    sb("pt0", [128, 512], BF16)
    sb("pt1", [128, 512], BF16)
    sb("rec1", [1, 512], F32)
    sb("rb1", [64, 512], F32)
    sb("yhE", [64, OWN], BF16)
    sb("yhO", [64, OWN], BF16)
    sb("yfm_all", [128, 8 * OWN], BF16)
    sb("acc_t", [128, 1024], F32)
    for i in range(4):
        ps(f"PX{i}", [128, 512], F32)
    ps("PSC0", [128, 512], F32)
    ps("PSC1", [128, 512], F32)
    ps("PL", [128, 512], F32)
    P["ACC1_t"] = object()
    return P


def build_phaseE(NCH):
    T = NCH * 128
    OWN = T // 2
    nc = bass.Bass("TRN2", target_bir_lowering=False, debug=False)
    P = {}
    P["KF1"] = nc.dram_tensor("KF1", [H, 64, T], BF16, kind="ExternalInput").ap()
    P["QF1"] = nc.dram_tensor("QF1", [H, 64, OWN], BF16, kind="ExternalInput").ap()
    P["VA1"] = nc.dram_tensor("VA1", [T, H * 65], BF16, kind="ExternalInput").ap()
    P["L1_attn_Wo"] = nc.dram_tensor("L1_attn_Wo", [D, D], F32, kind="ExternalInput").ap()
    P["L1_attn_bo"] = nc.dram_tensor("L1_attn_bo", [1, D], F32, kind="ExternalInput").ap()
    for nm, shape, dt in [("ones1x64", [1, 64], F32), ("ones1x128", [1, 128], F32),
                          ("shiftE", [64, 128], BF16), ("shiftO", [64, 128], BF16),
                          ("neg18", [128, 1], F32)]:
        P[nm] = nc.dram_tensor(nm, shape, dt, kind="ExternalInput").ap()
    P["ACC1"] = nc.dram_tensor("ACC1", [OWN, D], F32, kind="ExternalOutput").ap()
    with contextlib.ExitStack() as S:
        alloc_phaseE(nc, S, P, NCH)
        mk = MK(nc)
        emit_phaseE(mk, P, NCH)
        emit_final_barrier(mk, [P["ACC1_t"]])
        mk.build(S)
    return nc


def emit_phaseF(mk, P, NCH, NCORES=8, CAPV=4096.0):
    T = NCH * 128
    OWN = T // 2
    NT = OWN // 128
    NTOK = OWN * NCORES
    NGT = NTOK // 128
    iota8 = P["s_iota8"]
    st1 = P["stat1"]
    PG = [P[f"PG{i}"] for i in range(4)]

    mk.dma("sync", "cst", P["s_iota8"][:, :], P["iota8"], writes=[P["s_iota8"]], tag="c_iota")
    mk.dma("sync", "cst", P["s_ones128"][:, :], P["ones1x128"], writes=[P["s_ones128"]], tag="c_o128")
    mk.dma("sync", "cst", P["s_identf"][:, :], P["identf32"], writes=[P["s_identf"]], tag="c_idf")
    mk.dma("sync", "cst", P["s_msel"][:, :], P["msel"], writes=[P["s_msel"]], tag="c_msel")

    for gt in range(NGT):
        lg = P["lg_t"]
        mk.dma("sync", f"lgld{gt%2}", lg[:, :], P["LOGALL"][gt * 128:(gt + 1) * 128, :], writes=[lg], tag=f"lgld_{gt}")
        mk.op("vector", lambda e, lg=lg: e.reduce_max(st1[:, 0:1], lg[:, :], axis=mybir.AxisListType.X),
              reads=[lg], writes=[(st1, 0)], tag=f"rm_{gt}")
        mk.op("vector", lambda e: e.tensor_scalar(st1[:, 1:2], st1[:, 0:1], -1.0, None, ALU.mult),
              reads=[(st1, 0)], writes=[(st1, 1)], tag=f"nm_{gt}")
        mk.op("scalar", lambda e, lg=lg: e.activation(P["prb"][:, :], lg[:, :], AF.Exp, bias=st1[:, 1:2]),
              reads=[lg, (st1, 1)], writes=[P["prb"]], tag=f"pe_{gt}")
        mk.op("vector", lambda e: e.reduce_sum(st1[:, 2:3], P["prb"][:, :], axis=mybir.AxisListType.X),
              reads=[P["prb"]], writes=[(st1, 2)], tag=f"ps_{gt}")
        mk.op("vector", lambda e: e.reciprocal(st1[:, 3:4], st1[:, 2:3]),
              reads=[(st1, 2)], writes=[(st1, 3)], tag=f"pr_{gt}")
        mk.op("vector", lambda e: e.tensor_scalar(P["prb"][:, :], P["prb"][:, :], st1[:, 3:4], None, ALU.mult),
              reads=[P["prb"], (st1, 3)], writes=[P["prb"]], tag=f"pn_{gt}")
        mk.op("vector", lambda e: e.reduce_max(st1[:, 4:5], P["prb"][:, :], axis=mybir.AxisListType.X),
              reads=[P["prb"]], writes=[(st1, 4)], tag=f"m1_{gt}")
        mk.op("vector", lambda e: e.tensor_scalar(P["eqb"][:, :], P["prb"][:, :], st1[:, 4:5], None, ALU.is_ge),
              reads=[P["prb"], (st1, 4)], writes=[P["eqb"]], tag=f"e1_{gt}")
        mk.op("vector", lambda e: e.tensor_scalar(P["eqb"][:, :], P["eqb"][:, :], -1e9, 1e9, ALU.mult, ALU.add),
              reads=[P["eqb"]], writes=[P["eqb"]], tag=f"e1b_{gt}")
        mk.op("vector", lambda e: e.tensor_tensor(P["eqb"][:, :], P["eqb"][:, :], iota8[:, :], ALU.add),
              reads=[P["eqb"], iota8], writes=[P["eqb"]], tag=f"e1c_{gt}")
        mk.op("vector", lambda e: e.tensor_reduce(st1[:, 5:6], P["eqb"][:, :], axis=mybir.AxisListType.X, op=ALU.min),
              reads=[P["eqb"]], writes=[(st1, 5)], tag=f"i1_{gt}")
        mk.op("vector", lambda e, gt=gt: e.tensor_scalar(P["OH1"][:, gt * 8:(gt + 1) * 8], iota8[:, :], st1[:, 5:6], None, ALU.is_equal),
              reads=[iota8, (st1, 5)], writes=[(P["OH1"], gt)], tag=f"o1_{gt}")
        mk.op("vector", lambda e, gt=gt: e.scalar_tensor_tensor(P["msk"][:, :], P["OH1"][:, gt * 8:(gt + 1) * 8], -1e9, P["prb"][:, :], ALU.mult, ALU.add),
              reads=[(P["OH1"], gt), P["prb"]], writes=[P["msk"]], tag=f"mk_{gt}")
        mk.op("vector", lambda e: e.reduce_max(st1[:, 6:7], P["msk"][:, :], axis=mybir.AxisListType.X),
              reads=[P["msk"]], writes=[(st1, 6)], tag=f"m2_{gt}")
        mk.op("vector", lambda e: e.tensor_scalar(P["eqb"][:, :], P["msk"][:, :], st1[:, 6:7], None, ALU.is_ge),
              reads=[P["msk"], (st1, 6)], writes=[P["eqb"]], tag=f"e2_{gt}")
        mk.op("vector", lambda e: e.tensor_scalar(P["eqb"][:, :], P["eqb"][:, :], -1e9, 1e9, ALU.mult, ALU.add),
              reads=[P["eqb"]], writes=[P["eqb"]], tag=f"e2b_{gt}")
        mk.op("vector", lambda e: e.tensor_tensor(P["eqb"][:, :], P["eqb"][:, :], iota8[:, :], ALU.add),
              reads=[P["eqb"], iota8], writes=[P["eqb"]], tag=f"e2c_{gt}")
        mk.op("vector", lambda e: e.tensor_reduce(st1[:, 7:8], P["eqb"][:, :], axis=mybir.AxisListType.X, op=ALU.min),
              reads=[P["eqb"]], writes=[(st1, 7)], tag=f"i2_{gt}")
        mk.op("vector", lambda e, gt=gt: e.tensor_scalar(P["OH2"][:, gt * 8:(gt + 1) * 8], iota8[:, :], st1[:, 7:8], None, ALU.is_equal),
              reads=[iota8, (st1, 7)], writes=[(P["OH2"], gt)], tag=f"o2_{gt}")
        mk.op("vector", lambda e, gt=gt: e.tensor_copy(P["W1"][:, gt:gt + 1], st1[:, 4:5]),
              reads=[(st1, 4)], writes=[(P["W1"], gt)], tag=f"w1_{gt}")
        mk.op("vector", lambda e, gt=gt: e.tensor_copy(P["W2"][:, gt:gt + 1], st1[:, 6:7]),
              reads=[(st1, 6)], writes=[(P["W2"], gt)], tag=f"w2_{gt}")
        mk.op("vector", lambda e, gt=gt: e.tensor_tensor(P["cnt_t"][:, 0:8], P["OH1"][:, gt * 8:(gt + 1) * 8], P["OH2"][:, gt * 8:(gt + 1) * 8], ALU.add),
              reads=[(P["OH1"], gt), (P["OH2"], gt)], writes=[P["cnt_t"]], tag=f"cn_{gt}")
        mk.op("tensor", lambda e: e.transpose(P["PT2"][0:8, 0:128], P["cnt_t"][:, 0:8], P["s_identf"][:, :]),
              reads=[P["cnt_t"], P["s_identf"], P["PT2"]], writes=[P["PT2"]], tag=f"ctr_{gt}")
        mk.op("vector", lambda e, gt=gt: e.tensor_copy(P["cntT"][0:8, (gt % (NGT // 4)) * 128:((gt % (NGT // 4)) + 1) * 128], P["PT2"][0:8, 0:128]),
              reads=[P["PT2"]], writes=[(P["cntT"], gt % (NGT // 4))], tag=f"cte_{gt}")
        if gt % (NGT // 4) == (NGT // 4) - 1:
            seg = gt // (NGT // 4)
            SW = (NGT // 4) * 128
            if seg == 0:
                mk.op("vector", lambda e: e.memset(P["carry"][0:8, 0:1], 0.0), writes=[P["carry"]], tag="car0")
            mk.op("vector", lambda e, SW=SW: e.tensor_tensor_scan(
                P["scanT"][0:8, 0:SW], P["cntT"][0:8, 0:SW], P["cntT"][0:8, 0:SW], 0.0, ALU.add, ALU.bypass),
                  reads=[(P["cntT"], k2) for k2 in range(NGT // 4)], writes=[P["scanT"]], tag=f"scan_{seg}")
            mk.op("vector", lambda e, SW=SW: e.tensor_scalar(P["scanT"][0:8, 0:SW], P["scanT"][0:8, 0:SW], P["carry"][0:8, 0:1], None, ALU.add),
                  reads=[P["scanT"], P["carry"]], writes=[P["scanT"]], tag=f"scac_{seg}")
            mk.op("vector", lambda e, SW=SW: e.tensor_copy(P["carry"][0:8, 0:1], P["scanT"][0:8, SW - 1:SW]),
                  reads=[P["scanT"]], writes=[P["carry"]], tag=f"carn_{seg}")
            mk.op("vector", lambda e, SW=SW: e.tensor_tensor(P["scanT"][0:8, 0:SW], P["scanT"][0:8, 0:SW], P["cntT"][0:8, 0:SW], ALU.subtract),
                  reads=[P["scanT"]] + [(P["cntT"], k2) for k2 in range(NGT // 4)], writes=[P["scanT"]], tag=f"sce_{seg}")
            for k in range(NGT // 4):
                gt2 = seg * (NGT // 4) + k
                mk.op("tensor", lambda e, k=k: e.transpose(P["PT2"][0:128, 0:8], P["scanT"][0:8, k * 128:(k + 1) * 128], P["s_identf"][0:8, 0:8]),
                      reads=[P["scanT"], P["s_identf"], P["PT2"]], writes=[P["PT2"]], tag=f"str_{gt2}")
                mk.op("vector", lambda e, gt2=gt2: e.tensor_copy(P["S_tm"][:, gt2 * 8:(gt2 + 1) * 8], P["PT2"][0:128, 0:8]),
                      reads=[P["PT2"]], writes=[(P["S_tm"], gt2)], tag=f"ste_{gt2}")

    CT = P["ct_all"]
    for tt in range(NT):
        for nm in ["xS", "xO1", "xO2"]:
            mk.op("vector", lambda e, nm=nm: e.memset(P[nm][:, :], 0.0), writes=[P[nm]], tag=f"z{nm}_{tt}")
        mk.op("vector", lambda e: e.memset(P["xW"][:, 0:2], 0.0), writes=[P["xW"]], tag=f"zw_{tt}")
        for ci in range(NCORES):
            gt2 = ci * NT + tt
            mk.op("vector", lambda e, gt2=gt2, ci=ci: e.scalar_tensor_tensor(
                P["xS"][:, :], P["S_tm"][:, gt2 * 8:(gt2 + 1) * 8], P["s_msel"][:, ci:ci + 1], P["xS"][:, :], ALU.mult, ALU.add),
                  reads=[(P["S_tm"], gt2), P["s_msel"], P["xS"]], writes=[P["xS"]], tag=f"xs{ci}_{tt}")
            mk.op("vector", lambda e, gt2=gt2, ci=ci: e.scalar_tensor_tensor(
                P["xO1"][:, :], P["OH1"][:, gt2 * 8:(gt2 + 1) * 8], P["s_msel"][:, ci:ci + 1], P["xO1"][:, :], ALU.mult, ALU.add),
                  reads=[(P["OH1"], gt2), P["s_msel"], P["xO1"]], writes=[P["xO1"]], tag=f"xo1{ci}_{tt}")
            mk.op("vector", lambda e, gt2=gt2, ci=ci: e.scalar_tensor_tensor(
                P["xO2"][:, :], P["OH2"][:, gt2 * 8:(gt2 + 1) * 8], P["s_msel"][:, ci:ci + 1], P["xO2"][:, :], ALU.mult, ALU.add),
                  reads=[(P["OH2"], gt2), P["s_msel"], P["xO2"]], writes=[P["xO2"]], tag=f"xo2{ci}_{tt}")
            mk.op("vector", lambda e, gt2=gt2, ci=ci: e.scalar_tensor_tensor(
                P["xW"][:, 0:1], P["W1"][:, gt2:gt2 + 1], P["s_msel"][:, ci:ci + 1], P["xW"][:, 0:1], ALU.mult, ALU.add),
                  reads=[(P["W1"], gt2), P["s_msel"], P["xW"]], writes=[P["xW"]], tag=f"xw1{ci}_{tt}")
            mk.op("vector", lambda e, gt2=gt2, ci=ci: e.scalar_tensor_tensor(
                P["xW"][:, 1:2], P["W2"][:, gt2:gt2 + 1], P["s_msel"][:, ci:ci + 1], P["xW"][:, 1:2], ALU.mult, ALU.add),
                  reads=[(P["W2"], gt2), P["s_msel"], P["xW"]], writes=[P["xW"]], tag=f"xw2{ci}_{tt}")
        mk.op("vector", lambda e: e.tensor_tensor(P["eqb"][:, :], P["xS"][:, :], P["xO1"][:, :], ALU.mult),
              reads=[P["xS"], P["xO1"]], writes=[P["eqb"]], tag=f"p1a_{tt}")
        mk.op("vector", lambda e: e.reduce_sum(st1[:, 0:1], P["eqb"][:, :], axis=mybir.AxisListType.X),
              reads=[P["eqb"]], writes=[(st1, 0)], tag=f"p1b_{tt}")
        mk.op("vector", lambda e: e.tensor_scalar(st1[:, 0:1], st1[:, 0:1], CAPV, None, ALU.is_lt),
              reads=[(st1, 0)], writes=[(st1, 0)], tag=f"v1_{tt}")
        mk.op("vector", lambda e: e.tensor_tensor(st1[:, 0:1], st1[:, 0:1], P["xW"][:, 0:1], ALU.mult),
              reads=[(st1, 0), P["xW"]], writes=[(st1, 0)], tag=f"cw1_{tt}")
        mk.op("vector", lambda e: e.tensor_tensor(P["eqb"][:, :], P["xS"][:, :], P["xO2"][:, :], ALU.mult),
              reads=[P["xS"], P["xO2"]], writes=[P["eqb"]], tag=f"p2a_{tt}")
        mk.op("vector", lambda e: e.reduce_sum(st1[:, 1:2], P["eqb"][:, :], axis=mybir.AxisListType.X),
              reads=[P["eqb"]], writes=[(st1, 1)], tag=f"p2b_{tt}")
        mk.op("vector", lambda e: e.tensor_scalar(st1[:, 1:2], st1[:, 1:2], CAPV, None, ALU.is_lt),
              reads=[(st1, 1)], writes=[(st1, 1)], tag=f"v2_{tt}")
        mk.op("vector", lambda e: e.tensor_tensor(st1[:, 1:2], st1[:, 1:2], P["xW"][:, 1:2], ALU.mult),
              reads=[(st1, 1), P["xW"]], writes=[(st1, 1)], tag=f"cw2_{tt}")
        mk.op("vector", lambda e, tt=tt: e.tensor_scalar(CT[:, tt * 8:(tt + 1) * 8], P["xO1"][:, :], st1[:, 0:1], None, ALU.mult),
              reads=[P["xO1"], (st1, 0)], writes=[(CT, tt)], tag=f"c1_{tt}")
        mk.op("vector", lambda e, tt=tt: e.scalar_tensor_tensor(CT[:, tt * 8:(tt + 1) * 8], P["xO2"][:, :], st1[:, 1:2], CT[:, tt * 8:(tt + 1) * 8], ALU.mult, ALU.add),
              reads=[P["xO2"], (st1, 1), (CT, tt)], writes=[(CT, tt)], tag=f"c2_{tt}")
    mk.op("tensor", lambda e: e.transpose(P["PT2"][0:NT * 8, 0:128], CT[:, 0:NT * 8], P["s_identf"][:, :]),
          reads=[CT] + [(CT, tt2) for tt2 in range(NT)] + [P["s_identf"], P["PT2"]], writes=[P["PT2"]], tag="cttr")
    mk.op("vector", lambda e: e.tensor_copy(P["ctT"][0:NT * 8, 0:128], P["PT2"][0:NT * 8, 0:128]),
          reads=[P["PT2"]], writes=[P["ctT"]], tag="ctev")
    mk.dma("sync", "ctd", P["CTD"].rearrange("a b t -> (a b) t"), P["ctT"][0:NT * 8, 0:128],
           reads=[P["ctT"]], writes=[(P["CTD_t"], 0)], tag="ctd")

    for ex in range(8):
        emit_moe_expert(mk, P, NCH, f"f{ex}", P["L1_e_gate"][ex], P["L1_e_up"][ex],
                        P["L1_e_down"][ex], 12, True, ex, P["XN1F"], P["ACC1"], "ACC1_t")

    for tt in range(NT):
        a1 = P["fx_a"]
        a3 = P["fx_c"]
        mk.dma("sync", f"f1{tt%2}", a1[:, 0:1024], P["ACC1"][tt * 128:(tt + 1) * 128, :],
               reads=[(P["ACC1_t"], tt)], writes=[a1, (a1, 0), (a1, 1)], tag=f"f1_{tt}")
        mk.dma("gpsimd", f"f3{tt%2}", a3[:, 0:1024], P["X1OWN"][tt * 128:(tt + 1) * 128, :],
               writes=[a3, (a3, 0), (a3, 1)], tag=f"f3_{tt}")
        mk.op("vector", lambda e, a1=a1, a3=a3: e.tensor_tensor(a1[:, 0:1024], a1[:, 0:1024], a3[:, 0:1024], ALU.add),
              reads=[a3, a1], writes=[a1], tag=f"fb_{tt}")
        mk.dma("sync", f"x1d{tt%2}", P["OUT"][tt * 128:(tt + 1) * 128, :], a1[:, 0:1024],
               reads=[a1], writes=[(P["OUT_t"], tt)], tag=f"outd_{tt}")


def alloc_phaseF(nc, S, P, NCH, NCORES=8):
    T = NCH * 128
    OWN = T // 2
    NT = OWN // 128
    NTOK = OWN * NCORES
    NGT = NTOK // 128
    HHALF = OWN // 2

    def sb(name, shape, dt=F32):
        t = S.enter_context(nc.sbuf_tensor(name, shape, dt))
        P[name] = t
        return t

    def ps(name, shape, dt=F32):
        t = S.enter_context(nc.psum_tensor(name, shape, dt))
        P[name] = t
        return t

    sb("s_iota8", [128, 8], F32)
    sb("s_ones128", [1, 128], F32)
    sb("s_identf", [128, 128], F32)
    sb("s_msel", [128, 8], F32)
    sb("stat1", [128, 8], F32)
    sb("lg_t", [128, 8], F32)
    sb("prb", [128, 8], F32)
    sb("eqb", [128, 8], F32)
    sb("msk", [128, 8], F32)
    sb("cnt_t", [128, 8], F32)
    sb("OH1", [128, NGT * 8], F32)
    sb("OH2", [128, NGT * 8], F32)
    sb("W1", [128, NGT], F32)
    sb("W2", [128, NGT], F32)
    sb("S_tm", [128, NGT * 8], F32)
    sb("cntT", [8, (NGT // 4) * 128], F32)
    sb("scanT", [8, (NGT // 4) * 128], F32)
    sb("carry", [8, 1], F32)
    sb("xS", [128, 8], F32)
    sb("xO1", [128, 8], F32)
    sb("xO2", [128, 8], F32)
    sb("xW", [128, 2], F32)
    sb("ct_all", [128, NT * 8], F32)
    sb("ctT", [128, 128], F32)
    sb("ce_row", [1, OWN], F32)
    sb("ce_b", [128, OWN], BF16)
    sb("gate_sb", [128, 8 * 1536], BF16)
    sb("up_sb", [128, 8 * 1536], BF16)
    sb("down_sb", [128, 12 * 1024], BF16)
    P["wstage"] = [sb("wstage0", [128, 1536], F32), sb("wstage1", [128, 1536], F32)]
    sb("h2f_a", [128, 8 * HHALF], BF16)
    sb("h2f_b", [128, 8 * HHALF], BF16)
    sb("act_sb", [128, 12 * HHALF], BF16)
    sb("scrS", [128, 512], BF16)
    sb("scrS2", [128, 512], BF16)
    sb("dn_t", [128, 1024], F32)
    P["fx_a"] = P["dn_t"]
    P["fx_c"] = P["wstage"][0]
    for i in range(4):
        ps(f"PG{i}", [128, 512], F32)
        ps(f"PU{i}", [128, 512], F32)
    P["PT2"] = P["PU3"]
    for nm in ["CTD_t", "ACC1_t", "OUT_t"]:
        P[nm] = object()
    return P


def build_phaseF(NCH, NCORES=8, CAPV=4096.0):
    T = NCH * 128
    OWN = T // 2
    NTOK = OWN * NCORES
    nc = bass.Bass("TRN2", target_bir_lowering=False, debug=False)
    P = {}
    P["LOGALL"] = nc.dram_tensor("LOGALL", [NTOK, 8], F32, kind="ExternalInput").ap()
    P["XN1F"] = nc.dram_tensor("XN1F", [D, OWN], BF16, kind="ExternalInput").ap()
    P["ACC1IN"] = nc.dram_tensor("ACC1IN", [OWN, D], F32, kind="ExternalInput").ap()
    P["X1OWN"] = nc.dram_tensor("X1OWN", [OWN, D], F32, kind="ExternalInput").ap()
    P["L1_e_gate"] = nc.dram_tensor("L1_e_gate", [8, D, 1536], F32, kind="ExternalInput").ap()
    P["L1_e_up"] = nc.dram_tensor("L1_e_up", [8, D, 1536], F32, kind="ExternalInput").ap()
    P["L1_e_down"] = nc.dram_tensor("L1_e_down", [8, 1536, D], F32, kind="ExternalInput").ap()
    for nm, shape, dt in [("identf32", [128, 128], F32), ("ones1x128", [1, 128], F32),
                          ("iota8", [128, 8], F32), ("msel", [128, 8], F32)]:
        P[nm] = nc.dram_tensor(nm, shape, dt, kind="ExternalInput").ap()
    P["ACC1"] = nc.dram_tensor("ACC1W", [OWN, D], F32)
    P["CTD"] = nc.dram_tensor("CTD", [OWN // 128, 8, 128], F32)
    P["OUT"] = nc.dram_tensor("OUT", [OWN, D], F32, kind="ExternalOutput").ap()
    with contextlib.ExitStack() as S:
        alloc_phaseF(nc, S, P, NCH, NCORES)
        mk = MK(nc)
        NT = OWN // 128
        for tt in range(NT):
            mk.dma("sync", f"acs{tt%2}", P["ACC1"][tt * 128:(tt + 1) * 128, :], P["ACC1IN"][tt * 128:(tt + 1) * 128, :],
                   writes=[(P["ACC1_t"], tt)], tag=f"acs_{tt}")
        emit_phaseF(mk, P, NCH, NCORES, CAPV)
        emit_final_barrier(mk, [P["OUT_t"]])
        mk.build(S)
    return nc


def emit_final_barrier(mk, markers):
    toks = []
    for m in markers:
        for (kid, sub), st in mk.state.items():
            if kid == id(m) and st["w"] is not None:
                toks.append(st["w"])
    mk.op("sync", lambda e: e.nop(), also=toks, tag="final")




def alloc_phaseA(nc, S, P):
    def sb(name, shape, dt=F32):
        t = S.enter_context(nc.sbuf_tensor(name, shape, dt))
        P[name] = t
        return t

    def ps(name, shape, dt=F32):
        t = S.enter_context(nc.psum_tensor(name, shape, dt))
        P[name] = t
        return t

    for nm in ["WqT_sb", "WkT_sb", "WvT_sb", "WpT_sb", "wq_sb", "wk_sb", "wv_sb"]:
        sb(nm, [128, 8 * 1024], BF16)

    sb("s_ident", [128, 128], BF16)
    sb("s_tri", [128, 128], F32)
    sb("s_shE", [64, 128], BF16)
    sb("s_shO", [64, 128], BF16)
    sb("s_ones128", [1, 128], F32)
    P["s_ones64"] = P["s_ones128"]
    sb("s_maskA", [128, 1], F32)
    sb("s_maskB", [128, 1], F32)
    sb("s_bpA", [128, 1024], F32)
    sb("s_bpB", [128, 1024], F32)
    sb("s_n1b", [128, 1024], F32)
    S32a = S.enter_context(nc.sbuf_tensor("S32a", [64, H * 65], F32))
    S16a = S.enter_context(nc.sbuf_tensor("S16a", [64, H * 65], BF16))
    P["S32"] = [S32a[:, i * 65:(i + 1) * 65] for i in range(H)]
    P["S16"] = [S16a[:, i * 65:(i + 1) * 65] for i in range(H)]
    for nm in ["sx0", "sx1", "xn0", "xnsel0", "attmp0",
               "scrA", "scrH"]:
        sb(nm, [128, 1024], F32)
    sb("scrB", [128, 512], F32)
    P["wstage"] = [P["sx0"], P["sx1"]]
    for nm in ["xnb0", "xnf0", "xnf1", "kp0", "qp0",
               "hb0", "hfm0", "hfm1",
               "yfm0", "yfm1"]:
        sb(nm, [128, 1024], BF16)
    for nm in ["vaug0", "vaugf0"]:
        sb(nm, [128, H * 65], BF16)
    for i in range(4):
        sb(f"yh{i}", [64, 128], BF16)
    for nm in ["qpfa0", "kpfa0", "kfa0", "qfa0"]:
        sb(nm, [64, 2048], BF16)
    sb("Pm0", [128, 128], BF16)
    sb("rb0", [64, 128], F32)
    sb("statL", [128, 10], F32)
    sb("qkss", [128, 32], F32)
    sb("hss16", [128, 16], F32)
    sb("qkT", [32, 129], F32)
    sb("s_identf", [128, 128], F32)
    sb("den", [1, 128], F32)
    sb("rec", [1, 128], F32)
    ps("PP0", [128, 512], F32)
    ps("PP1", [128, 512], F32)
    ps("PA", [128, 512], F32)
    ps("PO", [128, 512], F32)
    ps("PC", [64, 512], F32)
    ps("PY", [128, 512], F32)
    ps("PS", [64, 512], F32)
    ps("PBT", [128, 1024], BF16)
    for nm in ["XN0_t", "AT0_t", "VA0_t", "KF0_t", "QF0_t", "QKM0_t"]:
        P[nm] = object()
    return P


def build_phaseA(NCH):
    T = NCH * 128
    nc = bass.Bass("TRN2", target_bir_lowering=False, debug=False)
    P = {}
    P["XIN"] = nc.dram_tensor("XIN", [T, D], F32, kind="ExternalInput").ap()
    for nm in ["L0_attn_WqT", "L0_attn_WkT", "L0_attn_WvT", "L0_attn_WpT",
               "L0_ffn_wq", "L0_ffn_wk", "L0_ffn_wv"]:
        P[nm] = nc.dram_tensor(nm, [D, D], F32, kind="ExternalInput").ap()
    P["L0_attn_bp"] = nc.dram_tensor("L0_attn_bp", [1, D], F32, kind="ExternalInput").ap()
    P["L0_ffn_norm1_w"] = nc.dram_tensor("L0_ffn_norm1_w", [1, D], F32, kind="ExternalInput").ap()
    for nm, shape, dt in [("identbf", [128, 128], BF16), ("trimask", [128, 128], F32),
                          ("shiftE", [64, 128], BF16), ("shiftO", [64, 128], BF16),
                          ("ones1x64", [1, 64], F32), ("ones1x128", [1, 128], F32),
                          ("maskA", [128, 1], F32), ("maskB", [128, 1], F32),
                          ("identf32", [128, 128], F32)]:
        P[nm] = nc.dram_tensor(nm, shape, dt, kind="ExternalInput").ap()
    P["XN0"] = nc.dram_tensor("XN0", [T // 2, D], F32, kind="ExternalOutput").ap()
    P["AT0"] = nc.dram_tensor("AT0", [T // 2, D], F32, kind="ExternalOutput").ap()
    P["VA0"] = nc.dram_tensor("VA0", [T, H * 65], BF16, kind="ExternalOutput").ap()
    P["KF0"] = nc.dram_tensor("KF0", [H, 64, T], BF16, kind="ExternalOutput").ap()
    P["QF0"] = nc.dram_tensor("QF0", [H, 64, T // 2], BF16, kind="ExternalOutput").ap()
    P["QKM0"] = nc.dram_tensor("QKM0", [32, 1], F32, kind="ExternalOutput").ap()

    with contextlib.ExitStack() as S:
        alloc_phaseA(nc, S, P)
        mk = MK(nc)
        emit_phaseA(mk, P, NCH)
        emit_final_barrier(mk, [P["XN0_t"], P["AT0_t"], P["VA0_t"], P["KF0_t"], P["QF0_t"], P["QKM0_t"]])
        mk.build(S)
    return nc


def host_constants():
    import ml_dtypes as mld
    c = {}
    c["identbf"] = np.eye(128, dtype=np.float32).astype(mld.bfloat16)
    tri = np.zeros((128, 128), dtype=np.float32)
    for j in range(128):
        tri[j, j:] = 1.0
    c["trimask"] = tri
    shE = np.zeros((64, 128), dtype=np.float32)
    shO = np.zeros((64, 128), dtype=np.float32)
    for p in range(64):
        shE[p, p] = 1.0
        shO[p, p + 64] = 1.0
    c["shiftE"] = shE.astype(mld.bfloat16)
    c["shiftO"] = shO.astype(mld.bfloat16)
    c["ones1x64"] = np.ones((1, 64), dtype=np.float32)
    c["ones1x128"] = np.ones((1, 128), dtype=np.float32)
    c["identf32"] = np.eye(128, dtype=np.float32)
    return c


def phaseA_inmap(x_full_batch, weights, h):
    c = host_constants()
    m = {
        "XIN": np.ascontiguousarray(x_full_batch),
        "L0_attn_WqT": np.ascontiguousarray(weights["L0_attn_Wq"].T),
        "L0_attn_WkT": np.ascontiguousarray(weights["L0_attn_Wk"].T),
        "L0_attn_WvT": np.ascontiguousarray(weights["L0_attn_Wv"].T),
        "L0_attn_WpT": np.ascontiguousarray(weights["L0_attn_Wp"].T),
        "L0_ffn_wq": np.ascontiguousarray(weights["L0_ffn_wq"]),
        "L0_ffn_wk": np.ascontiguousarray(weights["L0_ffn_wk"]),
        "L0_ffn_wv": np.ascontiguousarray(weights["L0_ffn_wv"]),
        "L0_attn_bp": np.ascontiguousarray(weights["L0_attn_bp"]).reshape(1, D),
        "L0_ffn_norm1_w": np.ascontiguousarray(weights["L0_ffn_norm1_w"]).reshape(1, D),
        "maskA": np.full((128, 1), 1.0 if h == 0 else 0.0, np.float32),
        "maskB": np.full((128, 1), 0.0 if h == 0 else 1.0, np.float32),
    }
    m.update(c)
    return m


def rope_tables_np(T):
    half = 32
    inv = 1.0 / (10000.0 ** (np.arange(half, dtype=np.float64) / half))
    pos = np.arange(T, dtype=np.float64)
    ang = pos[:, None] * inv[None, :]
    sin = np.tile(np.sin(ang), (1, 16)).astype(np.float32)
    cos = np.tile(np.cos(ang), (1, 16)).astype(np.float32)
    return np.ascontiguousarray(sin), np.ascontiguousarray(cos)


NCH_FULL = 32


def kernel(**inputs):
    ins = {k: np.asarray(v, dtype=np.float32) for k, v in inputs.items()}
    NCH = NCH_FULL
    T = NCH * 128
    OWN = T // 2
    cores = list(range(8))
    c = host_constants()
    identf32 = np.eye(128, dtype=np.float32)
    iota8 = np.ascontiguousarray(np.broadcast_to(np.arange(8, dtype=np.float32), (128, 8)))
    x = ins["L0_attn_x"]

    ncA = build_phaseA(NCH)
    in_maps = []
    wA = {
        "L0_attn_WqT": np.ascontiguousarray(ins["L0_attn_Wq"].T),
        "L0_attn_WkT": np.ascontiguousarray(ins["L0_attn_Wk"].T),
        "L0_attn_WvT": np.ascontiguousarray(ins["L0_attn_Wv"].T),
        "L0_attn_WpT": np.ascontiguousarray(ins["L0_attn_Wp"].T),
        "L0_ffn_wq": np.ascontiguousarray(ins["L0_ffn_wq"]),
        "L0_ffn_wk": np.ascontiguousarray(ins["L0_ffn_wk"]),
        "L0_ffn_wv": np.ascontiguousarray(ins["L0_ffn_wv"]),
        "L0_attn_bp": np.ascontiguousarray(ins["L0_attn_bp"]).reshape(1, D),
        "L0_ffn_norm1_w": np.ascontiguousarray(ins["L0_ffn_norm1_w"]).reshape(1, D),
    }
    wA.update(c)
    for ci in cores:
        b, h = ci // 2, ci % 2
        m = dict(wA)
        m["XIN"] = np.ascontiguousarray(x[b])
        m["maskA"] = np.full((128, 1), 1.0 if h == 0 else 0.0, np.float32)
        m["maskB"] = np.full((128, 1), 0.0 if h == 0 else 1.0, np.float32)
        in_maps.append(m)
    resA = run_bass_kernel_spmd(ncA, in_maps, core_ids=cores)
    outA = resA.results

    ncB = build_phaseB(NCH)
    wB = {
        "L0_ffn_wo": np.ascontiguousarray(ins["L0_ffn_wo"]),
        "L0_ffn_router_w": np.ascontiguousarray(ins["L0_ffn_router_w"]),
        "L0_ffn_norm2_w": np.ascontiguousarray(ins["L0_ffn_norm2_w"]).reshape(1, D),
        "identf32": identf32,
        "ones1x64": c["ones1x64"], "ones1x128": c["ones1x128"],
        "shiftE": c["shiftE"], "shiftO": c["shiftO"],
    }
    in_maps = []
    for ci in cores:
        m = dict(wB)
        for nm in ["KF0", "QF0", "VA0", "XN0", "QKM0"]:
            m[nm] = outA[ci][nm]
        in_maps.append(m)
    resB = run_bass_kernel_spmd(ncB, in_maps, core_ids=cores)
    outB = resB.results

    ncC = build_phaseC(NCH)
    wC = {
        "L0_ffn_sh_gate": np.ascontiguousarray(ins["L0_ffn_sh_gate"]),
        "L0_ffn_sh_up": np.ascontiguousarray(ins["L0_ffn_sh_up"]),
        "L0_ffn_sh_down": np.ascontiguousarray(ins["L0_ffn_sh_down"]),
        "L0_ffn_e_gate": np.ascontiguousarray(ins["L0_ffn_e_gate"]),
        "L0_ffn_e_up": np.ascontiguousarray(ins["L0_ffn_e_up"]),
        "L0_ffn_e_down": np.ascontiguousarray(ins["L0_ffn_e_down"]),
        "identf32": identf32, "ones1x128": c["ones1x128"], "iota8": iota8,
    }
    in_maps = []
    for ci in cores:
        b, h = ci // 2, ci % 2
        m = dict(wC)
        m["H2F"] = outB[ci]["H2F"]
        m["LOG0"] = outB[ci]["LOG0"]
        m["ACCIN"] = outB[ci]["ACC"]
        m["AT0"] = outA[ci]["AT0"]
        m["XOWN"] = np.ascontiguousarray(x[b, h * OWN:(h + 1) * OWN])
        in_maps.append(m)
    resC = run_bass_kernel_spmd(ncC, in_maps, core_ids=cores)
    outC = resC.results

    ncD = build_phaseD(NCH)
    Wqkv = ins["L1_attn_Wqkv"]
    sint, cost = rope_tables_np(T)
    wD = {
        "L1_WQ": np.ascontiguousarray(Wqkv[:, 0:D]),
        "L1_WK": np.ascontiguousarray(Wqkv[:, D:2 * D]),
        "L1_WV": np.ascontiguousarray(Wqkv[:, 2 * D:3 * D]),
        "L1_ffn_w_gate": np.ascontiguousarray(ins["L1_ffn_w_gate"]),
        "SINT": sint, "COST": cost,
        "identbf": c["identbf"], "identf32": identf32,
    }
    in_maps = []
    for ci in cores:
        b, h = ci // 2, ci % 2
        m = dict(wD)
        m["X1F"] = np.concatenate([np.asarray(outC[2 * b]["X1"], np.float32),
                                   np.asarray(outC[2 * b + 1]["X1"], np.float32)], axis=0)
        m["maskA"] = np.full((128, 1), 1.0 if h == 0 else 0.0, np.float32)
        m["maskB"] = np.full((128, 1), 0.0 if h == 0 else 1.0, np.float32)
        in_maps.append(m)
    resD = run_bass_kernel_spmd(ncD, in_maps, core_ids=cores)
    outD = resD.results

    ncE = build_phaseE(NCH)
    wE = {
        "L1_attn_Wo": np.ascontiguousarray(ins["L1_attn_Wo"]),
        "L1_attn_bo": np.ascontiguousarray(ins["L1_attn_bo"]).reshape(1, D),
        "ones1x64": c["ones1x64"], "ones1x128": c["ones1x128"],
        "shiftE": c["shiftE"], "shiftO": c["shiftO"],
        "neg18": np.full((128, 1), -0.125, np.float32),
    }
    in_maps = []
    for ci in cores:
        m = dict(wE)
        for nm in ["KF1", "QF1", "VA1"]:
            m[nm] = outD[ci][nm]
        in_maps.append(m)
    resE = run_bass_kernel_spmd(ncE, in_maps, core_ids=cores)
    outE = resE.results

    ncF = build_phaseF(NCH)
    c_fc = ins["L1_ffn_c_fc"]
    wF = {
        "L1_e_gate": np.ascontiguousarray(c_fc[:, :, 0:1536]),
        "L1_e_up": np.ascontiguousarray(c_fc[:, :, 1536:3072]),
        "L1_e_down": np.ascontiguousarray(ins["L1_ffn_c_proj"]),
        "identf32": identf32, "ones1x128": c["ones1x128"], "iota8": iota8,
    }
    LOGALL = np.concatenate([np.asarray(outD[ci]["LOG1"], np.float32) for ci in cores], axis=0)
    in_maps = []
    for ci in cores:
        m = dict(wF)
        m["LOGALL"] = LOGALL
        m["XN1F"] = outD[ci]["XN1F"]
        m["ACC1IN"] = np.asarray(outE[ci]["ACC1"], np.float32)
        m["X1OWN"] = np.asarray(outC[ci]["X1"], np.float32)
        msel = np.zeros((128, 8), np.float32)
        msel[:, ci] = 1.0
        m["msel"] = msel
        in_maps.append(m)
    resF = run_bass_kernel_spmd(ncF, in_maps, core_ids=cores)
    outF = resF.results

    out = np.zeros((4, 2 * OWN, D), np.float32)
    for ci in cores:
        b, h = ci // 2, ci % 2
        out[b, h * OWN:(h + 1) * OWN] = np.asarray(outF[ci]["OUT"], np.float32)
    return out


# version 04
# speedup vs baseline: 1.0615x; 1.0615x over previous
import contextlib
import numpy as np
import concourse.bass as bass
import concourse.mybir as mybir
from concourse.bass_utils import run_bass_kernel_spmd

F32 = mybir.dt.float32
BF16 = mybir.dt.bfloat16
I32 = mybir.dt.int32
AF = mybir.ActivationFunctionType
ALU = mybir.AluOpType

ENGINES = ("sync", "vector", "scalar", "tensor", "gpsimd")

D = 1024
H = 16
DH = 64



class _Op:
    __slots__ = ("eng", "fn", "after", "dma_sem", "needs_inc", "inc_val", "pos", "tag", "dma_floor")

    def __init__(self, eng, fn, after, dma_sem, tag):
        self.eng = eng
        self.fn = fn
        self.after = after
        self.dma_sem = dma_sem
        self.needs_inc = False
        self.inc_val = None
        self.pos = None
        self.tag = tag
        self.dma_floor = None


class Tok:
    __slots__ = ("op",)

    def __init__(self, op):
        self.op = op


def _key(b):
    if isinstance(b, tuple):
        t, k = b
        return (id(t), k)
    return (id(b), "")


class MK:
    def __init__(self, nc):
        self.nc = nc
        self.eng_ops = {e: [] for e in ENGINES}
        self.dma_sem_names = []
        self.state = {}
        self.dma_counts = {}
        self.last_dma = {}

    def _deps(self, reads, writes):
        deps = []
        for b in reads:
            st = self.state.get(_key(b))
            if st and st["w"] is not None:
                deps.append(st["w"])
        for b in writes:
            st = self.state.get(_key(b))
            if st:
                if st["w"] is not None:
                    deps.append(st["w"])
                deps.extend(st["r"])
        return deps

    def _commit(self, tok, reads, writes):
        for b in reads:
            st = self.state.setdefault(_key(b), {"w": None, "r": []})
            st["r"].append(tok)
        for b in writes:
            st = self.state.setdefault(_key(b), {"w": None, "r": []})
            st["w"] = tok
            st["r"] = []

    def op(self, eng, fn, reads=(), writes=(), also=(), tag="", track=True):
        after = self._deps(reads, writes) + [t for t in also if t is not None]
        o = _Op(eng, fn, after, None, tag)
        o.pos = len(self.eng_ops[eng])
        if any(t.op.dma_sem is not None for t in after):
            o.dma_floor = dict(self.dma_counts)
        self.eng_ops[eng].append(o)
        tok = Tok(o)
        if track:
            self._commit(tok, reads, writes)
        return tok

    def raw_op(self, eng, fn, also=(), tag=""):
        return self.op(eng, fn, also=[t for t in also if t is not None], tag=tag, track=False)

    def dma(self, eng, sem, out, in_, reads=(), writes=(), also=(), accum=False, tag="", **kw):
        if sem not in self.dma_sem_names:
            self.dma_sem_names.append(sem)
        if accum:
            kw = dict(kw)
            kw["accum_op"] = ALU.add
            assert eng == "gpsimd", "accum dma must use gpsimd (software DGE)"
        reads = list(reads)
        writes = list(writes)
        after = self._deps(reads, writes) + [t for t in also if t is not None]

        def fn(e, out=out, in_=in_, kw=kw):
            return e.dma_start(out=out, in_=in_, **kw)

        o = _Op(eng, fn, after, sem, tag)
        o.pos = len(self.eng_ops[eng])
        if any(t.op.dma_sem is not None for t in after):
            o.dma_floor = dict(self.dma_counts)
        self.dma_counts[sem] = self.dma_counts.get(sem, 0) + 16
        o.inc_val = self.dma_counts[sem]
        self.eng_ops[eng].append(o)
        tok = Tok(o)
        self.last_dma[sem] = tok
        self._commit(tok, reads, writes)
        return tok

    def mm_group(self, mms, reads=(), writes=(), also=(), tag=""):
        toks = []
        for i, fn in enumerate(mms):
            if i == 0:
                t = self.op("tensor", fn, reads=reads, writes=writes, also=also, tag=tag + "_0")
            else:
                t = self.raw_op("tensor", fn, tag=f"{tag}_{i}")
            toks.append(t)
        for b in writes:
            st = self.state.setdefault(_key(b), {"w": None, "r": []})
            st["w"] = toks[-1]
            st["r"] = []
        if len(toks) > 1:
            for b in reads:
                st = self.state.setdefault(_key(b), {"w": None, "r": []})
                st["r"].append(toks[-1])
        return toks[-1]

    def writer_of(self, b):
        st = self.state.get(_key(b))
        return st["w"] if st else None

    def phase_barrier(self, tag="bar"):
        toks = []
        for eng in ENGINES:
            if self.eng_ops[eng]:
                o = self.eng_ops[eng][-1]
                if o.dma_sem is None:
                    o.needs_inc = True
                toks.append(Tok(o))
        toks.extend(self.last_dma.values())
        for eng in ENGINES:
            if not self.eng_ops[eng]:
                continue
            self.op(eng, lambda e: e.nop(), also=toks, tag=f"{tag}_{eng}", track=False)
        self.state = {}

    def build(self, stack):
        nc = self.nc
        for eng in ENGINES:
            for o in self.eng_ops[eng]:
                for t in o.after:
                    if t.op.dma_sem is None and t.op.eng != o.eng:
                        t.op.needs_inc = True
        eng_counts = {e: 0 for e in ENGINES}
        for eng in ENGINES:
            for o in self.eng_ops[eng]:
                if o.dma_sem is None and o.needs_inc:
                    eng_counts[eng] += 1
                    o.inc_val = eng_counts[eng]
        eng_sems = {e: stack.enter_context(nc.semaphore(f"S_{e}")) for e in ENGINES if self.eng_ops[e]}
        dma_sems = {s: stack.enter_context(nc.semaphore(f"D_{s}")) for s in self.dma_sem_names}
        block = stack.enter_context(nc.Block())
        mkself = self

        def make_body(eng):
            def body(e):
                highwater = {}
                drained_upto = 0
                cur_pos = 0
                for o in mkself.eng_ops[eng]:
                    need_drain = False
                    for t in o.after:
                        top = t.op
                        if top.dma_sem is not None:
                            key = ("d", top.dma_sem)
                            val = top.inc_val
                            if o.dma_floor is not None:
                                val = max(val, o.dma_floor.get(top.dma_sem, val))
                            if highwater.get(key, 0) < val:
                                e.wait_ge(dma_sems[top.dma_sem], val)
                                highwater[key] = val
                        elif top.eng != eng:
                            key = ("e", top.eng)
                            val = top.inc_val
                            if highwater.get(key, 0) < val:
                                e.wait_ge(eng_sems[top.eng], val)
                                highwater[key] = val
                        else:
                            if top.pos >= drained_upto:
                                need_drain = True
                    if need_drain and eng != "tensor":
                        e.drain()
                        drained_upto = cur_pos
                    inst = o.fn(e)
                    if o.dma_sem is not None:
                        inst.then_inc(dma_sems[o.dma_sem], 16)
                    elif o.needs_inc:
                        inst.then_inc(eng_sems[eng], 1)
                    cur_pos += 1

            return body

        for eng in ENGINES:
            if not self.eng_ops[eng]:
                continue
            getattr(block, eng)(make_body(eng))


class Cyc:
    def __init__(self, items):
        self.items = list(items)

    def __getitem__(self, i):
        return self.items[i % len(self.items)]




def emit_load_weight_bf16(mk, P, src_dram, dst_sb, rows, cols, tag):
    nch = rows // 128
    for c in range(nch):
        stg = P["wstage"][c % 2]
        mk.dma("sync", f"wld{c%2}", stg[:, 0:cols], src_dram[c * 128:(c + 1) * 128, :],
               writes=[stg], tag=f"wld_{tag}_{c}")
        mk.op("vector",
              lambda e, dst_sb=dst_sb, c=c, stg=stg, cols=cols: e.tensor_copy(
                  dst_sb[:, c * cols:(c + 1) * cols], stg[:, 0:cols]),
              reads=[stg], writes=[(dst_sb, f"c{c}")], tag=f"wcv_{tag}_{c}")


def emit_bcast_row(mk, P, psum, src_handle, src_cols, dst_sb, tag, nparts=128):
    nch = (src_cols + 511) // 512
    t = None
    for i in range(nch):
        lo, hi = i * 512, min(src_cols, i * 512 + 512)
        mk.op("tensor",
              lambda e, lo=lo, hi=hi: e.matmul(
                  psum[0:nparts, 0:hi - lo], P["s_ones128"][0:1, 0:nparts], src_handle[0:1, lo:hi],
                  start=True, stop=True),
              reads=[P["s_ones128"], src_handle], writes=[psum], tag=f"{tag}_mm{i}")
        t = mk.op("vector",
                  lambda e, lo=lo, hi=hi: e.tensor_copy(dst_sb[0:nparts, lo:hi], psum[0:nparts, 0:hi - lo]),
                  reads=[psum], writes=[(dst_sb, f"bc{i}")], tag=f"{tag}_ev{i}")
    st = mk.state.setdefault(_key(dst_sb), {"w": None, "r": []})
    st["w"] = t
    st["r"] = []
    return t




def mk_s16_read(mk, P, hh):
    return P["S16"][hh]


def emit_phaseA(mk, P, NCH):
    T = NCH * 128
    PP0, PP1 = P["PP0"], P["PP1"]
    PA, PO, PC, PY, PBT = P["PA"], P["PO"], P["PC"], P["PY"], P["PBT"]
    ident = P["s_ident"]
    statL = P["statL"]
    scrA, scrB, scrH = P["scrA"], P["scrB"], P["scrH"]

    emit_load_weight_bf16(mk, P, P["L0_attn_WqT"], P["WqT_sb"], D, D, "WqT")
    emit_load_weight_bf16(mk, P, P["L0_attn_WkT"], P["WkT_sb"], D, D, "WkT")
    emit_load_weight_bf16(mk, P, P["L0_attn_WvT"], P["WvT_sb"], D, D, "WvT")
    emit_load_weight_bf16(mk, P, P["L0_attn_WpT"], P["WpT_sb"], D, D, "WpT")
    emit_load_weight_bf16(mk, P, P["L0_ffn_wq"], P["wq_sb"], D, D, "wq")
    emit_load_weight_bf16(mk, P, P["L0_ffn_wk"], P["wk_sb"], D, D, "wk")
    emit_load_weight_bf16(mk, P, P["L0_ffn_wv"], P["wv_sb"], D, D, "wv")

    for nm, dst in [("identbf", "s_ident"), ("trimask", "s_tri"), ("shiftE", "s_shE"),
                    ("shiftO", "s_shO"), ("ones1x128", "s_ones128"),
                    ("maskA", "s_maskA"), ("maskB", "s_maskB"), ("identf32", "s_identf")]:
        mk.dma("sync", "cst", P[dst][:, :], P[nm], writes=[P[dst]], tag=f"c_{nm}")
    mk.dma("sync", "crow", P["scrH"][0:1, 0:D], P["L0_attn_bp"], writes=[P["scrH"]], tag="c_bp")
    emit_bcast_row(mk, P, PP0, P["scrH"], D, P["scrA"], "bcbp")
    mk.op("vector", lambda e: e.tensor_scalar(P["s_bpA"][:, :], P["scrA"][:, :], P["s_maskA"][:, 0:1], None, ALU.mult),
          reads=[P["scrA"], P["s_maskA"]], writes=[P["s_bpA"]], tag="bpA")
    mk.op("vector", lambda e: e.tensor_scalar(P["s_bpB"][:, :], P["scrA"][:, :], P["s_maskB"][:, 0:1], None, ALU.mult),
          reads=[P["scrA"], P["s_maskB"]], writes=[P["s_bpB"]], tag="bpB")
    mk.dma("sync", "crow", P["scrH"][0:1, 0:D], P["L0_ffn_norm1_w"], writes=[P["scrH"]], tag="c_n1")
    emit_bcast_row(mk, P, PP1, P["scrH"], D, P["s_n1b"], "bcn1")

    for hh in range(H):
        mk.op("vector", lambda e, hh=hh: e.memset(P["S32"][hh][:, :], 0.0), writes=[P["S32"][hh]], tag=f"S0_{hh}")
        mk.op("vector", lambda e, hh=hh: e.memset(P["S16"][hh][:, :], 0.0), writes=[P["S16"][hh]], tag=f"Sb0_{hh}")
    mk.op("vector", lambda e: e.memset(P["qkss"][:, :], 0.0), writes=[P["qkss"]], tag="qkss0")

    c_sx = Cyc([P["sx0"], P["sx1"]])
    c_xn = Cyc([P["xn0"]])
    c_xnsel = Cyc([P["xnsel0"]])
    c_attmp = Cyc([P["attmp0"]])
    c_vaug = Cyc([P["vaug0"]])
    c_vaugf = Cyc([P["vaugf0"]])
    c_yfm = Cyc([P["yfm0"], P["yfm1"]])
    c_qp = Cyc([P["qp0"]])
    c_kp = Cyc([P["kp0"]])
    c_hb = Cyc([P["hb0"]])
    c_hfm = Cyc([P["hfm0"]])
    c_xnf = Cyc([P["xnf0"]])
    c_xnb = Cyc([P["xnb0"]])
    c_qpf = Cyc([P["qpfa0"]])
    c_kpf = Cyc([P["kpfa0"]])
    c_kfa = Cyc([P["kfa0"]])
    c_qfa = Cyc([P["qfa0"]])

    def proj_tokmajor(g, src_fm, W_sb, dst_bf, featmap, tag):
        for half in range(2):
            pp = PP0 if half == 0 else PP1
            mms = [
                (lambda e, c=c, half=half, pp=pp, src_fm=src_fm, W_sb=W_sb: e.matmul(
                    pp[:, 0:512], src_fm[:, c * 128:(c + 1) * 128],
                    W_sb[:, c * 1024 + half * 512: c * 1024 + half * 512 + 512],
                    start=(c == 0), stop=(c == 7)))
                for c in range(8)
            ]
            mk.mm_group(mms, reads=[src_fm, W_sb], writes=[pp], tag=f"{tag}mm{half}_g{g}")
            sl = slice(half * 512, half * 512 + 512)
            if not featmap:
                mk.op("vector", lambda e, pp=pp, sl=sl, dst_bf=dst_bf: e.tensor_copy(dst_bf[:, sl], pp[:, 0:512]),
                      reads=[pp], writes=[(dst_bf, f"h{half}")], tag=f"{tag}ev{half}_g{g}")
            else:
                mk.op("vector", lambda e, pp=pp, sl=sl: e.tensor_scalar(scrA[:, sl], pp[:, 0:512], 0.0, None, ALU.min),
                      reads=[pp], writes=[(scrA, half)], tag=f"{tag}f1{half}_g{g}")
                mk.op("scalar", lambda e, sl=sl: e.activation(scrB[:, 0:512], scrA[:, sl], AF.Exp),
                      reads=[(scrA, half)], writes=[scrB], tag=f"{tag}f2{half}_g{g}")
                mk.op("vector", lambda e, pp=pp, sl=sl: e.tensor_scalar(scrA[:, sl], pp[:, 0:512], 0.0, None, ALU.max),
                      reads=[pp], writes=[(scrA, half)], tag=f"{tag}f3{half}_g{g}")
                mk.op("vector", lambda e, sl=sl, dst_bf=dst_bf: e.tensor_tensor(dst_bf[:, sl], scrA[:, sl], scrB[:, 0:512], ALU.add),
                      reads=[(scrA, half), scrB], writes=[(dst_bf, f"h{half}")],
                      tag=f"{tag}f4{half}_g{g}")

    def transpose_128(g, src_bf, dst_fm, tag):
        last = None
        for c in range(8):
            last = mk.op("tensor",
                         lambda e, c=c, src_bf=src_bf: e.transpose(PBT[:, c * 128:(c + 1) * 128], src_bf[:, c * 128:(c + 1) * 128], ident[:, :]),
                         reads=[src_bf, ident], writes=[PBT] if c == 0 else [], tag=f"{tag}tr{c}_g{g}", track=(c == 0))
        st = mk.state[_key(PBT)]
        st["w"] = last
        st["r"] = []
        mk.op("vector", lambda e, dst_fm=dst_fm: e.tensor_copy(dst_fm[:, :], PBT[:, 0:1024]),
              reads=[PBT], writes=[dst_fm], tag=f"{tag}trev_g{g}")

    def head_transposes(g, src_tm, src_keys, dst_all, tag):
        for half in range(2):
            last = None
            first = True
            for hl in range(8):
                hh = half * 8 + hl
                rd = [(src_tm, src_keys[half]), ident] if first else []
                last = mk.op("tensor",
                             lambda e, hh=hh, hl=hl, src_tm=src_tm: e.transpose(
                                 PBT[0:64, hl * 128:(hl + 1) * 128], src_tm[:, hh * 64:(hh + 1) * 64], ident[:, :]),
                             reads=rd + ([PBT] if first else []), writes=[PBT] if first else [],
                             tag=f"{tag}ht{hh}_g{g}", track=first)
                first = False
            st = mk.state[_key(PBT)]
            st["w"] = last
            st["r"] = []
            mk.op("vector",
                  lambda e, half=half, dst_all=dst_all: e.tensor_copy(
                      dst_all[:, half * 1024:(half + 1) * 1024], PBT[0:64, 0:1024]),
                  reads=[PBT], writes=[(dst_all, half)], tag=f"{tag}hte{half}_g{g}")
        st = mk.state.setdefault(_key(dst_all), {"w": None, "r": []})
        st["w"] = mk.state[_key((dst_all, 1))]["w"]
        st["r"] = []

    for g in range(NCH):
        own_idx = g % (NCH // 2)
        first_half = g < NCH // 2
        mask_ap = P["s_maskA"] if first_half else P["s_maskB"]
        bp_masked = P["s_bpA"] if first_half else P["s_bpB"]

        sx = c_sx[g]
        mk.dma("sync", f"ldx{g%2}", sx[:, :], P["XIN"][g * 128:(g + 1) * 128, :], writes=[sx], tag=f"x_g{g}")

        mk.op("vector", lambda e, sx=sx: e.reduce_sum(statL[:, 0:1], sx[:, :], axis=mybir.AxisListType.X),
              reads=[sx], writes=[(statL, 0)], tag=f"s1_g{g}")
        mk.op("scalar", lambda e, sx=sx: e.square(scrH[:, 0:1024], sx[:, :]),
              reads=[sx], writes=[scrH], tag=f"sq_g{g}")
        mk.op("vector", lambda e: e.reduce_sum(statL[:, 1:2], scrH[:, 0:1024], axis=mybir.AxisListType.X),
              reads=[scrH], writes=[(statL, 1)], tag=f"s2_g{g}")
        mk.op("vector", lambda e: e.tensor_scalar(statL[:, 2:3], statL[:, 0:1], -1.0 / D, None, ALU.mult),
              reads=[(statL, 0)], writes=[(statL, 2)], tag=f"mn_g{g}")
        mk.op("vector", lambda e: e.tensor_tensor(statL[:, 3:4], statL[:, 2:3], statL[:, 2:3], ALU.mult),
              reads=[(statL, 2)], writes=[(statL, 3)], tag=f"m2_g{g}")
        mk.op("vector", lambda e: e.tensor_scalar(statL[:, 4:5], statL[:, 1:2], 1.0 / D, statL[:, 3:4], ALU.mult, ALU.subtract),
              reads=[(statL, 1), (statL, 3)], writes=[(statL, 4)], tag=f"va_g{g}")
        mk.op("vector", lambda e: e.tensor_scalar(statL[:, 4:5], statL[:, 4:5], 1e-5, None, ALU.add),
              reads=[(statL, 4)], writes=[(statL, 4)], tag=f"vae_g{g}")
        mk.op("scalar", lambda e: e.activation(statL[:, 5:6], statL[:, 4:5], AF.Ln),
              reads=[(statL, 4)], writes=[(statL, 5)], tag=f"ln_g{g}")
        mk.op("scalar", lambda e: e.activation(statL[:, 6:7], statL[:, 5:6], AF.Exp, scale=-0.5),
              reads=[(statL, 5)], writes=[(statL, 6)], tag=f"rs_g{g}")
        mk.op("vector", lambda e: e.tensor_tensor(statL[:, 7:8], statL[:, 2:3], statL[:, 6:7], ALU.mult),
              reads=[(statL, 2), (statL, 6)], writes=[(statL, 7)], tag=f"bb_g{g}")
        xn = c_xn[g]
        mk.op("vector", lambda e, sx=sx, xn=xn: e.tensor_scalar(xn[:, :], sx[:, :], statL[:, 6:7], statL[:, 7:8], ALU.mult, ALU.add),
              reads=[sx, (statL, 6), (statL, 7)], writes=[xn], tag=f"xn_g{g}")

        xnb = c_xnb[g]
        mk.op("vector", lambda e, xn=xn, xnb=xnb: e.tensor_copy(xnb[:, :], xn[:, :]),
              reads=[xn], writes=[xnb], tag=f"xnb_g{g}")
        xnf = c_xnf[g]
        transpose_128(g, xnb, xnf, "xn")

        xnsel = c_xnsel[g]
        mk.op("vector", lambda e, xn=xn, xnsel=xnsel, mask_ap=mask_ap: e.tensor_scalar(
            xnsel[:, :], xn[:, :], mask_ap[:, 0:1], None, ALU.mult),
              reads=[xn, mask_ap], writes=[xnsel], tag=f"xs_g{g}")
        mk.dma("gpsimd", f"xno{g%2}", P["XN0"][own_idx * 128:(own_idx + 1) * 128, :], xnsel[:, :],
               reads=[xnsel], writes=[(P["XN0_t"], own_idx)], accum=not first_half, tag=f"xsd_g{g}")

        qp = c_qp[g]
        proj_tokmajor(g, xnf, P["WqT_sb"], qp, True, "q_")
        kp = c_kp[g]
        proj_tokmajor(g, xnf, P["WkT_sb"], kp, True, "k_")
        qpfa = c_qpf[g]
        head_transposes(g, qp, ["h0", "h1"], qpfa, "qp")
        kpfa = c_kpf[g]
        head_transposes(g, kp, ["h0", "h1"], kpfa, "kp")

        vaug = c_vaug[g]
        mk.op("vector", lambda e, vaug=vaug: e.memset(vaug[:, :], 1.0),
              writes=[vaug, (vaug, "v0"), (vaug, "v1")], tag=f"vm_g{g}")
        for half in range(2):
            pp = PP0 if half == 0 else PP1
            mms = [
                (lambda e, c=c, pp=pp, half=half, xnf=xnf: e.matmul(
                    pp[:, 0:512], xnf[:, c * 128:(c + 1) * 128],
                    P["WvT_sb"][:, c * 1024 + half * 512: c * 1024 + half * 512 + 512],
                    start=(c == 0), stop=(c == 7)))
                for c in range(8)
            ]
            mk.mm_group(mms, reads=[xnf, P["WvT_sb"]], writes=[pp], tag=f"vmm{half}_g{g}")
            mk.op("vector",
                  lambda e, pp=pp, half=half, vaug=vaug: e.tensor_copy(
                      vaug[:, :].rearrange("p (h s) -> p h s", h=H)[:, 8 * half:8 * half + 8, 0:64],
                      pp[:, 0:512].rearrange("p (h s) -> p h s", h=8)),
                  reads=[pp, vaug], writes=[(vaug, f"v{half}")], tag=f"vev{half}_g{g}")

        yfm = c_yfm[g]
        mk.op("vector", lambda e, yfm=yfm: e.memset(yfm[:, :], 0.0),
              writes=[yfm] + [(yfm, f"p{pi2}") for pi2 in range(8)], tag=f"yfmz_g{g}")
        BANKS = [PA, PO, PC, PY]
        for w in range(4):
            bank = BANKS[w]
            for s in range(4):
                hh = 4 * w + s
                mk.op("tensor", lambda e, hh=hh, s=s, bank=bank: e.matmul(
                    bank[0:128, s * 128:(s + 1) * 128],
                    kpfa[:, hh * 128:(hh + 1) * 128], qpfa[:, hh * 128:(hh + 1) * 128],
                    start=True, stop=True),
                      reads=[(kpfa, hh // 8), (qpfa, hh // 8)], writes=[bank] if s == 0 else [],
                      track=(s == 0), tag=f"at_h{hh}_g{g}")
            st = mk.state[_key(bank)]
            st["w"] = Tok(mk.eng_ops["tensor"][-1])
            st["r"] = []
            mk.op("vector", lambda e, w=w, bank=bank: e.tensor_tensor(
                P["Pm_w"][:, 0:512].rearrange("p (s c) -> p s c", s=4),
                bank[0:128, 0:512].rearrange("p (s c) -> p s c", s=4),
                P["s_tri"][:, :].rearrange("p c -> p () c").broadcast(1, 4) if False else P["s_tri"][:, :],
                ALU.mult),
                  reads=[bank, P["s_tri"]], writes=[P["Pm_w"]], tag=f"pm_w{w}_g{g}") if False else None
            for s in range(4):
                hh = 4 * w + s
                mk.op("vector", lambda e, s=s, bank=bank: e.tensor_tensor(
                    P["Pm_w"][:, s * 128:(s + 1) * 128], bank[0:128, s * 128:(s + 1) * 128], P["s_tri"][:, :], ALU.mult),
                      reads=[bank, P["s_tri"]], writes=[(P["Pm_w"], s)], tag=f"pm_h{hh}_g{g}")
            first = True
            for s in range(4):
                hh = 4 * w + s
                mms = [
                    (lambda e, hh=hh, s=s, bank=bank, vaug=vaug: e.matmul(
                        bank[0:65, s * 128:(s + 1) * 128], vaug[:, hh * 65:(hh + 1) * 65],
                        P["Pm_w"][:, s * 128:(s + 1) * 128], start=True, stop=False)),
                    (lambda e, hh=hh, s=s, bank=bank: e.matmul(
                        bank[0:65, s * 128:(s + 1) * 128], P["S16"][hh][:, :], qpfa[:, hh * 128:(hh + 1) * 128],
                        start=False, stop=True)),
                ]
                mk.mm_group(mms, reads=[(vaug, f"v{hh//8}"), (P["Pm_w"], s), P["S16"][hh], (qpfa, hh // 8)],
                            writes=[bank] if first else [], tag=f"o65_h{hh}_g{g}")
                if not first:
                    st = mk.state[_key(bank)]
                    st["w"] = Tok(mk.eng_ops["tensor"][-1])
                first = False
            st = mk.state[_key(bank)]
            st["w"] = Tok(mk.eng_ops["tensor"][-1])
            st["r"] = []
            mk.op("vector", lambda e, bank=bank: e.tensor_scalar(
                P["den_b"][0:1, 0:512], bank[64:65, 0:512], 1e-6, None, ALU.add),
                  reads=[bank], writes=[P["den_b"]], tag=f"dn_w{w}_g{g}")
            mk.op("vector", lambda e: e.reciprocal(P["rec_b"][0:1, 0:512], P["den_b"][0:1, 0:512]),
                  reads=[P["den_b"]], writes=[P["rec_b"]], tag=f"rc_w{w}_g{g}")
            for s in range(4):
                mk.op("tensor", lambda e, s=s: e.matmul(
                    PP0[0:64, s * 128:(s + 1) * 128], P["s_ones64"][0:1, 0:64], P["rec_b"][0:1, s * 128:(s + 1) * 128],
                    start=True, stop=True),
                      reads=[P["rec_b"], P["s_ones64"]], writes=[PP0] if s == 0 else [], track=(s == 0),
                      tag=f"bc_w{w}s{s}_g{g}")
            st = mk.state[_key(PP0)]
            st["w"] = Tok(mk.eng_ops["tensor"][-1])
            st["r"] = []
            mk.op("vector", lambda e: e.tensor_copy(P["rb_w"][0:64, 0:512], PP0[0:64, 0:512]),
                  reads=[PP0], writes=[P["rb_w"]], tag=f"rbe_w{w}_g{g}")
            for s in range(4):
                hh = 4 * w + s
                mk.op("vector", lambda e, s=s, bank=bank: e.tensor_tensor(
                    P["y_w"][:, s * 128:(s + 1) * 128], bank[0:64, s * 128:(s + 1) * 128],
                    P["rb_w"][0:64, s * 128:(s + 1) * 128], ALU.mult),
                      reads=[bank, P["rb_w"]], writes=[(P["y_w"], s)], tag=f"y_h{hh}_g{g}")
            for s in range(4):
                hh = 4 * w + s
                mk.op("tensor", lambda e, hh=hh, s=s, kp=kp, vaug=vaug: e.matmul(
                    PP1[0:64, s * 128:s * 128 + 65], kp[:, hh * 64:(hh + 1) * 64], vaug[:, hh * 65:(hh + 1) * 65],
                    start=True, stop=True),
                      reads=[(kp, f"h{hh//8}"), (vaug, f"v{hh//8}")] + [mk_s16_read(mk, P, hh)],
                      writes=[PP1] if s == 0 else [], track=(s == 0), tag=f"su_h{hh}_g{g}")
            st = mk.state[_key(PP1)]
            st["w"] = Tok(mk.eng_ops["tensor"][-1])
            st["r"] = []
            mk.op("vector", lambda e, w=w: e.tensor_tensor(
                P["S32a_v"][:, :].rearrange("p (h c) -> p h c", c=65)[:, 4 * w:4 * w + 4, :],
                P["S32a_v"][:, :].rearrange("p (h c) -> p h c", c=65)[:, 4 * w:4 * w + 4, :],
                PP1[0:64, 0:512].rearrange("p (s c) -> p s c", s=4)[:, :, 0:65],
                ALU.add),
                  reads=[PP1] + [P["S32"][4 * w + s2] for s2 in range(4)],
                  writes=[P["S32"][4 * w + s2] for s2 in range(4)], tag=f"sa_w{w}_g{g}")
            mk.op("scalar", lambda e, w=w: e.copy(
                P["S16a_v"][:, :].rearrange("p (h c) -> p h c", c=65)[:, 4 * w:4 * w + 4, :],
                P["S32a_v"][:, :].rearrange("p (h c) -> p h c", c=65)[:, 4 * w:4 * w + 4, :]),
                  reads=[P["S32"][4 * w + s2] for s2 in range(4)],
                  writes=[P["S16"][4 * w + s2] for s2 in range(4)], tag=f"sc_w{w}_g{g}")
            for pp_i in range(2):
                pi = 2 * w + pp_i
                mms = [
                    (lambda e, pp_i=pp_i: e.matmul(PP1[0:128, pp_i * 128:(pp_i + 1) * 128], P["s_shE"][:, :],
                                                   P["y_w"][:, (2 * pp_i) * 128:(2 * pp_i + 1) * 128], start=True, stop=False)),
                    (lambda e, pp_i=pp_i: e.matmul(PP1[0:128, pp_i * 128:(pp_i + 1) * 128], P["s_shO"][:, :],
                                                   P["y_w"][:, (2 * pp_i + 1) * 128:(2 * pp_i + 2) * 128], start=False, stop=True)),
                ]
                mk.mm_group(mms, reads=[(P["y_w"], 2 * pp_i), (P["y_w"], 2 * pp_i + 1), P["s_shE"], P["s_shO"]],
                            writes=[PP1] if pp_i == 0 else [], tag=f"sh_p{pi}_g{g}")
                if pp_i != 0:
                    st = mk.state[_key(PP1)]
                    st["w"] = Tok(mk.eng_ops["tensor"][-1])
            mk.op("vector", lambda e, w=w, yfm=yfm: e.tensor_copy(
                yfm[:, (2 * w) * 128:(2 * w + 2) * 128], PP1[0:128, 0:256]),
                  reads=[PP1], writes=[(yfm, f"p{2*w}"), (yfm, f"p{2*w+1}")], tag=f"she_w{w}_g{g}")

        attmp = c_attmp[g]
        for half in range(2):
            pp = PP0 if half == 0 else PP1
            mms = [
                (lambda e, c=c, pp=pp, half=half, yfm=yfm: e.matmul(
                    pp[:, 0:512], yfm[:, c * 128:(c + 1) * 128],
                    P["WpT_sb"][:, c * 1024 + half * 512: c * 1024 + half * 512 + 512],
                    start=(c == 0), stop=(c == 7)))
                for c in range(8)
            ]
            mk.mm_group(mms, reads=[(yfm, f"p{c2}") for c2 in range(8)] + [P["WpT_sb"]], writes=[pp], tag=f"opmm{half}_g{g}")
            sl = slice(half * 512, half * 512 + 512)
            mk.op("vector", lambda e, pp=pp, sl=sl, attmp=attmp, bp_masked=bp_masked, mask_ap=mask_ap: e.scalar_tensor_tensor(
                attmp[:, sl], pp[:, 0:512], mask_ap[:, 0:1], bp_masked[:, sl], ALU.mult, ALU.add),
                  reads=[pp, mask_ap, bp_masked], writes=[(attmp, half)], tag=f"ope{half}_g{g}")
        mk.dma("gpsimd", f"at0{g%2}", P["AT0"][own_idx * 128:(own_idx + 1) * 128, :], attmp[:, :],
               reads=[(attmp, 0), (attmp, 1)], writes=[(P["AT0_t"], own_idx)], accum=not first_half, tag=f"atd_g{g}")

        mk.op("scalar", lambda e, xn=xn: e.square(scrH[:, 0:1024], xn[:, :]),
              reads=[xn], writes=[scrH], tag=f"hsq_g{g}")
        mk.op("vector", lambda e: e.reduce_sum(statL[:, 0:1], scrH[:, 0:1024], axis=mybir.AxisListType.X),
              reads=[scrH], writes=[(statL, 0)], tag=f"hss_g{g}")
        mk.op("vector", lambda e: e.tensor_scalar(statL[:, 0:1], statL[:, 0:1], 1.0 / D, 1e-6, ALU.mult, ALU.add),
              reads=[(statL, 0)], writes=[(statL, 0)], tag=f"hle_g{g}")
        mk.op("scalar", lambda e: e.activation(statL[:, 1:2], statL[:, 0:1], AF.Ln),
              reads=[(statL, 0)], writes=[(statL, 1)], tag=f"hln_g{g}")
        mk.op("scalar", lambda e: e.activation(statL[:, 2:3], statL[:, 1:2], AF.Exp, scale=-0.5),
              reads=[(statL, 1)], writes=[(statL, 2)], tag=f"hrs_g{g}")
        mk.op("vector", lambda e, xn=xn: e.tensor_scalar(scrH[:, 0:1024], xn[:, :], statL[:, 2:3], None, ALU.mult),
              reads=[xn, (statL, 2)], writes=[scrH], tag=f"h1_g{g}")
        hb = c_hb[g]
        mk.op("vector", lambda e, hb=hb: e.tensor_tensor(hb[:, :], scrH[:, 0:1024], P["s_n1b"][:, :], ALU.mult),
              reads=[scrH, P["s_n1b"]], writes=[hb], tag=f"hb_g{g}")
        hfm = c_hfm[g]
        transpose_128(g, hb, hfm, "h")

        kfa = c_kfa[g]
        proj_tokmajor(g, hfm, P["wk_sb"], P["qp0"], False, "fk_")
        head_transposes(g, P["qp0"], ["h0", "h1"], kfa, "fk")
        mk.dma("sync", f"kfd{g%2}", P["KF0"].rearrange("h p s -> p h s")[:, :, g * 128:(g + 1) * 128], kfa[:, :].rearrange("p (h s) -> p h s", h=H),
               reads=[(kfa, 0), (kfa, 1)], writes=[(P["KF0_t"], g)], tag=f"fkd_g{g}")
        mk.op("scalar", lambda e: e.square(P["scrA"][:, 0:1024], P["qp0"][:, :]),
              reads=[(P["qp0"], "h0"), (P["qp0"], "h1")], writes=[P["scrA"]], tag=f"ksq_g{g}")
        mk.op("vector", lambda e: e.tensor_reduce(statL[:, 8:9].rearrange("p a -> p a") if False else P["hss16"][:, 0:16], P["scrA"][:, 0:1024].rearrange("p (h d) -> p h d", d=64), axis=mybir.AxisListType.X, op=ALU.add),
              reads=[P["scrA"]], writes=[P["hss16"]], tag=f"kss_g{g}")
        mk.op("vector", lambda e: e.tensor_tensor(P["qkss"][:, 16:32], P["qkss"][:, 16:32], P["hss16"][:, 0:16], ALU.max),
              reads=[P["hss16"], P["qkss"]], writes=[P["qkss"]], tag=f"kssm_g{g}")
        qfa = c_qfa[g]
        proj_tokmajor(g, hfm, P["wq_sb"], P["qp0"], False, "fq_")
        head_transposes(g, P["qp0"], ["h0", "h1"], qfa, "fq")
        mk.op("vector", lambda e, qfa=qfa, mask_ap=mask_ap: e.tensor_scalar(
            qfa[:, :], qfa[:, :], mask_ap[0:64, 0:1], None, ALU.mult),
              reads=[(qfa, 0), (qfa, 1), mask_ap], writes=[(qfa, 0), (qfa, 1)], tag=f"fqm_g{g}")
        mk.dma("gpsimd", f"qfd{g%2}", P["QF0"].rearrange("h p s -> p h s")[:, :, own_idx * 128:(own_idx + 1) * 128], qfa[:, :].rearrange("p (h s) -> p h s", h=H),
               reads=[(qfa, 0), (qfa, 1)], writes=[(P["QF0_t"], own_idx)], accum=not first_half, tag=f"fqd_g{g}")
        mk.op("scalar", lambda e: e.square(P["scrA"][:, 0:1024], P["qp0"][:, :]),
              reads=[(P["qp0"], "h0"), (P["qp0"], "h1")], writes=[P["scrA"]], tag=f"qsq_g{g}")
        mk.op("vector", lambda e: e.tensor_reduce(P["hss16"][:, 0:16], P["scrA"][:, 0:1024].rearrange("p (h d) -> p h d", d=64), axis=mybir.AxisListType.X, op=ALU.add),
              reads=[P["scrA"]], writes=[P["hss16"]], tag=f"qss_g{g}")
        mk.op("vector", lambda e: e.tensor_tensor(P["qkss"][:, 0:16], P["qkss"][:, 0:16], P["hss16"][:, 0:16], ALU.max),
              reads=[P["hss16"], P["qkss"]], writes=[P["qkss"]], tag=f"qssm_g{g}")

        vaugf = c_vaugf[g]
        mk.op("vector", lambda e, vaugf=vaugf: e.memset(vaugf[:, :], 1.0),
              writes=[vaugf, (vaugf, "v0"), (vaugf, "v1")], tag=f"vfm_g{g}")
        for half in range(2):
            pp = PP0 if half == 0 else PP1
            mms = [
                (lambda e, c=c, pp=pp, half=half, hfm=hfm: e.matmul(
                    pp[:, 0:512], hfm[:, c * 128:(c + 1) * 128],
                    P["wv_sb"][:, c * 1024 + half * 512: c * 1024 + half * 512 + 512],
                    start=(c == 0), stop=(c == 7)))
                for c in range(8)
            ]
            mk.mm_group(mms, reads=[hfm, P["wv_sb"]], writes=[pp], tag=f"fvmm{half}_g{g}")
            mk.op("vector",
                  lambda e, pp=pp, half=half, vaugf=vaugf: e.tensor_copy(
                      vaugf[:, :].rearrange("p (h s) -> p h s", h=H)[:, 8 * half:8 * half + 8, 0:64],
                      pp[:, 0:512].rearrange("p (h s) -> p h s", h=8)),
                  reads=[pp, vaugf], writes=[(vaugf, f"v{half}")], tag=f"fve{half}_g{g}")
        mk.dma("sync", f"va0{g%2}", P["VA0"][g * 128:(g + 1) * 128, :], vaugf[:, :],
               reads=[(vaugf, "v0"), (vaugf, "v1")], writes=[(P["VA0_t"], g)], tag=f"vad_g{g}")

    mk.op("tensor", lambda e: e.transpose(P["PA"][0:32, 0:128], P["qkss"][:, 0:32], P["s_identf"][:, :]),
          reads=[P["qkss"], P["s_identf"], P["PA"]], writes=[P["PA"]], tag="qkstr")
    mk.op("vector", lambda e: e.tensor_copy(P["hss16"][0:32, 16:17].rearrange("p a -> p a") if False else P["qkT"][0:32, 0:128], P["PA"][0:32, 0:128]),
          reads=[P["PA"]], writes=[P["qkT"]], tag="qkse")
    mk.op("vector", lambda e: e.reduce_max(P["qkT"][0:32, 128:129], P["qkT"][0:32, 0:128], axis=mybir.AxisListType.X),
          reads=[P["qkT"]], writes=[(P["qkT"], "m")], tag="qksm")
    mk.dma("sync", "qkm", P["QKM0"][:, :], P["qkT"][0:32, 128:129], reads=[(P["qkT"], "m")], writes=[(P["QKM0_t"], 0)], tag="qkm")


def emit_phaseB(mk, P, NCH):
    T = NCH * 128
    OWN = T // 2
    NT = OWN // 128
    NQQ = max(1, OWN // 512)
    QW = OWN // NQQ
    PX = [P[f"PX{i}"] for i in range(4)]
    PSC0, PSC1 = P["PSC0"], P["PSC1"]
    PL, PT = P["PL"], P["PT"]
    identf = P["s_identf"]

    emit_load_weight_bf16(mk, P, P["L0_ffn_wo"], P["wo_sb"], D, D, "wo")
    for nm, dst in [("identf32", "s_identf"), ("ones1x64", "s_ones64"), ("ones1x128", "s_ones128"),
                    ("shiftE", "s_shE"), ("shiftO", "s_shO")]:
        mk.dma("sync", "cst", P[dst][:, :], P[nm], writes=[P[dst]], tag=f"c_{nm}")
    for c in range(8):
        mk.dma("sync", "cst", P["router_sb"][:, c * 8:(c + 1) * 8], P["L0_ffn_router_w"][c * 128:(c + 1) * 128, :],
               writes=[(P["router_sb"], c)], tag=f"c_rw{c}")
    mk.dma("sync", "crow", P["qkrow"][0:1, 0:32], P["QKM0"].rearrange("a b -> b a")[0:1, :],
           writes=[P["qkrow"]], tag="c_qkm")
    mk.dma("sync", "crow", P["scrq"][0:1, 0:D], P["L0_ffn_norm2_w"], writes=[P["scrq"]], tag="c_n2")
    emit_bcast_row(mk, P, PX[0], P["scrq"], D, P["s_n2b"], "bcn2")

    mk.op("vector", lambda e: e.reduce_max(P["stat1"][0:1, 0:1], P["qkrow"][0:1, 0:16], axis=mybir.AxisListType.X),
          reads=[P["qkrow"]], writes=[(P["stat1"], 0)], tag="qmx")
    mk.op("vector", lambda e: e.reduce_max(P["stat1"][0:1, 1:2], P["qkrow"][0:1, 16:32], axis=mybir.AxisListType.X),
          reads=[P["qkrow"]], writes=[(P["stat1"], 1)], tag="kmx")
    mk.op("vector", lambda e: e.tensor_tensor(P["stat1"][0:1, 2:3], P["stat1"][0:1, 0:1], P["stat1"][0:1, 1:2], ALU.mult),
          reads=[(P["stat1"], 0), (P["stat1"], 1)], writes=[(P["stat1"], 2)], tag="qk")
    mk.op("vector", lambda e: e.tensor_scalar(P["stat1"][0:1, 2:3], P["stat1"][0:1, 2:3], 1.0, 1e-12, ALU.mult, ALU.add),
          reads=[(P["stat1"], 2)], writes=[(P["stat1"], 2)], tag="qke")
    mk.op("scalar", lambda e: e.activation(P["stat1"][0:1, 3:4], P["stat1"][0:1, 2:3], AF.Ln),
          reads=[(P["stat1"], 2)], writes=[(P["stat1"], 3)], tag="qkl")
    mk.op("scalar", lambda e: e.activation(P["stat1"][0:1, 2:3], P["stat1"][0:1, 3:4], AF.Exp, scale=0.5),
          reads=[(P["stat1"], 3)], writes=[(P["stat1"], 2)], tag="qkx")
    mk.op("vector", lambda e: e.tensor_scalar(P["stat1"][0:1, 3:4], P["stat1"][0:1, 2:3], -0.125, -480.0, ALU.mult, ALU.max),
          reads=[(P["stat1"], 2)], writes=[(P["stat1"], 3)], tag="negc")
    mk.op("tensor", lambda e: e.matmul(PL[0:128, 0:1], P["s_ones128"][0:1, 0:128], P["stat1"][0:1, 3:4], start=True, stop=True),
          reads=[P["s_ones128"], (P["stat1"], 3)], writes=[PL], tag="negc_bc")
    mk.op("vector", lambda e: e.tensor_copy(P["negCb"][:, :], PL[0:128, 0:1]),
          reads=[PL], writes=[P["negCb"]], tag="negc_ev")

    c_kh = Cyc([P["kh0"], P["kh1"]])
    c_qh = Cyc([P["qh0"], P["qh1"]])
    c_va = Cyc([P[f"va{i}"] for i in range(4)])
    c_pt = Cyc([P["pt0"], P["pt1"]])
    yfm = P["yfm_all"]

    for hh in range(H):
        kh = c_kh[hh]
        mk.dma("sync", f"khld{hh%2}", kh[:, :], P["KF0"][hh, :, :], writes=[kh], tag=f"khld_{hh}")
        qh = c_qh[hh]
        mk.dma("scalar", f"qhld{hh%2}", qh[:, :], P["QF0"][hh, :, :], writes=[qh], tag=f"qhld_{hh}")
        ctx_open = [False] * 4
        for j in range(NCH):
            va = c_va[hh * NCH + j]
            mk.dma("gpsimd", f"vald{(hh*NCH+j)%4}", va[:, :], P["VA0"][j * 128:(j + 1) * 128, hh * 65:(hh + 1) * 65],
                   writes=[va], tag=f"vald_{hh}_{j}")
            for qq in range(NQQ):
                psc = PSC0 if qq % 2 == 0 else PSC1
                mk.op("tensor", lambda e, kh=kh, qh=qh, j=j, qq=qq, psc=psc: e.matmul(
                    psc[:, 0:QW], kh[:, j * 128:(j + 1) * 128], qh[:, qq * QW:(qq + 1) * QW],
                    start=True, stop=True),
                      reads=[kh, qh], writes=[psc], tag=f"sc_h{hh}_j{j}_q{qq}")
                pt = c_pt[qq % 2]
                mk.op("scalar", lambda e, pt=pt, psc=psc: e.activation(
                    pt[:, 0:QW], psc[:, 0:QW], AF.Exp, bias=P["negCb"][:, 0:1], scale=0.125),
                      reads=[psc, P["negCb"]], writes=[pt], tag=f"ex_h{hh}_j{j}_q{qq}")
                px = PX[qq]
                if not ctx_open[qq]:
                    mk.op("tensor", lambda e, va=va, pt=pt, px=px, j=j: e.matmul(
                        px[0:65, 0:QW], va[:, :], pt[:, 0:QW],
                        start=True, stop=(j == NCH - 1)),
                          reads=[va, pt], writes=[px], tag=f"cx_h{hh}_j{j}_q{qq}")
                    ctx_open[qq] = True
                else:
                    tok = mk.raw_op("tensor", lambda e, va=va, pt=pt, px=px, j=j: e.matmul(
                        px[0:65, 0:QW], va[:, :], pt[:, 0:QW],
                        start=False, stop=(j == NCH - 1)),
                        also=[mk.writer_of(pt), mk.writer_of(va)], tag=f"cx_h{hh}_j{j}_q{qq}")
                    st = mk.state[_key(px)]
                    st["w"] = tok
                    mk.state[_key(pt)]["r"].append(tok)
                    mk.state[_key(va)]["r"].append(tok)
        yh_buf = P["yhE"] if hh % 2 == 0 else P["yhO"]
        for qq in range(NQQ):
            px = PX[qq]
            mk.op("vector", lambda e, px=px: e.reciprocal(P["rec1"][0:1, 0:QW], px[64:65, 0:QW]),
                  reads=[px], writes=[P["rec1"]], tag=f"rc_h{hh}_q{qq}")
            psc = PSC0 if qq % 2 == 0 else PSC1
            mk.op("tensor", lambda e, psc=psc: e.matmul(psc[0:64, 0:QW], P["s_ones64"][0:1, 0:64], P["rec1"][0:1, 0:QW], start=True, stop=True),
                  reads=[P["rec1"], P["s_ones64"]], writes=[psc], tag=f"bc_h{hh}_q{qq}")
            mk.op("vector", lambda e, psc=psc: e.tensor_copy(P["rb1"][0:64, 0:QW], psc[0:64, 0:QW]),
                  reads=[psc], writes=[P["rb1"]], tag=f"rbe_h{hh}_q{qq}")
            mk.op("vector", lambda e, px=px, qq=qq, yh_buf=yh_buf: e.tensor_tensor(
                yh_buf[:, qq * QW:(qq + 1) * QW], px[0:64, 0:QW], P["rb1"][0:64, 0:QW], ALU.mult),
                  reads=[px, P["rb1"]], writes=[(yh_buf, qq)], tag=f"y_h{hh}_q{qq}")
        if hh % 2 == 1:
            pi = hh // 2
            for seg in range(NQQ):
                mms = [
                    (lambda e, seg=seg: e.matmul(PSC0[:, 0:QW], P["s_shE"][:, :], P["yhE"][:, seg * QW:(seg + 1) * QW], start=True, stop=False)),
                    (lambda e, seg=seg: e.matmul(PSC0[:, 0:QW], P["s_shO"][:, :], P["yhO"][:, seg * QW:(seg + 1) * QW], start=False, stop=True)),
                ]
                mk.mm_group(mms, reads=[(P["yhE"], seg), (P["yhO"], seg), P["s_shE"], P["s_shO"]],
                            writes=[PSC0], tag=f"sh_p{pi}_s{seg}")
                mk.op("vector", lambda e, pi=pi, seg=seg: e.tensor_copy(
                    yfm[:, pi * OWN + seg * QW: pi * OWN + (seg + 1) * QW], PSC0[:, 0:QW]),
                      reads=[PSC0], writes=[(yfm, f"{pi}_{seg}")], tag=f"she_p{pi}_s{seg}")

    c_xt = Cyc([P["xt0"], P["xt1"]])
    for tt in range(NT):
        for half in range(2):
            px = PX[half]
            mms = [
                (lambda e, pr=pr, half=half, tt=tt, px=px: e.matmul(
                    px[:, 0:512], yfm[:, pr * OWN + tt * 128: pr * OWN + tt * 128 + 128],
                    P["wo_sb"][:, pr * 1024 + half * 512: pr * 1024 + half * 512 + 512],
                    start=(pr == 0), stop=(pr == 7)))
                for pr in range(8)
            ]
            mk.mm_group(mms, reads=[(yfm, f"{pr2}_{(tt*128)//QW}") for pr2 in range(8)] + [P["wo_sb"]],
                        writes=[px], tag=f"wo{half}_t{tt}")
        xt = c_xt[tt]
        mk.dma("sync", f"xnld{tt%2}", xt[:, :], P["XN0"][tt * 128:(tt + 1) * 128, :], writes=[xt], tag=f"xnld_t{tt}")
        for half in range(2):
            px = PX[half]
            mk.op("vector", lambda e, xt=xt, half=half, px=px: e.tensor_tensor(
                P["acc_t"][:, half * 512:(half + 1) * 512], xt[:, half * 512:(half + 1) * 512], px[:, 0:512], ALU.add),
                  reads=[xt, px], writes=[(P["acc_t"], half)], tag=f"acc{half}_t{tt}")
        mk.dma("gpsimd", f"accd{tt%2}", P["ACC"][tt * 128:(tt + 1) * 128, :], P["acc_t"][:, :],
               reads=[(P["acc_t"], 0), (P["acc_t"], 1)], writes=[(P["ACC_t"], tt)], tag=f"accd_t{tt}")
        mk.op("scalar", lambda e: e.square(P["scrq"][:, 0:1024], P["acc_t"][:, :]),
              reads=[(P["acc_t"], 0), (P["acc_t"], 1)], writes=[P["scrq"]], tag=f"h2sq_t{tt}")
        mk.op("vector", lambda e: e.reduce_sum(P["stat1"][0:128, 4:5], P["scrq"][:, 0:1024], axis=mybir.AxisListType.X),
              reads=[P["scrq"]], writes=[(P["stat1"], 4)], tag=f"h2ss_t{tt}")
        mk.op("vector", lambda e: e.tensor_scalar(P["stat1"][0:128, 4:5], P["stat1"][0:128, 4:5], 1.0 / D, 1e-6, ALU.mult, ALU.add),
              reads=[(P["stat1"], 4)], writes=[(P["stat1"], 4)], tag=f"h2e_t{tt}")
        mk.op("scalar", lambda e: e.activation(P["stat1"][0:128, 5:6], P["stat1"][0:128, 4:5], AF.Ln),
              reads=[(P["stat1"], 4)], writes=[(P["stat1"], 5)], tag=f"h2l_t{tt}")
        mk.op("scalar", lambda e: e.activation(P["stat1"][0:128, 6:7], P["stat1"][0:128, 5:6], AF.Exp, scale=-0.5),
              reads=[(P["stat1"], 5)], writes=[(P["stat1"], 6)], tag=f"h2r_t{tt}")
        mk.op("vector", lambda e: e.tensor_scalar(P["scrq"][:, 0:1024], P["acc_t"][:, :], P["stat1"][0:128, 6:7], None, ALU.mult),
              reads=[(P["acc_t"], 0), (P["acc_t"], 1), (P["stat1"], 6)], writes=[P["scrq"]], tag=f"h2a_t{tt}")
        mk.op("vector", lambda e: e.tensor_tensor(P["scrq"][:, 0:1024], P["scrq"][:, 0:1024], P["s_n2b"][:, :], ALU.mult),
              reads=[P["scrq"], P["s_n2b"]], writes=[P["scrq"]], tag=f"h2w_t{tt}")
        mk.op("vector", lambda e: e.tensor_copy(P["h2b_t"][:, :], P["scrq"][:, 0:1024]),
              reads=[P["scrq"]], writes=[P["h2b_t"]], tag=f"h2b_t{tt}")
        mk.dma("scalar", f"h2td{tt%2}", P["H2T"][tt * 128:(tt + 1) * 128, :], P["h2b_t"][:, :],
               reads=[P["h2b_t"]], writes=[(P["H2T_t"], tt)], tag=f"h2td_t{tt}")
        for rnd in range(2):
            last = None
            first = True
            for cc in range(4):
                c = rnd * 4 + cc
                last = mk.op("tensor", lambda e, c=c, cc=cc: e.transpose(
                    PT[:, cc * 128:(cc + 1) * 128], P["scrq"][:, c * 128:(c + 1) * 128], identf[:, :]),
                    reads=([P["scrq"], identf, PT] if first else []), writes=[PT] if first else [],
                    tag=f"h2tr{c}_t{tt}", track=first)
                first = False
            st = mk.state[_key(PT)]
            st["w"] = last
            st["r"] = []
            mk.op("vector", lambda e, rnd=rnd: e.tensor_copy(P["h2fm"][:, rnd * 512:(rnd + 1) * 512], PT[:, 0:512]),
                  reads=[PT], writes=[(P["h2fm"], rnd)], tag=f"h2te{rnd}_t{tt}")
        mk.op("vector", lambda e: e.tensor_copy(P["h2fmb"][:, :], P["h2fm"][:, 0:1024]),
              reads=[(P["h2fm"], 0), (P["h2fm"], 1)], writes=[P["h2fmb"]], tag=f"h2fb_t{tt}")
        mk.dma("sync", f"h2fd{tt%2}", P["H2F"].rearrange("(c p) t -> p c t", p=128)[:, :, tt * 128:(tt + 1) * 128],
               P["h2fmb"][:, :].rearrange("p (c t) -> p c t", c=8),
               reads=[P["h2fmb"]], writes=[(P["H2F_t"], tt)], tag=f"h2fd_t{tt}")
        mms = [
            (lambda e, c=c: e.matmul(PL[0:128, 0:8], P["h2fm"][:, c * 128:(c + 1) * 128],
                                     P["router_sb"][:, c * 8:(c + 1) * 8], start=(c == 0), stop=(c == 7)))
            for c in range(8)
        ]
        mk.mm_group(mms, reads=[(P["h2fm"], 0), (P["h2fm"], 1)] + [(P["router_sb"], c2) for c2 in range(8)],
                    writes=[PL], tag=f"lg_t{tt}")
        mk.op("vector", lambda e: e.tensor_copy(P["lg_t"][:, :], PL[0:128, 0:8]),
              reads=[PL], writes=[P["lg_t"]], tag=f"lge_t{tt}")
        mk.dma("gpsimd", f"lgd{tt%2}", P["LOG0"][tt * 128:(tt + 1) * 128, :], P["lg_t"][:, :],
               reads=[P["lg_t"]], writes=[(P["LOG0_t"], tt)], tag=f"lgd_t{tt}")


def alloc_phaseB(nc, S, P, NCH, prefix=""):
    def sb(name, shape, dt=F32):
        t = S.enter_context(nc.sbuf_tensor(prefix + name, shape, dt))
        P[name] = t
        return t

    def ps(name, shape, dt=F32):
        t = S.enter_context(nc.psum_tensor(prefix + name, shape, dt))
        P[name] = t
        return t

    sb("wo_sb", [128, 8 * 1024], BF16)
    P["wstage"] = [sb("xt0", [128, 1024], F32), sb("xt1", [128, 1024], F32)]
    sb("s_identf", [128, 128], F32)
    sb("s_ones64", [1, 64], F32)
    sb("s_ones128", [1, 128], F32)
    sb("s_shE", [64, 128], BF16)
    sb("s_shO", [64, 128], BF16)
    sb("router_sb", [128, 64], F32)
    sb("s_n2b", [128, 1024], F32)
    sb("qkrow", [1, 128], F32)
    sb("stat1", [128, 8], F32)
    sb("negCb", [128, 1], F32)
    T = NCH * 128
    OWN = T // 2
    sb("kh0", [64, T], BF16)
    sb("kh1", [64, T], BF16)
    sb("qh0", [64, OWN], BF16)
    sb("qh1", [64, OWN], BF16)
    for i in range(4):
        sb(f"va{i}", [128, 65], BF16)
    sb("pt0", [128, 512], BF16)
    sb("pt1", [128, 512], BF16)
    sb("den1", [1, 512], F32)
    sb("rec1", [1, 512], F32)
    sb("rb1", [64, 512], F32)
    sb("yhE", [64, OWN], BF16)
    sb("yhO", [64, OWN], BF16)
    sb("yfm_all", [128, 8 * OWN], BF16)
    sb("acc_t", [128, 1024], F32)
    sb("scrq", [128, 1024], F32)
    sb("h2b_t", [128, 1024], BF16)
    sb("h2fm", [128, 1024], F32)
    sb("h2fmb", [128, 1024], BF16)
    sb("lg_t", [128, 8], F32)
    for i in range(4):
        ps(f"PX{i}", [128, 512], F32)
    ps("PSC0", [128, 512], F32)
    ps("PSC1", [128, 512], F32)
    ps("PL", [128, 512], F32)
    ps("PT", [128, 512], F32)
    for nm in ["ACC_t", "H2T_t", "H2F_t", "LOG0_t"]:
        P[nm] = object()
    return P


def build_phaseB(NCH):
    T = NCH * 128
    nc = bass.Bass("TRN2", target_bir_lowering=False, debug=False)
    P = {}
    P["KF0"] = nc.dram_tensor("KF0", [H, 64, T], BF16, kind="ExternalInput").ap()
    P["QF0"] = nc.dram_tensor("QF0", [H, 64, T // 2], BF16, kind="ExternalInput").ap()
    P["VA0"] = nc.dram_tensor("VA0", [T, H * 65], BF16, kind="ExternalInput").ap()
    P["XN0"] = nc.dram_tensor("XN0", [T // 2, D], F32, kind="ExternalInput").ap()
    P["QKM0"] = nc.dram_tensor("QKM0", [32, 1], F32, kind="ExternalInput").ap()
    P["L0_ffn_wo"] = nc.dram_tensor("L0_ffn_wo", [D, D], F32, kind="ExternalInput").ap()
    P["L0_ffn_router_w"] = nc.dram_tensor("L0_ffn_router_w", [D, 8], F32, kind="ExternalInput").ap()
    P["L0_ffn_norm2_w"] = nc.dram_tensor("L0_ffn_norm2_w", [1, D], F32, kind="ExternalInput").ap()
    for nm, shape, dt in [("identf32", [128, 128], F32), ("ones1x64", [1, 64], F32),
                          ("ones1x128", [1, 128], F32), ("shiftE", [64, 128], BF16), ("shiftO", [64, 128], BF16)]:
        P[nm] = nc.dram_tensor(nm, shape, dt, kind="ExternalInput").ap()
    P["ACC"] = nc.dram_tensor("ACC", [T // 2, D], F32, kind="ExternalOutput").ap()
    P["H2T"] = nc.dram_tensor("H2T", [T // 2, D], BF16, kind="ExternalOutput").ap()
    P["H2F"] = nc.dram_tensor("H2F", [D, T // 2], BF16, kind="ExternalOutput").ap()
    P["LOG0"] = nc.dram_tensor("LOG0", [T // 2, 8], F32, kind="ExternalOutput").ap()
    with contextlib.ExitStack() as S:
        alloc_phaseB(nc, S, P, NCH)
        mk = MK(nc)
        emit_phaseB(mk, P, NCH)
        emit_final_barrier(mk, [P["ACC_t"], P["H2T_t"], P["H2F_t"], P["LOG0_t"]])
        mk.build(S)
    return nc


def emit_moe_expert(mk, P, NCH, ex_tag, gsrc, usrc, dsrc, NFC, use_ce, ce_idx, xf_dram, acc_dram, acc_key):
    T = NCH * 128
    OWN = T // 2
    HHALF = OWN // 2
    NTT = HHALF // 128
    PG = [P[f"PG{i}"] for i in range(4)]
    PU = [P[f"PU{i}"] for i in range(4)]
    DFF = NFC * 128
    c_h2 = Cyc([P["h2f_a"], P["h2f_b"]])
    for c in range(8):
        stg = P["wstage"][c % 2]
        mk.dma("sync", f"wld{c%2}", stg[:, 0:DFF], gsrc[c * 128:(c + 1) * 128, :], writes=[stg], tag=f"gld{ex_tag}_{c}")
        mk.op("vector", lambda e, c=c, stg=stg: e.tensor_copy(P["gate_sb"][:, c * DFF:(c + 1) * DFF], stg[:, 0:DFF]),
              reads=[stg], writes=[(P["gate_sb"], c)], tag=f"gcv{ex_tag}_{c}")
    for c in range(8):
        stg = P["wstage"][c % 2]
        mk.dma("sync", f"wld{c%2}", stg[:, 0:DFF], usrc[c * 128:(c + 1) * 128, :], writes=[stg], tag=f"uld{ex_tag}_{c}")
        mk.op("scalar", lambda e, c=c, stg=stg: e.copy(P["up_sb"][:, c * DFF:(c + 1) * DFF], stg[:, 0:DFF]),
              reads=[stg], writes=[(P["up_sb"], c)], tag=f"ucv{ex_tag}_{c}")
    for c in range(NFC):
        stg = P["wstage"][c % 2]
        mk.dma("scalar", f"wl2{c%2}", stg[:, 0:1024], dsrc[c * 128:(c + 1) * 128, :], writes=[stg], tag=f"dld{ex_tag}_{c}")
        mk.op("vector", lambda e, c=c, stg=stg: e.tensor_copy(P["down_sb"][:, c * 1024:(c + 1) * 1024], stg[:, 0:1024]),
              reads=[stg], writes=[(P["down_sb"], c)], tag=f"dcv{ex_tag}_{c}")
    if use_ce:
        mk.dma("sync", "ced", P["ce_row"][0:1, 0:OWN], P["CTD"].rearrange("a b t -> b a t")[ce_idx:ce_idx + 1, :, :],
               reads=[(P["CTD_t"], 0)], writes=[P["ce_row"]], tag=f"celd{ex_tag}")
        for qq in range(max(1, OWN // 512)):
            w_ = min(512, OWN)
            mk.op("tensor", lambda e, qq=qq, w_=w_: e.matmul(PG[0][0:128, 0:w_], P["s_ones128"][0:1, 0:128], P["ce_row"][0:1, qq * w_:(qq + 1) * w_], start=True, stop=True),
                  reads=[P["ce_row"], P["s_ones128"]], writes=[PG[0]], tag=f"cebc{ex_tag}_{qq}")
            mk.op("vector", lambda e, qq=qq, w_=w_: e.tensor_copy(P["ce_b"][:, qq * w_:(qq + 1) * w_], PG[0][0:128, 0:w_]),
                  reads=[PG[0]], writes=[(P["ce_b"], qq)], tag=f"cebe{ex_tag}_{qq}")
    for half in range(2):
        h2f = c_h2[half]
        mk.dma("gpsimd", f"h2ld{half}", h2f[:, 0:8 * HHALF].rearrange("p (c t) -> p c t", c=8),
               xf_dram.rearrange("(c p) t -> p c t", p=128)[:, :, half * HHALF:(half + 1) * HHALF],
               writes=[h2f], tag=f"h2ld{ex_tag}_{half}")
        NWQ = max(1, HHALF // 512)
        WQ = HHALF // NWQ
        for fc in range(NFC):
            for wq_i in range(NWQ):
                pg = PG[wq_i % 4]
                pu = PU[wq_i % 4]
                mms = [
                    (lambda e, c=c, fc=fc, wq_i=wq_i, pg=pg, h2f=h2f, WQ=WQ, HHALF=HHALF, DFF=DFF: e.matmul(
                        pg[:, 0:WQ], P["gate_sb"][:, c * DFF + fc * 128: c * DFF + fc * 128 + 128],
                        h2f[:, c * HHALF + wq_i * WQ: c * HHALF + wq_i * WQ + WQ],
                        start=(c == 0), stop=(c == 7)))
                    for c in range(8)
                ]
                mk.mm_group(mms, reads=[h2f] + [(P["gate_sb"], c2) for c2 in range(8)], writes=[pg], tag=f"g{ex_tag}_{half}_{fc}_{wq_i}")
                mms = [
                    (lambda e, c=c, fc=fc, wq_i=wq_i, pu=pu, h2f=h2f, WQ=WQ, HHALF=HHALF, DFF=DFF: e.matmul(
                        pu[:, 0:WQ], P["up_sb"][:, c * DFF + fc * 128: c * DFF + fc * 128 + 128],
                        h2f[:, c * HHALF + wq_i * WQ: c * HHALF + wq_i * WQ + WQ],
                        start=(c == 0), stop=(c == 7)))
                    for c in range(8)
                ]
                mk.mm_group(mms, reads=[h2f] + [(P["up_sb"], c2) for c2 in range(8)], writes=[pu], tag=f"u{ex_tag}_{half}_{fc}_{wq_i}")
                mk.op("scalar", lambda e, pg=pg, WQ=WQ: e.activation(P["scrS"][:, 0:WQ], pg[:, 0:WQ], AF.Silu),
                      reads=[pg], writes=[P["scrS"]], tag=f"si{ex_tag}_{half}_{fc}_{wq_i}")
                if not use_ce:
                    mk.op("vector", lambda e, pu=pu, fc=fc, wq_i=wq_i, WQ=WQ, HHALF=HHALF: e.tensor_tensor(
                        P["act_sb"][:, fc * HHALF + wq_i * WQ: fc * HHALF + wq_i * WQ + WQ], P["scrS"][:, 0:WQ], pu[:, 0:WQ], ALU.mult),
                          reads=[P["scrS"], pu], writes=[(P["act_sb"], (fc, wq_i))], tag=f"a{ex_tag}_{half}_{fc}_{wq_i}")
                else:
                    mk.op("vector", lambda e, pu=pu, WQ=WQ: e.tensor_tensor(P["scrS2"][:, 0:WQ], P["scrS"][:, 0:WQ], pu[:, 0:WQ], ALU.mult),
                          reads=[P["scrS"], pu], writes=[P["scrS2"]], tag=f"am{ex_tag}_{half}_{fc}_{wq_i}")
                    mk.op("vector", lambda e, fc=fc, wq_i=wq_i, half=half, WQ=WQ, HHALF=HHALF: e.tensor_tensor(
                        P["act_sb"][:, fc * HHALF + wq_i * WQ: fc * HHALF + wq_i * WQ + WQ], P["scrS2"][:, 0:WQ],
                        P["ce_b"][:, half * HHALF + wq_i * WQ: half * HHALF + wq_i * WQ + WQ],
                        ALU.mult),
                          reads=[P["scrS2"], (P["ce_b"], 0), (P["ce_b"], 1), (P["ce_b"], 2), (P["ce_b"], 3)],
                          writes=[(P["act_sb"], (fc, wq_i))], tag=f"a{ex_tag}_{half}_{fc}_{wq_i}")
        for tt in range(NTT):
            gt = tt + half * NTT
            WQ2 = HHALF // max(1, HHALF // 512)
            for s2 in range(2):
                pg = PG[s2]
                mms = [
                    (lambda e, fc=fc, tt=tt, s2=s2, pg=pg, HHALF=HHALF: e.matmul(
                        pg[:, 0:512], P["act_sb"][:, fc * HHALF + tt * 128: fc * HHALF + tt * 128 + 128],
                        P["down_sb"][:, fc * 1024 + s2 * 512: fc * 1024 + s2 * 512 + 512],
                        start=(fc == 0), stop=(fc == NFC - 1)))
                    for fc in range(NFC)
                ]
                mk.mm_group(mms, reads=[(P["act_sb"], (fc2, (tt * 128) // WQ2)) for fc2 in range(NFC)] + [(P["down_sb"], c2) for c2 in range(NFC)],
                            writes=[pg], tag=f"d{ex_tag}_{half}_{tt}_{s2}")
                mk.op("vector", lambda e, s2=s2, pg=pg: e.tensor_copy(P["dn_t"][:, s2 * 512:(s2 + 1) * 512], pg[:, 0:512]),
                      reads=[pg], writes=[(P["dn_t"], s2)], tag=f"de{ex_tag}_{half}_{tt}_{s2}")
            mk.dma("gpsimd", f"dnd{gt%2}", acc_dram[gt * 128:(gt + 1) * 128, :], P["dn_t"][:, :],
                   reads=[(P["dn_t"], 0), (P["dn_t"], 1)], writes=[(P[acc_key], gt)], accum=True, tag=f"dnd{ex_tag}_{gt}")


def emit_phaseC(mk, P, NCH):
    T = NCH * 128
    OWN = T // 2
    NT = OWN // 128
    PG = [P[f"PG{i}"] for i in range(4)]
    PU = [P[f"PU{i}"] for i in range(4)]
    iota8 = P["s_iota8"]

    mk.dma("sync", "cst", P["s_iota8"][:, :], P["iota8"], writes=[P["s_iota8"]], tag="c_iota")
    mk.dma("sync", "cst", P["s_ones128"][:, :], P["ones1x128"], writes=[P["s_ones128"]], tag="c_o128")
    mk.dma("sync", "cst", P["s_identf"][:, :], P["identf32"], writes=[P["s_identf"]], tag="c_idf")

    st1 = P["stat1"]
    CT = P["ct_all"]
    for tt in range(NT):
        lg = P["lg_t"]
        mk.dma("sync", f"lgld{tt%2}", lg[:, :], P["LOG0"][tt * 128:(tt + 1) * 128, :], writes=[lg], tag=f"lgld_{tt}")
        mk.op("vector", lambda e, lg=lg: e.reduce_max(st1[:, 0:1], lg[:, :], axis=mybir.AxisListType.X),
              reads=[lg], writes=[(st1, 0)], tag=f"rm_{tt}")
        mk.op("vector", lambda e: e.tensor_scalar(st1[:, 1:2], st1[:, 0:1], -1.0, None, ALU.mult),
              reads=[(st1, 0)], writes=[(st1, 1)], tag=f"nm_{tt}")
        mk.op("scalar", lambda e, lg=lg: e.activation(P["prb"][:, :], lg[:, :], AF.Exp, bias=st1[:, 1:2]),
              reads=[lg, (st1, 1)], writes=[P["prb"]], tag=f"pe_{tt}")
        mk.op("vector", lambda e: e.reduce_sum(st1[:, 2:3], P["prb"][:, :], axis=mybir.AxisListType.X),
              reads=[P["prb"]], writes=[(st1, 2)], tag=f"ps_{tt}")
        mk.op("vector", lambda e: e.reciprocal(st1[:, 3:4], st1[:, 2:3]),
              reads=[(st1, 2)], writes=[(st1, 3)], tag=f"pr_{tt}")
        mk.op("vector", lambda e: e.tensor_scalar(P["prb"][:, :], P["prb"][:, :], st1[:, 3:4], None, ALU.mult),
              reads=[P["prb"], (st1, 3)], writes=[P["prb"]], tag=f"pn_{tt}")
        mk.op("vector", lambda e: e.reduce_max(st1[:, 4:5], P["prb"][:, :], axis=mybir.AxisListType.X),
              reads=[P["prb"]], writes=[(st1, 4)], tag=f"m1_{tt}")
        mk.op("vector", lambda e: e.tensor_scalar(P["eqb"][:, :], P["prb"][:, :], st1[:, 4:5], None, ALU.is_ge),
              reads=[P["prb"], (st1, 4)], writes=[P["eqb"]], tag=f"e1_{tt}")
        mk.op("vector", lambda e: e.tensor_scalar(P["eqb"][:, :], P["eqb"][:, :], -1e9, 1e9, ALU.mult, ALU.add),
              reads=[P["eqb"]], writes=[P["eqb"]], tag=f"e1b_{tt}")
        mk.op("vector", lambda e: e.tensor_tensor(P["eqb"][:, :], P["eqb"][:, :], iota8[:, :], ALU.add),
              reads=[P["eqb"], iota8], writes=[P["eqb"]], tag=f"e1c_{tt}")
        mk.op("vector", lambda e: e.tensor_reduce(st1[:, 5:6], P["eqb"][:, :], axis=mybir.AxisListType.X, op=ALU.min),
              reads=[P["eqb"]], writes=[(st1, 5)], tag=f"i1_{tt}")
        mk.op("vector", lambda e: e.tensor_scalar(P["oh1"][:, :], iota8[:, :], st1[:, 5:6], None, ALU.is_equal),
              reads=[iota8, (st1, 5)], writes=[P["oh1"]], tag=f"o1_{tt}")
        mk.op("vector", lambda e: e.scalar_tensor_tensor(P["msk"][:, :], P["oh1"][:, :], -1e9, P["prb"][:, :], ALU.mult, ALU.add),
              reads=[P["oh1"], P["prb"]], writes=[P["msk"]], tag=f"mk_{tt}")
        mk.op("vector", lambda e: e.reduce_max(st1[:, 6:7], P["msk"][:, :], axis=mybir.AxisListType.X),
              reads=[P["msk"]], writes=[(st1, 6)], tag=f"m2_{tt}")
        mk.op("vector", lambda e: e.tensor_scalar(P["eqb"][:, :], P["msk"][:, :], st1[:, 6:7], None, ALU.is_ge),
              reads=[P["msk"], (st1, 6)], writes=[P["eqb"]], tag=f"e2_{tt}")
        mk.op("vector", lambda e: e.tensor_scalar(P["eqb"][:, :], P["eqb"][:, :], -1e9, 1e9, ALU.mult, ALU.add),
              reads=[P["eqb"]], writes=[P["eqb"]], tag=f"e2b_{tt}")
        mk.op("vector", lambda e: e.tensor_tensor(P["eqb"][:, :], P["eqb"][:, :], iota8[:, :], ALU.add),
              reads=[P["eqb"], iota8], writes=[P["eqb"]], tag=f"e2c_{tt}")
        mk.op("vector", lambda e: e.tensor_reduce(st1[:, 7:8], P["eqb"][:, :], axis=mybir.AxisListType.X, op=ALU.min),
              reads=[P["eqb"]], writes=[(st1, 7)], tag=f"i2_{tt}")
        mk.op("vector", lambda e: e.tensor_scalar(P["oh2"][:, :], iota8[:, :], st1[:, 7:8], None, ALU.is_equal),
              reads=[iota8, (st1, 7)], writes=[P["oh2"]], tag=f"o2_{tt}")
        mk.op("scalar", lambda e: e.activation(st1[:, 0:1], st1[:, 4:5], AF.Exp),
              reads=[(st1, 4)], writes=[(st1, 0)], tag=f"x1_{tt}")
        mk.op("scalar", lambda e: e.activation(st1[:, 1:2], st1[:, 6:7], AF.Exp),
              reads=[(st1, 6)], writes=[(st1, 1)], tag=f"x2_{tt}")
        mk.op("vector", lambda e: e.tensor_tensor(st1[:, 2:3], st1[:, 0:1], st1[:, 1:2], ALU.add),
              reads=[(st1, 0), (st1, 1)], writes=[(st1, 2)], tag=f"sw_{tt}")
        mk.op("vector", lambda e: e.reciprocal(st1[:, 3:4], st1[:, 2:3]),
              reads=[(st1, 2)], writes=[(st1, 3)], tag=f"rw_{tt}")
        mk.op("vector", lambda e: e.tensor_tensor(st1[:, 0:1], st1[:, 0:1], st1[:, 3:4], ALU.mult),
              reads=[(st1, 0), (st1, 3)], writes=[(st1, 0)], tag=f"w1_{tt}")
        mk.op("vector", lambda e: e.tensor_tensor(st1[:, 1:2], st1[:, 1:2], st1[:, 3:4], ALU.mult),
              reads=[(st1, 1), (st1, 3)], writes=[(st1, 1)], tag=f"w2_{tt}")
        mk.op("vector", lambda e, tt=tt: e.tensor_scalar(CT[:, tt * 8:(tt + 1) * 8], P["oh1"][:, :], st1[:, 0:1], None, ALU.mult),
              reads=[P["oh1"], (st1, 0)], writes=[(CT, tt)], tag=f"c1_{tt}")
        mk.op("vector", lambda e, tt=tt: e.scalar_tensor_tensor(CT[:, tt * 8:(tt + 1) * 8], P["oh2"][:, :], st1[:, 1:2], CT[:, tt * 8:(tt + 1) * 8], ALU.mult, ALU.add),
              reads=[P["oh2"], (st1, 1), (CT, tt)], writes=[(CT, tt)], tag=f"c2_{tt}")
    mk.op("tensor", lambda e: e.transpose(P["PT2"][0:NT * 8, 0:128], CT[:, 0:NT * 8], P["s_identf"][:, :]),
          reads=[CT] + [(CT, tt2) for tt2 in range(NT)] + [P["s_identf"]], writes=[P["PT2"]], tag="cttr")
    mk.op("vector", lambda e: e.tensor_copy(P["ctT"][0:NT * 8, 0:128], P["PT2"][0:NT * 8, 0:128]),
          reads=[P["PT2"]], writes=[P["ctT"]], tag="ctev")
    mk.dma("sync", "ctd", P["CTD"].rearrange("a b t -> (a b) t"), P["ctT"][0:NT * 8, 0:128],
           reads=[P["ctT"]], writes=[(P["CTD_t"], 0)], tag="ctd")

    for ex in range(9):
        if ex == 0:
            emit_moe_expert(mk, P, NCH, "s", P["L0_ffn_sh_gate"], P["L0_ffn_sh_up"], P["L0_ffn_sh_down"],
                            16, False, 0, P["H2F"], P["ACC"], "ACC_t")
        else:
            emit_moe_expert(mk, P, NCH, f"e{ex-1}", P["L0_ffn_e_gate"][ex - 1], P["L0_ffn_e_up"][ex - 1],
                            P["L0_ffn_e_down"][ex - 1], 16, True, ex - 1, P["H2F"], P["ACC"], "ACC_t")

    for tt in range(NT):
        a1 = P["fx_a"]
        a2 = P["fx_b"]
        a3 = P["fx_c"]
        mk.dma("sync", f"f1{tt%2}", a1[:, 0:1024], P["ACC"][tt * 128:(tt + 1) * 128, :],
               reads=[(P["ACC_t"], tt)], writes=[a1, (a1, 0), (a1, 1)], tag=f"f1_{tt}")
        mk.dma("scalar", f"f2{tt%2}", a2[:, 0:1024], P["AT0"][tt * 128:(tt + 1) * 128, :],
               writes=[a2, (a2, 0), (a2, 1)], tag=f"f2_{tt}")
        mk.dma("gpsimd", f"f3{tt%2}", a3[:, 0:1024], P["XOWN"][tt * 128:(tt + 1) * 128, :],
               writes=[a3, (a3, 0), (a3, 1)], tag=f"f3_{tt}")
        mk.op("vector", lambda e, a1=a1, a2=a2: e.tensor_tensor(a1[:, 0:1024], a1[:, 0:1024], a2[:, 0:1024], ALU.add),
              reads=[a2] + [a1], writes=[a1], tag=f"fa_{tt}")
        mk.op("vector", lambda e, a1=a1, a3=a3: e.tensor_tensor(a1[:, 0:1024], a1[:, 0:1024], a3[:, 0:1024], ALU.add),
              reads=[a3, a1], writes=[a1], tag=f"fb_{tt}")
        mk.dma("sync", f"x1d{tt%2}", P["X1"][tt * 128:(tt + 1) * 128, :], a1[:, 0:1024],
               reads=[a1], writes=[(P["X1_t"], tt)], tag=f"x1d_{tt}")


def alloc_phaseC(nc, S, P, NCH, prefix=""):
    T = NCH * 128
    OWN = T // 2
    NT = OWN // 128
    HHALF = OWN // 2

    def sb(name, shape, dt=F32):
        t = S.enter_context(nc.sbuf_tensor(prefix + name, shape, dt))
        P[name] = t
        return t

    def ps(name, shape, dt=F32):
        t = S.enter_context(nc.psum_tensor(prefix + name, shape, dt))
        P[name] = t
        return t

    sb("s_iota8", [128, 8], F32)
    sb("s_ones128", [1, 128], F32)
    sb("s_identf", [128, 128], F32)
    sb("stat1", [128, 8], F32)
    sb("lg_t", [128, 8], F32)
    sb("prb", [128, 8], F32)
    sb("eqb", [128, 8], F32)
    sb("msk", [128, 8], F32)
    sb("oh1", [128, 8], F32)
    sb("oh2", [128, 8], F32)
    sb("ct_all", [128, NT * 8], F32)
    sb("ctT", [128, 128], F32)
    sb("ce_row", [1, OWN], F32)
    sb("ce_b", [128, OWN], BF16)
    sb("gate_sb", [128, 8 * 2048], BF16)
    sb("up_sb", [128, 8 * 2048], BF16)
    sb("down_sb", [128, 16 * 1024], BF16)
    P["wstage"] = [sb("wstage0", [128, 2048], F32), sb("wstage1", [128, 2048], F32)]
    sb("h2f_a", [128, 8 * HHALF], BF16)
    sb("h2f_b", [128, 8 * HHALF], BF16)
    sb("act_sb", [128, 16 * HHALF], BF16)
    sb("scrS", [128, 512], BF16)
    sb("scrS2", [128, 512], BF16)
    sb("dn_t", [128, 1024], F32)
    P["fx_a"] = P["dn_t"]
    P["fx_b"] = P["wstage"][0]
    P["fx_c"] = P["wstage"][1]
    for i in range(4):
        ps(f"PG{i}", [128, 512], F32)
        ps(f"PU{i}", [128, 512], F32)
    P["PT2"] = P["PU3"]
    for nm in ["CTD_t", "ACC_t", "X1_t"]:
        P[nm] = object()
    return P


def build_phaseC(NCH):
    T = NCH * 128
    OWN = T // 2
    nc = bass.Bass("TRN2", target_bir_lowering=False, debug=False)
    P = {}
    P["H2F"] = nc.dram_tensor("H2F", [D, OWN], BF16, kind="ExternalInput").ap()
    P["LOG0"] = nc.dram_tensor("LOG0", [OWN, 8], F32, kind="ExternalInput").ap()
    P["ACCIN"] = nc.dram_tensor("ACCIN", [OWN, D], F32, kind="ExternalInput").ap()
    P["AT0"] = nc.dram_tensor("AT0", [OWN, D], F32, kind="ExternalInput").ap()
    P["XOWN"] = nc.dram_tensor("XOWN", [OWN, D], F32, kind="ExternalInput").ap()
    P["L0_ffn_sh_gate"] = nc.dram_tensor("L0_ffn_sh_gate", [D, 2048], F32, kind="ExternalInput").ap()
    P["L0_ffn_sh_up"] = nc.dram_tensor("L0_ffn_sh_up", [D, 2048], F32, kind="ExternalInput").ap()
    P["L0_ffn_sh_down"] = nc.dram_tensor("L0_ffn_sh_down", [2048, D], F32, kind="ExternalInput").ap()
    P["L0_ffn_e_gate"] = nc.dram_tensor("L0_ffn_e_gate", [8, D, 2048], F32, kind="ExternalInput").ap()
    P["L0_ffn_e_up"] = nc.dram_tensor("L0_ffn_e_up", [8, D, 2048], F32, kind="ExternalInput").ap()
    P["L0_ffn_e_down"] = nc.dram_tensor("L0_ffn_e_down", [8, 2048, D], F32, kind="ExternalInput").ap()
    for nm, shape, dt in [("identf32", [128, 128], F32), ("ones1x128", [1, 128], F32), ("iota8", [128, 8], F32)]:
        P[nm] = nc.dram_tensor(nm, shape, dt, kind="ExternalInput").ap()
    P["ACC"] = nc.dram_tensor("ACCW", [OWN, D], F32)
    P["CTD"] = nc.dram_tensor("CTD", [OWN // 128, 8, 128], F32)
    P["X1"] = nc.dram_tensor("X1", [OWN, D], F32, kind="ExternalOutput").ap()
    with contextlib.ExitStack() as S:
        alloc_phaseC(nc, S, P, NCH)
        mk = MK(nc)
        NT = OWN // 128
        for tt in range(NT):
            mk.dma("sync", f"acs{tt%2}", P["ACC"][tt * 128:(tt + 1) * 128, :], P["ACCIN"][tt * 128:(tt + 1) * 128, :],
                   writes=[(P["ACC_t"], tt)], tag=f"acs_{tt}")
        emit_phaseC(mk, P, NCH)
        emit_final_barrier(mk, [P["X1_t"]])
        mk.build(S)
    return nc


def emit_phaseD(mk, P, NCH):
    T = NCH * 128
    OWN = T // 2
    PP0, PP1 = P["PP0"], P["PP1"]
    PBT, PT = P["PBT"], P["PT"]
    ident = P["s_ident"]
    identf = P["s_identf"]
    statL = P["statL"]
    scrA, scrH = P["scrA"], P["scrH"]

    emit_load_weight_bf16(mk, P, P["L1_WQ"], P["wq_sb"], D, D, "wq1")
    emit_load_weight_bf16(mk, P, P["L1_WK"], P["wk_sb"], D, D, "wk1")
    emit_load_weight_bf16(mk, P, P["L1_WV"], P["wv_sb"], D, D, "wv1")
    for nm, dst in [("identbf", "s_ident"), ("identf32", "s_identf"),
                    ("maskA", "s_maskA"), ("maskB", "s_maskB")]:
        mk.dma("sync", "cst", P[dst][:, :], P[nm], writes=[P[dst]], tag=f"c_{nm}")
    for c in range(8):
        mk.dma("sync", "cst", P["router_sb"][:, c * 8:(c + 1) * 8], P["L1_ffn_w_gate"][c * 128:(c + 1) * 128, :],
               writes=[(P["router_sb"], c)], tag=f"c_rw{c}")

    c_sx = Cyc([P["sx0"], P["sx1"]])
    c_xnb = Cyc([P["xnb0"]])
    c_xnf = Cyc([P["xnf0"]])
    c_qt = Cyc([P["qp0"]])
    c_kt = Cyc([P["kp0"]])
    c_vaug = Cyc([P["vaug0"]])
    c_qfa = Cyc([P["qfa0"]])
    c_kfa = Cyc([P["kfa0"]])

    def proj_tok(g, src_fm, W_sb, dst_f32_keyed, tag):
        for half in range(2):
            pp = PP0 if half == 0 else PP1
            mms = [
                (lambda e, c=c, half=half, pp=pp, src_fm=src_fm, W_sb=W_sb: e.matmul(
                    pp[:, 0:512], src_fm[:, c * 128:(c + 1) * 128],
                    W_sb[:, c * 1024 + half * 512: c * 1024 + half * 512 + 512],
                    start=(c == 0), stop=(c == 7)))
                for c in range(8)
            ]
            mk.mm_group(mms, reads=[src_fm, W_sb], writes=[pp], tag=f"{tag}mm{half}_g{g}")
            mk.op("vector", lambda e, pp=pp, half=half: e.tensor_copy(dst_f32_keyed[:, half * 512:(half + 1) * 512], pp[:, 0:512]),
                  reads=[pp], writes=[(dst_f32_keyed, half)], tag=f"{tag}ev{half}_g{g}")

    def rope_norm(g, buf, rtag, n_use):
        sl = P["ropeS"]
        cl = P["ropeC"]
        mk.dma("sync", f"rp{g%2}", sl[:, :], P["SINT"][g * 128:(g + 1) * 128, :], writes=[sl], tag=f"rps_{rtag}_g{g}")
        mk.dma("sync", f"rp{g%2}", cl[:, :], P["COST"][g * 128:(g + 1) * 128, :], writes=[cl], tag=f"rpc_{rtag}_g{g}")
        v3 = lambda t, o: t[:, :].rearrange("p (h two q) -> p (h two) q", two=2, q=32)[:, o::2, :]
        tv = P["ropeT"]
        mk.op("vector", lambda e, buf=buf: e.tensor_tensor(v3(tv, 0), v3(buf, 0), cl[:, :].rearrange("p (h q) -> p h q", q=32), ALU.mult),
              reads=[buf, (buf, 0), (buf, 1), cl], writes=[(tv, 0)], tag=f"r1a_{rtag}_g{g}")
        mk.op("vector", lambda e, buf=buf: e.tensor_tensor(P["ropeU"][:, 0:512].rearrange("p (h q) -> p h q", q=32), v3(buf, 1), sl[:, :].rearrange("p (h q) -> p h q", q=32), ALU.mult),
              reads=[buf, (buf, 0), (buf, 1), sl], writes=[P["ropeU"]], tag=f"r1b_{rtag}_g{g}")
        mk.op("vector", lambda e: e.tensor_tensor(v3(tv, 0), v3(tv, 0), P["ropeU"][:, 0:512].rearrange("p (h q) -> p h q", q=32), ALU.subtract),
              reads=[(tv, 0), P["ropeU"]], writes=[(tv, 0)], tag=f"r1c_{rtag}_g{g}")
        mk.op("vector", lambda e, buf=buf: e.tensor_tensor(v3(tv, 1), v3(buf, 0), sl[:, :].rearrange("p (h q) -> p h q", q=32), ALU.mult),
              reads=[buf, sl], writes=[(tv, 1)], tag=f"r2a_{rtag}_g{g}")
        mk.op("vector", lambda e, buf=buf: e.tensor_tensor(P["ropeU"][:, 0:512].rearrange("p (h q) -> p h q", q=32), v3(buf, 1), cl[:, :].rearrange("p (h q) -> p h q", q=32), ALU.mult),
              reads=[buf, (buf, 0), (buf, 1), cl], writes=[P["ropeU"]], tag=f"r2b_{rtag}_g{g}")
        mk.op("vector", lambda e: e.tensor_tensor(v3(tv, 1), v3(tv, 1), P["ropeU"][:, 0:512].rearrange("p (h q) -> p h q", q=32), ALU.add),
              reads=[(tv, 1), P["ropeU"]], writes=[(tv, 1)], tag=f"r2c_{rtag}_g{g}")
        mk.op("scalar", lambda e: e.square(scrH[:, 0:1024], tv[:, :]),
              reads=[(tv, 0), (tv, 1)], writes=[scrH], tag=f"nsq_{rtag}_g{g}")
        mk.op("vector", lambda e: e.tensor_reduce(P["hn"][:, 0:16], scrH[:, 0:1024].rearrange("p (h d) -> p h d", d=64), axis=mybir.AxisListType.X, op=ALU.add),
              reads=[scrH], writes=[P["hn"]], tag=f"nss_{rtag}_g{g}")
        mk.op("vector", lambda e: e.tensor_scalar(P["hn"][:, 0:16], P["hn"][:, 0:16], 1.0, 1e-6, ALU.mult, ALU.add),
              reads=[P["hn"]], writes=[P["hn"]], tag=f"ne_{rtag}_g{g}")
        mk.op("scalar", lambda e: e.activation(P["hn"][:, 16:32], P["hn"][:, 0:16], AF.Ln),
              reads=[P["hn"]], writes=[(P["hn"], "l")], tag=f"nl_{rtag}_g{g}")
        mk.op("scalar", lambda e: e.activation(P["hn"][:, 16:32], P["hn"][:, 16:32], AF.Exp, scale=-0.5),
              reads=[(P["hn"], "l")], writes=[(P["hn"], "l")], tag=f"nr_{rtag}_g{g}")
        for hh in range(H):
            mk.op("vector", lambda e, hh=hh, buf=buf: e.tensor_scalar(
                buf[:, hh * 64:(hh + 1) * 64], tv[:, hh * 64:(hh + 1) * 64], P["hn"][:, 16 + hh:17 + hh], None, ALU.mult),
                  reads=[(tv, 0), (tv, 1), (P["hn"], "l")], writes=[(buf, f"n{hh//8}")], tag=f"na{hh}_{rtag}_g{g}")

    def head_transposes2(g, src_tm, src_keys, dst_all, tag, to_bf):
        for half in range(2):
            last = None
            first = True
            for hl in range(8):
                hh = half * 8 + hl
                rd = [(src_tm, src_keys[half]), ident] if first else []
                last = mk.op("tensor",
                             lambda e, hh=hh, hl=hl, src_tm=src_tm: e.transpose(
                                 PBT[0:64, hl * 128:(hl + 1) * 128], src_tm[:, hh * 64:(hh + 1) * 64], ident[:, :]),
                             reads=rd + ([PBT] if first else []), writes=[PBT] if first else [],
                             tag=f"{tag}ht{hh}_g{g}", track=first)
                first = False
            st = mk.state[_key(PBT)]
            st["w"] = last
            st["r"] = []
            mk.op("vector",
                  lambda e, half=half, dst_all=dst_all: e.tensor_copy(
                      dst_all[:, half * 1024:(half + 1) * 1024], PBT[0:64, 0:1024]),
                  reads=[PBT], writes=[(dst_all, half)], tag=f"{tag}hte{half}_g{g}")

    for g in range(NCH):
        own_idx = g % (NCH // 2)
        first_half = g < NCH // 2
        mask_ap = P["s_maskA"] if first_half else P["s_maskB"]

        sx = c_sx[g]
        mk.dma("sync", f"ldx{g%2}", sx[:, :], P["X1F"][g * 128:(g + 1) * 128, :], writes=[sx], tag=f"x_g{g}")
        mk.op("vector", lambda e, sx=sx: e.reduce_sum(statL[:, 0:1], sx[:, :], axis=mybir.AxisListType.X),
              reads=[sx], writes=[(statL, 0)], tag=f"s1_g{g}")
        mk.op("scalar", lambda e, sx=sx: e.square(scrH[:, 0:1024], sx[:, :]),
              reads=[sx], writes=[scrH], tag=f"sq_g{g}")
        mk.op("vector", lambda e: e.reduce_sum(statL[:, 1:2], scrH[:, 0:1024], axis=mybir.AxisListType.X),
              reads=[scrH], writes=[(statL, 1)], tag=f"s2_g{g}")
        mk.op("vector", lambda e: e.tensor_scalar(statL[:, 2:3], statL[:, 0:1], -1.0 / D, None, ALU.mult),
              reads=[(statL, 0)], writes=[(statL, 2)], tag=f"mn_g{g}")
        mk.op("vector", lambda e: e.tensor_tensor(statL[:, 3:4], statL[:, 2:3], statL[:, 2:3], ALU.mult),
              reads=[(statL, 2)], writes=[(statL, 3)], tag=f"m2_g{g}")
        mk.op("vector", lambda e: e.tensor_scalar(statL[:, 4:5], statL[:, 1:2], 1.0 / D, statL[:, 3:4], ALU.mult, ALU.subtract),
              reads=[(statL, 1), (statL, 3)], writes=[(statL, 4)], tag=f"va_g{g}")
        mk.op("vector", lambda e: e.tensor_scalar(statL[:, 4:5], statL[:, 4:5], 1e-5, None, ALU.add),
              reads=[(statL, 4)], writes=[(statL, 4)], tag=f"vae_g{g}")
        mk.op("scalar", lambda e: e.activation(statL[:, 5:6], statL[:, 4:5], AF.Ln),
              reads=[(statL, 4)], writes=[(statL, 5)], tag=f"ln_g{g}")
        mk.op("scalar", lambda e: e.activation(statL[:, 6:7], statL[:, 5:6], AF.Exp, scale=-0.5),
              reads=[(statL, 5)], writes=[(statL, 6)], tag=f"rs_g{g}")
        mk.op("vector", lambda e: e.tensor_tensor(statL[:, 7:8], statL[:, 2:3], statL[:, 6:7], ALU.mult),
              reads=[(statL, 2), (statL, 6)], writes=[(statL, 7)], tag=f"bb_g{g}")
        xn = P["xn0"]
        mk.op("vector", lambda e, sx=sx, xn=xn: e.tensor_scalar(xn[:, :], sx[:, :], statL[:, 6:7], statL[:, 7:8], ALU.mult, ALU.add),
              reads=[sx, (statL, 6), (statL, 7)], writes=[xn], tag=f"xn_g{g}")

        mk.op("vector", lambda e, xn=xn, mask_ap=mask_ap: e.tensor_scalar(P["xnsel0"][:, :], xn[:, :], mask_ap[:, 0:1], None, ALU.mult),
              reads=[xn, mask_ap], writes=[P["xnsel0"]], tag=f"xs_g{g}")
        for rnd in range(2):
            last = None
            first = True
            for cc in range(4):
                c = rnd * 4 + cc
                last = mk.op("tensor", lambda e, c=c, cc=cc: e.transpose(
                    PT[:, cc * 128:(cc + 1) * 128], P["xnsel0"][:, c * 128:(c + 1) * 128], identf[:, :]),
                    reads=([P["xnsel0"], identf, PT] if first else []), writes=[PT] if first else [],
                    tag=f"str{c}_g{g}", track=first)
                first = False
            st = mk.state[_key(PT)]
            st["w"] = last
            st["r"] = []
            mk.op("vector", lambda e, rnd=rnd: e.tensor_copy(P["h2fm"][:, rnd * 512:(rnd + 1) * 512], PT[:, 0:512]),
                  reads=[PT], writes=[(P["h2fm"], rnd)], tag=f"ste{rnd}_g{g}")
        mk.op("vector", lambda e: e.tensor_copy(P["h2fmb"][:, :], P["h2fm"][:, 0:1024]),
              reads=[(P["h2fm"], 0), (P["h2fm"], 1)], writes=[P["h2fmb"]], tag=f"sfb_g{g}")
        mk.dma("gpsimd", f"xfd{g%2}", P["XN1F"].rearrange("(c p) t -> p c t", p=128)[:, :, own_idx * 128:(own_idx + 1) * 128],
               P["h2fmb"][:, :].rearrange("p (c t) -> p c t", c=8),
               reads=[P["h2fmb"]], writes=[(P["XN1F_t"], own_idx)], accum=not first_half, tag=f"xfd_g{g}")
        mms = [
            (lambda e, c=c: e.matmul(P["PL"][0:128, 0:8], P["h2fm"][:, c * 128:(c + 1) * 128],
                                     P["router_sb"][:, c * 8:(c + 1) * 8], start=(c == 0), stop=(c == 7)))
            for c in range(8)
        ]
        mk.mm_group(mms, reads=[(P["h2fm"], 0), (P["h2fm"], 1)] + [(P["router_sb"], c2) for c2 in range(8)],
                    writes=[P["PL"]], tag=f"lg_g{g}")
        mk.op("vector", lambda e: e.tensor_copy(P["lg_t"][:, :], P["PL"][0:128, 0:8]),
              reads=[P["PL"]], writes=[P["lg_t"]], tag=f"lge_g{g}")
        mk.dma("gpsimd", f"lgd{g%2}", P["LOG1"][own_idx * 128:(own_idx + 1) * 128, :], P["lg_t"][:, :],
               reads=[P["lg_t"]], writes=[(P["LOG1_t"], own_idx)], accum=not first_half, tag=f"lgd_g{g}")

        xnb = c_xnb[g]
        mk.op("vector", lambda e, xn=xn, xnb=xnb: e.tensor_copy(xnb[:, :], xn[:, :]),
              reads=[xn], writes=[xnb], tag=f"xnb_g{g}")
        xnf = c_xnf[g]
        last = None
        first = True
        for c in range(8):
            last = mk.op("tensor",
                         lambda e, c=c, xnb=xnb: e.transpose(PBT[:, c * 128:(c + 1) * 128], xnb[:, c * 128:(c + 1) * 128], ident[:, :]),
                         reads=([xnb, ident, PBT] if first else []), writes=[PBT] if first else [], tag=f"xtr{c}_g{g}", track=first)
            first = False
        st = mk.state[_key(PBT)]
        st["w"] = last
        st["r"] = []
        mk.op("vector", lambda e, xnf=xnf: e.tensor_copy(xnf[:, :], PBT[:, 0:1024]),
              reads=[PBT], writes=[xnf], tag=f"xtre_g{g}")

        qt = c_qt[g]
        proj_tok(g, xnf, P["wq_sb"], qt, "q_")
        rope_norm(g, qt, "q", True)
        mk.op("vector", lambda e, qt=qt, mask_ap=mask_ap: e.tensor_scalar(qt[:, :], qt[:, :], mask_ap[:, 0:1], None, ALU.mult),
              reads=[(qt, "n0"), (qt, "n1"), mask_ap], writes=[(qt, "n0"), (qt, "n1")], tag=f"qm_g{g}")
        kt = c_kt[g]
        proj_tok(g, xnf, P["wk_sb"], kt, "k_")
        rope_norm(g, kt, "k", True)
        mk.op("vector", lambda e, qt=qt: e.tensor_copy(P["qbf"][:, :], qt[:, :]),
              reads=[(qt, "n0"), (qt, "n1"), (qt, 0), (qt, 1)], writes=[(P["qbf"], "h0"), (P["qbf"], "h1")], tag=f"qb_g{g}")
        mk.op("vector", lambda e, kt=kt: e.tensor_copy(P["kbf"][:, :], kt[:, :]),
              reads=[(kt, "n0"), (kt, "n1"), (kt, 0), (kt, 1)], writes=[(P["kbf"], "h0"), (P["kbf"], "h1")], tag=f"kb_g{g}")
        qfa = c_qfa[g]
        head_transposes2(g, P["qbf"], ["h0", "h1"], qfa, "fq", True)
        mk.dma("gpsimd", f"qfd{g%2}", P["QF1"].rearrange("h p s -> p h s")[:, :, own_idx * 128:(own_idx + 1) * 128], qfa[:, :].rearrange("p (h s) -> p h s", h=H),
               reads=[(qfa, 0), (qfa, 1)], writes=[(P["QF1_t"], own_idx)], accum=not first_half, tag=f"fqd_g{g}")
        kfa = c_kfa[g]
        head_transposes2(g, P["kbf"], ["h0", "h1"], kfa, "fk", True)
        mk.dma("sync", f"kfd{g%2}", P["KF1"].rearrange("h p s -> p h s")[:, :, g * 128:(g + 1) * 128], kfa[:, :].rearrange("p (h s) -> p h s", h=H),
               reads=[(kfa, 0), (kfa, 1)], writes=[(P["KF1_t"], g)], tag=f"fkd_g{g}")
        vaug = c_vaug[g]
        mk.op("vector", lambda e, vaug=vaug: e.memset(vaug[:, :], 1.0),
              writes=[vaug, (vaug, "v0"), (vaug, "v1")], tag=f"vm_g{g}")
        for half in range(2):
            pp = PP0 if half == 0 else PP1
            mms = [
                (lambda e, c=c, pp=pp, half=half, xnf=xnf: e.matmul(
                    pp[:, 0:512], xnf[:, c * 128:(c + 1) * 128],
                    P["wv_sb"][:, c * 1024 + half * 512: c * 1024 + half * 512 + 512],
                    start=(c == 0), stop=(c == 7)))
                for c in range(8)
            ]
            mk.mm_group(mms, reads=[xnf, P["wv_sb"]], writes=[pp], tag=f"vmm{half}_g{g}")
            mk.op("vector",
                  lambda e, pp=pp, half=half, vaug=vaug: e.tensor_copy(
                      vaug[:, :].rearrange("p (h s) -> p h s", h=H)[:, 8 * half:8 * half + 8, 0:64],
                      pp[:, 0:512].rearrange("p (h s) -> p h s", h=8)),
                  reads=[pp, vaug], writes=[(vaug, f"v{half}")], tag=f"vev{half}_g{g}")
        mk.dma("sync", f"va0{g%2}", P["VA1"][g * 128:(g + 1) * 128, :], vaug[:, :],
               reads=[(vaug, "v0"), (vaug, "v1")], writes=[(P["VA1_t"], g)], tag=f"vad_g{g}")


def alloc_phaseD(nc, S, P, NCH, prefix=""):
    def sb(name, shape, dt=F32):
        t = S.enter_context(nc.sbuf_tensor(prefix + name, shape, dt))
        P[name] = t
        return t

    def ps(name, shape, dt=F32):
        t = S.enter_context(nc.psum_tensor(prefix + name, shape, dt))
        P[name] = t
        return t

    for nm in ["wq_sb", "wk_sb", "wv_sb"]:
        sb(nm, [128, 8 * 1024], BF16)
    sb("s_ident", [128, 128], BF16)
    sb("s_identf", [128, 128], F32)
    sb("s_maskA", [128, 1], F32)
    sb("s_maskB", [128, 1], F32)
    sb("router_sb", [128, 64], F32)
    for nm in ["sx0", "sx1", "xn0", "xnsel0", "scrA", "scrH", "h2fm"]:
        sb(nm, [128, 1024], F32)
    P["wstage"] = [P["sx0"], P["sx1"]]
    sb("qp0", [128, 1024], F32)
    sb("kp0", [128, 1024], F32)
    sb("ropeT", [128, 1024], F32)
    sb("ropeU", [128, 512], F32)
    sb("ropeS", [128, 512], F32)
    sb("ropeC", [128, 512], F32)
    sb("hn", [128, 32], F32)
    sb("statL", [128, 10], F32)
    sb("lg_t", [128, 8], F32)
    for nm in ["xnb0", "xnf0", "xnf1", "h2fmb", "qbf", "kbf"]:
        sb(nm, [128, 1024], BF16)
    sb("qfa0", [64, 2048], BF16)
    sb("kfa0", [64, 2048], BF16)
    sb("vaug0", [128, H * 65], BF16)
    ps("PP0", [128, 512], F32)
    ps("PP1", [128, 512], F32)
    ps("PT", [128, 512], F32)
    ps("PL", [128, 512], F32)
    ps("PBT", [128, 1024], BF16)
    for nm in ["XN1F_t", "LOG1_t", "QF1_t", "KF1_t", "VA1_t"]:
        P[nm] = object()
    return P


def build_phaseD(NCH):
    T = NCH * 128
    OWN = T // 2
    nc = bass.Bass("TRN2", target_bir_lowering=False, debug=False)
    P = {}
    P["X1F"] = nc.dram_tensor("X1F", [T, D], F32, kind="ExternalInput").ap()
    P["L1_WQ"] = nc.dram_tensor("L1_WQ", [D, D], F32, kind="ExternalInput").ap()
    P["L1_WK"] = nc.dram_tensor("L1_WK", [D, D], F32, kind="ExternalInput").ap()
    P["L1_WV"] = nc.dram_tensor("L1_WV", [D, D], F32, kind="ExternalInput").ap()
    P["L1_ffn_w_gate"] = nc.dram_tensor("L1_ffn_w_gate", [D, 8], F32, kind="ExternalInput").ap()
    P["SINT"] = nc.dram_tensor("SINT", [T, 512], F32, kind="ExternalInput").ap()
    P["COST"] = nc.dram_tensor("COST", [T, 512], F32, kind="ExternalInput").ap()
    for nm, shape, dt in [("identbf", [128, 128], BF16), ("identf32", [128, 128], F32),
                          ("maskA", [128, 1], F32), ("maskB", [128, 1], F32)]:
        P[nm] = nc.dram_tensor(nm, shape, dt, kind="ExternalInput").ap()
    P["KF1"] = nc.dram_tensor("KF1", [H, 64, T], BF16, kind="ExternalOutput").ap()
    P["QF1"] = nc.dram_tensor("QF1", [H, 64, OWN], BF16, kind="ExternalOutput").ap()
    P["VA1"] = nc.dram_tensor("VA1", [T, H * 65], BF16, kind="ExternalOutput").ap()
    P["XN1F"] = nc.dram_tensor("XN1F", [D, OWN], BF16, kind="ExternalOutput").ap()
    P["LOG1"] = nc.dram_tensor("LOG1", [OWN, 8], F32, kind="ExternalOutput").ap()
    with contextlib.ExitStack() as S:
        alloc_phaseD(nc, S, P, NCH)
        mk = MK(nc)
        emit_phaseD(mk, P, NCH)
        emit_final_barrier(mk, [P["XN1F_t"], P["LOG1_t"], P["QF1_t"], P["KF1_t"], P["VA1_t"]])
        mk.build(S)
    return nc


def emit_phaseE(mk, P, NCH):
    T = NCH * 128
    OWN = T // 2
    NT = OWN // 128
    NQQ = max(1, OWN // 512)
    QW = OWN // NQQ
    PX = [P[f"PX{i}"] for i in range(4)]
    PSC0, PSC1 = P["PSC0"], P["PSC1"]
    PL = P["PL"]

    emit_load_weight_bf16(mk, P, P["L1_attn_Wo"], P["wo_sb"], D, D, "wo1")
    for nm, dst in [("ones1x64", "s_ones64"), ("ones1x128", "s_ones128"),
                    ("shiftE", "s_shE"), ("shiftO", "s_shO"), ("neg18", "negCb")]:
        mk.dma("sync", "cst", P[dst][:, :], P[nm], writes=[P[dst]], tag=f"c_{nm}")
    mk.dma("sync", "crow", P["scrq"][0:1, 0:D], P["L1_attn_bo"], writes=[P["scrq"]], tag="c_bo")
    emit_bcast_row(mk, P, PX[0], P["scrq"], D, P["bo_b"], "bcbo")

    c_kh = Cyc([P["kh0"], P["kh1"]])
    c_qh = Cyc([P["qh0"], P["qh1"]])
    c_va = Cyc([P[f"va{i}"] for i in range(4)])
    c_pt = Cyc([P["pt0"], P["pt1"]])
    yfm = P["yfm_all"]

    for hh in range(H):
        kh = c_kh[hh]
        mk.dma("sync", f"khld{hh%2}", kh[:, :], P["KF1"][hh, :, :], writes=[kh], tag=f"khld_{hh}")
        qh = c_qh[hh]
        mk.dma("scalar", f"qhld{hh%2}", qh[:, :], P["QF1"][hh, :, :], writes=[qh], tag=f"qhld_{hh}")
        ctx_open = [False] * 4
        SCB = [PSC0, PSC1, P["PL"], P["PT_E"]]
        for j in range(NCH):
            va = c_va[hh * NCH + j]
            mk.dma("gpsimd", f"vald{(hh*NCH+j)%4}", va[:, :], P["VA1"][j * 128:(j + 1) * 128, hh * 65:(hh + 1) * 65],
                   writes=[va], tag=f"vald_{hh}_{j}")
            for qq in range(NQQ):
                psc = SCB[qq]
                mk.op("tensor", lambda e, kh=kh, qh=qh, j=j, qq=qq, psc=psc: e.matmul(
                    psc[:, 0:QW], kh[:, j * 128:(j + 1) * 128], qh[:, qq * QW:(qq + 1) * QW],
                    start=True, stop=True),
                      reads=[kh, qh], writes=[psc], tag=f"sc_h{hh}_j{j}_q{qq}")
            for qq in range(NQQ):
                psc = SCB[qq]
                pt = c_pt[qq]
                mk.op("scalar", lambda e, pt=pt, psc=psc: e.activation(
                    pt[:, 0:QW], psc[:, 0:QW], AF.Exp, bias=P["negCb"][:, 0:1], scale=0.125),
                      reads=[psc, P["negCb"]], writes=[pt], tag=f"ex_h{hh}_j{j}_q{qq}")
            for qq in range(NQQ):
                pt = c_pt[qq]
                px = PX[qq]
                if not ctx_open[qq]:
                    mk.op("tensor", lambda e, va=va, pt=pt, px=px, j=j: e.matmul(
                        px[0:65, 0:QW], va[:, :], pt[:, 0:QW],
                        start=True, stop=(j == NCH - 1)),
                          reads=[va, pt], writes=[px], tag=f"cx_h{hh}_j{j}_q{qq}")
                    ctx_open[qq] = True
                else:
                    tok = mk.raw_op("tensor", lambda e, va=va, pt=pt, px=px, j=j: e.matmul(
                        px[0:65, 0:QW], va[:, :], pt[:, 0:QW],
                        start=False, stop=(j == NCH - 1)),
                        also=[mk.writer_of(pt), mk.writer_of(va)], tag=f"cx_h{hh}_j{j}_q{qq}")
                    st = mk.state[_key(px)]
                    st["w"] = tok
                    mk.state[_key(pt)]["r"].append(tok)
                    mk.state[_key(va)]["r"].append(tok)
        yh_buf = P["yhE"] if hh % 2 == 0 else P["yhO"]
        for qq in range(NQQ):
            px = PX[qq]
            mk.op("vector", lambda e, px=px: e.reciprocal(P["rec1"][0:1, 0:QW], px[64:65, 0:QW]),
                  reads=[px], writes=[P["rec1"]], tag=f"rc_h{hh}_q{qq}")
            psc = PSC0 if qq % 2 == 0 else PSC1
            mk.op("tensor", lambda e, psc=psc: e.matmul(psc[0:64, 0:QW], P["s_ones64"][0:1, 0:64], P["rec1"][0:1, 0:QW], start=True, stop=True),
                  reads=[P["rec1"], P["s_ones64"]], writes=[psc], tag=f"bc_h{hh}_q{qq}")
            mk.op("vector", lambda e, psc=psc: e.tensor_copy(P["rb1"][0:64, 0:QW], psc[0:64, 0:QW]),
                  reads=[psc], writes=[P["rb1"]], tag=f"rbe_h{hh}_q{qq}")
            mk.op("vector", lambda e, px=px, qq=qq, yh_buf=yh_buf: e.tensor_tensor(
                yh_buf[:, qq * QW:(qq + 1) * QW], px[0:64, 0:QW], P["rb1"][0:64, 0:QW], ALU.mult),
                  reads=[px, P["rb1"]], writes=[(yh_buf, qq)], tag=f"y_h{hh}_q{qq}")
        if hh % 2 == 1:
            pi = hh // 2
            for seg in range(NQQ):
                mms = [
                    (lambda e, seg=seg: e.matmul(PSC0[:, 0:QW], P["s_shE"][:, :], P["yhE"][:, seg * QW:(seg + 1) * QW], start=True, stop=False)),
                    (lambda e, seg=seg: e.matmul(PSC0[:, 0:QW], P["s_shO"][:, :], P["yhO"][:, seg * QW:(seg + 1) * QW], start=False, stop=True)),
                ]
                mk.mm_group(mms, reads=[(P["yhE"], seg), (P["yhO"], seg), P["s_shE"], P["s_shO"]],
                            writes=[PSC0], tag=f"sh_p{pi}_s{seg}")
                mk.op("vector", lambda e, pi=pi, seg=seg: e.tensor_copy(
                    yfm[:, pi * OWN + seg * QW: pi * OWN + (seg + 1) * QW], PSC0[:, 0:QW]),
                      reads=[PSC0], writes=[(yfm, f"{pi}_{seg}")], tag=f"she_p{pi}_s{seg}")

    for tt in range(NT):
        for half in range(2):
            px = PX[half]
            mms = [
                (lambda e, pr=pr, half=half, tt=tt, px=px: e.matmul(
                    px[:, 0:512], yfm[:, pr * OWN + tt * 128: pr * OWN + tt * 128 + 128],
                    P["wo_sb"][:, pr * 1024 + half * 512: pr * 1024 + half * 512 + 512],
                    start=(pr == 0), stop=(pr == 7)))
                for pr in range(8)
            ]
            mk.mm_group(mms, reads=[(yfm, f"{pr2}_{(tt*128)//QW}") for pr2 in range(8)] + [P["wo_sb"]],
                        writes=[px], tag=f"wo{half}_t{tt}")
            mk.op("vector", lambda e, half=half, px=px: e.tensor_tensor(
                P["acc_t"][:, half * 512:(half + 1) * 512], px[:, 0:512], P["bo_b"][:, half * 512:(half + 1) * 512], ALU.add),
                  reads=[px, P["bo_b"]], writes=[(P["acc_t"], half)], tag=f"ae{half}_t{tt}")
        mk.dma("gpsimd", f"accd{tt%2}", P["ACC1"][tt * 128:(tt + 1) * 128, :], P["acc_t"][:, :],
               reads=[(P["acc_t"], 0), (P["acc_t"], 1)], writes=[(P["ACC1_t"], tt)], tag=f"accd_t{tt}")


def alloc_phaseE(nc, S, P, NCH, prefix=""):
    T = NCH * 128
    OWN = T // 2

    def sb(name, shape, dt=F32):
        t = S.enter_context(nc.sbuf_tensor(prefix + name, shape, dt))
        P[name] = t
        return t

    def ps(name, shape, dt=F32):
        t = S.enter_context(nc.psum_tensor(prefix + name, shape, dt))
        P[name] = t
        return t

    sb("wo_sb", [128, 8 * 1024], BF16)
    P["wstage"] = [sb("wst0", [128, 1024], F32), sb("wst1", [128, 1024], F32)]
    sb("s_ones64", [1, 64], F32)
    sb("s_ones128", [1, 128], F32)
    sb("s_shE", [64, 128], BF16)
    sb("s_shO", [64, 128], BF16)
    sb("negCb", [128, 1], F32)
    sb("scrq", [128, 1024], F32)
    sb("bo_b", [128, 1024], F32)
    sb("kh0", [64, T], BF16)
    sb("kh1", [64, T], BF16)
    sb("qh0", [64, OWN], BF16)
    sb("qh1", [64, OWN], BF16)
    for i in range(4):
        sb(f"va{i}", [128, 65], BF16)
    sb("pt0", [128, 512], BF16)
    sb("pt1", [128, 512], BF16)
    sb("rec1", [1, 512], F32)
    sb("rb1", [64, 512], F32)
    sb("yhE", [64, OWN], BF16)
    sb("yhO", [64, OWN], BF16)
    sb("yfm_all", [128, 8 * OWN], BF16)
    sb("acc_t", [128, 1024], F32)
    for i in range(4):
        ps(f"PX{i}", [128, 512], F32)
    ps("PSC0", [128, 512], F32)
    ps("PSC1", [128, 512], F32)
    ps("PL", [128, 512], F32)
    ps("PT_E", [128, 512], F32)
    P["ACC1_t"] = object()
    return P


def build_phaseE(NCH):
    T = NCH * 128
    OWN = T // 2
    nc = bass.Bass("TRN2", target_bir_lowering=False, debug=False)
    P = {}
    P["KF1"] = nc.dram_tensor("KF1", [H, 64, T], BF16, kind="ExternalInput").ap()
    P["QF1"] = nc.dram_tensor("QF1", [H, 64, OWN], BF16, kind="ExternalInput").ap()
    P["VA1"] = nc.dram_tensor("VA1", [T, H * 65], BF16, kind="ExternalInput").ap()
    P["L1_attn_Wo"] = nc.dram_tensor("L1_attn_Wo", [D, D], F32, kind="ExternalInput").ap()
    P["L1_attn_bo"] = nc.dram_tensor("L1_attn_bo", [1, D], F32, kind="ExternalInput").ap()
    for nm, shape, dt in [("ones1x64", [1, 64], F32), ("ones1x128", [1, 128], F32),
                          ("shiftE", [64, 128], BF16), ("shiftO", [64, 128], BF16),
                          ("neg18", [128, 1], F32)]:
        P[nm] = nc.dram_tensor(nm, shape, dt, kind="ExternalInput").ap()
    P["ACC1"] = nc.dram_tensor("ACC1", [OWN, D], F32, kind="ExternalOutput").ap()
    with contextlib.ExitStack() as S:
        alloc_phaseE(nc, S, P, NCH)
        mk = MK(nc)
        emit_phaseE(mk, P, NCH)
        emit_final_barrier(mk, [P["ACC1_t"]])
        mk.build(S)
    return nc


def emit_phaseF(mk, P, NCH, NCORES=8, CAPV=4096.0):
    T = NCH * 128
    OWN = T // 2
    NT = OWN // 128
    NTOK = OWN * NCORES
    NGT = NTOK // 128
    iota8 = P["s_iota8"]
    st1 = P["stat1"]
    PG = [P[f"PG{i}"] for i in range(4)]

    mk.dma("sync", "cst", P["s_iota8"][:, :], P["iota8"], writes=[P["s_iota8"]], tag="c_iota")
    mk.dma("sync", "cst", P["s_ones128"][:, :], P["ones1x128"], writes=[P["s_ones128"]], tag="c_o128")
    mk.dma("sync", "cst", P["s_identf"][:, :], P["identf32"], writes=[P["s_identf"]], tag="c_idf")
    mk.dma("sync", "cst", P["s_msel"][:, :], P["msel"], writes=[P["s_msel"]], tag="c_msel")

    for gt in range(NGT):
        lg = P["lg_t"]
        mk.dma("sync", f"lgld{gt%2}", lg[:, :], P["LOGALL"][gt * 128:(gt + 1) * 128, :], writes=[lg], tag=f"lgld_{gt}")
        mk.op("vector", lambda e, lg=lg: e.reduce_max(st1[:, 0:1], lg[:, :], axis=mybir.AxisListType.X),
              reads=[lg], writes=[(st1, 0)], tag=f"rm_{gt}")
        mk.op("vector", lambda e: e.tensor_scalar(st1[:, 1:2], st1[:, 0:1], -1.0, None, ALU.mult),
              reads=[(st1, 0)], writes=[(st1, 1)], tag=f"nm_{gt}")
        mk.op("scalar", lambda e, lg=lg: e.activation(P["prb"][:, :], lg[:, :], AF.Exp, bias=st1[:, 1:2]),
              reads=[lg, (st1, 1)], writes=[P["prb"]], tag=f"pe_{gt}")
        mk.op("vector", lambda e: e.reduce_sum(st1[:, 2:3], P["prb"][:, :], axis=mybir.AxisListType.X),
              reads=[P["prb"]], writes=[(st1, 2)], tag=f"ps_{gt}")
        mk.op("vector", lambda e: e.reciprocal(st1[:, 3:4], st1[:, 2:3]),
              reads=[(st1, 2)], writes=[(st1, 3)], tag=f"pr_{gt}")
        mk.op("vector", lambda e: e.tensor_scalar(P["prb"][:, :], P["prb"][:, :], st1[:, 3:4], None, ALU.mult),
              reads=[P["prb"], (st1, 3)], writes=[P["prb"]], tag=f"pn_{gt}")
        mk.op("vector", lambda e: e.reduce_max(st1[:, 4:5], P["prb"][:, :], axis=mybir.AxisListType.X),
              reads=[P["prb"]], writes=[(st1, 4)], tag=f"m1_{gt}")
        mk.op("vector", lambda e: e.tensor_scalar(P["eqb"][:, :], P["prb"][:, :], st1[:, 4:5], None, ALU.is_ge),
              reads=[P["prb"], (st1, 4)], writes=[P["eqb"]], tag=f"e1_{gt}")
        mk.op("vector", lambda e: e.tensor_scalar(P["eqb"][:, :], P["eqb"][:, :], -1e9, 1e9, ALU.mult, ALU.add),
              reads=[P["eqb"]], writes=[P["eqb"]], tag=f"e1b_{gt}")
        mk.op("vector", lambda e: e.tensor_tensor(P["eqb"][:, :], P["eqb"][:, :], iota8[:, :], ALU.add),
              reads=[P["eqb"], iota8], writes=[P["eqb"]], tag=f"e1c_{gt}")
        mk.op("vector", lambda e: e.tensor_reduce(st1[:, 5:6], P["eqb"][:, :], axis=mybir.AxisListType.X, op=ALU.min),
              reads=[P["eqb"]], writes=[(st1, 5)], tag=f"i1_{gt}")
        mk.op("vector", lambda e, gt=gt: e.tensor_scalar(P["OH1"][:, gt * 8:(gt + 1) * 8], iota8[:, :], st1[:, 5:6], None, ALU.is_equal),
              reads=[iota8, (st1, 5)], writes=[(P["OH1"], gt)], tag=f"o1_{gt}")
        mk.op("vector", lambda e, gt=gt: e.scalar_tensor_tensor(P["msk"][:, :], P["OH1"][:, gt * 8:(gt + 1) * 8], -1e9, P["prb"][:, :], ALU.mult, ALU.add),
              reads=[(P["OH1"], gt), P["prb"]], writes=[P["msk"]], tag=f"mk_{gt}")
        mk.op("vector", lambda e: e.reduce_max(st1[:, 6:7], P["msk"][:, :], axis=mybir.AxisListType.X),
              reads=[P["msk"]], writes=[(st1, 6)], tag=f"m2_{gt}")
        mk.op("vector", lambda e: e.tensor_scalar(P["eqb"][:, :], P["msk"][:, :], st1[:, 6:7], None, ALU.is_ge),
              reads=[P["msk"], (st1, 6)], writes=[P["eqb"]], tag=f"e2_{gt}")
        mk.op("vector", lambda e: e.tensor_scalar(P["eqb"][:, :], P["eqb"][:, :], -1e9, 1e9, ALU.mult, ALU.add),
              reads=[P["eqb"]], writes=[P["eqb"]], tag=f"e2b_{gt}")
        mk.op("vector", lambda e: e.tensor_tensor(P["eqb"][:, :], P["eqb"][:, :], iota8[:, :], ALU.add),
              reads=[P["eqb"], iota8], writes=[P["eqb"]], tag=f"e2c_{gt}")
        mk.op("vector", lambda e: e.tensor_reduce(st1[:, 7:8], P["eqb"][:, :], axis=mybir.AxisListType.X, op=ALU.min),
              reads=[P["eqb"]], writes=[(st1, 7)], tag=f"i2_{gt}")
        mk.op("vector", lambda e, gt=gt: e.tensor_scalar(P["OH2"][:, gt * 8:(gt + 1) * 8], iota8[:, :], st1[:, 7:8], None, ALU.is_equal),
              reads=[iota8, (st1, 7)], writes=[(P["OH2"], gt)], tag=f"o2_{gt}")
        mk.op("vector", lambda e, gt=gt: e.tensor_copy(P["W1"][:, gt:gt + 1], st1[:, 4:5]),
              reads=[(st1, 4)], writes=[(P["W1"], gt)], tag=f"w1_{gt}")
        mk.op("vector", lambda e, gt=gt: e.tensor_copy(P["W2"][:, gt:gt + 1], st1[:, 6:7]),
              reads=[(st1, 6)], writes=[(P["W2"], gt)], tag=f"w2_{gt}")
        mk.op("vector", lambda e, gt=gt: e.tensor_tensor(P["cnt_t"][:, 0:8], P["OH1"][:, gt * 8:(gt + 1) * 8], P["OH2"][:, gt * 8:(gt + 1) * 8], ALU.add),
              reads=[(P["OH1"], gt), (P["OH2"], gt)], writes=[P["cnt_t"]], tag=f"cn_{gt}")
        mk.op("tensor", lambda e: e.transpose(P["PT2"][0:8, 0:128], P["cnt_t"][:, 0:8], P["s_identf"][:, :]),
              reads=[P["cnt_t"], P["s_identf"], P["PT2"]], writes=[P["PT2"]], tag=f"ctr_{gt}")
        mk.op("vector", lambda e, gt=gt: e.tensor_copy(P["cntT"][0:8, (gt % (NGT // 4)) * 128:((gt % (NGT // 4)) + 1) * 128], P["PT2"][0:8, 0:128]),
              reads=[P["PT2"]], writes=[(P["cntT"], gt % (NGT // 4))], tag=f"cte_{gt}")
        if gt % (NGT // 4) == (NGT // 4) - 1:
            seg = gt // (NGT // 4)
            SW = (NGT // 4) * 128
            if seg == 0:
                mk.op("vector", lambda e: e.memset(P["carry"][0:8, 0:1], 0.0), writes=[P["carry"]], tag="car0")
            mk.op("vector", lambda e, SW=SW: e.tensor_tensor_scan(
                P["scanT"][0:8, 0:SW], P["cntT"][0:8, 0:SW], P["cntT"][0:8, 0:SW], 0.0, ALU.add, ALU.bypass),
                  reads=[(P["cntT"], k2) for k2 in range(NGT // 4)], writes=[P["scanT"]], tag=f"scan_{seg}")
            mk.op("vector", lambda e, SW=SW: e.tensor_scalar(P["scanT"][0:8, 0:SW], P["scanT"][0:8, 0:SW], P["carry"][0:8, 0:1], None, ALU.add),
                  reads=[P["scanT"], P["carry"]], writes=[P["scanT"]], tag=f"scac_{seg}")
            mk.op("vector", lambda e, SW=SW: e.tensor_copy(P["carry"][0:8, 0:1], P["scanT"][0:8, SW - 1:SW]),
                  reads=[P["scanT"]], writes=[P["carry"]], tag=f"carn_{seg}")
            mk.op("vector", lambda e, SW=SW: e.tensor_tensor(P["scanT"][0:8, 0:SW], P["scanT"][0:8, 0:SW], P["cntT"][0:8, 0:SW], ALU.subtract),
                  reads=[P["scanT"]] + [(P["cntT"], k2) for k2 in range(NGT // 4)], writes=[P["scanT"]], tag=f"sce_{seg}")
            for k in range(NGT // 4):
                gt2 = seg * (NGT // 4) + k
                mk.op("tensor", lambda e, k=k: e.transpose(P["PT2"][0:128, 0:8], P["scanT"][0:8, k * 128:(k + 1) * 128], P["s_identf"][0:8, 0:8]),
                      reads=[P["scanT"], P["s_identf"], P["PT2"]], writes=[P["PT2"]], tag=f"str_{gt2}")
                mk.op("vector", lambda e, gt2=gt2: e.tensor_copy(P["S_tm"][:, gt2 * 8:(gt2 + 1) * 8], P["PT2"][0:128, 0:8]),
                      reads=[P["PT2"]], writes=[(P["S_tm"], gt2)], tag=f"ste_{gt2}")

    CT = P["ct_all"]
    for tt in range(NT):
        for nm in ["xS", "xO1", "xO2"]:
            mk.op("vector", lambda e, nm=nm: e.memset(P[nm][:, :], 0.0), writes=[P[nm]], tag=f"z{nm}_{tt}")
        mk.op("vector", lambda e: e.memset(P["xW"][:, 0:2], 0.0), writes=[P["xW"]], tag=f"zw_{tt}")
        for ci in range(NCORES):
            gt2 = ci * NT + tt
            mk.op("vector", lambda e, gt2=gt2, ci=ci: e.scalar_tensor_tensor(
                P["xS"][:, :], P["S_tm"][:, gt2 * 8:(gt2 + 1) * 8], P["s_msel"][:, ci:ci + 1], P["xS"][:, :], ALU.mult, ALU.add),
                  reads=[(P["S_tm"], gt2), P["s_msel"], P["xS"]], writes=[P["xS"]], tag=f"xs{ci}_{tt}")
            mk.op("vector", lambda e, gt2=gt2, ci=ci: e.scalar_tensor_tensor(
                P["xO1"][:, :], P["OH1"][:, gt2 * 8:(gt2 + 1) * 8], P["s_msel"][:, ci:ci + 1], P["xO1"][:, :], ALU.mult, ALU.add),
                  reads=[(P["OH1"], gt2), P["s_msel"], P["xO1"]], writes=[P["xO1"]], tag=f"xo1{ci}_{tt}")
            mk.op("vector", lambda e, gt2=gt2, ci=ci: e.scalar_tensor_tensor(
                P["xO2"][:, :], P["OH2"][:, gt2 * 8:(gt2 + 1) * 8], P["s_msel"][:, ci:ci + 1], P["xO2"][:, :], ALU.mult, ALU.add),
                  reads=[(P["OH2"], gt2), P["s_msel"], P["xO2"]], writes=[P["xO2"]], tag=f"xo2{ci}_{tt}")
            mk.op("vector", lambda e, gt2=gt2, ci=ci: e.scalar_tensor_tensor(
                P["xW"][:, 0:1], P["W1"][:, gt2:gt2 + 1], P["s_msel"][:, ci:ci + 1], P["xW"][:, 0:1], ALU.mult, ALU.add),
                  reads=[(P["W1"], gt2), P["s_msel"], P["xW"]], writes=[P["xW"]], tag=f"xw1{ci}_{tt}")
            mk.op("vector", lambda e, gt2=gt2, ci=ci: e.scalar_tensor_tensor(
                P["xW"][:, 1:2], P["W2"][:, gt2:gt2 + 1], P["s_msel"][:, ci:ci + 1], P["xW"][:, 1:2], ALU.mult, ALU.add),
                  reads=[(P["W2"], gt2), P["s_msel"], P["xW"]], writes=[P["xW"]], tag=f"xw2{ci}_{tt}")
        mk.op("vector", lambda e: e.tensor_tensor(P["eqb"][:, :], P["xS"][:, :], P["xO1"][:, :], ALU.mult),
              reads=[P["xS"], P["xO1"]], writes=[P["eqb"]], tag=f"p1a_{tt}")
        mk.op("vector", lambda e: e.reduce_sum(st1[:, 0:1], P["eqb"][:, :], axis=mybir.AxisListType.X),
              reads=[P["eqb"]], writes=[(st1, 0)], tag=f"p1b_{tt}")
        mk.op("vector", lambda e: e.tensor_scalar(st1[:, 0:1], st1[:, 0:1], CAPV, None, ALU.is_lt),
              reads=[(st1, 0)], writes=[(st1, 0)], tag=f"v1_{tt}")
        mk.op("vector", lambda e: e.tensor_tensor(st1[:, 0:1], st1[:, 0:1], P["xW"][:, 0:1], ALU.mult),
              reads=[(st1, 0), P["xW"]], writes=[(st1, 0)], tag=f"cw1_{tt}")
        mk.op("vector", lambda e: e.tensor_tensor(P["eqb"][:, :], P["xS"][:, :], P["xO2"][:, :], ALU.mult),
              reads=[P["xS"], P["xO2"]], writes=[P["eqb"]], tag=f"p2a_{tt}")
        mk.op("vector", lambda e: e.reduce_sum(st1[:, 1:2], P["eqb"][:, :], axis=mybir.AxisListType.X),
              reads=[P["eqb"]], writes=[(st1, 1)], tag=f"p2b_{tt}")
        mk.op("vector", lambda e: e.tensor_scalar(st1[:, 1:2], st1[:, 1:2], CAPV, None, ALU.is_lt),
              reads=[(st1, 1)], writes=[(st1, 1)], tag=f"v2_{tt}")
        mk.op("vector", lambda e: e.tensor_tensor(st1[:, 1:2], st1[:, 1:2], P["xW"][:, 1:2], ALU.mult),
              reads=[(st1, 1), P["xW"]], writes=[(st1, 1)], tag=f"cw2_{tt}")
        mk.op("vector", lambda e, tt=tt: e.tensor_scalar(CT[:, tt * 8:(tt + 1) * 8], P["xO1"][:, :], st1[:, 0:1], None, ALU.mult),
              reads=[P["xO1"], (st1, 0)], writes=[(CT, tt)], tag=f"c1_{tt}")
        mk.op("vector", lambda e, tt=tt: e.scalar_tensor_tensor(CT[:, tt * 8:(tt + 1) * 8], P["xO2"][:, :], st1[:, 1:2], CT[:, tt * 8:(tt + 1) * 8], ALU.mult, ALU.add),
              reads=[P["xO2"], (st1, 1), (CT, tt)], writes=[(CT, tt)], tag=f"c2_{tt}")
    mk.op("tensor", lambda e: e.transpose(P["PT2"][0:NT * 8, 0:128], CT[:, 0:NT * 8], P["s_identf"][:, :]),
          reads=[CT] + [(CT, tt2) for tt2 in range(NT)] + [P["s_identf"], P["PT2"]], writes=[P["PT2"]], tag="cttr")
    mk.op("vector", lambda e: e.tensor_copy(P["ctT"][0:NT * 8, 0:128], P["PT2"][0:NT * 8, 0:128]),
          reads=[P["PT2"]], writes=[P["ctT"]], tag="ctev")
    mk.dma("sync", "ctd", P["CTD"].rearrange("a b t -> (a b) t"), P["ctT"][0:NT * 8, 0:128],
           reads=[P["ctT"]], writes=[(P["CTD_t"], 0)], tag="ctd")

    for ex in range(8):
        emit_moe_expert(mk, P, NCH, f"f{ex}", P["L1_e_gate"][ex], P["L1_e_up"][ex],
                        P["L1_e_down"][ex], 12, True, ex, P["XN1F"], P["ACC1"], "ACC1_t")

    for tt in range(NT):
        a1 = P["fx_a"]
        a3 = P["fx_c"]
        mk.dma("sync", f"f1{tt%2}", a1[:, 0:1024], P["ACC1"][tt * 128:(tt + 1) * 128, :],
               reads=[(P["ACC1_t"], tt)], writes=[a1, (a1, 0), (a1, 1)], tag=f"f1_{tt}")
        mk.dma("gpsimd", f"f3{tt%2}", a3[:, 0:1024], P["X1OWN"][tt * 128:(tt + 1) * 128, :],
               writes=[a3, (a3, 0), (a3, 1)], tag=f"f3_{tt}")
        mk.op("vector", lambda e, a1=a1, a3=a3: e.tensor_tensor(a1[:, 0:1024], a1[:, 0:1024], a3[:, 0:1024], ALU.add),
              reads=[a3, a1], writes=[a1], tag=f"fb_{tt}")
        mk.dma("sync", f"x1d{tt%2}", P["OUT"][tt * 128:(tt + 1) * 128, :], a1[:, 0:1024],
               reads=[a1], writes=[(P["OUT_t"], tt)], tag=f"outd_{tt}")


def alloc_phaseF(nc, S, P, NCH, NCORES=8, prefix=""):
    T = NCH * 128
    OWN = T // 2
    NT = OWN // 128
    NTOK = OWN * NCORES
    NGT = NTOK // 128
    HHALF = OWN // 2

    def sb(name, shape, dt=F32):
        t = S.enter_context(nc.sbuf_tensor(prefix + name, shape, dt))
        P[name] = t
        return t

    def ps(name, shape, dt=F32):
        t = S.enter_context(nc.psum_tensor(prefix + name, shape, dt))
        P[name] = t
        return t

    sb("s_iota8", [128, 8], F32)
    sb("s_ones128", [1, 128], F32)
    sb("s_identf", [128, 128], F32)
    sb("s_msel", [128, 8], F32)
    sb("stat1", [128, 8], F32)
    sb("lg_t", [128, 8], F32)
    sb("prb", [128, 8], F32)
    sb("eqb", [128, 8], F32)
    sb("msk", [128, 8], F32)
    sb("cnt_t", [128, 8], F32)
    sb("OH1", [128, NGT * 8], F32)
    sb("OH2", [128, NGT * 8], F32)
    sb("W1", [128, NGT], F32)
    sb("W2", [128, NGT], F32)
    sb("S_tm", [128, NGT * 8], F32)
    sb("cntT", [8, (NGT // 4) * 128], F32)
    sb("scanT", [8, (NGT // 4) * 128], F32)
    sb("carry", [8, 1], F32)
    sb("xS", [128, 8], F32)
    sb("xO1", [128, 8], F32)
    sb("xO2", [128, 8], F32)
    sb("xW", [128, 2], F32)
    sb("ct_all", [128, NT * 8], F32)
    sb("ctT", [128, 128], F32)
    sb("ce_row", [1, OWN], F32)
    sb("ce_b", [128, OWN], BF16)
    sb("gate_sb", [128, 8 * 1536], BF16)
    sb("up_sb", [128, 8 * 1536], BF16)
    sb("down_sb", [128, 12 * 1024], BF16)
    P["wstage"] = [sb("wstage0", [128, 1536], F32), sb("wstage1", [128, 1536], F32)]
    sb("h2f_a", [128, 8 * HHALF], BF16)
    sb("h2f_b", [128, 8 * HHALF], BF16)
    sb("act_sb", [128, 12 * HHALF], BF16)
    sb("scrS", [128, 512], BF16)
    sb("scrS2", [128, 512], BF16)
    sb("dn_t", [128, 1024], F32)
    P["fx_a"] = P["dn_t"]
    P["fx_c"] = P["wstage"][0]
    for i in range(4):
        ps(f"PG{i}", [128, 512], F32)
        ps(f"PU{i}", [128, 512], F32)
    P["PT2"] = P["PU3"]
    for nm in ["CTD_t", "ACC1_t", "OUT_t"]:
        P[nm] = object()
    return P


def build_phaseF(NCH, NCORES=8, CAPV=4096.0):
    T = NCH * 128
    OWN = T // 2
    NTOK = OWN * NCORES
    nc = bass.Bass("TRN2", target_bir_lowering=False, debug=False)
    P = {}
    P["LOGALL"] = nc.dram_tensor("LOGALL", [NTOK, 8], F32, kind="ExternalInput").ap()
    P["XN1F"] = nc.dram_tensor("XN1F", [D, OWN], BF16, kind="ExternalInput").ap()
    P["ACC1IN"] = nc.dram_tensor("ACC1IN", [OWN, D], F32, kind="ExternalInput").ap()
    P["X1OWN"] = nc.dram_tensor("X1OWN", [OWN, D], F32, kind="ExternalInput").ap()
    P["L1_e_gate"] = nc.dram_tensor("L1_e_gate", [8, D, 1536], F32, kind="ExternalInput").ap()
    P["L1_e_up"] = nc.dram_tensor("L1_e_up", [8, D, 1536], F32, kind="ExternalInput").ap()
    P["L1_e_down"] = nc.dram_tensor("L1_e_down", [8, 1536, D], F32, kind="ExternalInput").ap()
    for nm, shape, dt in [("identf32", [128, 128], F32), ("ones1x128", [1, 128], F32),
                          ("iota8", [128, 8], F32), ("msel", [128, 8], F32)]:
        P[nm] = nc.dram_tensor(nm, shape, dt, kind="ExternalInput").ap()
    P["ACC1"] = nc.dram_tensor("ACC1W", [OWN, D], F32)
    P["CTD"] = nc.dram_tensor("CTD", [OWN // 128, 8, 128], F32)
    P["OUT"] = nc.dram_tensor("OUT", [OWN, D], F32, kind="ExternalOutput").ap()
    with contextlib.ExitStack() as S:
        alloc_phaseF(nc, S, P, NCH, NCORES)
        mk = MK(nc)
        NT = OWN // 128
        for tt in range(NT):
            mk.dma("sync", f"acs{tt%2}", P["ACC1"][tt * 128:(tt + 1) * 128, :], P["ACC1IN"][tt * 128:(tt + 1) * 128, :],
                   writes=[(P["ACC1_t"], tt)], tag=f"acs_{tt}")
        emit_phaseF(mk, P, NCH, NCORES, CAPV)
        emit_final_barrier(mk, [P["OUT_t"]])
        mk.build(S)
    return nc


def emit_final_barrier(mk, markers):
    toks = []
    for m in markers:
        for (kid, sub), st in mk.state.items():
            if kid == id(m) and st["w"] is not None:
                toks.append(st["w"])
    mk.op("sync", lambda e: e.nop(), also=toks, tag="final")




def alloc_phaseA(nc, S, P, prefix=""):
    def sb(name, shape, dt=F32):
        t = S.enter_context(nc.sbuf_tensor(prefix + name, shape, dt))
        P[name] = t
        return t

    def ps(name, shape, dt=F32):
        t = S.enter_context(nc.psum_tensor(prefix + name, shape, dt))
        P[name] = t
        return t

    for nm in ["WqT_sb", "WkT_sb", "WvT_sb", "WpT_sb", "wq_sb", "wk_sb", "wv_sb"]:
        sb(nm, [128, 8 * 1024], BF16)

    sb("s_ident", [128, 128], BF16)
    sb("s_tri", [128, 128], F32)
    sb("s_shE", [64, 128], BF16)
    sb("s_shO", [64, 128], BF16)
    sb("s_ones128", [1, 128], F32)
    P["s_ones64"] = P["s_ones128"]
    sb("s_maskA", [128, 1], F32)
    sb("s_maskB", [128, 1], F32)
    sb("s_bpA", [128, 1024], F32)
    sb("s_bpB", [128, 1024], F32)
    sb("s_n1b", [128, 1024], F32)
    S32a = S.enter_context(nc.sbuf_tensor(prefix + "S32a", [64, H * 65], F32))
    S16a = S.enter_context(nc.sbuf_tensor(prefix + "S16a", [64, H * 65], BF16))
    P["S32a_v"] = S32a
    P["S16a_v"] = S16a
    P["S32"] = [S32a[:, i * 65:(i + 1) * 65] for i in range(H)]
    P["S16"] = [S16a[:, i * 65:(i + 1) * 65] for i in range(H)]
    for nm in ["sx0", "sx1", "xn0", "xnsel0", "attmp0",
               "scrA", "scrH"]:
        sb(nm, [128, 1024], F32)
    sb("scrB", [128, 512], F32)
    P["wstage"] = [P["sx0"], P["sx1"]]
    for nm in ["xnb0", "xnf0", "kp0", "qp0",
               "hb0", "hfm0",
               "yfm0", "yfm1"]:
        sb(nm, [128, 1024], BF16)
    sb("vaug0", [128, H * 65], BF16)
    P["vaugf0"] = P["vaug0"]
    for nm in ["qpfa0", "kpfa0", "kfa0", "qfa0"]:
        sb(nm, [64, 2048], BF16)
    sb("Pm_w", [128, 512], BF16)
    sb("y_w", [64, 512], BF16)
    sb("rb_w", [64, 512], F32)
    sb("den_b", [1, 512], F32)
    sb("rec_b", [1, 512], F32)
    sb("statL", [128, 10], F32)
    sb("qkss", [128, 32], F32)
    sb("hss16", [128, 16], F32)
    sb("qkT", [32, 129], F32)
    sb("s_identf", [128, 128], F32)
    ps("PP0", [128, 512], F32)
    ps("PP1", [128, 512], F32)
    ps("PA", [128, 512], F32)
    ps("PO", [128, 512], F32)
    ps("PC", [128, 512], F32)
    ps("PY", [128, 512], F32)
    ps("PBT", [128, 1024], BF16)
    for nm in ["XN0_t", "AT0_t", "VA0_t", "KF0_t", "QF0_t", "QKM0_t"]:
        P[nm] = object()
    return P


def build_phaseA(NCH):
    T = NCH * 128
    nc = bass.Bass("TRN2", target_bir_lowering=False, debug=False)
    P = {}
    P["XIN"] = nc.dram_tensor("XIN", [T, D], F32, kind="ExternalInput").ap()
    for nm in ["L0_attn_WqT", "L0_attn_WkT", "L0_attn_WvT", "L0_attn_WpT",
               "L0_ffn_wq", "L0_ffn_wk", "L0_ffn_wv"]:
        P[nm] = nc.dram_tensor(nm, [D, D], F32, kind="ExternalInput").ap()
    P["L0_attn_bp"] = nc.dram_tensor("L0_attn_bp", [1, D], F32, kind="ExternalInput").ap()
    P["L0_ffn_norm1_w"] = nc.dram_tensor("L0_ffn_norm1_w", [1, D], F32, kind="ExternalInput").ap()
    for nm, shape, dt in [("identbf", [128, 128], BF16), ("trimask", [128, 128], F32),
                          ("shiftE", [64, 128], BF16), ("shiftO", [64, 128], BF16),
                          ("ones1x64", [1, 64], F32), ("ones1x128", [1, 128], F32),
                          ("maskA", [128, 1], F32), ("maskB", [128, 1], F32),
                          ("identf32", [128, 128], F32)]:
        P[nm] = nc.dram_tensor(nm, shape, dt, kind="ExternalInput").ap()
    P["XN0"] = nc.dram_tensor("XN0", [T // 2, D], F32, kind="ExternalOutput").ap()
    P["AT0"] = nc.dram_tensor("AT0", [T // 2, D], F32, kind="ExternalOutput").ap()
    P["VA0"] = nc.dram_tensor("VA0", [T, H * 65], BF16, kind="ExternalOutput").ap()
    P["KF0"] = nc.dram_tensor("KF0", [H, 64, T], BF16, kind="ExternalOutput").ap()
    P["QF0"] = nc.dram_tensor("QF0", [H, 64, T // 2], BF16, kind="ExternalOutput").ap()
    P["QKM0"] = nc.dram_tensor("QKM0", [32, 1], F32, kind="ExternalOutput").ap()

    with contextlib.ExitStack() as S:
        alloc_phaseA(nc, S, P)
        mk = MK(nc)
        emit_phaseA(mk, P, NCH)
        emit_final_barrier(mk, [P["XN0_t"], P["AT0_t"], P["VA0_t"], P["KF0_t"], P["QF0_t"], P["QKM0_t"]])
        mk.build(S)
    return nc


def host_constants():
    import ml_dtypes as mld
    c = {}
    c["identbf"] = np.eye(128, dtype=np.float32).astype(mld.bfloat16)
    tri = np.zeros((128, 128), dtype=np.float32)
    for j in range(128):
        tri[j, j:] = 1.0
    c["trimask"] = tri
    shE = np.zeros((64, 128), dtype=np.float32)
    shO = np.zeros((64, 128), dtype=np.float32)
    for p in range(64):
        shE[p, p] = 1.0
        shO[p, p + 64] = 1.0
    c["shiftE"] = shE.astype(mld.bfloat16)
    c["shiftO"] = shO.astype(mld.bfloat16)
    c["ones1x64"] = np.ones((1, 64), dtype=np.float32)
    c["ones1x128"] = np.ones((1, 128), dtype=np.float32)
    c["identf32"] = np.eye(128, dtype=np.float32)
    return c


def phaseA_inmap(x_full_batch, weights, h):
    c = host_constants()
    m = {
        "XIN": np.ascontiguousarray(x_full_batch),
        "L0_attn_WqT": np.ascontiguousarray(weights["L0_attn_Wq"].T),
        "L0_attn_WkT": np.ascontiguousarray(weights["L0_attn_Wk"].T),
        "L0_attn_WvT": np.ascontiguousarray(weights["L0_attn_Wv"].T),
        "L0_attn_WpT": np.ascontiguousarray(weights["L0_attn_Wp"].T),
        "L0_ffn_wq": np.ascontiguousarray(weights["L0_ffn_wq"]),
        "L0_ffn_wk": np.ascontiguousarray(weights["L0_ffn_wk"]),
        "L0_ffn_wv": np.ascontiguousarray(weights["L0_ffn_wv"]),
        "L0_attn_bp": np.ascontiguousarray(weights["L0_attn_bp"]).reshape(1, D),
        "L0_ffn_norm1_w": np.ascontiguousarray(weights["L0_ffn_norm1_w"]).reshape(1, D),
        "maskA": np.full((128, 1), 1.0 if h == 0 else 0.0, np.float32),
        "maskB": np.full((128, 1), 0.0 if h == 0 else 1.0, np.float32),
    }
    m.update(c)
    return m


def rope_tables_np(T):
    half = 32
    inv = 1.0 / (10000.0 ** (np.arange(half, dtype=np.float64) / half))
    pos = np.arange(T, dtype=np.float64)
    ang = pos[:, None] * inv[None, :]
    sin = np.tile(np.sin(ang), (1, 16)).astype(np.float32)
    cos = np.tile(np.cos(ang), (1, 16)).astype(np.float32)
    return np.ascontiguousarray(sin), np.ascontiguousarray(cos)


NCH_FULL = 32


def _kernel_unfused(**inputs):
    ins = {k: np.asarray(v, dtype=np.float32) for k, v in inputs.items()}
    NCH = NCH_FULL
    T = NCH * 128
    OWN = T // 2
    cores = list(range(8))
    c = host_constants()
    identf32 = np.eye(128, dtype=np.float32)
    iota8 = np.ascontiguousarray(np.broadcast_to(np.arange(8, dtype=np.float32), (128, 8)))
    x = ins["L0_attn_x"]

    ncA = build_phaseA(NCH)
    in_maps = []
    wA = {
        "L0_attn_WqT": np.ascontiguousarray(ins["L0_attn_Wq"].T),
        "L0_attn_WkT": np.ascontiguousarray(ins["L0_attn_Wk"].T),
        "L0_attn_WvT": np.ascontiguousarray(ins["L0_attn_Wv"].T),
        "L0_attn_WpT": np.ascontiguousarray(ins["L0_attn_Wp"].T),
        "L0_ffn_wq": np.ascontiguousarray(ins["L0_ffn_wq"]),
        "L0_ffn_wk": np.ascontiguousarray(ins["L0_ffn_wk"]),
        "L0_ffn_wv": np.ascontiguousarray(ins["L0_ffn_wv"]),
        "L0_attn_bp": np.ascontiguousarray(ins["L0_attn_bp"]).reshape(1, D),
        "L0_ffn_norm1_w": np.ascontiguousarray(ins["L0_ffn_norm1_w"]).reshape(1, D),
    }
    wA.update(c)
    for ci in cores:
        b, h = ci // 2, ci % 2
        m = dict(wA)
        m["XIN"] = np.ascontiguousarray(x[b])
        m["maskA"] = np.full((128, 1), 1.0 if h == 0 else 0.0, np.float32)
        m["maskB"] = np.full((128, 1), 0.0 if h == 0 else 1.0, np.float32)
        in_maps.append(m)
    resA = run_bass_kernel_spmd(ncA, in_maps, core_ids=cores)
    outA = resA.results

    ncB = build_phaseB(NCH)
    wB = {
        "L0_ffn_wo": np.ascontiguousarray(ins["L0_ffn_wo"]),
        "L0_ffn_router_w": np.ascontiguousarray(ins["L0_ffn_router_w"]),
        "L0_ffn_norm2_w": np.ascontiguousarray(ins["L0_ffn_norm2_w"]).reshape(1, D),
        "identf32": identf32,
        "ones1x64": c["ones1x64"], "ones1x128": c["ones1x128"],
        "shiftE": c["shiftE"], "shiftO": c["shiftO"],
    }
    in_maps = []
    for ci in cores:
        m = dict(wB)
        for nm in ["KF0", "QF0", "VA0", "XN0", "QKM0"]:
            m[nm] = outA[ci][nm]
        in_maps.append(m)
    resB = run_bass_kernel_spmd(ncB, in_maps, core_ids=cores)
    outB = resB.results

    ncC = build_phaseC(NCH)
    wC = {
        "L0_ffn_sh_gate": np.ascontiguousarray(ins["L0_ffn_sh_gate"]),
        "L0_ffn_sh_up": np.ascontiguousarray(ins["L0_ffn_sh_up"]),
        "L0_ffn_sh_down": np.ascontiguousarray(ins["L0_ffn_sh_down"]),
        "L0_ffn_e_gate": np.ascontiguousarray(ins["L0_ffn_e_gate"]),
        "L0_ffn_e_up": np.ascontiguousarray(ins["L0_ffn_e_up"]),
        "L0_ffn_e_down": np.ascontiguousarray(ins["L0_ffn_e_down"]),
        "identf32": identf32, "ones1x128": c["ones1x128"], "iota8": iota8,
    }
    in_maps = []
    for ci in cores:
        b, h = ci // 2, ci % 2
        m = dict(wC)
        m["H2F"] = outB[ci]["H2F"]
        m["LOG0"] = outB[ci]["LOG0"]
        m["ACCIN"] = outB[ci]["ACC"]
        m["AT0"] = outA[ci]["AT0"]
        m["XOWN"] = np.ascontiguousarray(x[b, h * OWN:(h + 1) * OWN])
        in_maps.append(m)
    resC = run_bass_kernel_spmd(ncC, in_maps, core_ids=cores)
    outC = resC.results

    ncD = build_phaseD(NCH)
    Wqkv = ins["L1_attn_Wqkv"]
    sint, cost = rope_tables_np(T)
    wD = {
        "L1_WQ": np.ascontiguousarray(Wqkv[:, 0:D]),
        "L1_WK": np.ascontiguousarray(Wqkv[:, D:2 * D]),
        "L1_WV": np.ascontiguousarray(Wqkv[:, 2 * D:3 * D]),
        "L1_ffn_w_gate": np.ascontiguousarray(ins["L1_ffn_w_gate"]),
        "SINT": sint, "COST": cost,
        "identbf": c["identbf"], "identf32": identf32,
    }
    in_maps = []
    for ci in cores:
        b, h = ci // 2, ci % 2
        m = dict(wD)
        m["X1F"] = np.concatenate([np.asarray(outC[2 * b]["X1"], np.float32),
                                   np.asarray(outC[2 * b + 1]["X1"], np.float32)], axis=0)
        m["maskA"] = np.full((128, 1), 1.0 if h == 0 else 0.0, np.float32)
        m["maskB"] = np.full((128, 1), 0.0 if h == 0 else 1.0, np.float32)
        in_maps.append(m)
    resD = run_bass_kernel_spmd(ncD, in_maps, core_ids=cores)
    outD = resD.results

    ncE = build_phaseE(NCH)
    wE = {
        "L1_attn_Wo": np.ascontiguousarray(ins["L1_attn_Wo"]),
        "L1_attn_bo": np.ascontiguousarray(ins["L1_attn_bo"]).reshape(1, D),
        "ones1x64": c["ones1x64"], "ones1x128": c["ones1x128"],
        "shiftE": c["shiftE"], "shiftO": c["shiftO"],
        "neg18": np.full((128, 1), -0.125, np.float32),
    }
    in_maps = []
    for ci in cores:
        m = dict(wE)
        for nm in ["KF1", "QF1", "VA1"]:
            m[nm] = outD[ci][nm]
        in_maps.append(m)
    resE = run_bass_kernel_spmd(ncE, in_maps, core_ids=cores)
    outE = resE.results

    ncF = build_phaseF(NCH)
    c_fc = ins["L1_ffn_c_fc"]
    wF = {
        "L1_e_gate": np.ascontiguousarray(c_fc[:, :, 0:1536]),
        "L1_e_up": np.ascontiguousarray(c_fc[:, :, 1536:3072]),
        "L1_e_down": np.ascontiguousarray(ins["L1_ffn_c_proj"]),
        "identf32": identf32, "ones1x128": c["ones1x128"], "iota8": iota8,
    }
    LOGALL = np.concatenate([np.asarray(outD[ci]["LOG1"], np.float32) for ci in cores], axis=0)
    in_maps = []
    for ci in cores:
        m = dict(wF)
        m["LOGALL"] = LOGALL
        m["XN1F"] = outD[ci]["XN1F"]
        m["ACC1IN"] = np.asarray(outE[ci]["ACC1"], np.float32)
        m["X1OWN"] = np.asarray(outC[ci]["X1"], np.float32)
        msel = np.zeros((128, 8), np.float32)
        msel[:, ci] = 1.0
        m["msel"] = msel
        in_maps.append(m)
    resF = run_bass_kernel_spmd(ncF, in_maps, core_ids=cores)
    outF = resF.results

    out = np.zeros((4, 2 * OWN, D), np.float32)
    for ci in cores:
        b, h = ci // 2, ci % 2
        out[b, h * OWN:(h + 1) * OWN] = np.asarray(outF[ci]["OUT"], np.float32)
    return out




def build_fused(NCH, NCORES=8):
    T = NCH * 128
    OWN = T // 2
    nc = bass.Bass("TRN2", target_bir_lowering=False, debug=False)
    X = {}

    def ein(name, shape, dt=F32):
        X[name] = nc.dram_tensor(name, shape, dt, kind="ExternalInput").ap()
        return X[name]

    def internal(name, shape, dt=F32):
        X[name] = nc.dram_tensor(name, shape, dt).ap()
        return X[name]

    ein("XIN", [T, D])
    ein("XOWN", [OWN, D])
    for nm in ["L0_attn_WqT", "L0_attn_WkT", "L0_attn_WvT", "L0_attn_WpT",
               "L0_ffn_wq", "L0_ffn_wk", "L0_ffn_wv", "L0_ffn_wo",
               "L1_WQ", "L1_WK", "L1_WV", "L1_attn_Wo"]:
        ein(nm, [D, D])
    ein("L0_attn_bp", [1, D])
    ein("L0_ffn_norm1_w", [1, D])
    ein("L0_ffn_norm2_w", [1, D])
    ein("L0_ffn_router_w", [D, 8])
    ein("L1_ffn_w_gate", [D, 8])
    ein("L1_attn_bo", [1, D])
    ein("L0_ffn_sh_gate", [D, 2048])
    ein("L0_ffn_sh_up", [D, 2048])
    ein("L0_ffn_sh_down", [2048, D])
    ein("L0_ffn_e_gate", [8, D, 2048])
    ein("L0_ffn_e_up", [8, D, 2048])
    ein("L0_ffn_e_down", [8, 2048, D])
    ein("L1_e_gate", [8, D, 1536])
    ein("L1_e_up", [8, D, 1536])
    ein("L1_e_down", [8, 1536, D])
    ein("SINT", [T, 512])
    ein("COST", [T, 512])
    ein("identbf", [128, 128], BF16)
    ein("identf32", [128, 128])
    ein("trimask", [128, 128])
    ein("shiftE", [64, 128], BF16)
    ein("shiftO", [64, 128], BF16)
    ein("ones1x64", [1, 64])
    ein("ones1x128", [1, 128])
    ein("maskA", [128, 1])
    ein("maskB", [128, 1])
    ein("iota8", [128, 8])
    ein("msel", [128, 8])
    ein("neg18", [128, 1])
    XN0h = nc.dram_tensor("XN0", [OWN, D], F32)
    AT0h = nc.dram_tensor("AT0", [OWN, D], F32)
    VA0h = nc.dram_tensor("VA0", [T, H * 65], BF16)
    KF0h = nc.dram_tensor("KF0", [H, 64, T], BF16)
    QF0h = nc.dram_tensor("QF0", [H, 64, OWN], BF16)
    QKM0h = nc.dram_tensor("QKM0", [32, 1], F32)
    ACCh = nc.dram_tensor("ACCW", [OWN, D], F32)
    H2Th = nc.dram_tensor("H2T", [OWN, D], BF16)
    H2Fh = nc.dram_tensor("H2F", [D, OWN], BF16)
    LOG0h = nc.dram_tensor("LOG0", [OWN, 8], F32)
    CTDCh = nc.dram_tensor("CTDC", [OWN // 128, 8, 128], F32)
    X1h = nc.dram_tensor("X1", [OWN, D], F32)
    X1Fh = nc.dram_tensor("X1F", [T, D], F32)
    KF1h = nc.dram_tensor("KF1", [H, 64, T], BF16)
    QF1h = nc.dram_tensor("QF1", [H, 64, OWN], BF16)
    VA1h = nc.dram_tensor("VA1", [T, H * 65], BF16)
    XN1Fh = nc.dram_tensor("XN1F", [D, OWN], BF16)
    LOG1h = nc.dram_tensor("LOG1", [OWN, 8], F32)
    LOGALLh = nc.dram_tensor("LOGALL", [OWN * NCORES, 8], F32)
    ACC1h = nc.dram_tensor("ACC1W", [OWN, D], F32)
    CTDFh = nc.dram_tensor("CTDF", [OWN // 128, 8, 128], F32)
    OUTh = nc.dram_tensor("OUT", [OWN, D], F32, kind="ExternalOutput")

    mk = MK(nc)
    with contextlib.ExitStack() as S_outer:
        with contextlib.ExitStack() as S:
            P = dict(X)
            P["XN0"] = XN0h.ap()
            P["AT0"] = AT0h.ap()
            P["VA0"] = VA0h.ap()
            P["KF0"] = KF0h.ap()
            P["QF0"] = QF0h.ap()
            P["QKM0"] = QKM0h.ap()
            for nm in ["XN0_t", "AT0_t", "VA0_t", "KF0_t", "QF0_t", "QKM0_t"]:
                P[nm] = object()
            alloc_phaseA(nc, S, P, prefix="pa_")
            emit_phaseA(mk, P, NCH)
        mk.phase_barrier("barA")
        with contextlib.ExitStack() as S:
            P = dict(X)
            P["KF0"] = KF0h.ap()
            P["QF0"] = QF0h.ap()
            P["VA0"] = VA0h.ap()
            P["XN0"] = XN0h.ap()
            P["QKM0"] = QKM0h.ap()
            P["ACC"] = ACCh.ap()
            P["H2T"] = H2Th.ap()
            P["H2F"] = H2Fh.ap()
            P["LOG0"] = LOG0h.ap()
            for nm in ["ACC_t", "H2T_t", "H2F_t", "LOG0_t"]:
                P[nm] = object()
            alloc_phaseB(nc, S, P, NCH, prefix="pb_")
            emit_phaseB(mk, P, NCH)
        mk.phase_barrier("barB")
        with contextlib.ExitStack() as S:
            P = dict(X)
            P["H2F"] = H2Fh.ap()
            P["LOG0"] = LOG0h.ap()
            P["ACC"] = ACCh.ap()
            P["AT0"] = AT0h.ap()
            P["CTD"] = CTDCh.ap()
            P["X1"] = X1h.ap()
            for nm in ["CTD_t", "ACC_t", "X1_t"]:
                P[nm] = object()
            alloc_phaseC(nc, S, P, NCH, prefix="pc_")
            emit_phaseC(mk, P, NCH)
        mk.phase_barrier("barC")
        mk.op("gpsimd", lambda e: e.collective_compute(
            "AllGather", ALU.bypass,
            replica_groups=[[2 * i, 2 * i + 1] for i in range(NCORES // 2)],
            ins=[X1h.ap().opt()], outs=[X1Fh.ap().opt()]),
            tag="ccX1")
        mk.phase_barrier("barCC1")
        with contextlib.ExitStack() as S:
            P = dict(X)
            P["X1F"] = X1Fh.ap()
            P["KF1"] = KF1h.ap()
            P["QF1"] = QF1h.ap()
            P["VA1"] = VA1h.ap()
            P["XN1F"] = XN1Fh.ap()
            P["LOG1"] = LOG1h.ap()
            for nm in ["XN1F_t", "LOG1_t", "QF1_t", "KF1_t", "VA1_t"]:
                P[nm] = object()
            alloc_phaseD(nc, S, P, NCH, prefix="pd_")
            emit_phaseD(mk, P, NCH)
        mk.phase_barrier("barD")
        mk.op("gpsimd", lambda e: e.collective_compute(
            "AllGather", ALU.bypass,
            replica_groups=[list(range(NCORES))],
            ins=[LOG1h.ap().opt()], outs=[LOGALLh.ap().opt()]),
            tag="ccLG")
        mk.phase_barrier("barCC2")
        with contextlib.ExitStack() as S:
            P = dict(X)
            P["KF1"] = KF1h.ap()
            P["QF1"] = QF1h.ap()
            P["VA1"] = VA1h.ap()
            P["ACC1"] = ACC1h.ap()
            P["ACC1_t"] = object()
            alloc_phaseE(nc, S, P, NCH, prefix="pe_")
            emit_phaseE(mk, P, NCH)
        mk.phase_barrier("barE")
        with contextlib.ExitStack() as S:
            P = dict(X)
            P["LOGALL"] = LOGALLh.ap()
            P["XN1F"] = XN1Fh.ap()
            P["ACC1"] = ACC1h.ap()
            P["X1OWN"] = X1h.ap()
            P["CTD"] = CTDFh.ap()
            P["OUT"] = OUTh.ap()
            for nm in ["CTD_t", "ACC1_t", "OUT_t"]:
                P[nm] = object()
            alloc_phaseF(nc, S, P, NCH, NCORES, prefix="pf_")
            emit_phaseF(mk, P, NCH, NCORES)
            emit_final_barrier(mk, [P["OUT_t"]])
        mk.build(S_outer)
    return nc


def stage_fused_inmaps(ins, NCH, NCORES=8):
    T = NCH * 128
    OWN = T // 2
    c = host_constants()
    iota8 = np.ascontiguousarray(np.broadcast_to(np.arange(8, dtype=np.float32), (128, 8)))
    sint, cost = rope_tables_np(T)
    Wqkv = ins["L1_attn_Wqkv"]
    c_fc = ins["L1_ffn_c_fc"]
    x = ins["L0_attn_x"]
    shared = {
        "L0_attn_WqT": np.ascontiguousarray(ins["L0_attn_Wq"].T),
        "L0_attn_WkT": np.ascontiguousarray(ins["L0_attn_Wk"].T),
        "L0_attn_WvT": np.ascontiguousarray(ins["L0_attn_Wv"].T),
        "L0_attn_WpT": np.ascontiguousarray(ins["L0_attn_Wp"].T),
        "L0_ffn_wq": np.ascontiguousarray(ins["L0_ffn_wq"]),
        "L0_ffn_wk": np.ascontiguousarray(ins["L0_ffn_wk"]),
        "L0_ffn_wv": np.ascontiguousarray(ins["L0_ffn_wv"]),
        "L0_ffn_wo": np.ascontiguousarray(ins["L0_ffn_wo"]),
        "L1_WQ": np.ascontiguousarray(Wqkv[:, 0:D]),
        "L1_WK": np.ascontiguousarray(Wqkv[:, D:2 * D]),
        "L1_WV": np.ascontiguousarray(Wqkv[:, 2 * D:3 * D]),
        "L1_attn_Wo": np.ascontiguousarray(ins["L1_attn_Wo"]),
        "L0_attn_bp": np.ascontiguousarray(ins["L0_attn_bp"]).reshape(1, D),
        "L0_ffn_norm1_w": np.ascontiguousarray(ins["L0_ffn_norm1_w"]).reshape(1, D),
        "L0_ffn_norm2_w": np.ascontiguousarray(ins["L0_ffn_norm2_w"]).reshape(1, D),
        "L0_ffn_router_w": np.ascontiguousarray(ins["L0_ffn_router_w"]),
        "L1_ffn_w_gate": np.ascontiguousarray(ins["L1_ffn_w_gate"]),
        "L1_attn_bo": np.ascontiguousarray(ins["L1_attn_bo"]).reshape(1, D),
        "L0_ffn_sh_gate": np.ascontiguousarray(ins["L0_ffn_sh_gate"]),
        "L0_ffn_sh_up": np.ascontiguousarray(ins["L0_ffn_sh_up"]),
        "L0_ffn_sh_down": np.ascontiguousarray(ins["L0_ffn_sh_down"]),
        "L0_ffn_e_gate": np.ascontiguousarray(ins["L0_ffn_e_gate"]),
        "L0_ffn_e_up": np.ascontiguousarray(ins["L0_ffn_e_up"]),
        "L0_ffn_e_down": np.ascontiguousarray(ins["L0_ffn_e_down"]),
        "L1_e_gate": np.ascontiguousarray(c_fc[:, :, 0:1536]),
        "L1_e_up": np.ascontiguousarray(c_fc[:, :, 1536:3072]),
        "L1_e_down": np.ascontiguousarray(ins["L1_ffn_c_proj"]),
        "SINT": sint, "COST": cost,
        "identbf": c["identbf"], "identf32": c["identf32"],
        "trimask": c["trimask"], "shiftE": c["shiftE"], "shiftO": c["shiftO"],
        "ones1x64": c["ones1x64"], "ones1x128": c["ones1x128"],
        "iota8": iota8,
        "neg18": np.full((128, 1), -0.125, np.float32),
    }
    in_maps = []
    for ci in range(NCORES):
        b, h = ci // 2, ci % 2
        m = dict(shared)
        m["XIN"] = np.ascontiguousarray(x[b])
        m["XOWN"] = np.ascontiguousarray(x[b, h * OWN:(h + 1) * OWN])
        m["maskA"] = np.full((128, 1), 1.0 if h == 0 else 0.0, np.float32)
        m["maskB"] = np.full((128, 1), 0.0 if h == 0 else 1.0, np.float32)
        msel = np.zeros((128, 8), np.float32)
        msel[:, ci] = 1.0
        m["msel"] = msel
        in_maps.append(m)
    return in_maps


def kernel(**inputs):
    ins = {k: np.asarray(v, dtype=np.float32) for k, v in inputs.items()}
    NCH = NCH_FULL
    T = NCH * 128
    OWN = T // 2
    nc = build_fused(NCH)
    in_maps = stage_fused_inmaps(ins, NCH)
    res = run_bass_kernel_spmd(nc, in_maps, core_ids=list(range(8)))
    out = np.zeros((4, T, D), np.float32)
    for ci in range(8):
        b, h = ci // 2, ci % 2
        out[b, h * OWN:(h + 1) * OWN] = np.asarray(res.results[ci]["OUT"], np.float32)
    return out


# version 05
# speedup vs baseline: 1.1724x; 1.1045x over previous
import contextlib
import numpy as np
import concourse.bass as bass
import concourse.mybir as mybir
from concourse.bass_utils import run_bass_kernel_spmd

F32 = mybir.dt.float32
BF16 = mybir.dt.bfloat16
I32 = mybir.dt.int32
AF = mybir.ActivationFunctionType
ALU = mybir.AluOpType

ENGINES = ("sync", "vector", "scalar", "tensor", "gpsimd")

D = 1024
H = 16
DH = 64



class _Op:
    __slots__ = ("eng", "fn", "after", "dma_sem", "needs_inc", "inc_val", "pos", "tag", "dma_floor")

    def __init__(self, eng, fn, after, dma_sem, tag):
        self.eng = eng
        self.fn = fn
        self.after = after
        self.dma_sem = dma_sem
        self.needs_inc = False
        self.inc_val = None
        self.pos = None
        self.tag = tag
        self.dma_floor = None


class Tok:
    __slots__ = ("op",)

    def __init__(self, op):
        self.op = op


def _key(b):
    if isinstance(b, tuple):
        t, k = b
        return (id(t), k)
    return (id(b), "")


class MK:
    def __init__(self, nc):
        self.nc = nc
        self.eng_ops = {e: [] for e in ENGINES}
        self.dma_sem_names = []
        self.state = {}
        self.dma_counts = {}
        self.last_dma = {}

    def _deps(self, reads, writes):
        deps = []
        for b in reads:
            st = self.state.get(_key(b))
            if st and st["w"] is not None:
                deps.append(st["w"])
        for b in writes:
            st = self.state.get(_key(b))
            if st:
                if st["w"] is not None:
                    deps.append(st["w"])
                deps.extend(st["r"])
        return deps

    def _commit(self, tok, reads, writes):
        for b in reads:
            st = self.state.setdefault(_key(b), {"w": None, "r": []})
            st["r"].append(tok)
        for b in writes:
            st = self.state.setdefault(_key(b), {"w": None, "r": []})
            st["w"] = tok
            st["r"] = []

    def op(self, eng, fn, reads=(), writes=(), also=(), tag="", track=True):
        after = self._deps(reads, writes) + [t for t in also if t is not None]
        o = _Op(eng, fn, after, None, tag)
        o.pos = len(self.eng_ops[eng])
        if any(t.op.dma_sem is not None for t in after):
            o.dma_floor = dict(self.dma_counts)
        self.eng_ops[eng].append(o)
        tok = Tok(o)
        if track:
            self._commit(tok, reads, writes)
        return tok

    def raw_op(self, eng, fn, also=(), tag=""):
        return self.op(eng, fn, also=[t for t in also if t is not None], tag=tag, track=False)

    def dma(self, eng, sem, out, in_, reads=(), writes=(), also=(), accum=False, tag="", **kw):
        if sem not in self.dma_sem_names:
            self.dma_sem_names.append(sem)
        if accum:
            kw = dict(kw)
            kw["accum_op"] = ALU.add
            assert eng == "gpsimd", "accum dma must use gpsimd (software DGE)"
        reads = list(reads)
        writes = list(writes)
        after = self._deps(reads, writes) + [t for t in also if t is not None]

        def fn(e, out=out, in_=in_, kw=kw):
            return e.dma_start(out=out, in_=in_, **kw)

        o = _Op(eng, fn, after, sem, tag)
        o.pos = len(self.eng_ops[eng])
        if any(t.op.dma_sem is not None for t in after):
            o.dma_floor = dict(self.dma_counts)
        self.dma_counts[sem] = self.dma_counts.get(sem, 0) + 16
        o.inc_val = self.dma_counts[sem]
        self.eng_ops[eng].append(o)
        tok = Tok(o)
        self.last_dma[sem] = tok
        self._commit(tok, reads, writes)
        return tok

    def mm_group(self, mms, reads=(), writes=(), also=(), tag=""):
        toks = []
        for i, fn in enumerate(mms):
            if i == 0:
                t = self.op("tensor", fn, reads=reads, writes=writes, also=also, tag=tag + "_0")
            else:
                t = self.raw_op("tensor", fn, tag=f"{tag}_{i}")
            toks.append(t)
        for b in writes:
            st = self.state.setdefault(_key(b), {"w": None, "r": []})
            st["w"] = toks[-1]
            st["r"] = []
        if len(toks) > 1:
            for b in reads:
                st = self.state.setdefault(_key(b), {"w": None, "r": []})
                st["r"].append(toks[-1])
        return toks[-1]

    def writer_of(self, b):
        st = self.state.get(_key(b))
        return st["w"] if st else None

    def phase_barrier(self, tag="bar"):
        toks = []
        for eng in ENGINES:
            if self.eng_ops[eng]:
                o = self.eng_ops[eng][-1]
                if o.dma_sem is None:
                    o.needs_inc = True
                toks.append(Tok(o))
        toks.extend(self.last_dma.values())
        for eng in ENGINES:
            if not self.eng_ops[eng]:
                continue
            self.op(eng, lambda e: e.nop(), also=toks, tag=f"{tag}_{eng}", track=False)
        self.state = {}

    def build(self, stack):
        nc = self.nc
        for eng in ENGINES:
            for o in self.eng_ops[eng]:
                for t in o.after:
                    if t.op.dma_sem is None and t.op.eng != o.eng:
                        t.op.needs_inc = True
        eng_counts = {e: 0 for e in ENGINES}
        for eng in ENGINES:
            for o in self.eng_ops[eng]:
                if o.dma_sem is None and o.needs_inc:
                    eng_counts[eng] += 1
                    o.inc_val = eng_counts[eng]
        eng_sems = {e: stack.enter_context(nc.semaphore(f"S_{e}")) for e in ENGINES if self.eng_ops[e]}
        dma_sems = {s: stack.enter_context(nc.semaphore(f"D_{s}")) for s in self.dma_sem_names}
        block = stack.enter_context(nc.Block())
        mkself = self

        def make_body(eng):
            def body(e):
                highwater = {}
                drained_upto = 0
                cur_pos = 0
                for o in mkself.eng_ops[eng]:
                    need_drain = False
                    for t in o.after:
                        top = t.op
                        if top.dma_sem is not None:
                            key = ("d", top.dma_sem)
                            val = top.inc_val
                            if o.dma_floor is not None:
                                val = max(val, o.dma_floor.get(top.dma_sem, val))
                            if highwater.get(key, 0) < val:
                                e.wait_ge(dma_sems[top.dma_sem], val)
                                highwater[key] = val
                        elif top.eng != eng:
                            key = ("e", top.eng)
                            val = top.inc_val
                            if highwater.get(key, 0) < val:
                                e.wait_ge(eng_sems[top.eng], val)
                                highwater[key] = val
                        else:
                            if top.pos >= drained_upto:
                                need_drain = True
                    if need_drain and eng != "tensor":
                        e.drain()
                        drained_upto = cur_pos
                    inst = o.fn(e)
                    if o.dma_sem is not None:
                        inst.then_inc(dma_sems[o.dma_sem], 16)
                    elif o.needs_inc:
                        inst.then_inc(eng_sems[eng], 1)
                    cur_pos += 1

            return body

        for eng in ENGINES:
            if not self.eng_ops[eng]:
                continue
            getattr(block, eng)(make_body(eng))


class Cyc:
    def __init__(self, items):
        self.items = list(items)

    def __getitem__(self, i):
        return self.items[i % len(self.items)]




def emit_load_weight_bf16(mk, P, src_dram, dst_sb, rows, cols, tag):
    nch = rows // 128
    for c in range(nch):
        stg = P["wstage"][c % 2]
        mk.dma("sync", f"wld{c%2}", stg[:, 0:cols], src_dram[c * 128:(c + 1) * 128, :],
               writes=[stg], tag=f"wld_{tag}_{c}")
        mk.op("vector",
              lambda e, dst_sb=dst_sb, c=c, stg=stg, cols=cols: e.tensor_copy(
                  dst_sb[:, c * cols:(c + 1) * cols], stg[:, 0:cols]),
              reads=[stg], writes=[(dst_sb, f"c{c}")], tag=f"wcv_{tag}_{c}")


def emit_bcast_row(mk, P, psum, src_handle, src_cols, dst_sb, tag, nparts=128):
    nch = (src_cols + 511) // 512
    t = None
    for i in range(nch):
        lo, hi = i * 512, min(src_cols, i * 512 + 512)
        mk.op("tensor",
              lambda e, lo=lo, hi=hi: e.matmul(
                  psum[0:nparts, 0:hi - lo], P["s_ones128"][0:1, 0:nparts], src_handle[0:1, lo:hi],
                  start=True, stop=True),
              reads=[P["s_ones128"], src_handle], writes=[psum], tag=f"{tag}_mm{i}")
        t = mk.op("vector",
                  lambda e, lo=lo, hi=hi: e.tensor_copy(dst_sb[0:nparts, lo:hi], psum[0:nparts, 0:hi - lo]),
                  reads=[psum], writes=[(dst_sb, f"bc{i}")], tag=f"{tag}_ev{i}")
    st = mk.state.setdefault(_key(dst_sb), {"w": None, "r": []})
    st["w"] = t
    st["r"] = []
    return t




def mk_s16_read(mk, P, hh):
    return P["S16"][hh]


def emit_phaseA(mk, P, NCH):
    T = NCH * 128
    PP0, PP1 = P["PP0"], P["PP1"]
    PA, PO, PC, PY, PBT = P["PA"], P["PO"], P["PC"], P["PY"], P["PBT"]
    ident = P["s_ident"]
    statL = P["statL"]
    scrA, scrB, scrH = P["scrA"], P["scrB"], P["scrH"]

    emit_load_weight_bf16(mk, P, P["L0_attn_WqT"], P["WqT_sb"], D, D, "WqT")
    emit_load_weight_bf16(mk, P, P["L0_attn_WkT"], P["WkT_sb"], D, D, "WkT")
    emit_load_weight_bf16(mk, P, P["L0_attn_WvT"], P["WvT_sb"], D, D, "WvT")
    emit_load_weight_bf16(mk, P, P["L0_attn_WpT"], P["WpT_sb"], D, D, "WpT")
    emit_load_weight_bf16(mk, P, P["L0_ffn_wq"], P["wq_sb"], D, D, "wq")
    emit_load_weight_bf16(mk, P, P["L0_ffn_wk"], P["wk_sb"], D, D, "wk")
    emit_load_weight_bf16(mk, P, P["L0_ffn_wv"], P["wv_sb"], D, D, "wv")

    for nm, dst in [("identbf", "s_ident"), ("trimask", "s_tri"), ("shiftE", "s_shE"),
                    ("shiftO", "s_shO"), ("ones1x128", "s_ones128"),
                    ("maskA", "s_maskA"), ("maskB", "s_maskB"), ("identf32", "s_identf")]:
        mk.dma("sync", "cst", P[dst][:, :], P[nm], writes=[P[dst]], tag=f"c_{nm}")
    mk.dma("sync", "crow", P["scrH"][0:1, 0:D], P["L0_attn_bp"], writes=[P["scrH"]], tag="c_bp")
    emit_bcast_row(mk, P, PP0, P["scrH"], D, P["scrA"], "bcbp")
    mk.op("vector", lambda e: e.tensor_scalar(P["s_bpA"][:, :], P["scrA"][:, :], P["s_maskA"][:, 0:1], None, ALU.mult),
          reads=[P["scrA"], P["s_maskA"]], writes=[P["s_bpA"]], tag="bpA")
    mk.op("vector", lambda e: e.tensor_scalar(P["s_bpB"][:, :], P["scrA"][:, :], P["s_maskB"][:, 0:1], None, ALU.mult),
          reads=[P["scrA"], P["s_maskB"]], writes=[P["s_bpB"]], tag="bpB")
    mk.dma("sync", "crow", P["scrH"][0:1, 0:D], P["L0_ffn_norm1_w"], writes=[P["scrH"]], tag="c_n1")
    emit_bcast_row(mk, P, PP1, P["scrH"], D, P["s_n1b"], "bcn1")

    for hh in range(H):
        mk.op("vector", lambda e, hh=hh: e.memset(P["S32"][hh][:, :], 0.0), writes=[P["S32"][hh]], tag=f"S0_{hh}")
        mk.op("vector", lambda e, hh=hh: e.memset(P["S16"][hh][:, :], 0.0), writes=[P["S16"][hh]], tag=f"Sb0_{hh}")
    mk.op("vector", lambda e: e.memset(P["qkss"][:, :], 0.0), writes=[P["qkss"]], tag="qkss0")

    c_sx = Cyc([P["sx0"], P["sx1"]])
    c_xn = Cyc([P["xn0"]])
    c_xnsel = Cyc([P["xnsel0"]])
    c_attmp = Cyc([P["attmp0"]])
    c_vaug = Cyc([P["vaug0"]])
    c_vaugf = Cyc([P["vaugf0"]])
    c_yfm = Cyc([P["yfm0"], P["yfm1"]])
    c_qp = Cyc([P["qp0"]])
    c_kp = Cyc([P["kp0"]])
    c_hb = Cyc([P["hb0"]])
    c_hfm = Cyc([P["hfm0"]])
    c_xnf = Cyc([P["xnf0"]])
    c_xnb = Cyc([P["xnb0"]])
    c_qpf = Cyc([P["qpfa0"]])
    c_kpf = Cyc([P["kpfa0"]])
    c_kfa = Cyc([P["kfa0"]])
    c_qfa = Cyc([P["qfa0"]])

    def proj_tokmajor(g, src_fm, W_sb, dst_bf, featmap, tag):
        for half in range(2):
            pp = PP0 if half == 0 else PP1
            mms = [
                (lambda e, c=c, half=half, pp=pp, src_fm=src_fm, W_sb=W_sb: e.matmul(
                    pp[:, 0:512], src_fm[:, c * 128:(c + 1) * 128],
                    W_sb[:, c * 1024 + half * 512: c * 1024 + half * 512 + 512],
                    start=(c == 0), stop=(c == 7)))
                for c in range(8)
            ]
            mk.mm_group(mms, reads=[src_fm, W_sb], writes=[pp], tag=f"{tag}mm{half}_g{g}")
            sl = slice(half * 512, half * 512 + 512)
            if not featmap:
                mk.op("vector", lambda e, pp=pp, sl=sl, dst_bf=dst_bf: e.tensor_copy(dst_bf[:, sl], pp[:, 0:512]),
                      reads=[pp], writes=[(dst_bf, f"h{half}")], tag=f"{tag}ev{half}_g{g}")
            else:
                mk.op("vector", lambda e, pp=pp, sl=sl: e.tensor_scalar(scrA[:, sl], pp[:, 0:512], 0.0, None, ALU.min),
                      reads=[pp], writes=[(scrA, half)], tag=f"{tag}f1{half}_g{g}")
                mk.op("scalar", lambda e, sl=sl: e.activation(scrB[:, 0:512], scrA[:, sl], AF.Exp),
                      reads=[(scrA, half)], writes=[scrB], tag=f"{tag}f2{half}_g{g}")
                mk.op("vector", lambda e, pp=pp, sl=sl: e.tensor_scalar(scrA[:, sl], pp[:, 0:512], 0.0, None, ALU.max),
                      reads=[pp], writes=[(scrA, half)], tag=f"{tag}f3{half}_g{g}")
                mk.op("vector", lambda e, sl=sl, dst_bf=dst_bf: e.tensor_tensor(dst_bf[:, sl], scrA[:, sl], scrB[:, 0:512], ALU.add),
                      reads=[(scrA, half), scrB], writes=[(dst_bf, f"h{half}")],
                      tag=f"{tag}f4{half}_g{g}")

    def transpose_128(g, src_bf, dst_fm, tag):
        last = None
        for c in range(8):
            last = mk.op("tensor",
                         lambda e, c=c, src_bf=src_bf: e.transpose(PBT[:, c * 128:(c + 1) * 128], src_bf[:, c * 128:(c + 1) * 128], ident[:, :]),
                         reads=[src_bf, ident], writes=[PBT] if c == 0 else [], tag=f"{tag}tr{c}_g{g}", track=(c == 0))
        st = mk.state[_key(PBT)]
        st["w"] = last
        st["r"] = []
        mk.op("vector", lambda e, dst_fm=dst_fm: e.tensor_copy(dst_fm[:, :], PBT[:, 0:1024]),
              reads=[PBT], writes=[dst_fm], tag=f"{tag}trev_g{g}")

    def head_transposes(g, src_tm, src_keys, dst_all, tag):
        for half in range(2):
            last = None
            first = True
            for hl in range(8):
                hh = half * 8 + hl
                rd = [(src_tm, src_keys[half]), ident] if first else []
                last = mk.op("tensor",
                             lambda e, hh=hh, hl=hl, src_tm=src_tm: e.transpose(
                                 PBT[0:64, hl * 128:(hl + 1) * 128], src_tm[:, hh * 64:(hh + 1) * 64], ident[:, :]),
                             reads=rd + ([PBT] if first else []), writes=[PBT] if first else [],
                             tag=f"{tag}ht{hh}_g{g}", track=first)
                first = False
            st = mk.state[_key(PBT)]
            st["w"] = last
            st["r"] = []
            mk.op("vector",
                  lambda e, half=half, dst_all=dst_all: e.tensor_copy(
                      dst_all[:, half * 1024:(half + 1) * 1024], PBT[0:64, 0:1024]),
                  reads=[PBT], writes=[(dst_all, half)], tag=f"{tag}hte{half}_g{g}")
        st = mk.state.setdefault(_key(dst_all), {"w": None, "r": []})
        st["w"] = mk.state[_key((dst_all, 1))]["w"]
        st["r"] = []

    for g in range(NCH):
        own_idx = g % (NCH // 2)
        first_half = g < NCH // 2
        mask_ap = P["s_maskA"] if first_half else P["s_maskB"]
        bp_masked = P["s_bpA"] if first_half else P["s_bpB"]

        sx = c_sx[g]
        mk.dma("sync", f"ldx{g%2}", sx[:, :], P["XIN"][g * 128:(g + 1) * 128, :], writes=[sx], tag=f"x_g{g}")

        mk.op("vector", lambda e, sx=sx: e.reduce_sum(statL[:, 0:1], sx[:, :], axis=mybir.AxisListType.X),
              reads=[sx], writes=[(statL, 0)], tag=f"s1_g{g}")
        mk.op("scalar", lambda e, sx=sx: e.square(scrH[:, 0:1024], sx[:, :]),
              reads=[sx], writes=[scrH], tag=f"sq_g{g}")
        mk.op("vector", lambda e: e.reduce_sum(statL[:, 1:2], scrH[:, 0:1024], axis=mybir.AxisListType.X),
              reads=[scrH], writes=[(statL, 1)], tag=f"s2_g{g}")
        mk.op("vector", lambda e: e.tensor_scalar(statL[:, 2:3], statL[:, 0:1], -1.0 / D, None, ALU.mult),
              reads=[(statL, 0)], writes=[(statL, 2)], tag=f"mn_g{g}")
        mk.op("vector", lambda e: e.tensor_tensor(statL[:, 3:4], statL[:, 2:3], statL[:, 2:3], ALU.mult),
              reads=[(statL, 2)], writes=[(statL, 3)], tag=f"m2_g{g}")
        mk.op("vector", lambda e: e.tensor_scalar(statL[:, 4:5], statL[:, 1:2], 1.0 / D, statL[:, 3:4], ALU.mult, ALU.subtract),
              reads=[(statL, 1), (statL, 3)], writes=[(statL, 4)], tag=f"va_g{g}")
        mk.op("vector", lambda e: e.tensor_scalar(statL[:, 4:5], statL[:, 4:5], 1e-5, None, ALU.add),
              reads=[(statL, 4)], writes=[(statL, 4)], tag=f"vae_g{g}")
        mk.op("scalar", lambda e: e.activation(statL[:, 5:6], statL[:, 4:5], AF.Ln),
              reads=[(statL, 4)], writes=[(statL, 5)], tag=f"ln_g{g}")
        mk.op("scalar", lambda e: e.activation(statL[:, 6:7], statL[:, 5:6], AF.Exp, scale=-0.5),
              reads=[(statL, 5)], writes=[(statL, 6)], tag=f"rs_g{g}")
        mk.op("vector", lambda e: e.tensor_tensor(statL[:, 7:8], statL[:, 2:3], statL[:, 6:7], ALU.mult),
              reads=[(statL, 2), (statL, 6)], writes=[(statL, 7)], tag=f"bb_g{g}")
        xn = c_xn[g]
        mk.op("vector", lambda e, sx=sx, xn=xn: e.tensor_scalar(xn[:, :], sx[:, :], statL[:, 6:7], statL[:, 7:8], ALU.mult, ALU.add),
              reads=[sx, (statL, 6), (statL, 7)], writes=[xn], tag=f"xn_g{g}")

        xnb = c_xnb[g]
        mk.op("vector", lambda e, xn=xn, xnb=xnb: e.tensor_copy(xnb[:, :], xn[:, :]),
              reads=[xn], writes=[xnb], tag=f"xnb_g{g}")
        xnf = c_xnf[g]
        transpose_128(g, xnb, xnf, "xn")

        xnsel = c_xnsel[g]
        mk.op("vector", lambda e, xn=xn, xnsel=xnsel, mask_ap=mask_ap: e.tensor_scalar(
            xnsel[:, :], xn[:, :], mask_ap[:, 0:1], None, ALU.mult),
              reads=[xn, mask_ap], writes=[xnsel], tag=f"xs_g{g}")
        mk.dma("gpsimd", f"xno{g%2}", P["XN0"][own_idx * 128:(own_idx + 1) * 128, :], xnsel[:, :],
               reads=[xnsel], writes=[(P["XN0_t"], own_idx)], accum=not first_half, tag=f"xsd_g{g}")

        qp = c_qp[g]
        proj_tokmajor(g, xnf, P["WqT_sb"], qp, True, "q_")
        kp = c_kp[g]
        proj_tokmajor(g, xnf, P["WkT_sb"], kp, True, "k_")
        qpfa = c_qpf[g]
        head_transposes(g, qp, ["h0", "h1"], qpfa, "qp")
        kpfa = c_kpf[g]
        head_transposes(g, kp, ["h0", "h1"], kpfa, "kp")

        vaug = c_vaug[g]
        mk.op("vector", lambda e, vaug=vaug: e.memset(vaug[:, :], 1.0),
              writes=[vaug, (vaug, "v0"), (vaug, "v1")], tag=f"vm_g{g}")
        for half in range(2):
            pp = PP0 if half == 0 else PP1
            mms = [
                (lambda e, c=c, pp=pp, half=half, xnf=xnf: e.matmul(
                    pp[:, 0:512], xnf[:, c * 128:(c + 1) * 128],
                    P["WvT_sb"][:, c * 1024 + half * 512: c * 1024 + half * 512 + 512],
                    start=(c == 0), stop=(c == 7)))
                for c in range(8)
            ]
            mk.mm_group(mms, reads=[xnf, P["WvT_sb"]], writes=[pp], tag=f"vmm{half}_g{g}")
            mk.op("vector",
                  lambda e, pp=pp, half=half, vaug=vaug: e.tensor_copy(
                      vaug[:, :].rearrange("p (h s) -> p h s", h=H)[:, 8 * half:8 * half + 8, 0:64],
                      pp[:, 0:512].rearrange("p (h s) -> p h s", h=8)),
                  reads=[pp, vaug], writes=[(vaug, f"v{half}")], tag=f"vev{half}_g{g}")

        yfm = c_yfm[g]
        mk.op("vector", lambda e, yfm=yfm: e.memset(yfm[:, :], 0.0),
              writes=[yfm] + [(yfm, f"p{pi2}") for pi2 in range(8)], tag=f"yfmz_g{g}")
        BANKS = [PA, PO, PC, PY]
        for w in range(4):
            bank = BANKS[w]
            for s in range(4):
                hh = 4 * w + s
                mk.op("tensor", lambda e, hh=hh, s=s, bank=bank: e.matmul(
                    bank[0:128, s * 128:(s + 1) * 128],
                    kpfa[:, hh * 128:(hh + 1) * 128], qpfa[:, hh * 128:(hh + 1) * 128],
                    start=True, stop=True),
                      reads=[(kpfa, hh // 8), (qpfa, hh // 8)], writes=[bank] if s == 0 else [],
                      track=(s == 0), tag=f"at_h{hh}_g{g}")
            st = mk.state[_key(bank)]
            st["w"] = Tok(mk.eng_ops["tensor"][-1])
            st["r"] = []
            mk.op("vector", lambda e, w=w, bank=bank: e.tensor_tensor(
                P["Pm_w"][:, 0:512].rearrange("p (s c) -> p s c", s=4),
                bank[0:128, 0:512].rearrange("p (s c) -> p s c", s=4),
                P["s_tri"][:, :].rearrange("p c -> p () c").broadcast(1, 4) if False else P["s_tri"][:, :],
                ALU.mult),
                  reads=[bank, P["s_tri"]], writes=[P["Pm_w"]], tag=f"pm_w{w}_g{g}") if False else None
            for s in range(4):
                hh = 4 * w + s
                mk.op("vector", lambda e, s=s, bank=bank: e.tensor_tensor(
                    P["Pm_w"][:, s * 128:(s + 1) * 128], bank[0:128, s * 128:(s + 1) * 128], P["s_tri"][:, :], ALU.mult),
                      reads=[bank, P["s_tri"]], writes=[(P["Pm_w"], s)], tag=f"pm_h{hh}_g{g}")
            first = True
            for s in range(4):
                hh = 4 * w + s
                mms = [
                    (lambda e, hh=hh, s=s, bank=bank, vaug=vaug: e.matmul(
                        bank[0:65, s * 128:(s + 1) * 128], vaug[:, hh * 65:(hh + 1) * 65],
                        P["Pm_w"][:, s * 128:(s + 1) * 128], start=True, stop=False)),
                    (lambda e, hh=hh, s=s, bank=bank: e.matmul(
                        bank[0:65, s * 128:(s + 1) * 128], P["S16"][hh][:, :], qpfa[:, hh * 128:(hh + 1) * 128],
                        start=False, stop=True)),
                ]
                mk.mm_group(mms, reads=[(vaug, f"v{hh//8}"), (P["Pm_w"], s), P["S16"][hh], (qpfa, hh // 8)],
                            writes=[bank] if first else [], tag=f"o65_h{hh}_g{g}")
                if not first:
                    st = mk.state[_key(bank)]
                    st["w"] = Tok(mk.eng_ops["tensor"][-1])
                first = False
            st = mk.state[_key(bank)]
            st["w"] = Tok(mk.eng_ops["tensor"][-1])
            st["r"] = []
            mk.op("vector", lambda e, bank=bank: e.tensor_scalar(
                P["den_b"][0:1, 0:512], bank[64:65, 0:512], 1e-6, None, ALU.add),
                  reads=[bank], writes=[P["den_b"]], tag=f"dn_w{w}_g{g}")
            mk.op("vector", lambda e: e.reciprocal(P["rec_b"][0:1, 0:512], P["den_b"][0:1, 0:512]),
                  reads=[P["den_b"]], writes=[P["rec_b"]], tag=f"rc_w{w}_g{g}")
            for s in range(4):
                mk.op("tensor", lambda e, s=s: e.matmul(
                    PP0[0:64, s * 128:(s + 1) * 128], P["s_ones64"][0:1, 0:64], P["rec_b"][0:1, s * 128:(s + 1) * 128],
                    start=True, stop=True),
                      reads=[P["rec_b"], P["s_ones64"]], writes=[PP0] if s == 0 else [], track=(s == 0),
                      tag=f"bc_w{w}s{s}_g{g}")
            st = mk.state[_key(PP0)]
            st["w"] = Tok(mk.eng_ops["tensor"][-1])
            st["r"] = []
            mk.op("vector", lambda e: e.tensor_copy(P["rb_w"][0:64, 0:512], PP0[0:64, 0:512]),
                  reads=[PP0], writes=[P["rb_w"]], tag=f"rbe_w{w}_g{g}")
            for s in range(4):
                hh = 4 * w + s
                mk.op("vector", lambda e, s=s, bank=bank: e.tensor_tensor(
                    P["y_w"][:, s * 128:(s + 1) * 128], bank[0:64, s * 128:(s + 1) * 128],
                    P["rb_w"][0:64, s * 128:(s + 1) * 128], ALU.mult),
                      reads=[bank, P["rb_w"]], writes=[(P["y_w"], s)], tag=f"y_h{hh}_g{g}")
            for s in range(4):
                hh = 4 * w + s
                mk.op("tensor", lambda e, hh=hh, s=s, kp=kp, vaug=vaug: e.matmul(
                    PP1[0:64, s * 128:s * 128 + 65], kp[:, hh * 64:(hh + 1) * 64], vaug[:, hh * 65:(hh + 1) * 65],
                    start=True, stop=True),
                      reads=[(kp, f"h{hh//8}"), (vaug, f"v{hh//8}")] + [mk_s16_read(mk, P, hh)],
                      writes=[PP1] if s == 0 else [], track=(s == 0), tag=f"su_h{hh}_g{g}")
            st = mk.state[_key(PP1)]
            st["w"] = Tok(mk.eng_ops["tensor"][-1])
            st["r"] = []
            mk.op("vector", lambda e, w=w: e.tensor_tensor(
                P["S32a_v"][:, :].rearrange("p (h c) -> p h c", c=65)[:, 4 * w:4 * w + 4, :],
                P["S32a_v"][:, :].rearrange("p (h c) -> p h c", c=65)[:, 4 * w:4 * w + 4, :],
                PP1[0:64, 0:512].rearrange("p (s c) -> p s c", s=4)[:, :, 0:65],
                ALU.add),
                  reads=[PP1] + [P["S32"][4 * w + s2] for s2 in range(4)],
                  writes=[P["S32"][4 * w + s2] for s2 in range(4)], tag=f"sa_w{w}_g{g}")
            mk.op("scalar", lambda e, w=w: e.copy(
                P["S16a_v"][:, :].rearrange("p (h c) -> p h c", c=65)[:, 4 * w:4 * w + 4, :],
                P["S32a_v"][:, :].rearrange("p (h c) -> p h c", c=65)[:, 4 * w:4 * w + 4, :]),
                  reads=[P["S32"][4 * w + s2] for s2 in range(4)],
                  writes=[P["S16"][4 * w + s2] for s2 in range(4)], tag=f"sc_w{w}_g{g}")
            for pp_i in range(2):
                pi = 2 * w + pp_i
                mms = [
                    (lambda e, pp_i=pp_i: e.matmul(PP1[0:128, pp_i * 128:(pp_i + 1) * 128], P["s_shE"][:, :],
                                                   P["y_w"][:, (2 * pp_i) * 128:(2 * pp_i + 1) * 128], start=True, stop=False)),
                    (lambda e, pp_i=pp_i: e.matmul(PP1[0:128, pp_i * 128:(pp_i + 1) * 128], P["s_shO"][:, :],
                                                   P["y_w"][:, (2 * pp_i + 1) * 128:(2 * pp_i + 2) * 128], start=False, stop=True)),
                ]
                mk.mm_group(mms, reads=[(P["y_w"], 2 * pp_i), (P["y_w"], 2 * pp_i + 1), P["s_shE"], P["s_shO"]],
                            writes=[PP1] if pp_i == 0 else [], tag=f"sh_p{pi}_g{g}")
                if pp_i != 0:
                    st = mk.state[_key(PP1)]
                    st["w"] = Tok(mk.eng_ops["tensor"][-1])
            mk.op("vector", lambda e, w=w, yfm=yfm: e.tensor_copy(
                yfm[:, (2 * w) * 128:(2 * w + 2) * 128], PP1[0:128, 0:256]),
                  reads=[PP1], writes=[(yfm, f"p{2*w}"), (yfm, f"p{2*w+1}")], tag=f"she_w{w}_g{g}")

        attmp = c_attmp[g]
        for half in range(2):
            pp = PP0 if half == 0 else PP1
            mms = [
                (lambda e, c=c, pp=pp, half=half, yfm=yfm: e.matmul(
                    pp[:, 0:512], yfm[:, c * 128:(c + 1) * 128],
                    P["WpT_sb"][:, c * 1024 + half * 512: c * 1024 + half * 512 + 512],
                    start=(c == 0), stop=(c == 7)))
                for c in range(8)
            ]
            mk.mm_group(mms, reads=[(yfm, f"p{c2}") for c2 in range(8)] + [P["WpT_sb"]], writes=[pp], tag=f"opmm{half}_g{g}")
            sl = slice(half * 512, half * 512 + 512)
            mk.op("vector", lambda e, pp=pp, sl=sl, attmp=attmp, bp_masked=bp_masked, mask_ap=mask_ap: e.scalar_tensor_tensor(
                attmp[:, sl], pp[:, 0:512], mask_ap[:, 0:1], bp_masked[:, sl], ALU.mult, ALU.add),
                  reads=[pp, mask_ap, bp_masked], writes=[(attmp, half)], tag=f"ope{half}_g{g}")
        mk.dma("gpsimd", f"at0{g%2}", P["AT0"][own_idx * 128:(own_idx + 1) * 128, :], attmp[:, :],
               reads=[(attmp, 0), (attmp, 1)], writes=[(P["AT0_t"], own_idx)], accum=not first_half, tag=f"atd_g{g}")

        mk.op("scalar", lambda e, xn=xn: e.square(scrH[:, 0:1024], xn[:, :]),
              reads=[xn], writes=[scrH], tag=f"hsq_g{g}")
        mk.op("vector", lambda e: e.reduce_sum(statL[:, 0:1], scrH[:, 0:1024], axis=mybir.AxisListType.X),
              reads=[scrH], writes=[(statL, 0)], tag=f"hss_g{g}")
        mk.op("vector", lambda e: e.tensor_scalar(statL[:, 0:1], statL[:, 0:1], 1.0 / D, 1e-6, ALU.mult, ALU.add),
              reads=[(statL, 0)], writes=[(statL, 0)], tag=f"hle_g{g}")
        mk.op("scalar", lambda e: e.activation(statL[:, 1:2], statL[:, 0:1], AF.Ln),
              reads=[(statL, 0)], writes=[(statL, 1)], tag=f"hln_g{g}")
        mk.op("scalar", lambda e: e.activation(statL[:, 2:3], statL[:, 1:2], AF.Exp, scale=-0.5),
              reads=[(statL, 1)], writes=[(statL, 2)], tag=f"hrs_g{g}")
        mk.op("vector", lambda e, xn=xn: e.tensor_scalar(scrH[:, 0:1024], xn[:, :], statL[:, 2:3], None, ALU.mult),
              reads=[xn, (statL, 2)], writes=[scrH], tag=f"h1_g{g}")
        hb = c_hb[g]
        mk.op("vector", lambda e, hb=hb: e.tensor_tensor(hb[:, :], scrH[:, 0:1024], P["s_n1b"][:, :], ALU.mult),
              reads=[scrH, P["s_n1b"]], writes=[hb], tag=f"hb_g{g}")
        hfm = c_hfm[g]
        transpose_128(g, hb, hfm, "h")

        kfa = c_kfa[g]
        proj_tokmajor(g, hfm, P["wk_sb"], P["qp0"], False, "fk_")
        head_transposes(g, P["qp0"], ["h0", "h1"], kfa, "fk")
        mk.dma("sync", f"kfd{g%2}", P["KF0"].rearrange("h p s -> p h s")[:, :, g * 128:(g + 1) * 128], kfa[:, :].rearrange("p (h s) -> p h s", h=H),
               reads=[(kfa, 0), (kfa, 1)], writes=[(P["KF0_t"], g)], tag=f"fkd_g{g}")
        mk.op("scalar", lambda e: e.square(P["scrA"][:, 0:1024], P["qp0"][:, :]),
              reads=[(P["qp0"], "h0"), (P["qp0"], "h1")], writes=[P["scrA"]], tag=f"ksq_g{g}")
        mk.op("vector", lambda e: e.tensor_reduce(statL[:, 8:9].rearrange("p a -> p a") if False else P["hss16"][:, 0:16], P["scrA"][:, 0:1024].rearrange("p (h d) -> p h d", d=64), axis=mybir.AxisListType.X, op=ALU.add),
              reads=[P["scrA"]], writes=[P["hss16"]], tag=f"kss_g{g}")
        mk.op("vector", lambda e: e.tensor_tensor(P["qkss"][:, 16:32], P["qkss"][:, 16:32], P["hss16"][:, 0:16], ALU.max),
              reads=[P["hss16"], P["qkss"]], writes=[P["qkss"]], tag=f"kssm_g{g}")
        qfa = c_qfa[g]
        proj_tokmajor(g, hfm, P["wq_sb"], P["qp0"], False, "fq_")
        head_transposes(g, P["qp0"], ["h0", "h1"], qfa, "fq")
        mk.op("vector", lambda e, qfa=qfa, mask_ap=mask_ap: e.tensor_scalar(
            qfa[:, :], qfa[:, :], mask_ap[0:64, 0:1], None, ALU.mult),
              reads=[(qfa, 0), (qfa, 1), mask_ap], writes=[(qfa, 0), (qfa, 1)], tag=f"fqm_g{g}")
        mk.dma("gpsimd", f"qfd{g%2}", P["QF0"].rearrange("h p s -> p h s")[:, :, own_idx * 128:(own_idx + 1) * 128], qfa[:, :].rearrange("p (h s) -> p h s", h=H),
               reads=[(qfa, 0), (qfa, 1)], writes=[(P["QF0_t"], own_idx)], accum=not first_half, tag=f"fqd_g{g}")
        mk.op("scalar", lambda e: e.square(P["scrA"][:, 0:1024], P["qp0"][:, :]),
              reads=[(P["qp0"], "h0"), (P["qp0"], "h1")], writes=[P["scrA"]], tag=f"qsq_g{g}")
        mk.op("vector", lambda e: e.tensor_reduce(P["hss16"][:, 0:16], P["scrA"][:, 0:1024].rearrange("p (h d) -> p h d", d=64), axis=mybir.AxisListType.X, op=ALU.add),
              reads=[P["scrA"]], writes=[P["hss16"]], tag=f"qss_g{g}")
        mk.op("vector", lambda e: e.tensor_tensor(P["qkss"][:, 0:16], P["qkss"][:, 0:16], P["hss16"][:, 0:16], ALU.max),
              reads=[P["hss16"], P["qkss"]], writes=[P["qkss"]], tag=f"qssm_g{g}")

        vaugf = c_vaugf[g]
        mk.op("vector", lambda e, vaugf=vaugf: e.memset(vaugf[:, :], 1.0),
              writes=[vaugf, (vaugf, "v0"), (vaugf, "v1")], tag=f"vfm_g{g}")
        for half in range(2):
            pp = PP0 if half == 0 else PP1
            mms = [
                (lambda e, c=c, pp=pp, half=half, hfm=hfm: e.matmul(
                    pp[:, 0:512], hfm[:, c * 128:(c + 1) * 128],
                    P["wv_sb"][:, c * 1024 + half * 512: c * 1024 + half * 512 + 512],
                    start=(c == 0), stop=(c == 7)))
                for c in range(8)
            ]
            mk.mm_group(mms, reads=[hfm, P["wv_sb"]], writes=[pp], tag=f"fvmm{half}_g{g}")
            mk.op("vector",
                  lambda e, pp=pp, half=half, vaugf=vaugf: e.tensor_copy(
                      vaugf[:, :].rearrange("p (h s) -> p h s", h=H)[:, 8 * half:8 * half + 8, 0:64],
                      pp[:, 0:512].rearrange("p (h s) -> p h s", h=8)),
                  reads=[pp, vaugf], writes=[(vaugf, f"v{half}")], tag=f"fve{half}_g{g}")
        mk.dma("sync", f"va0{g%2}", P["VA0"][g * 128:(g + 1) * 128, :], vaugf[:, :],
               reads=[(vaugf, "v0"), (vaugf, "v1")], writes=[(P["VA0_t"], g)], tag=f"vad_g{g}")

    mk.op("tensor", lambda e: e.transpose(P["PA"][0:32, 0:128], P["qkss"][:, 0:32], P["s_identf"][:, :]),
          reads=[P["qkss"], P["s_identf"], P["PA"]], writes=[P["PA"]], tag="qkstr")
    mk.op("vector", lambda e: e.tensor_copy(P["hss16"][0:32, 16:17].rearrange("p a -> p a") if False else P["qkT"][0:32, 0:128], P["PA"][0:32, 0:128]),
          reads=[P["PA"]], writes=[P["qkT"]], tag="qkse")
    mk.op("vector", lambda e: e.reduce_max(P["qkT"][0:32, 128:129], P["qkT"][0:32, 0:128], axis=mybir.AxisListType.X),
          reads=[P["qkT"]], writes=[(P["qkT"], "m")], tag="qksm")
    mk.dma("sync", "qkm", P["QKM0"][:, :], P["qkT"][0:32, 128:129], reads=[(P["qkT"], "m")], writes=[(P["QKM0_t"], 0)], tag="qkm")


def emit_phaseB(mk, P, NCH):
    T = NCH * 128
    OWN = T // 2
    NT = OWN // 128
    NQQ = max(1, OWN // 512)
    QW = OWN // NQQ
    PL, PT = P["PL"], P["PT"]
    identf = P["s_identf"]

    emit_load_weight_bf16(mk, P, P["L0_ffn_wo"], P["wo_sb"], D, D, "wo")
    for nm, dst in [("identf32", "s_identf"), ("ones1x64", "s_ones64"), ("ones1x128", "s_ones128"),
                    ("shiftE", "s_shE"), ("shiftO", "s_shO")]:
        mk.dma("sync", "cst", P[dst][:, :], P[nm], writes=[P[dst]], tag=f"c_{nm}")
    for c in range(8):
        mk.dma("sync", "cst", P["router_sb"][:, c * 8:(c + 1) * 8], P["L0_ffn_router_w"][c * 128:(c + 1) * 128, :],
               writes=[(P["router_sb"], c)], tag=f"c_rw{c}")
    mk.dma("sync", "crow", P["qkrow"][0:1, 0:32], P["QKM0"].rearrange("a b -> b a")[0:1, :],
           writes=[P["qkrow"]], tag="c_qkm")
    mk.dma("sync", "crow", P["scrq"][0:1, 0:D], P["L0_ffn_norm2_w"], writes=[P["scrq"]], tag="c_n2")
    emit_bcast_row(mk, P, P["PXB"], P["scrq"], D, P["s_n2b"], "bcn2")

    mk.op("vector", lambda e: e.reduce_max(P["stat1"][0:1, 0:1], P["qkrow"][0:1, 0:16], axis=mybir.AxisListType.X),
          reads=[P["qkrow"]], writes=[(P["stat1"], 0)], tag="qmx")
    mk.op("vector", lambda e: e.reduce_max(P["stat1"][0:1, 1:2], P["qkrow"][0:1, 16:32], axis=mybir.AxisListType.X),
          reads=[P["qkrow"]], writes=[(P["stat1"], 1)], tag="kmx")
    mk.op("vector", lambda e: e.tensor_tensor(P["stat1"][0:1, 2:3], P["stat1"][0:1, 0:1], P["stat1"][0:1, 1:2], ALU.mult),
          reads=[(P["stat1"], 0), (P["stat1"], 1)], writes=[(P["stat1"], 2)], tag="qk")
    mk.op("vector", lambda e: e.tensor_scalar(P["stat1"][0:1, 2:3], P["stat1"][0:1, 2:3], 1.0, 1e-12, ALU.mult, ALU.add),
          reads=[(P["stat1"], 2)], writes=[(P["stat1"], 2)], tag="qke")
    mk.op("scalar", lambda e: e.activation(P["stat1"][0:1, 3:4], P["stat1"][0:1, 2:3], AF.Ln),
          reads=[(P["stat1"], 2)], writes=[(P["stat1"], 3)], tag="qkl")
    mk.op("scalar", lambda e: e.activation(P["stat1"][0:1, 2:3], P["stat1"][0:1, 3:4], AF.Exp, scale=0.5),
          reads=[(P["stat1"], 3)], writes=[(P["stat1"], 2)], tag="qkx")
    mk.op("vector", lambda e: e.tensor_scalar(P["stat1"][0:1, 3:4], P["stat1"][0:1, 2:3], -0.125, -480.0, ALU.mult, ALU.max),
          reads=[(P["stat1"], 2)], writes=[(P["stat1"], 3)], tag="negc")
    mk.op("tensor", lambda e: e.matmul(PL[0:128, 0:1], P["s_ones128"][0:1, 0:128], P["stat1"][0:1, 3:4], start=True, stop=True),
          reads=[P["s_ones128"], (P["stat1"], 3)], writes=[PL], tag="negc_bc")
    mk.op("vector", lambda e: e.tensor_copy(P["negCb"][:, :], PL[0:128, 0:1]),
          reads=[PL], writes=[P["negCb"]], tag="negc_ev")

    c_kh = Cyc([P["kh0"], P["kh1"]])
    c_qh = Cyc([P["qh0"], P["qh1"]])
    c_va = Cyc([P[f"va{i}"] for i in range(4)])
    c_pt = Cyc([P["pt0"], P["pt1"]])
    yfm = P["yfm_all"]

    for hh in range(H):
        kh = c_kh[hh]
        mk.dma("sync", f"khld{hh%2}", kh[:, :], P["KF0"][hh, :, :], writes=[kh], tag=f"khld_{hh}")
        qh = c_qh[hh]
        mk.dma("scalar", f"qhld{hh%2}", qh[:, :], P["QF0"][hh, :, :], writes=[qh], tag=f"qhld_{hh}")
        def emit_cx(jj, va_, pt_, hh=hh):
            for s4 in range(NQQ):
                if jj == 0 and s4 == 0:
                    mk.op("tensor", lambda e, va_=va_, pt_=pt_, jj=jj, s4=s4: e.matmul(
                        P["PXB"][0:65, s4 * QW:(s4 + 1) * QW], va_[:, :], pt_[:, s4 * QW:(s4 + 1) * QW],
                        start=True, stop=(jj == NCH - 1)),
                          reads=[va_, pt_], writes=[P["PXB"]], tag=f"cx_h{hh}_j{jj}_{s4}")
                else:
                    tok = mk.raw_op("tensor", lambda e, va_=va_, pt_=pt_, jj=jj, s4=s4: e.matmul(
                        P["PXB"][0:65, s4 * QW:(s4 + 1) * QW], va_[:, :], pt_[:, s4 * QW:(s4 + 1) * QW],
                        start=(jj == 0), stop=(jj == NCH - 1)),
                        also=[mk.writer_of(pt_), mk.writer_of(va_)], tag=f"cx_h{hh}_j{jj}_{s4}")
                    st2 = mk.state[_key(P["PXB"])]
                    st2["w"] = tok
                    mk.state[_key(pt_)]["r"].append(tok)
                    mk.state[_key(va_)]["r"].append(tok)

        for j in range(NCH):
            va = c_va[hh * NCH + j]
            mk.dma("gpsimd", f"vald{(hh*NCH+j)%4}", va[:, :], P["VA0"][j * 128:(j + 1) * 128, hh * 65:(hh + 1) * 65],
                   writes=[va], tag=f"vald_{hh}_{j}")
            first = True
            last = None
            for s4 in range(NQQ):
                last = mk.op("tensor", lambda e, kh=kh, qh=qh, j=j, s4=s4: e.matmul(
                    P["PSCB"][:, s4 * QW:(s4 + 1) * QW], kh[:, j * 128:(j + 1) * 128], qh[:, s4 * QW:(s4 + 1) * QW],
                    start=True, stop=True),
                      reads=([kh, qh] if first else []), writes=([P["PSCB"]] if first else []),
                      track=first, tag=f"sc_h{hh}_j{j}_{s4}")
                first = False
            st = mk.state[_key(P["PSCB"])]
            st["w"] = last
            st["r"] = []
            pt = c_pt[j]
            mk.op("scalar", lambda e, pt=pt: e.activation(
                pt[:, 0:OWN], P["PSCB"][:, 0:OWN], AF.Exp, bias=P["negCb"][:, 0:1], scale=0.125),
                  reads=[P["PSCB"], P["negCb"]], writes=[pt], tag=f"ex_h{hh}_j{j}")
            if j > 0:
                emit_cx(j - 1, c_va[hh * NCH + j - 1], c_pt[j - 1])
        emit_cx(NCH - 1, c_va[hh * NCH + NCH - 1], c_pt[NCH - 1])
        yh_buf = P["yhE"] if hh % 2 == 0 else P["yhO"]
        for qq in range(NQQ):
            sl = slice(qq * QW, (qq + 1) * QW)
            mk.op("vector", lambda e, sl=sl: e.reciprocal(P["rec1"][0:1, 0:QW], P["PXB"][64:65, sl]),
                  reads=[P["PXB"]], writes=[P["rec1"]], tag=f"rc_h{hh}_q{qq}")
            mk.op("tensor", lambda e, sl=sl: e.matmul(P["PSCB"][0:64, sl], P["s_ones64"][0:1, 0:64], P["rec1"][0:1, 0:QW], start=True, stop=True),
                  reads=[P["rec1"], P["s_ones64"]], writes=[P["PSCB"]], tag=f"bc_h{hh}_q{qq}")
            mk.op("vector", lambda e, sl=sl: e.tensor_copy(P["rb1"][0:64, 0:QW], P["PSCB"][0:64, sl]),
                  reads=[P["PSCB"]], writes=[P["rb1"]], tag=f"rbe_h{hh}_q{qq}")
            mk.op("vector", lambda e, sl=sl, qq=qq, yh_buf=yh_buf: e.tensor_tensor(
                yh_buf[:, qq * QW:(qq + 1) * QW], P["PXB"][0:64, sl], P["rb1"][0:64, 0:QW], ALU.mult),
                  reads=[P["PXB"], P["rb1"]], writes=[(yh_buf, qq)], tag=f"y_h{hh}_q{qq}")
        if hh % 2 == 1:
            pi = hh // 2
            for seg in range(NQQ):
                sl = slice(seg * QW, (seg + 1) * QW)
                mms = [
                    (lambda e, sl=sl: e.matmul(P["PSCB"][:, sl], P["s_shE"][:, :], P["yhE"][:, sl], start=True, stop=False)),
                    (lambda e, sl=sl: e.matmul(P["PSCB"][:, sl], P["s_shO"][:, :], P["yhO"][:, sl], start=False, stop=True)),
                ]
                mk.mm_group(mms, reads=[(P["yhE"], seg), (P["yhO"], seg), P["s_shE"], P["s_shO"]],
                            writes=[P["PSCB"]], tag=f"sh_p{pi}_s{seg}")
                mk.op("vector", lambda e, pi=pi, seg=seg, sl=sl: e.tensor_copy(
                    yfm[:, pi * OWN + seg * QW: pi * OWN + (seg + 1) * QW], P["PSCB"][:, sl]),
                      reads=[P["PSCB"]], writes=[(yfm, f"{pi}_{seg}")], tag=f"she_p{pi}_s{seg}")

    c_xt = Cyc([P["xt0"], P["xt1"]])
    for tt in range(NT):
        for half in range(2):
            sl = slice(half * 512, half * 512 + 512)
            mms = [
                (lambda e, pr=pr, half=half, tt=tt, sl=sl: e.matmul(
                    P["PXB"][:, sl], yfm[:, pr * OWN + tt * 128: pr * OWN + tt * 128 + 128],
                    P["wo_sb"][:, pr * 1024 + half * 512: pr * 1024 + half * 512 + 512],
                    start=(pr == 0), stop=(pr == 7)))
                for pr in range(8)
            ]
            mk.mm_group(mms, reads=[(yfm, f"{pr2}_{(tt*128)//QW}") for pr2 in range(8)] + [P["wo_sb"]],
                        writes=[P["PXB"]], tag=f"wo{half}_t{tt}")
        xt = c_xt[tt]
        mk.dma("sync", f"xnld{tt%2}", xt[:, :], P["XN0"][tt * 128:(tt + 1) * 128, :], writes=[xt], tag=f"xnld_t{tt}")
        for half in range(2):
            sl = slice(half * 512, half * 512 + 512)
            mk.op("vector", lambda e, xt=xt, half=half, sl=sl: e.tensor_tensor(
                P["acc_t"][:, half * 512:(half + 1) * 512], xt[:, half * 512:(half + 1) * 512], P["PXB"][:, sl], ALU.add),
                  reads=[xt, P["PXB"]], writes=[(P["acc_t"], half)], tag=f"acc{half}_t{tt}")
        mk.dma("gpsimd", f"accd{tt%2}", P["ACC"][tt * 128:(tt + 1) * 128, :], P["acc_t"][:, :],
               reads=[(P["acc_t"], 0), (P["acc_t"], 1)], writes=[(P["ACC_t"], tt)], tag=f"accd_t{tt}")
        mk.op("scalar", lambda e: e.square(P["scrq"][:, 0:1024], P["acc_t"][:, :]),
              reads=[(P["acc_t"], 0), (P["acc_t"], 1)], writes=[P["scrq"]], tag=f"h2sq_t{tt}")
        mk.op("vector", lambda e: e.reduce_sum(P["stat1"][0:128, 4:5], P["scrq"][:, 0:1024], axis=mybir.AxisListType.X),
              reads=[P["scrq"]], writes=[(P["stat1"], 4)], tag=f"h2ss_t{tt}")
        mk.op("vector", lambda e: e.tensor_scalar(P["stat1"][0:128, 4:5], P["stat1"][0:128, 4:5], 1.0 / D, 1e-6, ALU.mult, ALU.add),
              reads=[(P["stat1"], 4)], writes=[(P["stat1"], 4)], tag=f"h2e_t{tt}")
        mk.op("scalar", lambda e: e.activation(P["stat1"][0:128, 5:6], P["stat1"][0:128, 4:5], AF.Ln),
              reads=[(P["stat1"], 4)], writes=[(P["stat1"], 5)], tag=f"h2l_t{tt}")
        mk.op("scalar", lambda e: e.activation(P["stat1"][0:128, 6:7], P["stat1"][0:128, 5:6], AF.Exp, scale=-0.5),
              reads=[(P["stat1"], 5)], writes=[(P["stat1"], 6)], tag=f"h2r_t{tt}")
        mk.op("vector", lambda e: e.tensor_scalar(P["scrq"][:, 0:1024], P["acc_t"][:, :], P["stat1"][0:128, 6:7], None, ALU.mult),
              reads=[(P["acc_t"], 0), (P["acc_t"], 1), (P["stat1"], 6)], writes=[P["scrq"]], tag=f"h2a_t{tt}")
        mk.op("vector", lambda e: e.tensor_tensor(P["scrq"][:, 0:1024], P["scrq"][:, 0:1024], P["s_n2b"][:, :], ALU.mult),
              reads=[P["scrq"], P["s_n2b"]], writes=[P["scrq"]], tag=f"h2w_t{tt}")
        mk.op("vector", lambda e: e.tensor_copy(P["h2b_t"][:, :], P["scrq"][:, 0:1024]),
              reads=[P["scrq"]], writes=[P["h2b_t"]], tag=f"h2b_t{tt}")
        mk.dma("scalar", f"h2td{tt%2}", P["H2T"][tt * 128:(tt + 1) * 128, :], P["h2b_t"][:, :],
               reads=[P["h2b_t"]], writes=[(P["H2T_t"], tt)], tag=f"h2td_t{tt}")
        for rnd in range(2):
            last = None
            first = True
            for cc in range(4):
                c = rnd * 4 + cc
                last = mk.op("tensor", lambda e, c=c, cc=cc: e.transpose(
                    PT[:, cc * 128:(cc + 1) * 128], P["scrq"][:, c * 128:(c + 1) * 128], identf[:, :]),
                    reads=([P["scrq"], identf, PT] if first else []), writes=[PT] if first else [],
                    tag=f"h2tr{c}_t{tt}", track=first)
                first = False
            st = mk.state[_key(PT)]
            st["w"] = last
            st["r"] = []
            mk.op("vector", lambda e, rnd=rnd: e.tensor_copy(P["h2fm"][:, rnd * 512:(rnd + 1) * 512], PT[:, 0:512]),
                  reads=[PT], writes=[(P["h2fm"], rnd)], tag=f"h2te{rnd}_t{tt}")
        mk.op("vector", lambda e: e.tensor_copy(P["h2fmb"][:, :], P["h2fm"][:, 0:1024]),
              reads=[(P["h2fm"], 0), (P["h2fm"], 1)], writes=[P["h2fmb"]], tag=f"h2fb_t{tt}")
        mk.dma("sync", f"h2fd{tt%2}", P["H2F"].rearrange("(c p) t -> p c t", p=128)[:, :, tt * 128:(tt + 1) * 128],
               P["h2fmb"][:, :].rearrange("p (c t) -> p c t", c=8),
               reads=[P["h2fmb"]], writes=[(P["H2F_t"], tt)], tag=f"h2fd_t{tt}")
        mms = [
            (lambda e, c=c: e.matmul(PL[0:128, 0:8], P["h2fm"][:, c * 128:(c + 1) * 128],
                                     P["router_sb"][:, c * 8:(c + 1) * 8], start=(c == 0), stop=(c == 7)))
            for c in range(8)
        ]
        mk.mm_group(mms, reads=[(P["h2fm"], 0), (P["h2fm"], 1)] + [(P["router_sb"], c2) for c2 in range(8)],
                    writes=[PL], tag=f"lg_t{tt}")
        mk.op("vector", lambda e: e.tensor_copy(P["lg_t"][:, :], PL[0:128, 0:8]),
              reads=[PL], writes=[P["lg_t"]], tag=f"lge_t{tt}")
        mk.dma("gpsimd", f"lgd{tt%2}", P["LOG0"][tt * 128:(tt + 1) * 128, :], P["lg_t"][:, :],
               reads=[P["lg_t"]], writes=[(P["LOG0_t"], tt)], tag=f"lgd_t{tt}")


def alloc_phaseB(nc, S, P, NCH, prefix=""):
    def sb(name, shape, dt=F32):
        t = S.enter_context(nc.sbuf_tensor(prefix + name, shape, dt))
        P[name] = t
        return t

    def ps(name, shape, dt=F32):
        t = S.enter_context(nc.psum_tensor(prefix + name, shape, dt))
        P[name] = t
        return t

    sb("wo_sb", [128, 8 * 1024], BF16)
    P["wstage"] = [sb("xt0", [128, 1024], F32), sb("xt1", [128, 1024], F32)]
    sb("s_identf", [128, 128], F32)
    sb("s_ones64", [1, 64], F32)
    sb("s_ones128", [1, 128], F32)
    sb("s_shE", [64, 128], BF16)
    sb("s_shO", [64, 128], BF16)
    sb("router_sb", [128, 64], F32)
    sb("s_n2b", [128, 1024], F32)
    sb("qkrow", [1, 128], F32)
    sb("stat1", [128, 8], F32)
    sb("negCb", [128, 1], F32)
    T = NCH * 128
    OWN = T // 2
    sb("kh0", [64, T], BF16)
    sb("kh1", [64, T], BF16)
    sb("qh0", [64, OWN], BF16)
    sb("qh1", [64, OWN], BF16)
    for i in range(4):
        sb(f"va{i}", [128, 65], BF16)
    sb("pt0", [128, max(OWN, 512)], BF16)
    sb("pt1", [128, max(OWN, 512)], BF16)
    sb("den1", [1, 512], F32)
    sb("rec1", [1, 512], F32)
    sb("rb1", [64, 512], F32)
    sb("yhE", [64, OWN], BF16)
    sb("yhO", [64, OWN], BF16)
    sb("yfm_all", [128, 8 * OWN], BF16)
    sb("acc_t", [128, 1024], F32)
    sb("scrq", [128, 1024], F32)
    sb("h2b_t", [128, 1024], BF16)
    sb("h2fm", [128, 1024], F32)
    sb("h2fmb", [128, 1024], BF16)
    sb("lg_t", [128, 8], F32)
    ps("PSCB", [128, max(OWN, 1024)], F32)
    ps("PXB", [128, max(OWN, 1024)], F32)
    P["PL"] = P["PSCB"]
    P["PT"] = P["PXB"]
    for nm in ["ACC_t", "H2T_t", "H2F_t", "LOG0_t"]:
        P[nm] = object()
    return P


def build_phaseB(NCH):
    T = NCH * 128
    nc = bass.Bass("TRN2", target_bir_lowering=False, debug=False)
    P = {}
    P["KF0"] = nc.dram_tensor("KF0", [H, 64, T], BF16, kind="ExternalInput").ap()
    P["QF0"] = nc.dram_tensor("QF0", [H, 64, T // 2], BF16, kind="ExternalInput").ap()
    P["VA0"] = nc.dram_tensor("VA0", [T, H * 65], BF16, kind="ExternalInput").ap()
    P["XN0"] = nc.dram_tensor("XN0", [T // 2, D], F32, kind="ExternalInput").ap()
    P["QKM0"] = nc.dram_tensor("QKM0", [32, 1], F32, kind="ExternalInput").ap()
    P["L0_ffn_wo"] = nc.dram_tensor("L0_ffn_wo", [D, D], F32, kind="ExternalInput").ap()
    P["L0_ffn_router_w"] = nc.dram_tensor("L0_ffn_router_w", [D, 8], F32, kind="ExternalInput").ap()
    P["L0_ffn_norm2_w"] = nc.dram_tensor("L0_ffn_norm2_w", [1, D], F32, kind="ExternalInput").ap()
    for nm, shape, dt in [("identf32", [128, 128], F32), ("ones1x64", [1, 64], F32),
                          ("ones1x128", [1, 128], F32), ("shiftE", [64, 128], BF16), ("shiftO", [64, 128], BF16)]:
        P[nm] = nc.dram_tensor(nm, shape, dt, kind="ExternalInput").ap()
    P["ACC"] = nc.dram_tensor("ACC", [T // 2, D], F32, kind="ExternalOutput").ap()
    P["H2T"] = nc.dram_tensor("H2T", [T // 2, D], BF16, kind="ExternalOutput").ap()
    P["H2F"] = nc.dram_tensor("H2F", [D, T // 2], BF16, kind="ExternalOutput").ap()
    P["LOG0"] = nc.dram_tensor("LOG0", [T // 2, 8], F32, kind="ExternalOutput").ap()
    with contextlib.ExitStack() as S:
        alloc_phaseB(nc, S, P, NCH)
        mk = MK(nc)
        emit_phaseB(mk, P, NCH)
        emit_final_barrier(mk, [P["ACC_t"], P["H2T_t"], P["H2F_t"], P["LOG0_t"]])
        mk.build(S)
    return nc


def emit_moe_expert(mk, P, NCH, ex_tag, gsrc, usrc, dsrc, NFC, use_ce, ce_idx, xf_dram, acc_dram, acc_key):
    T = NCH * 128
    OWN = T // 2
    HHALF = OWN // 2
    NTT = HHALF // 128
    PG = [P[f"PG{i}"] for i in range(4)]
    PU = [P[f"PU{i}"] for i in range(4)]
    DFF = NFC * 128
    c_h2 = Cyc([P["h2f_a"], P["h2f_b"]])
    for c in range(8):
        stg = P["wstage"][c % 2]
        mk.dma("sync", f"wld{c%2}", stg[:, 0:DFF], gsrc[c * 128:(c + 1) * 128, :], writes=[stg], tag=f"gld{ex_tag}_{c}")
        mk.op("vector", lambda e, c=c, stg=stg: e.tensor_copy(P["gate_sb"][:, c * DFF:(c + 1) * DFF], stg[:, 0:DFF]),
              reads=[stg], writes=[(P["gate_sb"], c)], tag=f"gcv{ex_tag}_{c}")
    for c in range(8):
        stg = P["wstage"][c % 2]
        mk.dma("sync", f"wld{c%2}", stg[:, 0:DFF], usrc[c * 128:(c + 1) * 128, :], writes=[stg], tag=f"uld{ex_tag}_{c}")
        mk.op("scalar", lambda e, c=c, stg=stg: e.copy(P["up_sb"][:, c * DFF:(c + 1) * DFF], stg[:, 0:DFF]),
              reads=[stg], writes=[(P["up_sb"], c)], tag=f"ucv{ex_tag}_{c}")
    for c in range(NFC):
        stg = P["wstage"][c % 2]
        mk.dma("scalar", f"wl2{c%2}", stg[:, 0:1024], dsrc[c * 128:(c + 1) * 128, :], writes=[stg], tag=f"dld{ex_tag}_{c}")
        mk.op("vector", lambda e, c=c, stg=stg: e.tensor_copy(P["down_sb"][:, c * 1024:(c + 1) * 1024], stg[:, 0:1024]),
              reads=[stg], writes=[(P["down_sb"], c)], tag=f"dcv{ex_tag}_{c}")
    if use_ce:
        mk.dma("sync", "ced", P["ce_row"][0:1, 0:OWN], P["CTD"].rearrange("a b t -> b a t")[ce_idx:ce_idx + 1, :, :],
               reads=[(P["CTD_t"], 0)], writes=[P["ce_row"]], tag=f"celd{ex_tag}")
        for qq in range(max(1, OWN // 512)):
            w_ = min(512, OWN)
            mk.op("tensor", lambda e, qq=qq, w_=w_: e.matmul(PG[0][0:128, 0:w_], P["s_ones128"][0:1, 0:128], P["ce_row"][0:1, qq * w_:(qq + 1) * w_], start=True, stop=True),
                  reads=[P["ce_row"], P["s_ones128"]], writes=[PG[0]], tag=f"cebc{ex_tag}_{qq}")
            mk.op("vector", lambda e, qq=qq, w_=w_: e.tensor_copy(P["ce_b"][:, qq * w_:(qq + 1) * w_], PG[0][0:128, 0:w_]),
                  reads=[PG[0]], writes=[(P["ce_b"], qq)], tag=f"cebe{ex_tag}_{qq}")
    for half in range(2):
        h2f = c_h2[half]
        mk.dma("gpsimd", f"h2ld{half}", h2f[:, 0:8 * HHALF].rearrange("p (c t) -> p c t", c=8),
               xf_dram.rearrange("(c p) t -> p c t", p=128)[:, :, half * HHALF:(half + 1) * HHALF],
               writes=[h2f], tag=f"h2ld{ex_tag}_{half}")
        NWQ = max(1, HHALF // 512)
        WQ = HHALF // NWQ
        for fc in range(NFC):
            for wq_i in range(NWQ):
                pg = PG[wq_i % 4]
                pu = PU[wq_i % 4]
                mms = [
                    (lambda e, c=c, fc=fc, wq_i=wq_i, pg=pg, h2f=h2f, WQ=WQ, HHALF=HHALF, DFF=DFF: e.matmul(
                        pg[:, 0:WQ], P["gate_sb"][:, c * DFF + fc * 128: c * DFF + fc * 128 + 128],
                        h2f[:, c * HHALF + wq_i * WQ: c * HHALF + wq_i * WQ + WQ],
                        start=(c == 0), stop=(c == 7)))
                    for c in range(8)
                ]
                mk.mm_group(mms, reads=[h2f] + [(P["gate_sb"], c2) for c2 in range(8)], writes=[pg], tag=f"g{ex_tag}_{half}_{fc}_{wq_i}")
                mms = [
                    (lambda e, c=c, fc=fc, wq_i=wq_i, pu=pu, h2f=h2f, WQ=WQ, HHALF=HHALF, DFF=DFF: e.matmul(
                        pu[:, 0:WQ], P["up_sb"][:, c * DFF + fc * 128: c * DFF + fc * 128 + 128],
                        h2f[:, c * HHALF + wq_i * WQ: c * HHALF + wq_i * WQ + WQ],
                        start=(c == 0), stop=(c == 7)))
                    for c in range(8)
                ]
                mk.mm_group(mms, reads=[h2f] + [(P["up_sb"], c2) for c2 in range(8)], writes=[pu], tag=f"u{ex_tag}_{half}_{fc}_{wq_i}")
                mk.op("scalar", lambda e, pg=pg, WQ=WQ: e.activation(P["scrS"][:, 0:WQ], pg[:, 0:WQ], AF.Silu),
                      reads=[pg], writes=[P["scrS"]], tag=f"si{ex_tag}_{half}_{fc}_{wq_i}")
                if not use_ce:
                    mk.op("vector", lambda e, pu=pu, fc=fc, wq_i=wq_i, WQ=WQ, HHALF=HHALF: e.tensor_tensor(
                        P["act_sb"][:, fc * HHALF + wq_i * WQ: fc * HHALF + wq_i * WQ + WQ], P["scrS"][:, 0:WQ], pu[:, 0:WQ], ALU.mult),
                          reads=[P["scrS"], pu], writes=[(P["act_sb"], (fc, wq_i))], tag=f"a{ex_tag}_{half}_{fc}_{wq_i}")
                else:
                    mk.op("vector", lambda e, pu=pu, WQ=WQ: e.tensor_tensor(P["scrS2"][:, 0:WQ], P["scrS"][:, 0:WQ], pu[:, 0:WQ], ALU.mult),
                          reads=[P["scrS"], pu], writes=[P["scrS2"]], tag=f"am{ex_tag}_{half}_{fc}_{wq_i}")
                    mk.op("vector", lambda e, fc=fc, wq_i=wq_i, half=half, WQ=WQ, HHALF=HHALF: e.tensor_tensor(
                        P["act_sb"][:, fc * HHALF + wq_i * WQ: fc * HHALF + wq_i * WQ + WQ], P["scrS2"][:, 0:WQ],
                        P["ce_b"][:, half * HHALF + wq_i * WQ: half * HHALF + wq_i * WQ + WQ],
                        ALU.mult),
                          reads=[P["scrS2"], (P["ce_b"], 0), (P["ce_b"], 1), (P["ce_b"], 2), (P["ce_b"], 3)],
                          writes=[(P["act_sb"], (fc, wq_i))], tag=f"a{ex_tag}_{half}_{fc}_{wq_i}")
        for tt in range(NTT):
            gt = tt + half * NTT
            WQ2 = HHALF // max(1, HHALF // 512)
            for s2 in range(2):
                pg = PG[s2]
                mms = [
                    (lambda e, fc=fc, tt=tt, s2=s2, pg=pg, HHALF=HHALF: e.matmul(
                        pg[:, 0:512], P["act_sb"][:, fc * HHALF + tt * 128: fc * HHALF + tt * 128 + 128],
                        P["down_sb"][:, fc * 1024 + s2 * 512: fc * 1024 + s2 * 512 + 512],
                        start=(fc == 0), stop=(fc == NFC - 1)))
                    for fc in range(NFC)
                ]
                mk.mm_group(mms, reads=[(P["act_sb"], (fc2, (tt * 128) // WQ2)) for fc2 in range(NFC)] + [(P["down_sb"], c2) for c2 in range(NFC)],
                            writes=[pg], tag=f"d{ex_tag}_{half}_{tt}_{s2}")
                mk.op("vector", lambda e, s2=s2, pg=pg: e.tensor_copy(P["dn_t"][:, s2 * 512:(s2 + 1) * 512], pg[:, 0:512]),
                      reads=[pg], writes=[(P["dn_t"], s2)], tag=f"de{ex_tag}_{half}_{tt}_{s2}")
            mk.dma("gpsimd", f"dnd{gt%2}", acc_dram[gt * 128:(gt + 1) * 128, :], P["dn_t"][:, :],
                   reads=[(P["dn_t"], 0), (P["dn_t"], 1)], writes=[(P[acc_key], gt)], accum=True, tag=f"dnd{ex_tag}_{gt}")


def emit_phaseC(mk, P, NCH):
    T = NCH * 128
    OWN = T // 2
    NT = OWN // 128
    PG = [P[f"PG{i}"] for i in range(4)]
    PU = [P[f"PU{i}"] for i in range(4)]
    iota8 = P["s_iota8"]

    mk.dma("sync", "cst", P["s_iota8"][:, :], P["iota8"], writes=[P["s_iota8"]], tag="c_iota")
    mk.dma("sync", "cst", P["s_ones128"][:, :], P["ones1x128"], writes=[P["s_ones128"]], tag="c_o128")
    mk.dma("sync", "cst", P["s_identf"][:, :], P["identf32"], writes=[P["s_identf"]], tag="c_idf")

    st1 = P["stat1"]
    CT = P["ct_all"]
    for tt in range(NT):
        lg = P["lg_t"]
        mk.dma("sync", f"lgld{tt%2}", lg[:, :], P["LOG0"][tt * 128:(tt + 1) * 128, :], writes=[lg], tag=f"lgld_{tt}")
        mk.op("vector", lambda e, lg=lg: e.reduce_max(st1[:, 0:1], lg[:, :], axis=mybir.AxisListType.X),
              reads=[lg], writes=[(st1, 0)], tag=f"rm_{tt}")
        mk.op("vector", lambda e: e.tensor_scalar(st1[:, 1:2], st1[:, 0:1], -1.0, None, ALU.mult),
              reads=[(st1, 0)], writes=[(st1, 1)], tag=f"nm_{tt}")
        mk.op("scalar", lambda e, lg=lg: e.activation(P["prb"][:, :], lg[:, :], AF.Exp, bias=st1[:, 1:2]),
              reads=[lg, (st1, 1)], writes=[P["prb"]], tag=f"pe_{tt}")
        mk.op("vector", lambda e: e.reduce_sum(st1[:, 2:3], P["prb"][:, :], axis=mybir.AxisListType.X),
              reads=[P["prb"]], writes=[(st1, 2)], tag=f"ps_{tt}")
        mk.op("vector", lambda e: e.reciprocal(st1[:, 3:4], st1[:, 2:3]),
              reads=[(st1, 2)], writes=[(st1, 3)], tag=f"pr_{tt}")
        mk.op("vector", lambda e: e.tensor_scalar(P["prb"][:, :], P["prb"][:, :], st1[:, 3:4], None, ALU.mult),
              reads=[P["prb"], (st1, 3)], writes=[P["prb"]], tag=f"pn_{tt}")
        mk.op("vector", lambda e: e.reduce_max(st1[:, 4:5], P["prb"][:, :], axis=mybir.AxisListType.X),
              reads=[P["prb"]], writes=[(st1, 4)], tag=f"m1_{tt}")
        mk.op("vector", lambda e: e.tensor_scalar(P["eqb"][:, :], P["prb"][:, :], st1[:, 4:5], None, ALU.is_ge),
              reads=[P["prb"], (st1, 4)], writes=[P["eqb"]], tag=f"e1_{tt}")
        mk.op("vector", lambda e: e.tensor_scalar(P["eqb"][:, :], P["eqb"][:, :], -1e9, 1e9, ALU.mult, ALU.add),
              reads=[P["eqb"]], writes=[P["eqb"]], tag=f"e1b_{tt}")
        mk.op("vector", lambda e: e.tensor_tensor(P["eqb"][:, :], P["eqb"][:, :], iota8[:, :], ALU.add),
              reads=[P["eqb"], iota8], writes=[P["eqb"]], tag=f"e1c_{tt}")
        mk.op("vector", lambda e: e.tensor_reduce(st1[:, 5:6], P["eqb"][:, :], axis=mybir.AxisListType.X, op=ALU.min),
              reads=[P["eqb"]], writes=[(st1, 5)], tag=f"i1_{tt}")
        mk.op("vector", lambda e: e.tensor_scalar(P["oh1"][:, :], iota8[:, :], st1[:, 5:6], None, ALU.is_equal),
              reads=[iota8, (st1, 5)], writes=[P["oh1"]], tag=f"o1_{tt}")
        mk.op("vector", lambda e: e.scalar_tensor_tensor(P["msk"][:, :], P["oh1"][:, :], -1e9, P["prb"][:, :], ALU.mult, ALU.add),
              reads=[P["oh1"], P["prb"]], writes=[P["msk"]], tag=f"mk_{tt}")
        mk.op("vector", lambda e: e.reduce_max(st1[:, 6:7], P["msk"][:, :], axis=mybir.AxisListType.X),
              reads=[P["msk"]], writes=[(st1, 6)], tag=f"m2_{tt}")
        mk.op("vector", lambda e: e.tensor_scalar(P["eqb"][:, :], P["msk"][:, :], st1[:, 6:7], None, ALU.is_ge),
              reads=[P["msk"], (st1, 6)], writes=[P["eqb"]], tag=f"e2_{tt}")
        mk.op("vector", lambda e: e.tensor_scalar(P["eqb"][:, :], P["eqb"][:, :], -1e9, 1e9, ALU.mult, ALU.add),
              reads=[P["eqb"]], writes=[P["eqb"]], tag=f"e2b_{tt}")
        mk.op("vector", lambda e: e.tensor_tensor(P["eqb"][:, :], P["eqb"][:, :], iota8[:, :], ALU.add),
              reads=[P["eqb"], iota8], writes=[P["eqb"]], tag=f"e2c_{tt}")
        mk.op("vector", lambda e: e.tensor_reduce(st1[:, 7:8], P["eqb"][:, :], axis=mybir.AxisListType.X, op=ALU.min),
              reads=[P["eqb"]], writes=[(st1, 7)], tag=f"i2_{tt}")
        mk.op("vector", lambda e: e.tensor_scalar(P["oh2"][:, :], iota8[:, :], st1[:, 7:8], None, ALU.is_equal),
              reads=[iota8, (st1, 7)], writes=[P["oh2"]], tag=f"o2_{tt}")
        mk.op("scalar", lambda e: e.activation(st1[:, 0:1], st1[:, 4:5], AF.Exp),
              reads=[(st1, 4)], writes=[(st1, 0)], tag=f"x1_{tt}")
        mk.op("scalar", lambda e: e.activation(st1[:, 1:2], st1[:, 6:7], AF.Exp),
              reads=[(st1, 6)], writes=[(st1, 1)], tag=f"x2_{tt}")
        mk.op("vector", lambda e: e.tensor_tensor(st1[:, 2:3], st1[:, 0:1], st1[:, 1:2], ALU.add),
              reads=[(st1, 0), (st1, 1)], writes=[(st1, 2)], tag=f"sw_{tt}")
        mk.op("vector", lambda e: e.reciprocal(st1[:, 3:4], st1[:, 2:3]),
              reads=[(st1, 2)], writes=[(st1, 3)], tag=f"rw_{tt}")
        mk.op("vector", lambda e: e.tensor_tensor(st1[:, 0:1], st1[:, 0:1], st1[:, 3:4], ALU.mult),
              reads=[(st1, 0), (st1, 3)], writes=[(st1, 0)], tag=f"w1_{tt}")
        mk.op("vector", lambda e: e.tensor_tensor(st1[:, 1:2], st1[:, 1:2], st1[:, 3:4], ALU.mult),
              reads=[(st1, 1), (st1, 3)], writes=[(st1, 1)], tag=f"w2_{tt}")
        mk.op("vector", lambda e, tt=tt: e.tensor_scalar(CT[:, tt * 8:(tt + 1) * 8], P["oh1"][:, :], st1[:, 0:1], None, ALU.mult),
              reads=[P["oh1"], (st1, 0)], writes=[(CT, tt)], tag=f"c1_{tt}")
        mk.op("vector", lambda e, tt=tt: e.scalar_tensor_tensor(CT[:, tt * 8:(tt + 1) * 8], P["oh2"][:, :], st1[:, 1:2], CT[:, tt * 8:(tt + 1) * 8], ALU.mult, ALU.add),
              reads=[P["oh2"], (st1, 1), (CT, tt)], writes=[(CT, tt)], tag=f"c2_{tt}")
    mk.op("tensor", lambda e: e.transpose(P["PT2"][0:NT * 8, 0:128], CT[:, 0:NT * 8], P["s_identf"][:, :]),
          reads=[CT] + [(CT, tt2) for tt2 in range(NT)] + [P["s_identf"]], writes=[P["PT2"]], tag="cttr")
    mk.op("vector", lambda e: e.tensor_copy(P["ctT"][0:NT * 8, 0:128], P["PT2"][0:NT * 8, 0:128]),
          reads=[P["PT2"]], writes=[P["ctT"]], tag="ctev")
    mk.dma("sync", "ctd", P["CTD"].rearrange("a b t -> (a b) t"), P["ctT"][0:NT * 8, 0:128],
           reads=[P["ctT"]], writes=[(P["CTD_t"], 0)], tag="ctd")

    for ex in range(9):
        if ex == 0:
            emit_moe_expert(mk, P, NCH, "s", P["L0_ffn_sh_gate"], P["L0_ffn_sh_up"], P["L0_ffn_sh_down"],
                            16, False, 0, P["H2F"], P["ACC"], "ACC_t")
        else:
            emit_moe_expert(mk, P, NCH, f"e{ex-1}", P["L0_ffn_e_gate"][ex - 1], P["L0_ffn_e_up"][ex - 1],
                            P["L0_ffn_e_down"][ex - 1], 16, True, ex - 1, P["H2F"], P["ACC"], "ACC_t")

    for tt in range(NT):
        a1 = P["fx_a"]
        a2 = P["fx_b"]
        a3 = P["fx_c"]
        mk.dma("sync", f"f1{tt%2}", a1[:, 0:1024], P["ACC"][tt * 128:(tt + 1) * 128, :],
               reads=[(P["ACC_t"], tt)], writes=[a1, (a1, 0), (a1, 1)], tag=f"f1_{tt}")
        mk.dma("scalar", f"f2{tt%2}", a2[:, 0:1024], P["AT0"][tt * 128:(tt + 1) * 128, :],
               writes=[a2, (a2, 0), (a2, 1)], tag=f"f2_{tt}")
        mk.dma("gpsimd", f"f3{tt%2}", a3[:, 0:1024], P["XOWN"][tt * 128:(tt + 1) * 128, :],
               writes=[a3, (a3, 0), (a3, 1)], tag=f"f3_{tt}")
        mk.op("vector", lambda e, a1=a1, a2=a2: e.tensor_tensor(a1[:, 0:1024], a1[:, 0:1024], a2[:, 0:1024], ALU.add),
              reads=[a2] + [a1], writes=[a1], tag=f"fa_{tt}")
        mk.op("vector", lambda e, a1=a1, a3=a3: e.tensor_tensor(a1[:, 0:1024], a1[:, 0:1024], a3[:, 0:1024], ALU.add),
              reads=[a3, a1], writes=[a1], tag=f"fb_{tt}")
        mk.dma("sync", f"x1d{tt%2}", P["X1"][tt * 128:(tt + 1) * 128, :], a1[:, 0:1024],
               reads=[a1], writes=[(P["X1_t"], tt)], tag=f"x1d_{tt}")


def alloc_phaseC(nc, S, P, NCH, prefix=""):
    T = NCH * 128
    OWN = T // 2
    NT = OWN // 128
    HHALF = OWN // 2

    def sb(name, shape, dt=F32):
        t = S.enter_context(nc.sbuf_tensor(prefix + name, shape, dt))
        P[name] = t
        return t

    def ps(name, shape, dt=F32):
        t = S.enter_context(nc.psum_tensor(prefix + name, shape, dt))
        P[name] = t
        return t

    sb("s_iota8", [128, 8], F32)
    sb("s_ones128", [1, 128], F32)
    sb("s_identf", [128, 128], F32)
    sb("stat1", [128, 8], F32)
    sb("lg_t", [128, 8], F32)
    sb("prb", [128, 8], F32)
    sb("eqb", [128, 8], F32)
    sb("msk", [128, 8], F32)
    sb("oh1", [128, 8], F32)
    sb("oh2", [128, 8], F32)
    sb("ct_all", [128, NT * 8], F32)
    sb("ctT", [128, 128], F32)
    sb("ce_row", [1, OWN], F32)
    sb("ce_b", [128, OWN], BF16)
    sb("gate_sb", [128, 8 * 2048], BF16)
    sb("up_sb", [128, 8 * 2048], BF16)
    sb("down_sb", [128, 16 * 1024], BF16)
    P["wstage"] = [sb("wstage0", [128, 2048], F32), sb("wstage1", [128, 2048], F32)]
    sb("h2f_a", [128, 8 * HHALF], BF16)
    sb("h2f_b", [128, 8 * HHALF], BF16)
    sb("act_sb", [128, 16 * HHALF], BF16)
    sb("scrS", [128, 512], BF16)
    sb("scrS2", [128, 512], BF16)
    sb("dn_t", [128, 1024], F32)
    P["fx_a"] = P["dn_t"]
    P["fx_b"] = P["wstage"][0]
    P["fx_c"] = P["wstage"][1]
    for i in range(4):
        ps(f"PG{i}", [128, 512], F32)
        ps(f"PU{i}", [128, 512], F32)
    P["PT2"] = P["PU3"]
    for nm in ["CTD_t", "ACC_t", "X1_t"]:
        P[nm] = object()
    return P


def build_phaseC(NCH):
    T = NCH * 128
    OWN = T // 2
    nc = bass.Bass("TRN2", target_bir_lowering=False, debug=False)
    P = {}
    P["H2F"] = nc.dram_tensor("H2F", [D, OWN], BF16, kind="ExternalInput").ap()
    P["LOG0"] = nc.dram_tensor("LOG0", [OWN, 8], F32, kind="ExternalInput").ap()
    P["ACCIN"] = nc.dram_tensor("ACCIN", [OWN, D], F32, kind="ExternalInput").ap()
    P["AT0"] = nc.dram_tensor("AT0", [OWN, D], F32, kind="ExternalInput").ap()
    P["XOWN"] = nc.dram_tensor("XOWN", [OWN, D], F32, kind="ExternalInput").ap()
    P["L0_ffn_sh_gate"] = nc.dram_tensor("L0_ffn_sh_gate", [D, 2048], F32, kind="ExternalInput").ap()
    P["L0_ffn_sh_up"] = nc.dram_tensor("L0_ffn_sh_up", [D, 2048], F32, kind="ExternalInput").ap()
    P["L0_ffn_sh_down"] = nc.dram_tensor("L0_ffn_sh_down", [2048, D], F32, kind="ExternalInput").ap()
    P["L0_ffn_e_gate"] = nc.dram_tensor("L0_ffn_e_gate", [8, D, 2048], F32, kind="ExternalInput").ap()
    P["L0_ffn_e_up"] = nc.dram_tensor("L0_ffn_e_up", [8, D, 2048], F32, kind="ExternalInput").ap()
    P["L0_ffn_e_down"] = nc.dram_tensor("L0_ffn_e_down", [8, 2048, D], F32, kind="ExternalInput").ap()
    for nm, shape, dt in [("identf32", [128, 128], F32), ("ones1x128", [1, 128], F32), ("iota8", [128, 8], F32)]:
        P[nm] = nc.dram_tensor(nm, shape, dt, kind="ExternalInput").ap()
    P["ACC"] = nc.dram_tensor("ACCW", [OWN, D], F32)
    P["CTD"] = nc.dram_tensor("CTD", [OWN // 128, 8, 128], F32)
    P["X1"] = nc.dram_tensor("X1", [OWN, D], F32, kind="ExternalOutput").ap()
    with contextlib.ExitStack() as S:
        alloc_phaseC(nc, S, P, NCH)
        mk = MK(nc)
        NT = OWN // 128
        for tt in range(NT):
            mk.dma("sync", f"acs{tt%2}", P["ACC"][tt * 128:(tt + 1) * 128, :], P["ACCIN"][tt * 128:(tt + 1) * 128, :],
                   writes=[(P["ACC_t"], tt)], tag=f"acs_{tt}")
        emit_phaseC(mk, P, NCH)
        emit_final_barrier(mk, [P["X1_t"]])
        mk.build(S)
    return nc


def emit_phaseD(mk, P, NCH):
    T = NCH * 128
    OWN = T // 2
    PP0, PP1 = P["PP0"], P["PP1"]
    PBT, PT = P["PBT"], P["PT"]
    ident = P["s_ident"]
    identf = P["s_identf"]
    statL = P["statL"]
    scrA, scrH = P["scrA"], P["scrH"]

    emit_load_weight_bf16(mk, P, P["L1_WQ"], P["wq_sb"], D, D, "wq1")
    emit_load_weight_bf16(mk, P, P["L1_WK"], P["wk_sb"], D, D, "wk1")
    emit_load_weight_bf16(mk, P, P["L1_WV"], P["wv_sb"], D, D, "wv1")
    for nm, dst in [("identbf", "s_ident"), ("identf32", "s_identf"),
                    ("maskA", "s_maskA"), ("maskB", "s_maskB")]:
        mk.dma("sync", "cst", P[dst][:, :], P[nm], writes=[P[dst]], tag=f"c_{nm}")
    for c in range(8):
        mk.dma("sync", "cst", P["router_sb"][:, c * 8:(c + 1) * 8], P["L1_ffn_w_gate"][c * 128:(c + 1) * 128, :],
               writes=[(P["router_sb"], c)], tag=f"c_rw{c}")

    c_sx = Cyc([P["sx0"], P["sx1"]])
    c_xnb = Cyc([P["xnb0"]])
    c_xnf = Cyc([P["xnf0"]])
    c_qt = Cyc([P["qp0"]])
    c_kt = Cyc([P["kp0"]])
    c_vaug = Cyc([P["vaug0"]])
    c_qfa = Cyc([P["qfa0"]])
    c_kfa = Cyc([P["kfa0"]])

    def proj_tok(g, src_fm, W_sb, dst_f32_keyed, tag):
        for half in range(2):
            pp = PP0 if half == 0 else PP1
            mms = [
                (lambda e, c=c, half=half, pp=pp, src_fm=src_fm, W_sb=W_sb: e.matmul(
                    pp[:, 0:512], src_fm[:, c * 128:(c + 1) * 128],
                    W_sb[:, c * 1024 + half * 512: c * 1024 + half * 512 + 512],
                    start=(c == 0), stop=(c == 7)))
                for c in range(8)
            ]
            mk.mm_group(mms, reads=[src_fm, W_sb], writes=[pp], tag=f"{tag}mm{half}_g{g}")
            mk.op("vector", lambda e, pp=pp, half=half: e.tensor_copy(dst_f32_keyed[:, half * 512:(half + 1) * 512], pp[:, 0:512]),
                  reads=[pp], writes=[(dst_f32_keyed, half)], tag=f"{tag}ev{half}_g{g}")

    def rope_norm(g, buf, rtag, n_use):
        sl = P["ropeS"]
        cl = P["ropeC"]
        mk.dma("sync", f"rp{g%2}", sl[:, :], P["SINT"][g * 128:(g + 1) * 128, :], writes=[sl], tag=f"rps_{rtag}_g{g}")
        mk.dma("sync", f"rp{g%2}", cl[:, :], P["COST"][g * 128:(g + 1) * 128, :], writes=[cl], tag=f"rpc_{rtag}_g{g}")
        v3 = lambda t, o: t[:, :].rearrange("p (h two q) -> p (h two) q", two=2, q=32)[:, o::2, :]
        tv = P["ropeT"]
        mk.op("vector", lambda e, buf=buf: e.tensor_tensor(v3(tv, 0), v3(buf, 0), cl[:, :].rearrange("p (h q) -> p h q", q=32), ALU.mult),
              reads=[buf, (buf, 0), (buf, 1), cl], writes=[(tv, 0)], tag=f"r1a_{rtag}_g{g}")
        mk.op("vector", lambda e, buf=buf: e.tensor_tensor(P["ropeU"][:, 0:512].rearrange("p (h q) -> p h q", q=32), v3(buf, 1), sl[:, :].rearrange("p (h q) -> p h q", q=32), ALU.mult),
              reads=[buf, (buf, 0), (buf, 1), sl], writes=[P["ropeU"]], tag=f"r1b_{rtag}_g{g}")
        mk.op("vector", lambda e: e.tensor_tensor(v3(tv, 0), v3(tv, 0), P["ropeU"][:, 0:512].rearrange("p (h q) -> p h q", q=32), ALU.subtract),
              reads=[(tv, 0), P["ropeU"]], writes=[(tv, 0)], tag=f"r1c_{rtag}_g{g}")
        mk.op("vector", lambda e, buf=buf: e.tensor_tensor(v3(tv, 1), v3(buf, 0), sl[:, :].rearrange("p (h q) -> p h q", q=32), ALU.mult),
              reads=[buf, sl], writes=[(tv, 1)], tag=f"r2a_{rtag}_g{g}")
        mk.op("vector", lambda e, buf=buf: e.tensor_tensor(P["ropeU"][:, 0:512].rearrange("p (h q) -> p h q", q=32), v3(buf, 1), cl[:, :].rearrange("p (h q) -> p h q", q=32), ALU.mult),
              reads=[buf, (buf, 0), (buf, 1), cl], writes=[P["ropeU"]], tag=f"r2b_{rtag}_g{g}")
        mk.op("vector", lambda e: e.tensor_tensor(v3(tv, 1), v3(tv, 1), P["ropeU"][:, 0:512].rearrange("p (h q) -> p h q", q=32), ALU.add),
              reads=[(tv, 1), P["ropeU"]], writes=[(tv, 1)], tag=f"r2c_{rtag}_g{g}")
        mk.op("scalar", lambda e: e.square(scrH[:, 0:1024], tv[:, :]),
              reads=[(tv, 0), (tv, 1)], writes=[scrH], tag=f"nsq_{rtag}_g{g}")
        mk.op("vector", lambda e: e.tensor_reduce(P["hn"][:, 0:16], scrH[:, 0:1024].rearrange("p (h d) -> p h d", d=64), axis=mybir.AxisListType.X, op=ALU.add),
              reads=[scrH], writes=[P["hn"]], tag=f"nss_{rtag}_g{g}")
        mk.op("vector", lambda e: e.tensor_scalar(P["hn"][:, 0:16], P["hn"][:, 0:16], 1.0, 1e-6, ALU.mult, ALU.add),
              reads=[P["hn"]], writes=[P["hn"]], tag=f"ne_{rtag}_g{g}")
        mk.op("scalar", lambda e: e.activation(P["hn"][:, 16:32], P["hn"][:, 0:16], AF.Ln),
              reads=[P["hn"]], writes=[(P["hn"], "l")], tag=f"nl_{rtag}_g{g}")
        mk.op("scalar", lambda e: e.activation(P["hn"][:, 16:32], P["hn"][:, 16:32], AF.Exp, scale=-0.5),
              reads=[(P["hn"], "l")], writes=[(P["hn"], "l")], tag=f"nr_{rtag}_g{g}")
        for hh in range(H):
            mk.op("vector", lambda e, hh=hh, buf=buf: e.tensor_scalar(
                buf[:, hh * 64:(hh + 1) * 64], tv[:, hh * 64:(hh + 1) * 64], P["hn"][:, 16 + hh:17 + hh], None, ALU.mult),
                  reads=[(tv, 0), (tv, 1), (P["hn"], "l")], writes=[(buf, f"n{hh//8}")], tag=f"na{hh}_{rtag}_g{g}")

    def head_transposes2(g, src_tm, src_keys, dst_all, tag, to_bf):
        for half in range(2):
            last = None
            first = True
            for hl in range(8):
                hh = half * 8 + hl
                rd = [(src_tm, src_keys[half]), ident] if first else []
                last = mk.op("tensor",
                             lambda e, hh=hh, hl=hl, src_tm=src_tm: e.transpose(
                                 PBT[0:64, hl * 128:(hl + 1) * 128], src_tm[:, hh * 64:(hh + 1) * 64], ident[:, :]),
                             reads=rd + ([PBT] if first else []), writes=[PBT] if first else [],
                             tag=f"{tag}ht{hh}_g{g}", track=first)
                first = False
            st = mk.state[_key(PBT)]
            st["w"] = last
            st["r"] = []
            mk.op("vector",
                  lambda e, half=half, dst_all=dst_all: e.tensor_copy(
                      dst_all[:, half * 1024:(half + 1) * 1024], PBT[0:64, 0:1024]),
                  reads=[PBT], writes=[(dst_all, half)], tag=f"{tag}hte{half}_g{g}")

    for g in range(NCH):
        own_idx = g % (NCH // 2)
        first_half = g < NCH // 2
        mask_ap = P["s_maskA"] if first_half else P["s_maskB"]

        sx = c_sx[g]
        mk.dma("sync", f"ldx{g%2}", sx[:, :], P["X1F"][g * 128:(g + 1) * 128, :], writes=[sx], tag=f"x_g{g}")
        mk.op("vector", lambda e, sx=sx: e.reduce_sum(statL[:, 0:1], sx[:, :], axis=mybir.AxisListType.X),
              reads=[sx], writes=[(statL, 0)], tag=f"s1_g{g}")
        mk.op("scalar", lambda e, sx=sx: e.square(scrH[:, 0:1024], sx[:, :]),
              reads=[sx], writes=[scrH], tag=f"sq_g{g}")
        mk.op("vector", lambda e: e.reduce_sum(statL[:, 1:2], scrH[:, 0:1024], axis=mybir.AxisListType.X),
              reads=[scrH], writes=[(statL, 1)], tag=f"s2_g{g}")
        mk.op("vector", lambda e: e.tensor_scalar(statL[:, 2:3], statL[:, 0:1], -1.0 / D, None, ALU.mult),
              reads=[(statL, 0)], writes=[(statL, 2)], tag=f"mn_g{g}")
        mk.op("vector", lambda e: e.tensor_tensor(statL[:, 3:4], statL[:, 2:3], statL[:, 2:3], ALU.mult),
              reads=[(statL, 2)], writes=[(statL, 3)], tag=f"m2_g{g}")
        mk.op("vector", lambda e: e.tensor_scalar(statL[:, 4:5], statL[:, 1:2], 1.0 / D, statL[:, 3:4], ALU.mult, ALU.subtract),
              reads=[(statL, 1), (statL, 3)], writes=[(statL, 4)], tag=f"va_g{g}")
        mk.op("vector", lambda e: e.tensor_scalar(statL[:, 4:5], statL[:, 4:5], 1e-5, None, ALU.add),
              reads=[(statL, 4)], writes=[(statL, 4)], tag=f"vae_g{g}")
        mk.op("scalar", lambda e: e.activation(statL[:, 5:6], statL[:, 4:5], AF.Ln),
              reads=[(statL, 4)], writes=[(statL, 5)], tag=f"ln_g{g}")
        mk.op("scalar", lambda e: e.activation(statL[:, 6:7], statL[:, 5:6], AF.Exp, scale=-0.5),
              reads=[(statL, 5)], writes=[(statL, 6)], tag=f"rs_g{g}")
        mk.op("vector", lambda e: e.tensor_tensor(statL[:, 7:8], statL[:, 2:3], statL[:, 6:7], ALU.mult),
              reads=[(statL, 2), (statL, 6)], writes=[(statL, 7)], tag=f"bb_g{g}")
        xn = P["xn0"]
        mk.op("vector", lambda e, sx=sx, xn=xn: e.tensor_scalar(xn[:, :], sx[:, :], statL[:, 6:7], statL[:, 7:8], ALU.mult, ALU.add),
              reads=[sx, (statL, 6), (statL, 7)], writes=[xn], tag=f"xn_g{g}")

        mk.op("vector", lambda e, xn=xn, mask_ap=mask_ap: e.tensor_scalar(P["xnsel0"][:, :], xn[:, :], mask_ap[:, 0:1], None, ALU.mult),
              reads=[xn, mask_ap], writes=[P["xnsel0"]], tag=f"xs_g{g}")
        for rnd in range(2):
            last = None
            first = True
            for cc in range(4):
                c = rnd * 4 + cc
                last = mk.op("tensor", lambda e, c=c, cc=cc: e.transpose(
                    PT[:, cc * 128:(cc + 1) * 128], P["xnsel0"][:, c * 128:(c + 1) * 128], identf[:, :]),
                    reads=([P["xnsel0"], identf, PT] if first else []), writes=[PT] if first else [],
                    tag=f"str{c}_g{g}", track=first)
                first = False
            st = mk.state[_key(PT)]
            st["w"] = last
            st["r"] = []
            mk.op("vector", lambda e, rnd=rnd: e.tensor_copy(P["h2fm"][:, rnd * 512:(rnd + 1) * 512], PT[:, 0:512]),
                  reads=[PT], writes=[(P["h2fm"], rnd)], tag=f"ste{rnd}_g{g}")
        mk.op("vector", lambda e: e.tensor_copy(P["h2fmb"][:, :], P["h2fm"][:, 0:1024]),
              reads=[(P["h2fm"], 0), (P["h2fm"], 1)], writes=[P["h2fmb"]], tag=f"sfb_g{g}")
        mk.dma("gpsimd", f"xfd{g%2}", P["XN1F"].rearrange("(c p) t -> p c t", p=128)[:, :, own_idx * 128:(own_idx + 1) * 128],
               P["h2fmb"][:, :].rearrange("p (c t) -> p c t", c=8),
               reads=[P["h2fmb"]], writes=[(P["XN1F_t"], own_idx)], accum=not first_half, tag=f"xfd_g{g}")
        mms = [
            (lambda e, c=c: e.matmul(P["PL"][0:128, 0:8], P["h2fm"][:, c * 128:(c + 1) * 128],
                                     P["router_sb"][:, c * 8:(c + 1) * 8], start=(c == 0), stop=(c == 7)))
            for c in range(8)
        ]
        mk.mm_group(mms, reads=[(P["h2fm"], 0), (P["h2fm"], 1)] + [(P["router_sb"], c2) for c2 in range(8)],
                    writes=[P["PL"]], tag=f"lg_g{g}")
        mk.op("vector", lambda e: e.tensor_copy(P["lg_t"][:, :], P["PL"][0:128, 0:8]),
              reads=[P["PL"]], writes=[P["lg_t"]], tag=f"lge_g{g}")
        mk.dma("gpsimd", f"lgd{g%2}", P["LOG1"][own_idx * 128:(own_idx + 1) * 128, :], P["lg_t"][:, :],
               reads=[P["lg_t"]], writes=[(P["LOG1_t"], own_idx)], accum=not first_half, tag=f"lgd_g{g}")

        xnb = c_xnb[g]
        mk.op("vector", lambda e, xn=xn, xnb=xnb: e.tensor_copy(xnb[:, :], xn[:, :]),
              reads=[xn], writes=[xnb], tag=f"xnb_g{g}")
        xnf = c_xnf[g]
        last = None
        first = True
        for c in range(8):
            last = mk.op("tensor",
                         lambda e, c=c, xnb=xnb: e.transpose(PBT[:, c * 128:(c + 1) * 128], xnb[:, c * 128:(c + 1) * 128], ident[:, :]),
                         reads=([xnb, ident, PBT] if first else []), writes=[PBT] if first else [], tag=f"xtr{c}_g{g}", track=first)
            first = False
        st = mk.state[_key(PBT)]
        st["w"] = last
        st["r"] = []
        mk.op("vector", lambda e, xnf=xnf: e.tensor_copy(xnf[:, :], PBT[:, 0:1024]),
              reads=[PBT], writes=[xnf], tag=f"xtre_g{g}")

        qt = c_qt[g]
        proj_tok(g, xnf, P["wq_sb"], qt, "q_")
        rope_norm(g, qt, "q", True)
        mk.op("vector", lambda e, qt=qt, mask_ap=mask_ap: e.tensor_scalar(qt[:, :], qt[:, :], mask_ap[:, 0:1], None, ALU.mult),
              reads=[(qt, "n0"), (qt, "n1"), mask_ap], writes=[(qt, "n0"), (qt, "n1")], tag=f"qm_g{g}")
        kt = c_kt[g]
        proj_tok(g, xnf, P["wk_sb"], kt, "k_")
        rope_norm(g, kt, "k", True)
        mk.op("vector", lambda e, qt=qt: e.tensor_copy(P["qbf"][:, :], qt[:, :]),
              reads=[(qt, "n0"), (qt, "n1"), (qt, 0), (qt, 1)], writes=[(P["qbf"], "h0"), (P["qbf"], "h1")], tag=f"qb_g{g}")
        mk.op("vector", lambda e, kt=kt: e.tensor_copy(P["kbf"][:, :], kt[:, :]),
              reads=[(kt, "n0"), (kt, "n1"), (kt, 0), (kt, 1)], writes=[(P["kbf"], "h0"), (P["kbf"], "h1")], tag=f"kb_g{g}")
        qfa = c_qfa[g]
        head_transposes2(g, P["qbf"], ["h0", "h1"], qfa, "fq", True)
        mk.dma("gpsimd", f"qfd{g%2}", P["QF1"].rearrange("h p s -> p h s")[:, :, own_idx * 128:(own_idx + 1) * 128], qfa[:, :].rearrange("p (h s) -> p h s", h=H),
               reads=[(qfa, 0), (qfa, 1)], writes=[(P["QF1_t"], own_idx)], accum=not first_half, tag=f"fqd_g{g}")
        kfa = c_kfa[g]
        head_transposes2(g, P["kbf"], ["h0", "h1"], kfa, "fk", True)
        mk.dma("sync", f"kfd{g%2}", P["KF1"].rearrange("h p s -> p h s")[:, :, g * 128:(g + 1) * 128], kfa[:, :].rearrange("p (h s) -> p h s", h=H),
               reads=[(kfa, 0), (kfa, 1)], writes=[(P["KF1_t"], g)], tag=f"fkd_g{g}")
        vaug = c_vaug[g]
        mk.op("vector", lambda e, vaug=vaug: e.memset(vaug[:, :], 1.0),
              writes=[vaug, (vaug, "v0"), (vaug, "v1")], tag=f"vm_g{g}")
        for half in range(2):
            pp = PP0 if half == 0 else PP1
            mms = [
                (lambda e, c=c, pp=pp, half=half, xnf=xnf: e.matmul(
                    pp[:, 0:512], xnf[:, c * 128:(c + 1) * 128],
                    P["wv_sb"][:, c * 1024 + half * 512: c * 1024 + half * 512 + 512],
                    start=(c == 0), stop=(c == 7)))
                for c in range(8)
            ]
            mk.mm_group(mms, reads=[xnf, P["wv_sb"]], writes=[pp], tag=f"vmm{half}_g{g}")
            mk.op("vector",
                  lambda e, pp=pp, half=half, vaug=vaug: e.tensor_copy(
                      vaug[:, :].rearrange("p (h s) -> p h s", h=H)[:, 8 * half:8 * half + 8, 0:64],
                      pp[:, 0:512].rearrange("p (h s) -> p h s", h=8)),
                  reads=[pp, vaug], writes=[(vaug, f"v{half}")], tag=f"vev{half}_g{g}")
        mk.dma("sync", f"va0{g%2}", P["VA1"][g * 128:(g + 1) * 128, :], vaug[:, :],
               reads=[(vaug, "v0"), (vaug, "v1")], writes=[(P["VA1_t"], g)], tag=f"vad_g{g}")


def alloc_phaseD(nc, S, P, NCH, prefix=""):
    def sb(name, shape, dt=F32):
        t = S.enter_context(nc.sbuf_tensor(prefix + name, shape, dt))
        P[name] = t
        return t

    def ps(name, shape, dt=F32):
        t = S.enter_context(nc.psum_tensor(prefix + name, shape, dt))
        P[name] = t
        return t

    for nm in ["wq_sb", "wk_sb", "wv_sb"]:
        sb(nm, [128, 8 * 1024], BF16)
    sb("s_ident", [128, 128], BF16)
    sb("s_identf", [128, 128], F32)
    sb("s_maskA", [128, 1], F32)
    sb("s_maskB", [128, 1], F32)
    sb("router_sb", [128, 64], F32)
    for nm in ["sx0", "sx1", "xn0", "xnsel0", "scrA", "scrH", "h2fm"]:
        sb(nm, [128, 1024], F32)
    P["wstage"] = [P["sx0"], P["sx1"]]
    sb("qp0", [128, 1024], F32)
    sb("kp0", [128, 1024], F32)
    sb("ropeT", [128, 1024], F32)
    sb("ropeU", [128, 512], F32)
    sb("ropeS", [128, 512], F32)
    sb("ropeC", [128, 512], F32)
    sb("hn", [128, 32], F32)
    sb("statL", [128, 10], F32)
    sb("lg_t", [128, 8], F32)
    for nm in ["xnb0", "xnf0", "xnf1", "h2fmb", "qbf", "kbf"]:
        sb(nm, [128, 1024], BF16)
    sb("qfa0", [64, 2048], BF16)
    sb("kfa0", [64, 2048], BF16)
    sb("vaug0", [128, H * 65], BF16)
    ps("PP0", [128, 512], F32)
    ps("PP1", [128, 512], F32)
    ps("PT", [128, 512], F32)
    ps("PL", [128, 512], F32)
    ps("PBT", [128, 1024], BF16)
    for nm in ["XN1F_t", "LOG1_t", "QF1_t", "KF1_t", "VA1_t"]:
        P[nm] = object()
    return P


def build_phaseD(NCH):
    T = NCH * 128
    OWN = T // 2
    nc = bass.Bass("TRN2", target_bir_lowering=False, debug=False)
    P = {}
    P["X1F"] = nc.dram_tensor("X1F", [T, D], F32, kind="ExternalInput").ap()
    P["L1_WQ"] = nc.dram_tensor("L1_WQ", [D, D], F32, kind="ExternalInput").ap()
    P["L1_WK"] = nc.dram_tensor("L1_WK", [D, D], F32, kind="ExternalInput").ap()
    P["L1_WV"] = nc.dram_tensor("L1_WV", [D, D], F32, kind="ExternalInput").ap()
    P["L1_ffn_w_gate"] = nc.dram_tensor("L1_ffn_w_gate", [D, 8], F32, kind="ExternalInput").ap()
    P["SINT"] = nc.dram_tensor("SINT", [T, 512], F32, kind="ExternalInput").ap()
    P["COST"] = nc.dram_tensor("COST", [T, 512], F32, kind="ExternalInput").ap()
    for nm, shape, dt in [("identbf", [128, 128], BF16), ("identf32", [128, 128], F32),
                          ("maskA", [128, 1], F32), ("maskB", [128, 1], F32)]:
        P[nm] = nc.dram_tensor(nm, shape, dt, kind="ExternalInput").ap()
    P["KF1"] = nc.dram_tensor("KF1", [H, 64, T], BF16, kind="ExternalOutput").ap()
    P["QF1"] = nc.dram_tensor("QF1", [H, 64, OWN], BF16, kind="ExternalOutput").ap()
    P["VA1"] = nc.dram_tensor("VA1", [T, H * 65], BF16, kind="ExternalOutput").ap()
    P["XN1F"] = nc.dram_tensor("XN1F", [D, OWN], BF16, kind="ExternalOutput").ap()
    P["LOG1"] = nc.dram_tensor("LOG1", [OWN, 8], F32, kind="ExternalOutput").ap()
    with contextlib.ExitStack() as S:
        alloc_phaseD(nc, S, P, NCH)
        mk = MK(nc)
        emit_phaseD(mk, P, NCH)
        emit_final_barrier(mk, [P["XN1F_t"], P["LOG1_t"], P["QF1_t"], P["KF1_t"], P["VA1_t"]])
        mk.build(S)
    return nc


def emit_phaseE(mk, P, NCH):
    T = NCH * 128
    OWN = T // 2
    NT = OWN // 128
    NQQ = max(1, OWN // 512)
    QW = OWN // NQQ
    emit_load_weight_bf16(mk, P, P["L1_attn_Wo"], P["wo_sb"], D, D, "wo1")
    for nm, dst in [("ones1x64", "s_ones64"), ("ones1x128", "s_ones128"),
                    ("shiftE", "s_shE"), ("shiftO", "s_shO"), ("neg18", "negCb")]:
        mk.dma("sync", "cst", P[dst][:, :], P[nm], writes=[P[dst]], tag=f"c_{nm}")
    mk.dma("sync", "crow", P["scrq"][0:1, 0:D], P["L1_attn_bo"], writes=[P["scrq"]], tag="c_bo")
    emit_bcast_row(mk, P, P["PXB"], P["scrq"], D, P["bo_b"], "bcbo")

    c_kh = Cyc([P["kh0"], P["kh1"]])
    c_qh = Cyc([P["qh0"], P["qh1"]])
    c_va = Cyc([P[f"va{i}"] for i in range(4)])
    c_pt = Cyc([P["pt0"], P["pt1"]])
    yfm = P["yfm_all"]

    for hh in range(H):
        kh = c_kh[hh]
        mk.dma("sync", f"khld{hh%2}", kh[:, :], P["KF1"][hh, :, :], writes=[kh], tag=f"khld_{hh}")
        qh = c_qh[hh]
        mk.dma("scalar", f"qhld{hh%2}", qh[:, :], P["QF1"][hh, :, :], writes=[qh], tag=f"qhld_{hh}")
        def emit_cx(jj, va_, pt_, hh=hh):
            for s4 in range(NQQ):
                if jj == 0 and s4 == 0:
                    mk.op("tensor", lambda e, va_=va_, pt_=pt_, jj=jj, s4=s4: e.matmul(
                        P["PXB"][0:65, s4 * QW:(s4 + 1) * QW], va_[:, :], pt_[:, s4 * QW:(s4 + 1) * QW],
                        start=True, stop=(jj == NCH - 1)),
                          reads=[va_, pt_], writes=[P["PXB"]], tag=f"cx_h{hh}_j{jj}_{s4}")
                else:
                    tok = mk.raw_op("tensor", lambda e, va_=va_, pt_=pt_, jj=jj, s4=s4: e.matmul(
                        P["PXB"][0:65, s4 * QW:(s4 + 1) * QW], va_[:, :], pt_[:, s4 * QW:(s4 + 1) * QW],
                        start=(jj == 0), stop=(jj == NCH - 1)),
                        also=[mk.writer_of(pt_), mk.writer_of(va_)], tag=f"cx_h{hh}_j{jj}_{s4}")
                    st2 = mk.state[_key(P["PXB"])]
                    st2["w"] = tok
                    mk.state[_key(pt_)]["r"].append(tok)
                    mk.state[_key(va_)]["r"].append(tok)

        for j in range(NCH):
            va = c_va[hh * NCH + j]
            mk.dma("gpsimd", f"vald{(hh*NCH+j)%4}", va[:, :], P["VA1"][j * 128:(j + 1) * 128, hh * 65:(hh + 1) * 65],
                   writes=[va], tag=f"vald_{hh}_{j}")
            first = True
            last = None
            for s4 in range(NQQ):
                last = mk.op("tensor", lambda e, kh=kh, qh=qh, j=j, s4=s4: e.matmul(
                    P["PSCB"][:, s4 * QW:(s4 + 1) * QW], kh[:, j * 128:(j + 1) * 128], qh[:, s4 * QW:(s4 + 1) * QW],
                    start=True, stop=True),
                      reads=([kh, qh] if first else []), writes=([P["PSCB"]] if first else []),
                      track=first, tag=f"sc_h{hh}_j{j}_{s4}")
                first = False
            st = mk.state[_key(P["PSCB"])]
            st["w"] = last
            st["r"] = []
            pt = c_pt[j]
            mk.op("scalar", lambda e, pt=pt: e.activation(
                pt[:, 0:OWN], P["PSCB"][:, 0:OWN], AF.Exp, bias=P["negCb"][:, 0:1], scale=0.125),
                  reads=[P["PSCB"], P["negCb"]], writes=[pt], tag=f"ex_h{hh}_j{j}")
            if j > 0:
                emit_cx(j - 1, c_va[hh * NCH + j - 1], c_pt[j - 1])
        emit_cx(NCH - 1, c_va[hh * NCH + NCH - 1], c_pt[NCH - 1])
        yh_buf = P["yhE"] if hh % 2 == 0 else P["yhO"]
        for qq in range(NQQ):
            sl = slice(qq * QW, (qq + 1) * QW)
            mk.op("vector", lambda e, sl=sl: e.reciprocal(P["rec1"][0:1, 0:QW], P["PXB"][64:65, sl]),
                  reads=[P["PXB"]], writes=[P["rec1"]], tag=f"rc_h{hh}_q{qq}")
            mk.op("tensor", lambda e, sl=sl: e.matmul(P["PSCB"][0:64, sl], P["s_ones64"][0:1, 0:64], P["rec1"][0:1, 0:QW], start=True, stop=True),
                  reads=[P["rec1"], P["s_ones64"]], writes=[P["PSCB"]], tag=f"bc_h{hh}_q{qq}")
            mk.op("vector", lambda e, sl=sl: e.tensor_copy(P["rb1"][0:64, 0:QW], P["PSCB"][0:64, sl]),
                  reads=[P["PSCB"]], writes=[P["rb1"]], tag=f"rbe_h{hh}_q{qq}")
            mk.op("vector", lambda e, sl=sl, qq=qq, yh_buf=yh_buf: e.tensor_tensor(
                yh_buf[:, qq * QW:(qq + 1) * QW], P["PXB"][0:64, sl], P["rb1"][0:64, 0:QW], ALU.mult),
                  reads=[P["PXB"], P["rb1"]], writes=[(yh_buf, qq)], tag=f"y_h{hh}_q{qq}")
        if hh % 2 == 1:
            pi = hh // 2
            for seg in range(NQQ):
                sl = slice(seg * QW, (seg + 1) * QW)
                mms = [
                    (lambda e, sl=sl: e.matmul(P["PSCB"][:, sl], P["s_shE"][:, :], P["yhE"][:, sl], start=True, stop=False)),
                    (lambda e, sl=sl: e.matmul(P["PSCB"][:, sl], P["s_shO"][:, :], P["yhO"][:, sl], start=False, stop=True)),
                ]
                mk.mm_group(mms, reads=[(P["yhE"], seg), (P["yhO"], seg), P["s_shE"], P["s_shO"]],
                            writes=[P["PSCB"]], tag=f"sh_p{pi}_s{seg}")
                mk.op("vector", lambda e, pi=pi, seg=seg, sl=sl: e.tensor_copy(
                    yfm[:, pi * OWN + seg * QW: pi * OWN + (seg + 1) * QW], P["PSCB"][:, sl]),
                      reads=[P["PSCB"]], writes=[(yfm, f"{pi}_{seg}")], tag=f"she_p{pi}_s{seg}")

    for tt in range(NT):
        for half in range(2):
            sl = slice(half * 512, half * 512 + 512)
            mms = [
                (lambda e, pr=pr, half=half, tt=tt, sl=sl: e.matmul(
                    P["PXB"][:, sl], yfm[:, pr * OWN + tt * 128: pr * OWN + tt * 128 + 128],
                    P["wo_sb"][:, pr * 1024 + half * 512: pr * 1024 + half * 512 + 512],
                    start=(pr == 0), stop=(pr == 7)))
                for pr in range(8)
            ]
            mk.mm_group(mms, reads=[(yfm, f"{pr2}_{(tt*128)//QW}") for pr2 in range(8)] + [P["wo_sb"]],
                        writes=[P["PXB"]], tag=f"wo{half}_t{tt}")
            mk.op("vector", lambda e, half=half, sl=sl: e.tensor_tensor(
                P["acc_t"][:, half * 512:(half + 1) * 512], P["PXB"][:, sl], P["bo_b"][:, half * 512:(half + 1) * 512], ALU.add),
                  reads=[P["PXB"], P["bo_b"]], writes=[(P["acc_t"], half)], tag=f"ae{half}_t{tt}")
        mk.dma("gpsimd", f"accd{tt%2}", P["ACC1"][tt * 128:(tt + 1) * 128, :], P["acc_t"][:, :],
               reads=[(P["acc_t"], 0), (P["acc_t"], 1)], writes=[(P["ACC1_t"], tt)], tag=f"accd_t{tt}")


def alloc_phaseE(nc, S, P, NCH, prefix=""):
    T = NCH * 128
    OWN = T // 2

    def sb(name, shape, dt=F32):
        t = S.enter_context(nc.sbuf_tensor(prefix + name, shape, dt))
        P[name] = t
        return t

    def ps(name, shape, dt=F32):
        t = S.enter_context(nc.psum_tensor(prefix + name, shape, dt))
        P[name] = t
        return t

    sb("wo_sb", [128, 8 * 1024], BF16)
    P["wstage"] = [sb("wst0", [128, 1024], F32), sb("wst1", [128, 1024], F32)]
    sb("s_ones64", [1, 64], F32)
    sb("s_ones128", [1, 128], F32)
    sb("s_shE", [64, 128], BF16)
    sb("s_shO", [64, 128], BF16)
    sb("negCb", [128, 1], F32)
    sb("scrq", [128, 1024], F32)
    sb("bo_b", [128, 1024], F32)
    sb("kh0", [64, T], BF16)
    sb("kh1", [64, T], BF16)
    sb("qh0", [64, OWN], BF16)
    sb("qh1", [64, OWN], BF16)
    for i in range(4):
        sb(f"va{i}", [128, 65], BF16)
    sb("pt0", [128, max(OWN, 512)], BF16)
    sb("pt1", [128, max(OWN, 512)], BF16)
    sb("rec1", [1, 512], F32)
    sb("rb1", [64, 512], F32)
    sb("yhE", [64, OWN], BF16)
    sb("yhO", [64, OWN], BF16)
    sb("yfm_all", [128, 8 * OWN], BF16)
    sb("acc_t", [128, 1024], F32)
    ps("PSCB", [128, max(OWN, 1024)], F32)
    ps("PXB", [128, max(OWN, 1024)], F32)
    P["ACC1_t"] = object()
    return P


def build_phaseE(NCH):
    T = NCH * 128
    OWN = T // 2
    nc = bass.Bass("TRN2", target_bir_lowering=False, debug=False)
    P = {}
    P["KF1"] = nc.dram_tensor("KF1", [H, 64, T], BF16, kind="ExternalInput").ap()
    P["QF1"] = nc.dram_tensor("QF1", [H, 64, OWN], BF16, kind="ExternalInput").ap()
    P["VA1"] = nc.dram_tensor("VA1", [T, H * 65], BF16, kind="ExternalInput").ap()
    P["L1_attn_Wo"] = nc.dram_tensor("L1_attn_Wo", [D, D], F32, kind="ExternalInput").ap()
    P["L1_attn_bo"] = nc.dram_tensor("L1_attn_bo", [1, D], F32, kind="ExternalInput").ap()
    for nm, shape, dt in [("ones1x64", [1, 64], F32), ("ones1x128", [1, 128], F32),
                          ("shiftE", [64, 128], BF16), ("shiftO", [64, 128], BF16),
                          ("neg18", [128, 1], F32)]:
        P[nm] = nc.dram_tensor(nm, shape, dt, kind="ExternalInput").ap()
    P["ACC1"] = nc.dram_tensor("ACC1", [OWN, D], F32, kind="ExternalOutput").ap()
    with contextlib.ExitStack() as S:
        alloc_phaseE(nc, S, P, NCH)
        mk = MK(nc)
        emit_phaseE(mk, P, NCH)
        emit_final_barrier(mk, [P["ACC1_t"]])
        mk.build(S)
    return nc


def emit_phaseF(mk, P, NCH, NCORES=8, CAPV=4096.0):
    T = NCH * 128
    OWN = T // 2
    NT = OWN // 128
    NTOK = OWN * NCORES
    NGT = NTOK // 128
    iota8 = P["s_iota8"]
    st1 = P["stat1"]
    PG = [P[f"PG{i}"] for i in range(4)]

    mk.dma("sync", "cst", P["s_iota8"][:, :], P["iota8"], writes=[P["s_iota8"]], tag="c_iota")
    mk.dma("sync", "cst", P["s_ones128"][:, :], P["ones1x128"], writes=[P["s_ones128"]], tag="c_o128")
    mk.dma("sync", "cst", P["s_identf"][:, :], P["identf32"], writes=[P["s_identf"]], tag="c_idf")
    mk.dma("sync", "cst", P["s_msel"][:, :], P["msel"], writes=[P["s_msel"]], tag="c_msel")

    for gt in range(NGT):
        lg = P["lg_t"]
        mk.dma("sync", f"lgld{gt%2}", lg[:, :], P["LOGALL"][gt * 128:(gt + 1) * 128, :], writes=[lg], tag=f"lgld_{gt}")
        mk.op("vector", lambda e, lg=lg: e.reduce_max(st1[:, 0:1], lg[:, :], axis=mybir.AxisListType.X),
              reads=[lg], writes=[(st1, 0)], tag=f"rm_{gt}")
        mk.op("vector", lambda e: e.tensor_scalar(st1[:, 1:2], st1[:, 0:1], -1.0, None, ALU.mult),
              reads=[(st1, 0)], writes=[(st1, 1)], tag=f"nm_{gt}")
        mk.op("scalar", lambda e, lg=lg: e.activation(P["prb"][:, :], lg[:, :], AF.Exp, bias=st1[:, 1:2]),
              reads=[lg, (st1, 1)], writes=[P["prb"]], tag=f"pe_{gt}")
        mk.op("vector", lambda e: e.reduce_sum(st1[:, 2:3], P["prb"][:, :], axis=mybir.AxisListType.X),
              reads=[P["prb"]], writes=[(st1, 2)], tag=f"ps_{gt}")
        mk.op("vector", lambda e: e.reciprocal(st1[:, 3:4], st1[:, 2:3]),
              reads=[(st1, 2)], writes=[(st1, 3)], tag=f"pr_{gt}")
        mk.op("vector", lambda e: e.tensor_scalar(P["prb"][:, :], P["prb"][:, :], st1[:, 3:4], None, ALU.mult),
              reads=[P["prb"], (st1, 3)], writes=[P["prb"]], tag=f"pn_{gt}")
        mk.op("vector", lambda e: e.reduce_max(st1[:, 4:5], P["prb"][:, :], axis=mybir.AxisListType.X),
              reads=[P["prb"]], writes=[(st1, 4)], tag=f"m1_{gt}")
        mk.op("vector", lambda e: e.tensor_scalar(P["eqb"][:, :], P["prb"][:, :], st1[:, 4:5], None, ALU.is_ge),
              reads=[P["prb"], (st1, 4)], writes=[P["eqb"]], tag=f"e1_{gt}")
        mk.op("vector", lambda e: e.tensor_scalar(P["eqb"][:, :], P["eqb"][:, :], -1e9, 1e9, ALU.mult, ALU.add),
              reads=[P["eqb"]], writes=[P["eqb"]], tag=f"e1b_{gt}")
        mk.op("vector", lambda e: e.tensor_tensor(P["eqb"][:, :], P["eqb"][:, :], iota8[:, :], ALU.add),
              reads=[P["eqb"], iota8], writes=[P["eqb"]], tag=f"e1c_{gt}")
        mk.op("vector", lambda e: e.tensor_reduce(st1[:, 5:6], P["eqb"][:, :], axis=mybir.AxisListType.X, op=ALU.min),
              reads=[P["eqb"]], writes=[(st1, 5)], tag=f"i1_{gt}")
        mk.op("vector", lambda e, gt=gt: e.tensor_scalar(P["OH1"][:, gt * 8:(gt + 1) * 8], iota8[:, :], st1[:, 5:6], None, ALU.is_equal),
              reads=[iota8, (st1, 5)], writes=[(P["OH1"], gt)], tag=f"o1_{gt}")
        mk.op("vector", lambda e, gt=gt: e.scalar_tensor_tensor(P["msk"][:, :], P["OH1"][:, gt * 8:(gt + 1) * 8], -1e9, P["prb"][:, :], ALU.mult, ALU.add),
              reads=[(P["OH1"], gt), P["prb"]], writes=[P["msk"]], tag=f"mk_{gt}")
        mk.op("vector", lambda e: e.reduce_max(st1[:, 6:7], P["msk"][:, :], axis=mybir.AxisListType.X),
              reads=[P["msk"]], writes=[(st1, 6)], tag=f"m2_{gt}")
        mk.op("vector", lambda e: e.tensor_scalar(P["eqb"][:, :], P["msk"][:, :], st1[:, 6:7], None, ALU.is_ge),
              reads=[P["msk"], (st1, 6)], writes=[P["eqb"]], tag=f"e2_{gt}")
        mk.op("vector", lambda e: e.tensor_scalar(P["eqb"][:, :], P["eqb"][:, :], -1e9, 1e9, ALU.mult, ALU.add),
              reads=[P["eqb"]], writes=[P["eqb"]], tag=f"e2b_{gt}")
        mk.op("vector", lambda e: e.tensor_tensor(P["eqb"][:, :], P["eqb"][:, :], iota8[:, :], ALU.add),
              reads=[P["eqb"], iota8], writes=[P["eqb"]], tag=f"e2c_{gt}")
        mk.op("vector", lambda e: e.tensor_reduce(st1[:, 7:8], P["eqb"][:, :], axis=mybir.AxisListType.X, op=ALU.min),
              reads=[P["eqb"]], writes=[(st1, 7)], tag=f"i2_{gt}")
        mk.op("vector", lambda e, gt=gt: e.tensor_scalar(P["OH2"][:, gt * 8:(gt + 1) * 8], iota8[:, :], st1[:, 7:8], None, ALU.is_equal),
              reads=[iota8, (st1, 7)], writes=[(P["OH2"], gt)], tag=f"o2_{gt}")
        mk.op("vector", lambda e, gt=gt: e.tensor_copy(P["W1"][:, gt:gt + 1], st1[:, 4:5]),
              reads=[(st1, 4)], writes=[(P["W1"], gt)], tag=f"w1_{gt}")
        mk.op("vector", lambda e, gt=gt: e.tensor_copy(P["W2"][:, gt:gt + 1], st1[:, 6:7]),
              reads=[(st1, 6)], writes=[(P["W2"], gt)], tag=f"w2_{gt}")
        mk.op("vector", lambda e, gt=gt: e.tensor_tensor(P["cnt_t"][:, 0:8], P["OH1"][:, gt * 8:(gt + 1) * 8], P["OH2"][:, gt * 8:(gt + 1) * 8], ALU.add),
              reads=[(P["OH1"], gt), (P["OH2"], gt)], writes=[P["cnt_t"]], tag=f"cn_{gt}")
        mk.op("tensor", lambda e: e.transpose(P["PT2"][0:8, 0:128], P["cnt_t"][:, 0:8], P["s_identf"][:, :]),
              reads=[P["cnt_t"], P["s_identf"], P["PT2"]], writes=[P["PT2"]], tag=f"ctr_{gt}")
        mk.op("vector", lambda e, gt=gt: e.tensor_copy(P["cntT"][0:8, (gt % (NGT // 4)) * 128:((gt % (NGT // 4)) + 1) * 128], P["PT2"][0:8, 0:128]),
              reads=[P["PT2"]], writes=[(P["cntT"], gt % (NGT // 4))], tag=f"cte_{gt}")
        if gt % (NGT // 4) == (NGT // 4) - 1:
            seg = gt // (NGT // 4)
            SW = (NGT // 4) * 128
            if seg == 0:
                mk.op("vector", lambda e: e.memset(P["carry"][0:8, 0:1], 0.0), writes=[P["carry"]], tag="car0")
            mk.op("vector", lambda e, SW=SW: e.tensor_tensor_scan(
                P["scanT"][0:8, 0:SW], P["cntT"][0:8, 0:SW], P["cntT"][0:8, 0:SW], 0.0, ALU.add, ALU.bypass),
                  reads=[(P["cntT"], k2) for k2 in range(NGT // 4)], writes=[P["scanT"]], tag=f"scan_{seg}")
            mk.op("vector", lambda e, SW=SW: e.tensor_scalar(P["scanT"][0:8, 0:SW], P["scanT"][0:8, 0:SW], P["carry"][0:8, 0:1], None, ALU.add),
                  reads=[P["scanT"], P["carry"]], writes=[P["scanT"]], tag=f"scac_{seg}")
            mk.op("vector", lambda e, SW=SW: e.tensor_copy(P["carry"][0:8, 0:1], P["scanT"][0:8, SW - 1:SW]),
                  reads=[P["scanT"]], writes=[P["carry"]], tag=f"carn_{seg}")
            mk.op("vector", lambda e, SW=SW: e.tensor_tensor(P["scanT"][0:8, 0:SW], P["scanT"][0:8, 0:SW], P["cntT"][0:8, 0:SW], ALU.subtract),
                  reads=[P["scanT"]] + [(P["cntT"], k2) for k2 in range(NGT // 4)], writes=[P["scanT"]], tag=f"sce_{seg}")
            for k in range(NGT // 4):
                gt2 = seg * (NGT // 4) + k
                mk.op("tensor", lambda e, k=k: e.transpose(P["PT2"][0:128, 0:8], P["scanT"][0:8, k * 128:(k + 1) * 128], P["s_identf"][0:8, 0:8]),
                      reads=[P["scanT"], P["s_identf"], P["PT2"]], writes=[P["PT2"]], tag=f"str_{gt2}")
                mk.op("vector", lambda e, gt2=gt2: e.tensor_copy(P["S_tm"][:, gt2 * 8:(gt2 + 1) * 8], P["PT2"][0:128, 0:8]),
                      reads=[P["PT2"]], writes=[(P["S_tm"], gt2)], tag=f"ste_{gt2}")

    CT = P["ct_all"]
    for tt in range(NT):
        for nm in ["xS", "xO1", "xO2"]:
            mk.op("vector", lambda e, nm=nm: e.memset(P[nm][:, :], 0.0), writes=[P[nm]], tag=f"z{nm}_{tt}")
        mk.op("vector", lambda e: e.memset(P["xW"][:, 0:2], 0.0), writes=[P["xW"]], tag=f"zw_{tt}")
        for ci in range(NCORES):
            gt2 = ci * NT + tt
            mk.op("vector", lambda e, gt2=gt2, ci=ci: e.scalar_tensor_tensor(
                P["xS"][:, :], P["S_tm"][:, gt2 * 8:(gt2 + 1) * 8], P["s_msel"][:, ci:ci + 1], P["xS"][:, :], ALU.mult, ALU.add),
                  reads=[(P["S_tm"], gt2), P["s_msel"], P["xS"]], writes=[P["xS"]], tag=f"xs{ci}_{tt}")
            mk.op("vector", lambda e, gt2=gt2, ci=ci: e.scalar_tensor_tensor(
                P["xO1"][:, :], P["OH1"][:, gt2 * 8:(gt2 + 1) * 8], P["s_msel"][:, ci:ci + 1], P["xO1"][:, :], ALU.mult, ALU.add),
                  reads=[(P["OH1"], gt2), P["s_msel"], P["xO1"]], writes=[P["xO1"]], tag=f"xo1{ci}_{tt}")
            mk.op("vector", lambda e, gt2=gt2, ci=ci: e.scalar_tensor_tensor(
                P["xO2"][:, :], P["OH2"][:, gt2 * 8:(gt2 + 1) * 8], P["s_msel"][:, ci:ci + 1], P["xO2"][:, :], ALU.mult, ALU.add),
                  reads=[(P["OH2"], gt2), P["s_msel"], P["xO2"]], writes=[P["xO2"]], tag=f"xo2{ci}_{tt}")
            mk.op("vector", lambda e, gt2=gt2, ci=ci: e.scalar_tensor_tensor(
                P["xW"][:, 0:1], P["W1"][:, gt2:gt2 + 1], P["s_msel"][:, ci:ci + 1], P["xW"][:, 0:1], ALU.mult, ALU.add),
                  reads=[(P["W1"], gt2), P["s_msel"], P["xW"]], writes=[P["xW"]], tag=f"xw1{ci}_{tt}")
            mk.op("vector", lambda e, gt2=gt2, ci=ci: e.scalar_tensor_tensor(
                P["xW"][:, 1:2], P["W2"][:, gt2:gt2 + 1], P["s_msel"][:, ci:ci + 1], P["xW"][:, 1:2], ALU.mult, ALU.add),
                  reads=[(P["W2"], gt2), P["s_msel"], P["xW"]], writes=[P["xW"]], tag=f"xw2{ci}_{tt}")
        mk.op("vector", lambda e: e.tensor_tensor(P["eqb"][:, :], P["xS"][:, :], P["xO1"][:, :], ALU.mult),
              reads=[P["xS"], P["xO1"]], writes=[P["eqb"]], tag=f"p1a_{tt}")
        mk.op("vector", lambda e: e.reduce_sum(st1[:, 0:1], P["eqb"][:, :], axis=mybir.AxisListType.X),
              reads=[P["eqb"]], writes=[(st1, 0)], tag=f"p1b_{tt}")
        mk.op("vector", lambda e: e.tensor_scalar(st1[:, 0:1], st1[:, 0:1], CAPV, None, ALU.is_lt),
              reads=[(st1, 0)], writes=[(st1, 0)], tag=f"v1_{tt}")
        mk.op("vector", lambda e: e.tensor_tensor(st1[:, 0:1], st1[:, 0:1], P["xW"][:, 0:1], ALU.mult),
              reads=[(st1, 0), P["xW"]], writes=[(st1, 0)], tag=f"cw1_{tt}")
        mk.op("vector", lambda e: e.tensor_tensor(P["eqb"][:, :], P["xS"][:, :], P["xO2"][:, :], ALU.mult),
              reads=[P["xS"], P["xO2"]], writes=[P["eqb"]], tag=f"p2a_{tt}")
        mk.op("vector", lambda e: e.reduce_sum(st1[:, 1:2], P["eqb"][:, :], axis=mybir.AxisListType.X),
              reads=[P["eqb"]], writes=[(st1, 1)], tag=f"p2b_{tt}")
        mk.op("vector", lambda e: e.tensor_scalar(st1[:, 1:2], st1[:, 1:2], CAPV, None, ALU.is_lt),
              reads=[(st1, 1)], writes=[(st1, 1)], tag=f"v2_{tt}")
        mk.op("vector", lambda e: e.tensor_tensor(st1[:, 1:2], st1[:, 1:2], P["xW"][:, 1:2], ALU.mult),
              reads=[(st1, 1), P["xW"]], writes=[(st1, 1)], tag=f"cw2_{tt}")
        mk.op("vector", lambda e, tt=tt: e.tensor_scalar(CT[:, tt * 8:(tt + 1) * 8], P["xO1"][:, :], st1[:, 0:1], None, ALU.mult),
              reads=[P["xO1"], (st1, 0)], writes=[(CT, tt)], tag=f"c1_{tt}")
        mk.op("vector", lambda e, tt=tt: e.scalar_tensor_tensor(CT[:, tt * 8:(tt + 1) * 8], P["xO2"][:, :], st1[:, 1:2], CT[:, tt * 8:(tt + 1) * 8], ALU.mult, ALU.add),
              reads=[P["xO2"], (st1, 1), (CT, tt)], writes=[(CT, tt)], tag=f"c2_{tt}")
    mk.op("tensor", lambda e: e.transpose(P["PT2"][0:NT * 8, 0:128], CT[:, 0:NT * 8], P["s_identf"][:, :]),
          reads=[CT] + [(CT, tt2) for tt2 in range(NT)] + [P["s_identf"], P["PT2"]], writes=[P["PT2"]], tag="cttr")
    mk.op("vector", lambda e: e.tensor_copy(P["ctT"][0:NT * 8, 0:128], P["PT2"][0:NT * 8, 0:128]),
          reads=[P["PT2"]], writes=[P["ctT"]], tag="ctev")
    mk.dma("sync", "ctd", P["CTD"].rearrange("a b t -> (a b) t"), P["ctT"][0:NT * 8, 0:128],
           reads=[P["ctT"]], writes=[(P["CTD_t"], 0)], tag="ctd")

    for ex in range(8):
        emit_moe_expert(mk, P, NCH, f"f{ex}", P["L1_e_gate"][ex], P["L1_e_up"][ex],
                        P["L1_e_down"][ex], 12, True, ex, P["XN1F"], P["ACC1"], "ACC1_t")

    for tt in range(NT):
        a1 = P["fx_a"]
        a3 = P["fx_c"]
        mk.dma("sync", f"f1{tt%2}", a1[:, 0:1024], P["ACC1"][tt * 128:(tt + 1) * 128, :],
               reads=[(P["ACC1_t"], tt)], writes=[a1, (a1, 0), (a1, 1)], tag=f"f1_{tt}")
        mk.dma("gpsimd", f"f3{tt%2}", a3[:, 0:1024], P["X1OWN"][tt * 128:(tt + 1) * 128, :],
               writes=[a3, (a3, 0), (a3, 1)], tag=f"f3_{tt}")
        mk.op("vector", lambda e, a1=a1, a3=a3: e.tensor_tensor(a1[:, 0:1024], a1[:, 0:1024], a3[:, 0:1024], ALU.add),
              reads=[a3, a1], writes=[a1], tag=f"fb_{tt}")
        mk.dma("sync", f"x1d{tt%2}", P["OUT"][tt * 128:(tt + 1) * 128, :], a1[:, 0:1024],
               reads=[a1], writes=[(P["OUT_t"], tt)], tag=f"outd_{tt}")


def alloc_phaseF(nc, S, P, NCH, NCORES=8, prefix=""):
    T = NCH * 128
    OWN = T // 2
    NT = OWN // 128
    NTOK = OWN * NCORES
    NGT = NTOK // 128
    HHALF = OWN // 2

    def sb(name, shape, dt=F32):
        t = S.enter_context(nc.sbuf_tensor(prefix + name, shape, dt))
        P[name] = t
        return t

    def ps(name, shape, dt=F32):
        t = S.enter_context(nc.psum_tensor(prefix + name, shape, dt))
        P[name] = t
        return t

    sb("s_iota8", [128, 8], F32)
    sb("s_ones128", [1, 128], F32)
    sb("s_identf", [128, 128], F32)
    sb("s_msel", [128, 8], F32)
    sb("stat1", [128, 8], F32)
    sb("lg_t", [128, 8], F32)
    sb("prb", [128, 8], F32)
    sb("eqb", [128, 8], F32)
    sb("msk", [128, 8], F32)
    sb("cnt_t", [128, 8], F32)
    sb("OH1", [128, NGT * 8], F32)
    sb("OH2", [128, NGT * 8], F32)
    sb("W1", [128, NGT], F32)
    sb("W2", [128, NGT], F32)
    sb("S_tm", [128, NGT * 8], F32)
    sb("cntT", [8, (NGT // 4) * 128], F32)
    sb("scanT", [8, (NGT // 4) * 128], F32)
    sb("carry", [8, 1], F32)
    sb("xS", [128, 8], F32)
    sb("xO1", [128, 8], F32)
    sb("xO2", [128, 8], F32)
    sb("xW", [128, 2], F32)
    sb("ct_all", [128, NT * 8], F32)
    sb("ctT", [128, 128], F32)
    sb("ce_row", [1, OWN], F32)
    sb("ce_b", [128, OWN], BF16)
    sb("gate_sb", [128, 8 * 1536], BF16)
    sb("up_sb", [128, 8 * 1536], BF16)
    sb("down_sb", [128, 12 * 1024], BF16)
    P["wstage"] = [sb("wstage0", [128, 1536], F32), sb("wstage1", [128, 1536], F32)]
    sb("h2f_a", [128, 8 * HHALF], BF16)
    sb("h2f_b", [128, 8 * HHALF], BF16)
    sb("act_sb", [128, 12 * HHALF], BF16)
    sb("scrS", [128, 512], BF16)
    sb("scrS2", [128, 512], BF16)
    sb("dn_t", [128, 1024], F32)
    P["fx_a"] = P["dn_t"]
    P["fx_c"] = P["wstage"][0]
    for i in range(4):
        ps(f"PG{i}", [128, 512], F32)
        ps(f"PU{i}", [128, 512], F32)
    P["PT2"] = P["PU3"]
    for nm in ["CTD_t", "ACC1_t", "OUT_t"]:
        P[nm] = object()
    return P


def build_phaseF(NCH, NCORES=8, CAPV=4096.0):
    T = NCH * 128
    OWN = T // 2
    NTOK = OWN * NCORES
    nc = bass.Bass("TRN2", target_bir_lowering=False, debug=False)
    P = {}
    P["LOGALL"] = nc.dram_tensor("LOGALL", [NTOK, 8], F32, kind="ExternalInput").ap()
    P["XN1F"] = nc.dram_tensor("XN1F", [D, OWN], BF16, kind="ExternalInput").ap()
    P["ACC1IN"] = nc.dram_tensor("ACC1IN", [OWN, D], F32, kind="ExternalInput").ap()
    P["X1OWN"] = nc.dram_tensor("X1OWN", [OWN, D], F32, kind="ExternalInput").ap()
    P["L1_e_gate"] = nc.dram_tensor("L1_e_gate", [8, D, 1536], F32, kind="ExternalInput").ap()
    P["L1_e_up"] = nc.dram_tensor("L1_e_up", [8, D, 1536], F32, kind="ExternalInput").ap()
    P["L1_e_down"] = nc.dram_tensor("L1_e_down", [8, 1536, D], F32, kind="ExternalInput").ap()
    for nm, shape, dt in [("identf32", [128, 128], F32), ("ones1x128", [1, 128], F32),
                          ("iota8", [128, 8], F32), ("msel", [128, 8], F32)]:
        P[nm] = nc.dram_tensor(nm, shape, dt, kind="ExternalInput").ap()
    P["ACC1"] = nc.dram_tensor("ACC1W", [OWN, D], F32)
    P["CTD"] = nc.dram_tensor("CTD", [OWN // 128, 8, 128], F32)
    P["OUT"] = nc.dram_tensor("OUT", [OWN, D], F32, kind="ExternalOutput").ap()
    with contextlib.ExitStack() as S:
        alloc_phaseF(nc, S, P, NCH, NCORES)
        mk = MK(nc)
        NT = OWN // 128
        for tt in range(NT):
            mk.dma("sync", f"acs{tt%2}", P["ACC1"][tt * 128:(tt + 1) * 128, :], P["ACC1IN"][tt * 128:(tt + 1) * 128, :],
                   writes=[(P["ACC1_t"], tt)], tag=f"acs_{tt}")
        emit_phaseF(mk, P, NCH, NCORES, CAPV)
        emit_final_barrier(mk, [P["OUT_t"]])
        mk.build(S)
    return nc


def emit_final_barrier(mk, markers):
    toks = []
    for m in markers:
        for (kid, sub), st in mk.state.items():
            if kid == id(m) and st["w"] is not None:
                toks.append(st["w"])
    mk.op("sync", lambda e: e.nop(), also=toks, tag="final")




def alloc_phaseA(nc, S, P, prefix=""):
    def sb(name, shape, dt=F32):
        t = S.enter_context(nc.sbuf_tensor(prefix + name, shape, dt))
        P[name] = t
        return t

    def ps(name, shape, dt=F32):
        t = S.enter_context(nc.psum_tensor(prefix + name, shape, dt))
        P[name] = t
        return t

    for nm in ["WqT_sb", "WkT_sb", "WvT_sb", "WpT_sb", "wq_sb", "wk_sb", "wv_sb"]:
        sb(nm, [128, 8 * 1024], BF16)

    sb("s_ident", [128, 128], BF16)
    sb("s_tri", [128, 128], F32)
    sb("s_shE", [64, 128], BF16)
    sb("s_shO", [64, 128], BF16)
    sb("s_ones128", [1, 128], F32)
    P["s_ones64"] = P["s_ones128"]
    sb("s_maskA", [128, 1], F32)
    sb("s_maskB", [128, 1], F32)
    sb("s_bpA", [128, 1024], F32)
    sb("s_bpB", [128, 1024], F32)
    sb("s_n1b", [128, 1024], F32)
    S32a = S.enter_context(nc.sbuf_tensor(prefix + "S32a", [64, H * 65], F32))
    S16a = S.enter_context(nc.sbuf_tensor(prefix + "S16a", [64, H * 65], BF16))
    P["S32a_v"] = S32a
    P["S16a_v"] = S16a
    P["S32"] = [S32a[:, i * 65:(i + 1) * 65] for i in range(H)]
    P["S16"] = [S16a[:, i * 65:(i + 1) * 65] for i in range(H)]
    for nm in ["sx0", "sx1", "xn0", "xnsel0", "attmp0",
               "scrA", "scrH"]:
        sb(nm, [128, 1024], F32)
    sb("scrB", [128, 512], F32)
    P["wstage"] = [P["sx0"], P["sx1"]]
    for nm in ["xnb0", "xnf0", "kp0", "qp0",
               "hb0", "hfm0",
               "yfm0", "yfm1"]:
        sb(nm, [128, 1024], BF16)
    sb("vaug0", [128, H * 65], BF16)
    P["vaugf0"] = P["vaug0"]
    for nm in ["qpfa0", "kpfa0", "kfa0", "qfa0"]:
        sb(nm, [64, 2048], BF16)
    sb("Pm_w", [128, 512], BF16)
    sb("y_w", [64, 512], BF16)
    sb("rb_w", [64, 512], F32)
    sb("den_b", [1, 512], F32)
    sb("rec_b", [1, 512], F32)
    sb("statL", [128, 10], F32)
    sb("qkss", [128, 32], F32)
    sb("hss16", [128, 16], F32)
    sb("qkT", [32, 129], F32)
    sb("s_identf", [128, 128], F32)
    ps("PP0", [128, 512], F32)
    ps("PP1", [128, 512], F32)
    ps("PA", [128, 512], F32)
    ps("PO", [128, 512], F32)
    ps("PC", [128, 512], F32)
    ps("PY", [128, 512], F32)
    ps("PBT", [128, 1024], BF16)
    for nm in ["XN0_t", "AT0_t", "VA0_t", "KF0_t", "QF0_t", "QKM0_t"]:
        P[nm] = object()
    return P


def build_phaseA(NCH):
    T = NCH * 128
    nc = bass.Bass("TRN2", target_bir_lowering=False, debug=False)
    P = {}
    P["XIN"] = nc.dram_tensor("XIN", [T, D], F32, kind="ExternalInput").ap()
    for nm in ["L0_attn_WqT", "L0_attn_WkT", "L0_attn_WvT", "L0_attn_WpT",
               "L0_ffn_wq", "L0_ffn_wk", "L0_ffn_wv"]:
        P[nm] = nc.dram_tensor(nm, [D, D], F32, kind="ExternalInput").ap()
    P["L0_attn_bp"] = nc.dram_tensor("L0_attn_bp", [1, D], F32, kind="ExternalInput").ap()
    P["L0_ffn_norm1_w"] = nc.dram_tensor("L0_ffn_norm1_w", [1, D], F32, kind="ExternalInput").ap()
    for nm, shape, dt in [("identbf", [128, 128], BF16), ("trimask", [128, 128], F32),
                          ("shiftE", [64, 128], BF16), ("shiftO", [64, 128], BF16),
                          ("ones1x64", [1, 64], F32), ("ones1x128", [1, 128], F32),
                          ("maskA", [128, 1], F32), ("maskB", [128, 1], F32),
                          ("identf32", [128, 128], F32)]:
        P[nm] = nc.dram_tensor(nm, shape, dt, kind="ExternalInput").ap()
    P["XN0"] = nc.dram_tensor("XN0", [T // 2, D], F32, kind="ExternalOutput").ap()
    P["AT0"] = nc.dram_tensor("AT0", [T // 2, D], F32, kind="ExternalOutput").ap()
    P["VA0"] = nc.dram_tensor("VA0", [T, H * 65], BF16, kind="ExternalOutput").ap()
    P["KF0"] = nc.dram_tensor("KF0", [H, 64, T], BF16, kind="ExternalOutput").ap()
    P["QF0"] = nc.dram_tensor("QF0", [H, 64, T // 2], BF16, kind="ExternalOutput").ap()
    P["QKM0"] = nc.dram_tensor("QKM0", [32, 1], F32, kind="ExternalOutput").ap()

    with contextlib.ExitStack() as S:
        alloc_phaseA(nc, S, P)
        mk = MK(nc)
        emit_phaseA(mk, P, NCH)
        emit_final_barrier(mk, [P["XN0_t"], P["AT0_t"], P["VA0_t"], P["KF0_t"], P["QF0_t"], P["QKM0_t"]])
        mk.build(S)
    return nc


def host_constants():
    import ml_dtypes as mld
    c = {}
    c["identbf"] = np.eye(128, dtype=np.float32).astype(mld.bfloat16)
    tri = np.zeros((128, 128), dtype=np.float32)
    for j in range(128):
        tri[j, j:] = 1.0
    c["trimask"] = tri
    shE = np.zeros((64, 128), dtype=np.float32)
    shO = np.zeros((64, 128), dtype=np.float32)
    for p in range(64):
        shE[p, p] = 1.0
        shO[p, p + 64] = 1.0
    c["shiftE"] = shE.astype(mld.bfloat16)
    c["shiftO"] = shO.astype(mld.bfloat16)
    c["ones1x64"] = np.ones((1, 64), dtype=np.float32)
    c["ones1x128"] = np.ones((1, 128), dtype=np.float32)
    c["identf32"] = np.eye(128, dtype=np.float32)
    return c


def phaseA_inmap(x_full_batch, weights, h):
    c = host_constants()
    m = {
        "XIN": np.ascontiguousarray(x_full_batch),
        "L0_attn_WqT": np.ascontiguousarray(weights["L0_attn_Wq"].T),
        "L0_attn_WkT": np.ascontiguousarray(weights["L0_attn_Wk"].T),
        "L0_attn_WvT": np.ascontiguousarray(weights["L0_attn_Wv"].T),
        "L0_attn_WpT": np.ascontiguousarray(weights["L0_attn_Wp"].T),
        "L0_ffn_wq": np.ascontiguousarray(weights["L0_ffn_wq"]),
        "L0_ffn_wk": np.ascontiguousarray(weights["L0_ffn_wk"]),
        "L0_ffn_wv": np.ascontiguousarray(weights["L0_ffn_wv"]),
        "L0_attn_bp": np.ascontiguousarray(weights["L0_attn_bp"]).reshape(1, D),
        "L0_ffn_norm1_w": np.ascontiguousarray(weights["L0_ffn_norm1_w"]).reshape(1, D),
        "maskA": np.full((128, 1), 1.0 if h == 0 else 0.0, np.float32),
        "maskB": np.full((128, 1), 0.0 if h == 0 else 1.0, np.float32),
    }
    m.update(c)
    return m


def rope_tables_np(T):
    half = 32
    inv = 1.0 / (10000.0 ** (np.arange(half, dtype=np.float64) / half))
    pos = np.arange(T, dtype=np.float64)
    ang = pos[:, None] * inv[None, :]
    sin = np.tile(np.sin(ang), (1, 16)).astype(np.float32)
    cos = np.tile(np.cos(ang), (1, 16)).astype(np.float32)
    return np.ascontiguousarray(sin), np.ascontiguousarray(cos)


NCH_FULL = 32


def _kernel_unfused(**inputs):
    ins = {k: np.asarray(v, dtype=np.float32) for k, v in inputs.items()}
    NCH = NCH_FULL
    T = NCH * 128
    OWN = T // 2
    cores = list(range(8))
    c = host_constants()
    identf32 = np.eye(128, dtype=np.float32)
    iota8 = np.ascontiguousarray(np.broadcast_to(np.arange(8, dtype=np.float32), (128, 8)))
    x = ins["L0_attn_x"]

    ncA = build_phaseA(NCH)
    in_maps = []
    wA = {
        "L0_attn_WqT": np.ascontiguousarray(ins["L0_attn_Wq"].T),
        "L0_attn_WkT": np.ascontiguousarray(ins["L0_attn_Wk"].T),
        "L0_attn_WvT": np.ascontiguousarray(ins["L0_attn_Wv"].T),
        "L0_attn_WpT": np.ascontiguousarray(ins["L0_attn_Wp"].T),
        "L0_ffn_wq": np.ascontiguousarray(ins["L0_ffn_wq"]),
        "L0_ffn_wk": np.ascontiguousarray(ins["L0_ffn_wk"]),
        "L0_ffn_wv": np.ascontiguousarray(ins["L0_ffn_wv"]),
        "L0_attn_bp": np.ascontiguousarray(ins["L0_attn_bp"]).reshape(1, D),
        "L0_ffn_norm1_w": np.ascontiguousarray(ins["L0_ffn_norm1_w"]).reshape(1, D),
    }
    wA.update(c)
    for ci in cores:
        b, h = ci // 2, ci % 2
        m = dict(wA)
        m["XIN"] = np.ascontiguousarray(x[b])
        m["maskA"] = np.full((128, 1), 1.0 if h == 0 else 0.0, np.float32)
        m["maskB"] = np.full((128, 1), 0.0 if h == 0 else 1.0, np.float32)
        in_maps.append(m)
    resA = run_bass_kernel_spmd(ncA, in_maps, core_ids=cores)
    outA = resA.results

    ncB = build_phaseB(NCH)
    wB = {
        "L0_ffn_wo": np.ascontiguousarray(ins["L0_ffn_wo"]),
        "L0_ffn_router_w": np.ascontiguousarray(ins["L0_ffn_router_w"]),
        "L0_ffn_norm2_w": np.ascontiguousarray(ins["L0_ffn_norm2_w"]).reshape(1, D),
        "identf32": identf32,
        "ones1x64": c["ones1x64"], "ones1x128": c["ones1x128"],
        "shiftE": c["shiftE"], "shiftO": c["shiftO"],
    }
    in_maps = []
    for ci in cores:
        m = dict(wB)
        for nm in ["KF0", "QF0", "VA0", "XN0", "QKM0"]:
            m[nm] = outA[ci][nm]
        in_maps.append(m)
    resB = run_bass_kernel_spmd(ncB, in_maps, core_ids=cores)
    outB = resB.results

    ncC = build_phaseC(NCH)
    wC = {
        "L0_ffn_sh_gate": np.ascontiguousarray(ins["L0_ffn_sh_gate"]),
        "L0_ffn_sh_up": np.ascontiguousarray(ins["L0_ffn_sh_up"]),
        "L0_ffn_sh_down": np.ascontiguousarray(ins["L0_ffn_sh_down"]),
        "L0_ffn_e_gate": np.ascontiguousarray(ins["L0_ffn_e_gate"]),
        "L0_ffn_e_up": np.ascontiguousarray(ins["L0_ffn_e_up"]),
        "L0_ffn_e_down": np.ascontiguousarray(ins["L0_ffn_e_down"]),
        "identf32": identf32, "ones1x128": c["ones1x128"], "iota8": iota8,
    }
    in_maps = []
    for ci in cores:
        b, h = ci // 2, ci % 2
        m = dict(wC)
        m["H2F"] = outB[ci]["H2F"]
        m["LOG0"] = outB[ci]["LOG0"]
        m["ACCIN"] = outB[ci]["ACC"]
        m["AT0"] = outA[ci]["AT0"]
        m["XOWN"] = np.ascontiguousarray(x[b, h * OWN:(h + 1) * OWN])
        in_maps.append(m)
    resC = run_bass_kernel_spmd(ncC, in_maps, core_ids=cores)
    outC = resC.results

    ncD = build_phaseD(NCH)
    Wqkv = ins["L1_attn_Wqkv"]
    sint, cost = rope_tables_np(T)
    wD = {
        "L1_WQ": np.ascontiguousarray(Wqkv[:, 0:D]),
        "L1_WK": np.ascontiguousarray(Wqkv[:, D:2 * D]),
        "L1_WV": np.ascontiguousarray(Wqkv[:, 2 * D:3 * D]),
        "L1_ffn_w_gate": np.ascontiguousarray(ins["L1_ffn_w_gate"]),
        "SINT": sint, "COST": cost,
        "identbf": c["identbf"], "identf32": identf32,
    }
    in_maps = []
    for ci in cores:
        b, h = ci // 2, ci % 2
        m = dict(wD)
        m["X1F"] = np.concatenate([np.asarray(outC[2 * b]["X1"], np.float32),
                                   np.asarray(outC[2 * b + 1]["X1"], np.float32)], axis=0)
        m["maskA"] = np.full((128, 1), 1.0 if h == 0 else 0.0, np.float32)
        m["maskB"] = np.full((128, 1), 0.0 if h == 0 else 1.0, np.float32)
        in_maps.append(m)
    resD = run_bass_kernel_spmd(ncD, in_maps, core_ids=cores)
    outD = resD.results

    ncE = build_phaseE(NCH)
    wE = {
        "L1_attn_Wo": np.ascontiguousarray(ins["L1_attn_Wo"]),
        "L1_attn_bo": np.ascontiguousarray(ins["L1_attn_bo"]).reshape(1, D),
        "ones1x64": c["ones1x64"], "ones1x128": c["ones1x128"],
        "shiftE": c["shiftE"], "shiftO": c["shiftO"],
        "neg18": np.full((128, 1), -0.125, np.float32),
    }
    in_maps = []
    for ci in cores:
        m = dict(wE)
        for nm in ["KF1", "QF1", "VA1"]:
            m[nm] = outD[ci][nm]
        in_maps.append(m)
    resE = run_bass_kernel_spmd(ncE, in_maps, core_ids=cores)
    outE = resE.results

    ncF = build_phaseF(NCH)
    c_fc = ins["L1_ffn_c_fc"]
    wF = {
        "L1_e_gate": np.ascontiguousarray(c_fc[:, :, 0:1536]),
        "L1_e_up": np.ascontiguousarray(c_fc[:, :, 1536:3072]),
        "L1_e_down": np.ascontiguousarray(ins["L1_ffn_c_proj"]),
        "identf32": identf32, "ones1x128": c["ones1x128"], "iota8": iota8,
    }
    LOGALL = np.concatenate([np.asarray(outD[ci]["LOG1"], np.float32) for ci in cores], axis=0)
    in_maps = []
    for ci in cores:
        m = dict(wF)
        m["LOGALL"] = LOGALL
        m["XN1F"] = outD[ci]["XN1F"]
        m["ACC1IN"] = np.asarray(outE[ci]["ACC1"], np.float32)
        m["X1OWN"] = np.asarray(outC[ci]["X1"], np.float32)
        msel = np.zeros((128, 8), np.float32)
        msel[:, ci] = 1.0
        m["msel"] = msel
        in_maps.append(m)
    resF = run_bass_kernel_spmd(ncF, in_maps, core_ids=cores)
    outF = resF.results

    out = np.zeros((4, 2 * OWN, D), np.float32)
    for ci in cores:
        b, h = ci // 2, ci % 2
        out[b, h * OWN:(h + 1) * OWN] = np.asarray(outF[ci]["OUT"], np.float32)
    return out




def build_fused(NCH, NCORES=8):
    T = NCH * 128
    OWN = T // 2
    nc = bass.Bass("TRN2", target_bir_lowering=False, debug=False)
    X = {}

    def ein(name, shape, dt=F32):
        X[name] = nc.dram_tensor(name, shape, dt, kind="ExternalInput").ap()
        return X[name]

    def internal(name, shape, dt=F32):
        X[name] = nc.dram_tensor(name, shape, dt).ap()
        return X[name]

    ein("XIN", [T, D])
    ein("XOWN", [OWN, D])
    for nm in ["L0_attn_WqT", "L0_attn_WkT", "L0_attn_WvT", "L0_attn_WpT",
               "L0_ffn_wq", "L0_ffn_wk", "L0_ffn_wv", "L0_ffn_wo",
               "L1_WQ", "L1_WK", "L1_WV", "L1_attn_Wo"]:
        ein(nm, [D, D])
    ein("L0_attn_bp", [1, D])
    ein("L0_ffn_norm1_w", [1, D])
    ein("L0_ffn_norm2_w", [1, D])
    ein("L0_ffn_router_w", [D, 8])
    ein("L1_ffn_w_gate", [D, 8])
    ein("L1_attn_bo", [1, D])
    ein("L0_ffn_sh_gate", [D, 2048])
    ein("L0_ffn_sh_up", [D, 2048])
    ein("L0_ffn_sh_down", [2048, D])
    ein("L0_ffn_e_gate", [8, D, 2048])
    ein("L0_ffn_e_up", [8, D, 2048])
    ein("L0_ffn_e_down", [8, 2048, D])
    ein("L1_e_gate", [8, D, 1536])
    ein("L1_e_up", [8, D, 1536])
    ein("L1_e_down", [8, 1536, D])
    ein("SINT", [T, 512])
    ein("COST", [T, 512])
    ein("identbf", [128, 128], BF16)
    ein("identf32", [128, 128])
    ein("trimask", [128, 128])
    ein("shiftE", [64, 128], BF16)
    ein("shiftO", [64, 128], BF16)
    ein("ones1x64", [1, 64])
    ein("ones1x128", [1, 128])
    ein("maskA", [128, 1])
    ein("maskB", [128, 1])
    ein("iota8", [128, 8])
    ein("msel", [128, 8])
    ein("neg18", [128, 1])
    XN0h = nc.dram_tensor("XN0", [OWN, D], F32)
    AT0h = nc.dram_tensor("AT0", [OWN, D], F32)
    VA0h = nc.dram_tensor("VA0", [T, H * 65], BF16)
    KF0h = nc.dram_tensor("KF0", [H, 64, T], BF16)
    QF0h = nc.dram_tensor("QF0", [H, 64, OWN], BF16)
    QKM0h = nc.dram_tensor("QKM0", [32, 1], F32)
    ACCh = nc.dram_tensor("ACCW", [OWN, D], F32)
    H2Th = nc.dram_tensor("H2T", [OWN, D], BF16)
    H2Fh = nc.dram_tensor("H2F", [D, OWN], BF16)
    LOG0h = nc.dram_tensor("LOG0", [OWN, 8], F32)
    CTDCh = nc.dram_tensor("CTDC", [OWN // 128, 8, 128], F32)
    X1h = nc.dram_tensor("X1", [OWN, D], F32)
    X1Fh = nc.dram_tensor("X1F", [T, D], F32)
    KF1h = nc.dram_tensor("KF1", [H, 64, T], BF16)
    QF1h = nc.dram_tensor("QF1", [H, 64, OWN], BF16)
    VA1h = nc.dram_tensor("VA1", [T, H * 65], BF16)
    XN1Fh = nc.dram_tensor("XN1F", [D, OWN], BF16)
    LOG1h = nc.dram_tensor("LOG1", [OWN, 8], F32)
    LOGALLh = nc.dram_tensor("LOGALL", [OWN * NCORES, 8], F32)
    ACC1h = nc.dram_tensor("ACC1W", [OWN, D], F32)
    CTDFh = nc.dram_tensor("CTDF", [OWN // 128, 8, 128], F32)
    OUTh = nc.dram_tensor("OUT", [OWN, D], F32, kind="ExternalOutput")

    mk = MK(nc)
    with contextlib.ExitStack() as S_outer:
        with contextlib.ExitStack() as S:
            P = dict(X)
            P["XN0"] = XN0h.ap()
            P["AT0"] = AT0h.ap()
            P["VA0"] = VA0h.ap()
            P["KF0"] = KF0h.ap()
            P["QF0"] = QF0h.ap()
            P["QKM0"] = QKM0h.ap()
            for nm in ["XN0_t", "AT0_t", "VA0_t", "KF0_t", "QF0_t", "QKM0_t"]:
                P[nm] = object()
            alloc_phaseA(nc, S, P, prefix="pa_")
            emit_phaseA(mk, P, NCH)
        mk.phase_barrier("barA")
        with contextlib.ExitStack() as S:
            P = dict(X)
            P["KF0"] = KF0h.ap()
            P["QF0"] = QF0h.ap()
            P["VA0"] = VA0h.ap()
            P["XN0"] = XN0h.ap()
            P["QKM0"] = QKM0h.ap()
            P["ACC"] = ACCh.ap()
            P["H2T"] = H2Th.ap()
            P["H2F"] = H2Fh.ap()
            P["LOG0"] = LOG0h.ap()
            for nm in ["ACC_t", "H2T_t", "H2F_t", "LOG0_t"]:
                P[nm] = object()
            alloc_phaseB(nc, S, P, NCH, prefix="pb_")
            emit_phaseB(mk, P, NCH)
        mk.phase_barrier("barB")
        with contextlib.ExitStack() as S:
            P = dict(X)
            P["H2F"] = H2Fh.ap()
            P["LOG0"] = LOG0h.ap()
            P["ACC"] = ACCh.ap()
            P["AT0"] = AT0h.ap()
            P["CTD"] = CTDCh.ap()
            P["X1"] = X1h.ap()
            for nm in ["CTD_t", "ACC_t", "X1_t"]:
                P[nm] = object()
            alloc_phaseC(nc, S, P, NCH, prefix="pc_")
            emit_phaseC(mk, P, NCH)
        mk.phase_barrier("barC")
        mk.op("gpsimd", lambda e: e.collective_compute(
            "AllGather", ALU.bypass,
            replica_groups=[[2 * i, 2 * i + 1] for i in range(NCORES // 2)],
            ins=[X1h.ap().opt()], outs=[X1Fh.ap().opt()]),
            tag="ccX1")
        mk.phase_barrier("barCC1")
        with contextlib.ExitStack() as S:
            P = dict(X)
            P["X1F"] = X1Fh.ap()
            P["KF1"] = KF1h.ap()
            P["QF1"] = QF1h.ap()
            P["VA1"] = VA1h.ap()
            P["XN1F"] = XN1Fh.ap()
            P["LOG1"] = LOG1h.ap()
            for nm in ["XN1F_t", "LOG1_t", "QF1_t", "KF1_t", "VA1_t"]:
                P[nm] = object()
            alloc_phaseD(nc, S, P, NCH, prefix="pd_")
            emit_phaseD(mk, P, NCH)
        mk.phase_barrier("barD")
        mk.op("gpsimd", lambda e: e.collective_compute(
            "AllGather", ALU.bypass,
            replica_groups=[list(range(NCORES))],
            ins=[LOG1h.ap().opt()], outs=[LOGALLh.ap().opt()]),
            tag="ccLG")
        mk.phase_barrier("barCC2")
        with contextlib.ExitStack() as S:
            P = dict(X)
            P["KF1"] = KF1h.ap()
            P["QF1"] = QF1h.ap()
            P["VA1"] = VA1h.ap()
            P["ACC1"] = ACC1h.ap()
            P["ACC1_t"] = object()
            alloc_phaseE(nc, S, P, NCH, prefix="pe_")
            emit_phaseE(mk, P, NCH)
        mk.phase_barrier("barE")
        with contextlib.ExitStack() as S:
            P = dict(X)
            P["LOGALL"] = LOGALLh.ap()
            P["XN1F"] = XN1Fh.ap()
            P["ACC1"] = ACC1h.ap()
            P["X1OWN"] = X1h.ap()
            P["CTD"] = CTDFh.ap()
            P["OUT"] = OUTh.ap()
            for nm in ["CTD_t", "ACC1_t", "OUT_t"]:
                P[nm] = object()
            alloc_phaseF(nc, S, P, NCH, NCORES, prefix="pf_")
            emit_phaseF(mk, P, NCH, NCORES)
            emit_final_barrier(mk, [P["OUT_t"]])
        mk.build(S_outer)
    return nc


def stage_fused_inmaps(ins, NCH, NCORES=8):
    T = NCH * 128
    OWN = T // 2
    c = host_constants()
    iota8 = np.ascontiguousarray(np.broadcast_to(np.arange(8, dtype=np.float32), (128, 8)))
    sint, cost = rope_tables_np(T)
    Wqkv = ins["L1_attn_Wqkv"]
    c_fc = ins["L1_ffn_c_fc"]
    x = ins["L0_attn_x"]
    shared = {
        "L0_attn_WqT": np.ascontiguousarray(ins["L0_attn_Wq"].T),
        "L0_attn_WkT": np.ascontiguousarray(ins["L0_attn_Wk"].T),
        "L0_attn_WvT": np.ascontiguousarray(ins["L0_attn_Wv"].T),
        "L0_attn_WpT": np.ascontiguousarray(ins["L0_attn_Wp"].T),
        "L0_ffn_wq": np.ascontiguousarray(ins["L0_ffn_wq"]),
        "L0_ffn_wk": np.ascontiguousarray(ins["L0_ffn_wk"]),
        "L0_ffn_wv": np.ascontiguousarray(ins["L0_ffn_wv"]),
        "L0_ffn_wo": np.ascontiguousarray(ins["L0_ffn_wo"]),
        "L1_WQ": np.ascontiguousarray(Wqkv[:, 0:D]),
        "L1_WK": np.ascontiguousarray(Wqkv[:, D:2 * D]),
        "L1_WV": np.ascontiguousarray(Wqkv[:, 2 * D:3 * D]),
        "L1_attn_Wo": np.ascontiguousarray(ins["L1_attn_Wo"]),
        "L0_attn_bp": np.ascontiguousarray(ins["L0_attn_bp"]).reshape(1, D),
        "L0_ffn_norm1_w": np.ascontiguousarray(ins["L0_ffn_norm1_w"]).reshape(1, D),
        "L0_ffn_norm2_w": np.ascontiguousarray(ins["L0_ffn_norm2_w"]).reshape(1, D),
        "L0_ffn_router_w": np.ascontiguousarray(ins["L0_ffn_router_w"]),
        "L1_ffn_w_gate": np.ascontiguousarray(ins["L1_ffn_w_gate"]),
        "L1_attn_bo": np.ascontiguousarray(ins["L1_attn_bo"]).reshape(1, D),
        "L0_ffn_sh_gate": np.ascontiguousarray(ins["L0_ffn_sh_gate"]),
        "L0_ffn_sh_up": np.ascontiguousarray(ins["L0_ffn_sh_up"]),
        "L0_ffn_sh_down": np.ascontiguousarray(ins["L0_ffn_sh_down"]),
        "L0_ffn_e_gate": np.ascontiguousarray(ins["L0_ffn_e_gate"]),
        "L0_ffn_e_up": np.ascontiguousarray(ins["L0_ffn_e_up"]),
        "L0_ffn_e_down": np.ascontiguousarray(ins["L0_ffn_e_down"]),
        "L1_e_gate": np.ascontiguousarray(c_fc[:, :, 0:1536]),
        "L1_e_up": np.ascontiguousarray(c_fc[:, :, 1536:3072]),
        "L1_e_down": np.ascontiguousarray(ins["L1_ffn_c_proj"]),
        "SINT": sint, "COST": cost,
        "identbf": c["identbf"], "identf32": c["identf32"],
        "trimask": c["trimask"], "shiftE": c["shiftE"], "shiftO": c["shiftO"],
        "ones1x64": c["ones1x64"], "ones1x128": c["ones1x128"],
        "iota8": iota8,
        "neg18": np.full((128, 1), -0.125, np.float32),
    }
    in_maps = []
    for ci in range(NCORES):
        b, h = ci // 2, ci % 2
        m = dict(shared)
        m["XIN"] = np.ascontiguousarray(x[b])
        m["XOWN"] = np.ascontiguousarray(x[b, h * OWN:(h + 1) * OWN])
        m["maskA"] = np.full((128, 1), 1.0 if h == 0 else 0.0, np.float32)
        m["maskB"] = np.full((128, 1), 0.0 if h == 0 else 1.0, np.float32)
        msel = np.zeros((128, 8), np.float32)
        msel[:, ci] = 1.0
        m["msel"] = msel
        in_maps.append(m)
    return in_maps


def kernel(**inputs):
    ins = {k: np.asarray(v, dtype=np.float32) for k, v in inputs.items()}
    NCH = NCH_FULL
    T = NCH * 128
    OWN = T // 2
    nc = build_fused(NCH)
    in_maps = stage_fused_inmaps(ins, NCH)
    res = run_bass_kernel_spmd(nc, in_maps, core_ids=list(range(8)))
    out = np.zeros((4, T, D), np.float32)
    for ci in range(8):
        b, h = ci // 2, ci % 2
        out[b, h * OWN:(h + 1) * OWN] = np.asarray(res.results[ci]["OUT"], np.float32)
    return out
